# version 03
# baseline (speedup 1.0000x reference)
import contextlib
import numpy as np
import ml_dtypes
import concourse.bass as bass
import concourse.mybir as mybir
from concourse.bass_utils import run_bass_kernel_spmd

F32 = mybir.dt.float32
F32R = mybir.dt.float32r
BF16 = mybir.dt.bfloat16
U32 = mybir.dt.uint32
AF = mybir.ActivationFunctionType
ALU = mybir.AluOpType
AX = mybir.AxisListType.X
bfdt = ml_dtypes.bfloat16

ENG = ("sync", "scalar", "vector", "tensor", "gpsimd")
QSEM = {"sync": "dS", "scalar": "dA", "gpsimd": "dP"}
NSLOT = 6

T_FULL = 4096
D = 1024
NDC = 8
TC = 512
NTC = T_FULL // TC
MAGIC = 12582912.0


class P:

    def __init__(self, nc, stack):
        self.nc = nc
        self.stack = stack
        self.ops = {e: [] for e in ENG}
        self.sem = {}
        self.cnt = {}
        self.waited = {e: {} for e in ENG}
        self.res = {}
        self.dma_idx = {}
        for e in ENG:
            self._mk("s_" + e)
        for q in set(QSEM.values()):
            self.dma_idx[q] = 0
            for i in range(NSLOT):
                self._mk(f"{q}{i}")

    def _mk(self, name):
        self.sem[name] = self.stack.enter_context(self.nc.semaphore(name))
        self.cnt[name] = 0

    def _deps(self, e, reads, writes):
        m = {}
        for r in reads:
            d = self.res.get(r)
            if d and d["w"] is not None:
                s, v = d["w"]
                m[s] = max(m.get(s, 0), v)
        for w in writes:
            d = self.res.get(w)
            if d:
                for s, v in ([d["w"]] if d["w"] else []) + d["r"]:
                    m[s] = max(m.get(s, 0), v)
        own = "s_" + e
        out = []
        for s, v in m.items():
            if s == own and e == "tensor":
                continue
            if s == own and self.cnt[own] < v:
                raise RuntimeError("self-dep on not-yet-emitted token")
            if self.waited[e].get(s, 0) >= v:
                continue
            out.append((s, v))
        return out

    def emit(self, e, reads, writes, fn, dma=False):
        waits = self._deps(e, reads, writes)
        if dma:
            q = QSEM[e]
            sname = f"{q}{self.dma_idx[q] % NSLOT}"
            self.dma_idx[q] += 1
            amount = 16
            prev = self.cnt[sname]
            if prev > 0 and self.waited[e].get(sname, 0) < prev and \
                    all(s != sname for s, _ in waits):
                waits = waits + [(sname, prev)]
        else:
            sname, amount = "s_" + e, 1
        self.cnt[sname] += amount
        tok = (sname, self.cnt[sname])
        for s, v in waits:
            self.waited[e][s] = max(self.waited[e].get(s, 0), v)
        sem = self.sem[sname]

        def closure(eng, waits=waits, sem=sem, amount=amount, fn=fn):
            for s, v in waits:
                eng.wait_ge(self.sem[s], v)
            fn(eng).then_inc(sem, amount)

        self.ops[e].append(closure)
        for r in reads:
            self.res.setdefault(r, {"w": None, "r": []})["r"].append(tok)
        for w in writes:
            d = self.res.setdefault(w, {"w": None, "r": []})
            d["w"] = tok
            d["r"] = []
        return tok

    def barrier(self):
        snap = dict(self.cnt)
        for e in ENG:
            for s, v in snap.items():
                if v > 0 and self.waited[e].get(s, 0) < v:
                    self.ops[e].append(
                        lambda eng, s=s, v=v: eng.wait_ge(self.sem[s], v))
                    self.waited[e][s] = v
        for d in self.res.values():
            d["r"] = []

    def wait_all(self, e="sync"):
        for s in list(self.cnt):
            v = self.cnt[s]
            if v > 0 and self.waited[e].get(s, 0) < v:
                self.ops[e].append(lambda eng, s=s, v=v: eng.wait_ge(self.sem[s], v))
                self.waited[e][s] = v

    def replay(self):
        with self.nc.Block() as block:
            @block.sync
            def _(eng):
                for f in self.ops["sync"]:
                    f(eng)

            @block.scalar
            def _(eng):
                for f in self.ops["scalar"]:
                    f(eng)

            @block.vector
            def _(eng):
                for f in self.ops["vector"]:
                    f(eng)

            @block.tensor
            def _(eng):
                for f in self.ops["tensor"]:
                    f(eng)

            @block.gpsimd
            def _(eng):
                for f in self.ops["gpsimd"]:
                    f(eng)


class Ctx:
    def __init__(self, nc, stack):
        self.nc = nc
        self.stack = stack
        self.p = P(nc, stack)
        self.pb = [stack.enter_context(nc.psum_tensor(f"pbank{i}", [128, 512], F32))
                   for i in range(8)]
        self._uid = 0

    def uid(self):
        self._uid += 1
        return self._uid



def dma_in(ctx, dst_ap, src_ap, res_w, res_r=(), eng="sync"):
    return ctx.p.emit(eng, list(res_r), [res_w],
                      lambda g: g.dma_start(out=dst_ap, in_=src_ap), dma=True)


def dma_out(ctx, dst_ap, src_ap, res_r=(), res_w="dram_out", eng="scalar"):
    return ctx.p.emit(eng, list(res_r), [res_w],
                      lambda g: g.dma_start(out=dst_ap, in_=src_ap), dma=True)


def mm(ctx, plist, reads, writes):
    def fn(t):
        inst = None
        for (o, l, r, st, sp) in plist:
            inst = t.matmul(o, l, r, start=st, stop=sp)
        return inst
    return ctx.p.emit("tensor", reads, writes, fn)


def kacc(ctx, out_ap, lhsT_list, rhs_list, reads, writes):
    n = len(lhsT_list)
    return mm(ctx, [(out_ap, lhsT_list[i], rhs_list[i], i == 0, i == n - 1)
                    for i in range(n)], reads, writes)


def kacc2(ctx, out_ap, lhsT_list, rhs_list, reads, writes, start, stop):
    n = len(lhsT_list)
    return mm(ctx, [(out_ap, lhsT_list[i], rhs_list[i], start and i == 0,
                     stop and i == n - 1) for i in range(n)], reads, writes)


def transpose(ctx, out_psum_ap, in_ap, ident_ap, reads, writes):
    return ctx.p.emit("tensor", reads, writes,
                      lambda t: t.transpose(out_psum_ap, in_ap, ident_ap))


def vop(ctx, fn, reads, writes, eng="vector"):
    return ctx.p.emit(eng, reads, writes, fn)


@contextlib.contextmanager
def scope(ctx):
    with contextlib.ExitStack() as bs:
        yield lambda n, s, d: bs.enter_context(
            ctx.nc.sbuf_tensor(n, s, d, allow_name_mangling=True))
    ctx.p.barrier()


def chain(ctx, fns, reads, writes, eng="vector"):
    u = f"ch{ctx.uid()}"
    n = len(fns)
    tok = None
    for i, f in enumerate(fns):
        r = list(reads) + ([u] if i > 0 else [])
        w = ([u] if i < n - 1 else list(writes) + [u])
        tok = ctx.p.emit(eng, r, w, f)
    return tok



class LNTmp:
    def __init__(self, ctx, sb, tag, tc=TC):
        t = tag
        self.tag = t
        self.tc = tc
        self.xb = sb(t + "ln_xb", [128, tc], BF16)
        self.xsq = sb(t + "ln_xsq", [128, tc], BF16)
        self.rowS = sb(t + "ln_rowS", [1, tc], F32)[0:1, :]
        self.rowQ = sb(t + "ln_rowQ", [1, tc], F32)[0:1, :]
        self.rstd = sb(t + "ln_rstd", [1, tc], F32)[0:1, :]
        self.mneg = sb(t + "ln_mneg", [1, tc], F32)[0:1, :]
        self.rowbA = sb(t + "ln_rowbA", [1, tc], BF16)[0:1, :]
        self.rowbB = sb(t + "ln_rowbB", [1, tc], BF16)[0:1, :]
        self.rbc = sb(t + "ln_rbc", [128, 2 * tc], BF16)
        self.tmp = sb(t + "ln_tmp", [128, tc], F32)
        self.ones_bf = sb(t + "ln_ones", [128, 1], BF16)
        self.ones1_bf = sb(t + "ln_ones1", [1, 128], BF16)
        self.eps = sb(t + "ln_eps", [1, 1], F32)
        vop(ctx, lambda v: v.memset(self.ones_bf[:, :], 1.0), [], [t + "ln_ones"])
        vop(ctx, lambda v: v.memset(self.ones1_bf[:, :], 1.0), [], [t + "ln_ones1"])


def emit_ln(ctx, ln, x_chunks, x_resl, out_bf, out_resl, out_f32=None,
            out_f32_resl=None, eps=1e-5, affine=None, affine_res=()):
    tc = ln.tc
    t = ln.tag
    pass
    pbS, pbQ, pbB = ctx.pb[6], ctx.pb[7], ctx.pb[5]
    nd = len(x_chunks)
    for i in range(nd):
        vop(ctx, lambda v, i=i: v.tensor_copy(ln.xb[:, :tc], x_chunks[i]),
            [x_resl[i]], [t + "ln_xb"])
        vop(ctx, lambda s, i=i: s.activation(ln.xsq[:, :tc], x_chunks[i], AF.Square),
            [x_resl[i]], [t + "ln_xsq"], eng="scalar")
        mm(ctx, [(pbS[0:1, :tc], ln.ones_bf[:, :], ln.xb[:, :tc], i == 0, i == nd - 1)],
           [t + "ln_xb", t + "ln_ones"], ["psum6"])
        mm(ctx, [(pbQ[0:1, :tc], ln.ones_bf[:, :], ln.xsq[:, :tc], i == 0, i == nd - 1)],
           [t + "ln_xsq", t + "ln_ones"], ["psum7"])

    chain(ctx, [
        lambda v: v.tensor_scalar_mul(ln.rowS[0:1, :tc], pbS[0:1, :tc], 1.0 / (nd * 128)),
        lambda v: v.tensor_mul(ln.mneg[0:1, :tc], ln.rowS[0:1, :tc], ln.rowS[0:1, :tc]),
        lambda v: v.scalar_tensor_tensor(ln.rowQ[0:1, :tc], pbQ[0:1, :tc],
                                         1.0 / (nd * 128), ln.mneg[0:1, :tc],
                                         ALU.mult, ALU.subtract),
    ], ["psum6", "psum7"], [t + "ln_rows"])
    vop(ctx, lambda v: v.memset(ln.eps[:, :], float(eps)), [], [ln.tag + "ln_eps"])
    vop(ctx, lambda s: s.activation(ln.rowQ[0:1, :tc], ln.rowQ[0:1, :tc], AF.Sqrt,
                                    bias=ln.eps[0:1, 0:1]),
        [t + "ln_rows", ln.tag + "ln_eps"], [t + "ln_rows"], eng="scalar")
    chain(ctx, [
        lambda v: v.reciprocal(ln.rstd[0:1, :tc], ln.rowQ[0:1, :tc]),
        lambda v: v.scalar_tensor_tensor(ln.mneg[0:1, :tc], ln.rowS[0:1, :tc], -1.0,
                                         ln.rstd[0:1, :tc], ALU.mult, ALU.mult),
        lambda v: v.tensor_copy(ln.rowbA[0:1, :tc], ln.rstd[0:1, :tc]),
        lambda v: v.tensor_copy(ln.rowbB[0:1, :tc], ln.mneg[0:1, :tc]),
    ], [t + "ln_rows"], [t + "ln_r2b"])
    mm(ctx, [(pbB[:, :tc], ln.ones1_bf[0:1, :], ln.rowbA[0:1, :tc], True, True)],
       [t + "ln_r2b", t + "ln_ones1"], ["psum5"])
    vop(ctx, lambda v: v.tensor_copy(ln.rbc[:, 0:tc], pbB[:, :tc]),
        ["psum5"], [t + "ln_rbcF"])
    mm(ctx, [(pbB[:, :tc], ln.ones1_bf[0:1, :], ln.rowbB[0:1, :tc], True, True)],
       [t + "ln_r2b", "psum5"], ["psum5"])
    vop(ctx, lambda v: v.tensor_copy(ln.rbc[:, tc:2 * tc], pbB[:, :tc]),
        ["psum5"], [t + "ln_rbcF"])
    rb = ln.rbc[:, 0:tc]
    mb = ln.rbc[:, tc:2 * tc]
    for i in range(nd):
        wl = (([out_resl[i]] if out_bf is not None else [])
              + ([out_f32_resl[i]] if out_f32 is not None else []))
        fns = [lambda v, i=i: v.tensor_mul(ln.tmp[:, :tc], x_chunks[i], rb)]
        if affine is not None:
            ga, be = affine
            fns.append(lambda v, i=i: v.tensor_add(ln.tmp[:, :tc], ln.tmp[:, :tc],
                                                   mb))
            fns.append(lambda v, i=i, ga=ga, be=be: v.scalar_tensor_tensor(
                out_bf[i], ln.tmp[:, :tc], ga[i],
                be[i].broadcast_to([128, tc]), ALU.mult, ALU.add))
        elif out_f32 is not None:
            fns.append(lambda v, i=i: v.tensor_add(out_f32[i], ln.tmp[:, :tc], mb))
            if out_bf is not None:
                fns.append(lambda v, i=i: v.tensor_copy(out_bf[i], out_f32[i]))
        else:
            fns.append(lambda v, i=i: v.tensor_add(out_bf[i], ln.tmp[:, :tc], mb))
        chain(ctx, fns, [x_resl[i], t + "ln_rbcF", t + "ln_tmpres"] + list(affine_res),
              wl + [t + "ln_tmpres"])



class QTmp:
    def __init__(self, sb, tag, width):
        w = width
        t = tag
        self.w = w
        self.t1 = sb(t + "q_t1", [128, w], F32)
        self.t2 = sb(t + "q_t2", [128, w], F32)
        self.sc = sb(t + "q_sc", [128, max(w // 32, 32)], F32)
        self.inv = sb(t + "q_inv", [128, max(w // 32, 32)], F32)
        self.scE = sb(t + "q_scE", [128, w], F32)
        self.invE = sb(t + "q_invE", [128, w], F32)
        self.q8 = sb(t + "q_q8", [128, w], F32)
        self.q4 = sb(t + "q_q4", [128, w], F32)
        self.cm = sb(t + "q_cm", [128, w], F32)
        self.res = t + "qtmp"


def emit_quant(ctx, qt, x_ap, out_ap, width, mode, reads, writes):
    w = width
    nb = w // 32
    fns = []
    a = fns.append
    if mode == "part":
        a(lambda v: v.transpose(qt.t1[:, :w], x_ap))
        src = qt.t1[:, :w]
    else:
        src = x_ap
    a(lambda v: v.tensor_reduce(qt.sc[:, :nb], src.rearrange("p (b j) -> p b j", j=32),
                                axis=AX, op=ALU.max, apply_absolute_value=True))
    a(lambda v: v.tensor_scalar_mul(qt.sc[:, :nb], qt.sc[:, :nb], 1.0 / 7.5))
    a(lambda v: v.tensor_scalar_max(qt.sc[:, :nb], qt.sc[:, :nb], 1e-30))
    a(lambda v: v.reciprocal(qt.inv[:, :nb], qt.sc[:, :nb]))
    a(lambda v: v.tensor_copy(qt.scE[:, :w].rearrange("p (b j) -> p b j", j=32),
                              qt.sc[:, :nb].unsqueeze(2).broadcast_to([128, nb, 32])))
    a(lambda v: v.tensor_copy(qt.invE[:, :w].rearrange("p (b j) -> p b j", j=32),
                              qt.inv[:, :nb].unsqueeze(2).broadcast_to([128, nb, 32])))
    if mode == "part":
        a(lambda v: v.transpose(qt.t1[:, :w], qt.scE[:, :w]))
        a(lambda v: v.transpose(qt.t2[:, :w], qt.invE[:, :w]))
        a(lambda v: v.tensor_copy(qt.scE[:, :w], qt.t1[:, :w]))
        a(lambda v: v.tensor_copy(qt.invE[:, :w], qt.t2[:, :w]))
    a(lambda v: v.tensor_mul(qt.t1[:, :w], x_ap, qt.invE[:, :w]))
    for f, dst in ((8.0, qt.q8), (4.0, qt.q4), (2.0, qt.t2)):
        a(lambda v, f=f: v.tensor_scalar_mul(qt.cm[:, :w], qt.t1[:, :w], f))
        a(lambda v: v.tensor_scalar_add(qt.cm[:, :w], qt.cm[:, :w], MAGIC))
        a(lambda v: v.tensor_scalar_add(qt.cm[:, :w], qt.cm[:, :w], -MAGIC))
        a(lambda v, f=f, dst=dst: v.tensor_scalar_mul(dst[:, :w], qt.cm[:, :w], 1.0 / f))
    a(lambda v: v.tensor_mul(qt.cm[:, :w], qt.t1[:, :w], qt.t1[:, :w]))
    a(lambda v: v.tensor_scalar(qt.t1[:, :w], qt.cm[:, :w], 16.0, None, ALU.is_ge))
    a(lambda v: v.copy_predicated(qt.q4[:, :w], qt.t1[:, :w].bitcast(U32),
                                  qt.t2[:, :w]))
    a(lambda v: v.tensor_scalar(qt.t1[:, :w], qt.cm[:, :w], 4.0, None, ALU.is_ge))
    a(lambda v: v.copy_predicated(qt.q8[:, :w], qt.t1[:, :w].bitcast(U32),
                                  qt.q4[:, :w]))
    a(lambda v: v.tensor_mul(out_ap, qt.q8[:, :w], qt.scE[:, :w]))
    chain(ctx, fns, list(reads) + [qt.res], list(writes) + [qt.res])



def emit_l3_attn(ctx, xin, xout, W):
    return _emit_mha(ctx, xin, xout, W, wk="a3_tk", wq="a3_tq", wv="a3_tv",
                     rope=False, wo="a3_wu", tag="3a", causal=False,
                     pre_proj="a3_wd", wo_bias="a3_bu")


def emit_l1_attn(ctx, xin, xout, W):
    return _emit_mha(ctx, xin, xout, W, wk="a1_wk", wq="a1_wq", wv="a1_wv",
                     rope=True, wo="a1_wo", tag="1a")


def emit_l2_attn(ctx, xin, xout, W):
    return _emit_mha(ctx, xin, xout, W, wk="a2_wk", wq="a2_wq", wv="a2_wv",
                     rope=False, wo=None, tag="2a")


def _emit_mha(ctx, xin, xout, W, wk, wq, wv, rope, wo, tag, causal=True,
              pre_proj=None, wo_bias=None):
    nc = ctx.nc
    u = f"{tag}{ctx.uid()}_"
    NQT = T_FULL // 128
    NQG = T_FULL // TC
    ao_scr = None
    if wo is not None:
        ao_scr = nc.dram_tensor(u + "ao", [NDC, 128, T_FULL], BF16,
                                kind="Internal").ap()
    with scope(ctx) as sb:
        xnb = [sb(f"{u}xn{i}", [128, T_FULL], BF16) for i in range(NDC)]
        xnres = [f"{u}xn{i}" for i in range(NDC)]
        ln = LNTmp(ctx, sb, u)
        identb = sb(u + "idb", [128, 128], BF16)
        identf = sb(u + "idf", [128, 128], F32)
        maskQ = sb(u + "mq", [128, 4 * TC], BF16)
        maskT = sb(u + "mt", [128, 4 * TC], BF16)
        dma_in(ctx, identb[:, :], W["ident_bf"], u + "idb")
        dma_in(ctx, identf[:, :], W["ident_f32"], u + "idf")
        dma_in(ctx, maskQ[:, :], W["maskQ"], u + "mq")
        dma_in(ctx, maskT[:, :], W["maskT"], u + "mt")

        with scope(ctx) as sbx:
            xs = [sbx(f"{u}x{i}", [128, TC], F32) for i in range(NDC)]
            for tcx in range(NTC):
                t0 = tcx * TC
                rxl = [f"{u}x{i}" for i in range(NDC)]
                for i in range(NDC):
                    dma_in(ctx, xs[i][:, :], xin[i][:, t0:t0 + TC], rxl[i])
                emit_ln(ctx, ln, [xs[i][:, :] for i in range(NDC)], rxl,
                        [xnb[i][:, t0:t0 + TC] for i in range(NDC)], xnres)

        if pre_proj is not None:
            with scope(ctx) as sbd:
                wdt = [sbd(f"{u}wd{k}", [128, 1024], BF16) for k in range(NDC)]
                xdb = [sbd(f"{u}xd{i}", [128, T_FULL], BF16) for i in range(NDC)]
                for k in range(NDC):
                    dma_in(ctx, wdt[k][:, :], W[pre_proj][k], f"{u}wd{k}",
                           eng="gpsimd")
                for tg in range(NTC):
                    t0 = tg * TC
                    for oc in range(NDC):
                        bank = oc % 4
                        kacc(ctx, ctx.pb[bank][:, :TC],
                             [wdt[k][:, 128 * oc:128 * (oc + 1)] for k in range(NDC)],
                             [xnb[k][:, t0:t0 + TC] for k in range(NDC)],
                             [f"{u}wd{k}" for k in range(NDC)] + xnres,
                             [f"psum{bank}"])
                        vop(ctx, lambda v, oc=oc, t0=t0, bank=bank: v.tensor_copy(
                            xnb[oc][:, t0:t0 + TC] if False else xdb[oc][:, t0:t0 + TC],
                            ctx.pb[bank][:, :TC]),
                            [f"psum{bank}"], [f"{u}xd{oc}"])
                for oc in range(NDC):
                    vop(ctx, lambda v, oc=oc: v.tensor_copy(
                        xnb[oc][:, :], xdb[oc][:, :]),
                        [f"{u}xd{oc}"], [xnres[oc]])

        with scope(ctx) as sb2:
            qh = [sb2(f"{u}qh{h}", [64, T_FULL], BF16) for h in range(2)]
            vT = sb2(u + "vT", [128, TC], BF16)
            kx = [sb2(f"{u}kx{h}", [65, T_FULL], BF16) for h in range(2)]
            vtm = sb2(u + "vtm", [128, 32 * 130], BF16)
            qx = sb2(u + "qx", [65, TC], BF16)
            mrow = sb2(u + "mr", [65, T_FULL], BF16)
            mneg = sb2(u + "mn", [128, 1], F32)
            mcol = [sb2(f"{u}mc{h}", [128, 97], F32) for h in range(2)]
            rec = sb2(u + "rec", [1, TC], BF16)
            recb = sb2(u + "recb", [64, TC], F32)
            aoc = sb2(u + "aoc", [128, T_FULL], BF16)
            xr = sb2(u + "xr", [128, TC], F32)
            yv = sb2(u + "yv", [128, TC], F32)
            wpk2 = [[sb2(f"{u}wpk{p}_{k}", [128, 128], BF16) for k in range(NDC)]
                    for p in range(2)]
            wpq2 = [[sb2(f"{u}wpq{p}_{k}", [128, 128], BF16) for k in range(NDC)]
                    for p in range(2)]
            wpv2 = [[sb2(f"{u}wpv{p}_{k}", [128, 128], BF16) for k in range(NDC)]
                    for p in range(2)]
            if rope:
                cB = sb2(u + "cB", [128, T_FULL], BF16)
                sB = sb2(u + "sB", [128, T_FULL], BF16)
                psw = sb2(u + "psw", [128, 128], BF16)
                praw = sb2(u + "praw", [128, TC], BF16)
                rt1 = sb2(u + "rt1", [128, TC], F32)
                rt2 = sb2(u + "rt2", [128, TC], F32)
                dma_in(ctx, cB[:, :], W["a1_cos"], u + "cB")
                dma_in(ctx, sB[:, :], W["a1_sin"], u + "sB")
                dma_in(ctx, psw[:, :], W["a1_pswap"], u + "psw")
            vt3 = vtm[:, :].rearrange("p (k c) -> p k c", c=130)
            vop(ctx, lambda v: v.memset(vt3[:, :, 64:65], 1.0), [], [u + "vtm"])
            vop(ctx, lambda v: v.memset(vt3[:, :, 129:130], 1.0), [], [u + "vtm"])

            def load_pair(pr):
                p = pr % 2
                if pre_proj is not None:
                    dma_in(ctx, wpk2[p][0][:, :], W[wk][pr], f"{u}wpk{p}_0",
                           eng="gpsimd")
                    dma_in(ctx, wpq2[p][0][:, :], W[wq][pr], f"{u}wpq{p}_0",
                           eng="gpsimd")
                    dma_in(ctx, wpv2[p][0][:, :], W[wv][pr], f"{u}wpv{p}_0",
                           eng="gpsimd")
                else:
                    for k in range(NDC):
                        dma_in(ctx, wpk2[p][k][:, :],
                               W[wk][k][:, 128 * pr:128 * (pr + 1)],
                               f"{u}wpk{p}_{k}", eng="gpsimd")
                        dma_in(ctx, wpq2[p][k][:, :],
                               W[wq][k][:, 128 * pr:128 * (pr + 1)],
                               f"{u}wpq{p}_{k}", eng="gpsimd")
                        dma_in(ctx, wpv2[p][k][:, :],
                               W[wv][k][:, 128 * pr:128 * (pr + 1)],
                               f"{u}wpv{p}_{k}", eng="gpsimd")

            load_pair(0)
            for pr in range(8):
                pp = pr % 2
                wpk, wpq, wpv = wpk2[pp], wpq2[pp], wpv2[pp]
                if pr + 1 < 8:
                    load_pair(pr + 1)
                for h in range(2):
                    vop(ctx, lambda v, h=h: v.memset(kx[h][64:65, :], 1.0),
                        [], [f"{u}kx{h}"])
                for tg in range(NTC):
                    t0 = tg * TC
                    if pre_proj is not None:
                        def pmm(bank, wg, wres, pr=pr, t0=t0):
                            mm(ctx, [(ctx.pb[bank][:, :TC], wg[0][:, :],
                                      xnb[pr][:, t0:t0 + TC], True, True)],
                               [wres + "_0", xnres[pr]], [f"psum{bank}"])
                    else:
                        def pmm(bank, wg, wres, t0=t0):
                            kacc(ctx, ctx.pb[bank][:, :TC],
                                 [wg[k][:, :] for k in range(NDC)],
                                 [xnb[k][:, t0:t0 + TC] for k in range(NDC)],
                                 [f"{wres}_{k}" for k in range(NDC)] + xnres,
                                 [f"psum{bank}"])
                    def evac_heads(bank, dsts, dres, drows, t0=t0):
                        if not rope:
                            for h in range(2):
                                vop(ctx, lambda v, h=h: v.tensor_copy(
                                    dsts[h][0:drows, t0:t0 + TC],
                                    ctx.pb[bank][64 * h:64 * h + 64, :TC]),
                                    [f"psum{bank}"], [dres[h]])
                            return
                        vop(ctx, lambda v: v.tensor_copy(
                            praw[:, :], ctx.pb[bank][:, :TC]),
                            [f"psum{bank}"], [u + "praw"])
                        mm(ctx, [(ctx.pb[4][:, :TC], psw[:, :], praw[:, :],
                                  True, True)],
                           [u + "psw", u + "praw"], ["psum4"])
                        chain(ctx, [
                            lambda v: v.tensor_mul(rt1[:, :], praw[:, :],
                                                   cB[:, t0:t0 + TC]),
                            lambda v: v.tensor_mul(rt2[:, :], ctx.pb[4][:, :TC],
                                                   sB[:, t0:t0 + TC]),
                        ], [u + "praw", u + "cB", u + "sB", "psum4"],
                            [u + "rt"])
                        for h in range(2):
                            vop(ctx, lambda v, h=h: v.tensor_add(
                                dsts[h][0:drows, t0:t0 + TC],
                                rt1[64 * h:64 * h + 64, :],
                                rt2[64 * h:64 * h + 64, :]),
                                [u + "rt"], [dres[h]])

                    pmm(0, wpk, f"{u}wpk{pp}")
                    evac_heads(0, kx, [f"{u}kx{h}" for h in range(2)], 64)
                    pmm(1, wpq, f"{u}wpq{pp}")
                    evac_heads(1, qh, [f"{u}qh{h}" for h in range(2)], 64)
                    pmm(2, wpv, f"{u}wpv{pp}")
                    vop(ctx, lambda v: v.tensor_copy(vT[:, :TC], ctx.pb[2][:, :TC]),
                        ["psum2"], [u + "vT"])
                    for j in range(4):
                        kk = 4 * tg + j
                        transpose(ctx, ctx.pb[4][:, 0:128].bitcast(BF16)[:, 0:128],
                                  vT[:, 128 * j:128 * (j + 1)], identb[:, :],
                                  [u + "vT", u + "idb"], ["psum4"])
                        chain(ctx, [
                            lambda v, kk=kk: v.tensor_copy(
                                vt3[:, kk, 0:64], ctx.pb[4][:, 0:128].bitcast(BF16)[:, 0:64]),
                            lambda v, kk=kk: v.tensor_copy(
                                vt3[:, kk, 65:129], ctx.pb[4][:, 0:128].bitcast(BF16)[:, 64:128]),
                        ], ["psum4"], [u + "vtm"])

                for h in range(2):
                    hq = qh[h][0:64, :]
                    hk = kx[h][0:64, :]
                    for qt in range(NQT):
                        nkc = (qt // 4 + 1) if causal else NQG
                        for c in range(nkc):
                            klen = min(TC, 128 * (qt + 1) - TC * c) if causal else TC
                            bank = c % 2
                            mm(ctx, [(ctx.pb[bank][:, :klen],
                                      hq[:, 128 * qt:128 * (qt + 1)],
                                      hk[:, TC * c:TC * c + klen], True, True)],
                               [f"{u}qh{h}", f"{u}kx{h}"], [f"psum{bank}"])
                            if causal and c == nkc - 1:
                                j = qt - 4 * c
                                vop(ctx, lambda v, bank=bank, j=j, klen=klen:
                                    v.tensor_tensor(ctx.pb[bank][:, :klen],
                                                    ctx.pb[bank][:, :klen],
                                                    maskQ[:, TC * j:TC * j + klen],
                                                    ALU.add),
                                    [u + "mq", f"psum{bank}"], [f"psum{bank}"])
                            if c == 0:
                                vop(ctx, lambda v, bank=bank, klen=klen: v.tensor_reduce(
                                    mneg[:, :], ctx.pb[bank][:, :klen], axis=AX,
                                    op=ALU.max, negate=True),
                                    [f"psum{bank}"], [u + "mn"])
                            else:
                                chain(ctx, [
                                    lambda v, bank=bank, klen=klen: v.tensor_reduce(
                                        ln.tmp[:, 0:1], ctx.pb[bank][:, :klen], axis=AX,
                                        op=ALU.max, negate=True),
                                    lambda v: v.tensor_tensor(
                                        mneg[:, :], mneg[:, :], ln.tmp[:, 0:1], ALU.min),
                                ], [f"psum{bank}", u + "mn"], [u + "mn", u + "ln_tmpres"])
                        vop(ctx, lambda v, qt=qt, h=h: v.tensor_copy(
                            mcol[h][:, 32 * (qt % 4):32 * (qt % 4) + 1], mneg[:, :]),
                            [u + "mn"], [f"{u}mc{h}"])
                        if qt % 4 == 3:
                            qg = qt // 4
                            transpose(ctx, ctx.pb[4][0:97, 0:128], mcol[h][:, :],
                                      identf[:, :], [f"{u}mc{h}", u + "idf"], ["psum4"])
                            for j in range(4):
                                vop(ctx, lambda v, h=h, qg=qg, j=j: v.tensor_copy(
                                    mrow[64 * h:64 * h + 1,
                                         TC * qg + 128 * j:TC * qg + 128 * (j + 1)],
                                    ctx.pb[4][32 * j:32 * j + 1, 0:128]),
                                    ["psum4"], [u + "mr"])

                for h in range(2):
                    obank = 2 + h
                    for qg in range(NQG):
                        chain(ctx, [
                            lambda v, h=h, qg=qg: v.tensor_copy(
                                qx[0:64, :], qh[h][0:64, TC * qg:TC * (qg + 1)]),
                            lambda v, h=h, qg=qg: v.tensor_copy(
                                qx[64:65, :],
                                mrow[64 * h:64 * h + 1, TC * qg:TC * (qg + 1)]),
                        ], [f"{u}qh{h}", u + "mr"], [u + "qx"])
                        nkc = 4 * (qg + 1) if causal else NQT
                        for kc in range(nkc):
                            bank = kc % 2
                            mm(ctx, [(ctx.pb[bank][:, :TC],
                                      kx[h][:, 128 * kc:128 * (kc + 1)],
                                      qx[:, :], True, True)],
                               [f"{u}kx{h}", u + "qx"], [f"psum{bank}"])
                            j = kc - 4 * qg
                            if causal and j >= 0:
                                vop(ctx, lambda v, bank=bank, j=j: v.tensor_tensor(
                                    ctx.pb[bank][:, :TC], ctx.pb[bank][:, :TC],
                                    maskT[:, TC * j:TC * (j + 1)], ALU.add),
                                    [u + "mt", f"psum{bank}"], [f"psum{bank}"])
                            vop(ctx, lambda s, bank=bank: s.activation(
                                ln.tmp[:, :].bitcast(BF16)[:, :TC], ctx.pb[bank][:, :TC],
                                AF.Exp, scale=0.125),
                                [f"psum{bank}"], [u + "ln_tmpres"], eng="scalar")
                            mm(ctx, [(ctx.pb[obank][0:65, :TC],
                                      vtm[:, :].rearrange("p (k c) -> p k c", c=130)[:, kc, 65 * h:65 * h + 65],
                                      ln.tmp[:, :].bitcast(BF16)[:, :TC],
                                      kc == 0, kc == nkc - 1)],
                               [u + "vtm", u + "ln_tmpres"], [f"psum{obank}"])
                        chain(ctx, [
                            lambda v, obank=obank: v.reciprocal(
                                recb[0:1, :TC], ctx.pb[obank][64:65, :TC]),
                            lambda v: v.tensor_copy(rec[0:1, :TC], recb[0:1, :TC]),
                        ], [f"psum{obank}"], [u + "rec"])
                        mm(ctx, [(ctx.pb[4][0:64, :TC], ln.ones1_bf[0:1, 0:64],
                                  rec[0:1, :TC], True, True)],
                           [u + "rec", u + "ln_ones1"], ["psum4"])
                        vop(ctx, lambda v: v.tensor_copy(recb[:, :TC], ctx.pb[4][0:64, :TC]),
                            ["psum4"], [u + "recb"])
                        vop(ctx, lambda v, h=h, qg=qg, obank=obank: v.tensor_mul(
                            aoc[64 * h:64 * h + 64, TC * qg:TC * (qg + 1)],
                            ctx.pb[obank][0:64, :TC], recb[:, :TC]),
                            [f"psum{obank}", u + "recb"], [u + "aoc"])

                if wo is None:
                    for tcx in range(NTC):
                        t0 = tcx * TC
                        dma_in(ctx, xr[:, :], xin[pr][:, t0:t0 + TC], u + "xr")
                        vop(ctx, lambda v, t0=t0: v.tensor_add(
                            yv[:, :], xr[:, :], aoc[:, t0:t0 + TC]),
                            [u + "xr", u + "aoc"], [u + "yv"])
                        dma_out(ctx, xout[pr][:, t0:t0 + TC], yv[:, :],
                                [u + "yv"], u + "xo")
                else:
                    for tcx in range(NTC):
                        t0 = tcx * TC
                        dma_out(ctx, ao_scr[pr][:, t0:t0 + TC], aoc[:, t0:t0 + TC],
                                [u + "aoc"], u + "aos")

        if wo is not None:
            with scope(ctx) as sb3:
                wot = [sb3(f"{u}wo{k}", [128, 1024], BF16) for k in range(NDC)]
                aot = [sb3(f"{u}aot{k}", [128, TC], BF16) for k in range(NDC)]
                xr2 = sb3(u + "xr2", [128, TC], F32)
                yv2 = sb3(u + "yv2", [128, TC], F32)
                for k in range(NDC):
                    dma_in(ctx, wot[k][:, :], W[wo][k], f"{u}wo{k}", eng="gpsimd")
                if wo_bias is not None:
                    wob = sb3(u + "wob", [128, 8], F32)
                    dma_in(ctx, wob[:, :], W[wo_bias], u + "wob")
                for tcx in range(NTC):
                    t0 = tcx * TC
                    for k in range(NDC):
                        dma_in(ctx, aot[k][:, :], ao_scr[k][:, t0:t0 + TC],
                               f"{u}aot{k}")
                    for oc in range(NDC):
                        bank = oc % 4
                        kacc(ctx, ctx.pb[bank][:, :TC],
                             [wot[k][:, 128 * oc:128 * (oc + 1)] for k in range(NDC)],
                             [aot[k][:, :] for k in range(NDC)],
                             [f"{u}wo{k}" for k in range(NDC)] +
                             [f"{u}aot{k}" for k in range(NDC)],
                             [f"psum{bank}"])
                        dma_in(ctx, xr2[:, :], xin[oc][:, t0:t0 + TC], u + "xr2")
                        if wo_bias is not None:
                            chain(ctx, [
                                lambda v, bank=bank, oc=oc: v.scalar_tensor_tensor(
                                    yv2[:, :], ctx.pb[bank][:, :TC],
                                    wob[:, oc:oc + 1], xr2[:, :], ALU.add, ALU.add),
                            ], [f"psum{bank}", u + "xr2", u + "wob"], [u + "yv2"])
                        else:
                            vop(ctx, lambda v, bank=bank: v.tensor_add(
                                yv2[:, :], ctx.pb[bank][:, :TC], xr2[:, :]),
                                [f"psum{bank}", u + "xr2"], [u + "yv2"])
                        dma_out(ctx, xout[oc][:, t0:t0 + TC], yv2[:, :],
                                [u + "yv2"], u + "xo2")





def emit_l0_attn(ctx, xin, xout, W):
    nc = ctx.nc
    u = f"0a{ctx.uid()}_"
    DP = 1025
    with scope(ctx) as sb:
        ln = LNTmp(ctx, sb, u)
        identb = sb(u + "idb", [128, 128], BF16)
        maskA = sb(u + "mA", [128, 4 * TC], BF16)
        wq = [sb(f"{u}wq{k}", [128, 3072], BF16) for k in range(NDC)]
        wg = [sb(f"{u}wg{k}", [128, 1024], BF16) for k in range(NDC)]
        wp = [sb(f"{u}wp{k}", [128, 1024], BF16) for k in range(NDC)]
        bqkv = sb(u + "bqkv", [128, 24], F32)
        bg = sb(u + "bg", [128, 8], F32)
        bp = sb(u + "bp", [128, 8], F32)
        lg = sb(u + "lg", [128, 8], F32)
        lb = sb(u + "lb", [128, 8], F32)
        Sb = [sb(f"{u}Sb{k}", [128, DP], BF16) for k in range(NDC)]
        xs = [sb(f"{u}x{i}", [128, TC], F32) for i in range(NDC)]
        xn0 = [sb(f"{u}xn0{i}", [128, TC], BF16) for i in range(NDC)]
        xnin = [sb(f"{u}xi{i}", [128, TC], BF16) for i in range(NDC)]
        qT = [sb(f"{u}qT{i}", [128, TC], BF16) for i in range(NDC)]
        kT = [sb(f"{u}kT{i}", [128, TC], BF16) for i in range(NDC)]
        vT = [sb(f"{u}vT{i}", [128, TC], BF16) for i in range(NDC)]
        gatet = sb(u + "gatet", [128, TC], BF16)
        tmpa = sb(u + "tmpa", [128, TC], F32)
        tmpb = sb(u + "tmpb", [128, TC], BF16)
        tmpc = sb(u + "tmpc", [128, TC], BF16)
        vtm = sb(u + "vtm", [128, 4 * 1026], BF16)
        ktm = sb(u + "ktm", [128, 4 * 1024], BF16)
        aT = [sb(f"{u}aT{j}", [128, TC], BF16) for j in range(4)]
        oT = vT
        recb = sb(u + "recb", [128, TC], BF16)
        recr = sb(u + "recr", [1, TC], BF16)
        yv = sb(u + "yv", [128, TC], F32)

        dma_in(ctx, identb[:, :], W["ident_bf"], u + "idb")
        dma_in(ctx, maskA[:, :], W["maskA"], u + "mA")
        for k in range(NDC):
            dma_in(ctx, wq[k][:, :], W["a0_wqkv"][k], f"{u}wq{k}", eng="gpsimd")
            dma_in(ctx, wg[k][:, :], W["a0_wg"][k], f"{u}wg{k}", eng="gpsimd")
            dma_in(ctx, wp[k][:, :], W["a0_wp"][k], f"{u}wp{k}", eng="gpsimd")
        for nm, tile, src in (("bqkv", bqkv, "a0_bqkv"), ("bg", bg, "a0_bg"),
                              ("bp", bp, "a0_bp"), ("lg", lg, "a0_lg"),
                              ("lb", lb, "a0_lb")):
            dma_in(ctx, tile[:, :], W[src], u + nm)
        for k in range(NDC):
            vop(ctx, lambda v, k=k: v.memset(Sb[k][:, :], 0.0), [], [f"{u}Sb{k}"])
        vt4 = vtm[:, :].rearrange("p (j c) -> p j c", c=1026)
        vop(ctx, lambda v: v.memset(vt4[:, :, 1024:1025], 1.0), [], [u + "vtm"])

        xres = [f"{u}x{i}" for i in range(NDC)]
        n0res = [f"{u}xn0{i}" for i in range(NDC)]
        xires = [f"{u}xi{i}" for i in range(NDC)]

        for g in range(NTC):
            t0 = g * TC
            for i in range(NDC):
                dma_in(ctx, xs[i][:, :], xin[i][:, t0:t0 + TC], xres[i])
            emit_ln(ctx, ln, [xs[i][:, :] for i in range(NDC)], xres,
                    None, [r + "_d" for r in n0res],
                    out_f32=[xn0[i][:, :] for i in range(NDC)], out_f32_resl=n0res)
            emit_ln(ctx, ln, [xn0[i][:, :] for i in range(NDC)], n0res,
                    [xnin[i][:, :] for i in range(NDC)], xires,
                    affine=([lg[:, i:i + 1] for i in range(NDC)],
                            [lb[:, i:i + 1] for i in range(NDC)]),
                    affine_res=[u + "lg", u + "lb"])
            for i in range(NDC):
                kacc(ctx, ctx.pb[0][:, :TC],
                     [wg[k][:, 128 * i:128 * (i + 1)] for k in range(NDC)],
                     [xnin[k][:, :] for k in range(NDC)],
                     [f"{u}wg{k}" for k in range(NDC)] + xires, ["psum0"])
                vop(ctx, lambda sc, i=i: sc.activation(
                    gatet[:, :], ctx.pb[0][:, :TC], AF.Sigmoid,
                    bias=bg[:, i:i + 1]),
                    ["psum0", u + "bg"], [u + "gatet"], eng="scalar")
                kacc(ctx, ctx.pb[1][:, :TC],
                     [wq[k][:, 128 * i:128 * (i + 1)] for k in range(NDC)],
                     [xnin[k][:, :] for k in range(NDC)],
                     [f"{u}wq{k}" for k in range(NDC)] + xires, ["psum1"])
                zb = ln.eps[0:1, 0:1]
                chain(ctx, [
                    lambda v, i=i: v.tensor_scalar(tmpa[:, :], ctx.pb[1][:, :TC],
                                                   bqkv[:, i:i + 1], 0.0, ALU.add,
                                                   ALU.min),
                    lambda v, i=i: v.tensor_scalar_min(tmpa[:, :], tmpa[:, :], 0.0),
                ], ["psum1", u + "bqkv"], [u + "tmpa"])
                vop(ctx, lambda sc: sc.activation(tmpb[:, :], tmpa[:, :], AF.Exp),
                    [u + "tmpa"], [u + "tmpb"], eng="scalar")
                chain(ctx, [
                    lambda v, i=i: v.tensor_scalar(tmpc[:, :], ctx.pb[1][:, :TC],
                                                   bqkv[:, i:i + 1], 0.0, ALU.add,
                                                   ALU.max),
                    lambda v, i=i: v.tensor_scalar_max(tmpc[:, :], tmpc[:, :], 0.0),
                    lambda v, i=i: v.tensor_add(qT[i][:, :], tmpb[:, :], tmpc[:, :]),
                ], ["psum1", u + "bqkv", u + "tmpb"], [f"{u}qT{i}", u + "tmpc"])
                kacc(ctx, ctx.pb[2][:, :TC],
                     [wq[k][:, 1024 + 128 * i:1024 + 128 * (i + 1)] for k in range(NDC)],
                     [xnin[k][:, :] for k in range(NDC)],
                     [f"{u}wq{k}" for k in range(NDC)] + xires, ["psum2"])
                chain(ctx, [
                    lambda v, i=i: v.scalar_tensor_tensor(
                        tmpa[:, :], ctx.pb[2][:, :TC], bqkv[:, 8 + i:9 + i],
                        gatet[:, :], ALU.add, ALU.mult),
                    lambda v: v.tensor_scalar_min(tmpb[:, :], tmpa[:, :], 0.0),
                ], ["psum2", u + "bqkv", u + "gatet"], [u + "tmpa", u + "tmpb"])
                vop(ctx, lambda sc: sc.activation(tmpb[:, :], tmpb[:, :], AF.Exp),
                    [u + "tmpb"], [u + "tmpb"], eng="scalar")
                chain(ctx, [
                    lambda v: v.tensor_scalar_max(tmpc[:, :], tmpa[:, :], 0.0),
                    lambda v, i=i: v.tensor_add(kT[i][:, :], tmpb[:, :], tmpc[:, :]),
                ], [u + "tmpa", u + "tmpb"], [f"{u}kT{i}", u + "tmpc"])
                kacc(ctx, ctx.pb[3][:, :TC],
                     [wq[k][:, 2048 + 128 * i:2048 + 128 * (i + 1)] for k in range(NDC)],
                     [xnin[k][:, :] for k in range(NDC)],
                     [f"{u}wq{k}" for k in range(NDC)] + xires, ["psum3"])
                vop(ctx, lambda v, i=i: v.tensor_scalar(
                    vT[i][:, :], ctx.pb[3][:, :TC], bqkv[:, 16 + i:17 + i], None,
                    ALU.add),
                    ["psum3", u + "bqkv"], [f"{u}vT{i}"])
            for j in range(4):
                for dc in range(NDC):
                    transpose(ctx, ctx.pb[4][:, 0:128].bitcast(BF16)[:, 0:128],
                              vT[dc][:, 128 * j:128 * (j + 1)], identb[:, :],
                              [f"{u}vT{dc}", u + "idb"], ["psum4"])
                    vop(ctx, lambda v, j=j, dc=dc: v.tensor_copy(
                        vt4[:, j, 128 * dc:128 * (dc + 1)],
                        ctx.pb[4][:, 0:128].bitcast(BF16)[:, 0:128]),
                        ["psum4"], [u + "vtm"])
                    transpose(ctx, ctx.pb[4][:, 0:128].bitcast(BF16)[:, 0:128],
                              kT[dc][:, 128 * j:128 * (j + 1)], identb[:, :],
                              [f"{u}kT{dc}", u + "idb"], ["psum4"])
                    vop(ctx, lambda v, j=j, dc=dc: v.tensor_copy(
                        ktm[:, 1024 * j + 128 * dc:1024 * j + 128 * (dc + 1)],
                        ctx.pb[4][:, 0:128].bitcast(BF16)[:, 0:128]),
                        ["psum4"], [u + "ktm"])
            for j in range(4):
                kacc(ctx, ctx.pb[3][:, :TC],
                     [kT[dc][:, 128 * j:128 * (j + 1)] for dc in range(NDC)],
                     [qT[dc][:, :] for dc in range(NDC)],
                     [f"{u}kT{dc}" for dc in range(NDC)] +
                     [f"{u}qT{dc}" for dc in range(NDC)], ["psum3"])
                vop(ctx, lambda v, j=j: v.tensor_mul(
                    aT[j][:, :], ctx.pb[3][:, :TC], maskA[:, TC * j:TC * (j + 1)]),
                    ["psum3", u + "mA"], [f"{u}aT{j}"])
            for m in range(NDC):
                bank = m % 2
                plist = []
                for j in range(4):
                    plist.append((ctx.pb[bank][:, :TC],
                                  vt4[:, j, 128 * m:128 * (m + 1)], aT[j][:, :],
                                  j == 0, False))
                for dc in range(NDC):
                    plist.append((ctx.pb[bank][:, :TC],
                                  Sb[dc][:, 128 * m:128 * (m + 1)], qT[dc][:, :],
                                  False, dc == NDC - 1))
                mm(ctx, plist,
                   [u + "vtm"] + [f"{u}aT{j}" for j in range(4)] +
                   [f"{u}Sb{dc}" for dc in range(NDC)] +
                   [f"{u}qT{dc}" for dc in range(NDC)], [f"psum{bank}"])
                if m == 0:
                    plist = []
                    for j in range(4):
                        plist.append((ctx.pb[2][0:1, :TC],
                                      vt4[:, j, 1024:1025], aT[j][:, :],
                                      j == 0, False))
                    for dc in range(NDC):
                        plist.append((ctx.pb[2][0:1, :TC],
                                      Sb[dc][:, 1024:1025], qT[dc][:, :],
                                      False, dc == NDC - 1))
                    mm(ctx, plist,
                       [u + "vtm"] + [f"{u}aT{j}" for j in range(4)] +
                       [f"{u}Sb{dc}" for dc in range(NDC)] +
                       [f"{u}qT{dc}" for dc in range(NDC)], ["psum2"])
                    chain(ctx, [
                        lambda v: v.tensor_scalar_add(recr[0:1, :].bitcast(F32)[0:1, :TC // 2] if False else tmpa[0:1, :], ctx.pb[2][0:1, :TC], 1e-6),
                        lambda v: v.reciprocal(tmpa[0:1, :], tmpa[0:1, :]),
                        lambda v: v.tensor_copy(recr[0:1, :], tmpa[0:1, :]),
                    ], ["psum2"], [u + "recr", u + "tmpa"])
                    mm(ctx, [(ctx.pb[2][:, :TC], ln.ones1_bf[0:1, :], recr[0:1, :],
                              True, True)],
                       [u + "recr", u + "ln_ones1"], ["psum2"])
                    vop(ctx, lambda v: v.tensor_copy(recb[:, :], ctx.pb[2][:, :TC]),
                        ["psum2"], [u + "recb"])
                vop(ctx, lambda v, m=m, bank=bank: v.tensor_mul(
                    oT[m][:, :], ctx.pb[bank][:, :TC], recb[:, :]),
                    [f"psum{bank}", u + "recb"], [f"{u}vT{m}"])
            for oc in range(NDC):
                bank = oc % 2
                kacc(ctx, ctx.pb[bank][:, :TC],
                     [wp[k][:, 128 * oc:128 * (oc + 1)] for k in range(NDC)],
                     [oT[k][:, :] for k in range(NDC)],
                     [f"{u}wp{k}" for k in range(NDC)] +
                     [f"{u}vT{k}" for k in range(NDC)], [f"psum{bank}"])
                vop(ctx, lambda v, oc=oc, bank=bank: v.scalar_tensor_tensor(
                    yv[:, :], ctx.pb[bank][:, :TC], bp[:, oc:oc + 1], xs[oc][:, :],
                    ALU.add, ALU.add),
                    [f"psum{bank}", u + "bp", xres[oc]], [u + "yv"])
                dma_out(ctx, xout[oc][:, t0:t0 + TC], yv[:, :], [u + "yv"], u + "xo")
            for dc in range(NDC):
                for (c0, c1, bank) in ((0, 512, 4), (512, 1024, 5), (1024, 1025, 6)):
                    plist = []
                    for j in range(4):
                        plist.append((ctx.pb[bank][:, 0:c1 - c0],
                                      ktm[:, 1024 * j + 128 * dc:1024 * j + 128 * (dc + 1)],
                                      vt4[:, j, c0:c1], j == 0, j == 3))
                    mm(ctx, plist, [u + "ktm", u + "vtm"], [f"psum{bank}"])
                chain(ctx, [
                    lambda v, dc=dc: v.tensor_add(Sb[dc][:, 0:512], Sb[dc][:, 0:512],
                                                  ctx.pb[4][:, :TC]),
                    lambda v, dc=dc: v.tensor_add(Sb[dc][:, 512:1024],
                                                  Sb[dc][:, 512:1024],
                                                  ctx.pb[5][:, :TC]),
                    lambda v, dc=dc: v.tensor_add(Sb[dc][:, 1024:1025],
                                                  Sb[dc][:, 1024:1025],
                                                  ctx.pb[6][:, 0:1]),
                ], ["psum4", "psum5", "psum6"], [f"{u}Sb{dc}"])



def emit_l0_ffn(ctx, xin, xout, W):
    nc = ctx.nc
    u = f"0f{ctx.uid()}_"
    HT = T_FULL // 2
    NHC = HT // TC
    with scope(ctx) as sb:
        xnbh = [sb(f"{u}xh{i}", [128, HT], BF16) for i in range(NDC)]
        xnres = [f"{u}xh{i}" for i in range(NDC)]
        yac = [sb(f"{u}ya{i}", [128, HT], BF16) for i in range(NDC)]
        cwf = sb(u + "cwf", [16, HT], BF16)
        ln = LNTmp(ctx, sb, u)
        identf = sb(u + "idf", [128, 128], F32)
        rwt = [sb(f"{u}rwt{k}", [128, 16], F32) for k in range(NDC)]
        e16 = sb(u + "e16", [16, 16 * 128], BF16)
        ebias = sb(u + "eb", [16, 1], F32)
        scf = sb(u + "scf", [16, TC], F32)
        sbf_ = sb(u + "sbf", [16, TC], F32)
        sc_tm = sb(u + "sctm", [128, 16], F32)
        sr_tm = sb(u + "srtm", [128, 16], F32)
        gs = sb(u + "gs", [128, 4], F32)
        gt4 = sb(u + "gt4", [128, 4], F32)
        gcnt4 = sb(u + "gcnt4", [128, 4], F32)
        mt = sb(u + "mt16", [128, 16], F32)
        t16 = sb(u + "t16", [128, 16], F32)
        c1 = sb(u + "c1", [128, 16], F32)
        c2 = sb(u + "c2", [128, 16], F32)
        gcol = sb(u + "gcol", [128, 1], F32)
        gwb = [[sb(f"{u}gwb{p}_{k}", [128, 512], BF16) for k in range(NDC)]
               for p in range(2)]
        uwb = [[sb(f"{u}uwb{p}_{k}", [128, 512], BF16) for k in range(NDC)]
               for p in range(2)]
        dwb = [[sb(f"{u}dwb{p}_{k}", [128, 1024], BF16) for k in range(4)]
               for p in range(2)]
        hsl = [sb(f"{u}hsl{k}", [128, TC], BF16) for k in range(8)]
        cwb = sb(u + "cwb", [128, HT], BF16)
        sil = sb(u + "sil", [128, TC], F32)
        dma_in(ctx, identf[:, :], W["ident_f32"], u + "idf")
        dma_in(ctx, e16[:, :], W["E16"], u + "e16")
        dma_in(ctx, ebias[:, :], W["f0_eb"], u + "eb")
        for k in range(NDC):
            dma_in(ctx, rwt[k][:, :], W["f0_rw"][k], f"{u}rwt{k}")

        for half in range(2):
            hb = half * HT
            with scope(ctx) as sbr:
                xs = [sbr(f"{u}x{i}", [128, TC], F32) for i in range(NDC)]
                xnf = [sbr(f"{u}xnf{i}", [128, TC], F32) for i in range(NDC)]
                for tcx in range(NHC):
                    t0 = hb + tcx * TC
                    th = tcx * TC
                    rxl = [f"{u}x{i}" for i in range(NDC)]
                    rnl = [f"{u}xnf{i}" for i in range(NDC)]
                    for i in range(NDC):
                        dma_in(ctx, xs[i][:, :], xin[i][:, t0:t0 + TC], rxl[i])
                    emit_ln(ctx, ln, [xs[i][:, :] for i in range(NDC)], rxl,
                            [xnbh[i][:, th:th + TC] for i in range(NDC)], xnres,
                            out_f32=[xnf[i][:, :] for i in range(NDC)],
                            out_f32_resl=rnl)
                    kacc(ctx, ctx.pb[0][0:16, :TC],
                         [rwt[k][:, :] for k in range(NDC)],
                         [xnf[k][:, :] for k in range(NDC)],
                         [f"{u}rwt{k}" for k in range(NDC)] + rnl, ["psum0"])
                    chain(ctx, [
                        lambda sc: sc.activation(scf[0:16, :], ctx.pb[0][0:16, :TC],
                                                 AF.Sigmoid),
                    ], ["psum0"], [u + "scf"], eng="scalar")
                    vop(ctx, lambda v: v.tensor_scalar(
                        sbf_[0:16, :], scf[0:16, :], ebias[0:16, 0:1], None, ALU.add),
                        [u + "scf", u + "eb"], [u + "sbf"])
                    for tt in range(TC // 128):
                        transpose(ctx, ctx.pb[4][0:128, 0:16],
                                  scf[0:16, 128 * tt:128 * (tt + 1)],
                                  identf[0:16, 0:16], [u + "scf", u + "idf"], ["psum4"])
                        vop(ctx, lambda v: v.tensor_copy(sr_tm[:, :],
                                                         ctx.pb[4][0:128, 0:16]),
                            ["psum4"], [u + "srtm"])
                        transpose(ctx, ctx.pb[4][0:128, 0:16],
                                  sbf_[0:16, 128 * tt:128 * (tt + 1)],
                                  identf[0:16, 0:16],
                                  [u + "sbf", u + "idf", u + "srtm"], ["psum4"])
                        vop(ctx, lambda v: v.tensor_copy(sc_tm[:, :],
                                                         ctx.pb[4][0:128, 0:16]),
                            ["psum4"], [u + "sctm"])
                        sc3 = sc_tm[:, :].rearrange("p (g j) -> p g j", j=4)
                        fns = []
                        first = [True]
                        for a in range(4):
                            for b in range(a + 1, 4):
                                if first[0]:
                                    fns.append(lambda v, a=a, b=b: v.tensor_add(
                                        gs[:, :], sc3[:, :, a], sc3[:, :, b]))
                                    first[0] = False
                                else:
                                    fns.append(lambda v, a=a, b=b: v.tensor_add(
                                        gt4[:, :], sc3[:, :, a], sc3[:, :, b]))
                                    fns.append(lambda v: v.tensor_max(
                                        gs[:, :], gs[:, :], gt4[:, :]))
                        for g in range(4):
                            fns.append(lambda v, g=g: v.tensor_scalar(
                                gt4[:, :], gs[:, :], gs[:, g:g + 1], None,
                                ALU.is_gt, ALU.add, accum_out=gcnt4[:, g:g + 1]))
                        fns.append(lambda v: v.tensor_scalar(
                            gcnt4[:, :], gcnt4[:, :], 2.0, None, ALU.is_lt))
                        fns.append(lambda v: v.tensor_copy(
                            mt[:, :].rearrange("p (g j) -> p g j", j=4),
                            gcnt4[:, :].unsqueeze(2).broadcast_to([128, 4, 4])))
                        fns.append(lambda v: v.tensor_mul(mt[:, :], mt[:, :],
                                                          sc_tm[:, :]))
                        for e in range(16):
                            fns.append(lambda v, e=e: v.tensor_scalar(
                                t16[:, :], mt[:, :], mt[:, e:e + 1], None,
                                ALU.is_gt, ALU.add, accum_out=c1[:, e:e + 1]))
                        fns.append(lambda v: v.memset(c2[:, 0:1], 0.0))
                        for e in range(1, 16):
                            fns.append(lambda v, e=e: v.tensor_scalar(
                                t16[:, 0:e], mt[:, 0:e], mt[:, e:e + 1], None,
                                ALU.is_equal, ALU.add, accum_out=c2[:, e:e + 1]))
                        fns.append(lambda v: v.tensor_add(c1[:, :], c1[:, :],
                                                          c2[:, :]))
                        fns.append(lambda v: v.tensor_scalar(
                            c1[:, :], c1[:, :], 4.0, None, ALU.is_lt))
                        fns.append(lambda v: v.tensor_mul(t16[:, :], sr_tm[:, :],
                                                          c1[:, :]))
                        fns.append(lambda v: v.reduce_sum(gcol[:, 0:1], t16[:, :],
                                                          axis=AX))
                        fns.append(lambda v: v.tensor_scalar_add(gcol[:, 0:1],
                                                                 gcol[:, 0:1], 1e-20))
                        fns.append(lambda v: v.reciprocal(gcol[:, :], gcol[:, :]))
                        fns.append(lambda v: v.tensor_scalar_mul(gcol[:, 0:1],
                                                                 gcol[:, 0:1], 2.5))
                        fns.append(lambda v: v.tensor_scalar(
                            t16[:, :], t16[:, :], gcol[:, 0:1], None, ALU.mult))
                        chain(ctx, fns, [u + "sctm", u + "srtm"],
                              [u + "t16", u + "gcol", u + "c1", u + "c2",
                               u + "gs", u + "gt4", u + "gcnt4", u + "mt16"])
                        transpose(ctx, ctx.pb[4][0:16, 0:128], t16[:, :],
                                  identf[:, :], [u + "t16", u + "idf"], ["psum4"])
                        vop(ctx, lambda v, tcx=tcx, tt=tt: v.tensor_copy(
                            cwf[0:16, tcx * TC + 128 * tt:tcx * TC + 128 * (tt + 1)],
                            ctx.pb[4][0:16, 0:128]),
                            ["psum4"], [u + "cwf"])

            for i in range(NDC):
                vop(ctx, lambda v, i=i: v.memset(yac[i][:, :], 0.0), [],
                    [f"{u}ya{i}"])

            def load_routed(e):
                p = e % 2
                for k in range(NDC):
                    dma_in(ctx, gwb[p][k][:, :], W["f0_gw"][e][k], f"{u}gwb{p}_{k}",
                           eng="gpsimd")
                    dma_in(ctx, uwb[p][k][:, :], W["f0_uw"][e][k], f"{u}uwb{p}_{k}",
                           eng="gpsimd")
                for k in range(4):
                    dma_in(ctx, dwb[p][k][:, :], W["f0_dw"][e][k], f"{u}dwb{p}_{k}",
                           eng="gpsimd")

            def run_expert(p, nhc, gtiles, utiles, dtiles, gres, ures, dres,
                           use_cw, e):
                if use_cw:
                    for tcx in range(NHC):
                        mm(ctx, [(ctx.pb[4][:, :TC], e16[:, 128 * e:128 * (e + 1)],
                                  cwf[0:16, tcx * TC:(tcx + 1) * TC], True, True)],
                           [u + "e16", u + "cwf"], ["psum4"])
                        vop(ctx, lambda v, tcx=tcx: v.tensor_copy(
                            cwb[:, tcx * TC:(tcx + 1) * TC], ctx.pb[4][:, :TC]),
                            ["psum4"], [u + "cwb"])
                for tcx in range(NHC):
                    th = tcx * TC
                    for o4 in range(nhc):
                        kacc(ctx, ctx.pb[0][:, :TC],
                             [gtiles[o4][k] for k in range(NDC)],
                             [xnbh[k][:, th:th + TC] for k in range(NDC)],
                             gres + xnres, ["psum0"])
                        kacc(ctx, ctx.pb[1][:, :TC],
                             [utiles[o4][k] for k in range(NDC)],
                             [xnbh[k][:, th:th + TC] for k in range(NDC)],
                             ures + xnres, ["psum1"])
                        vop(ctx, lambda sc: sc.activation(
                            sil[:, :], ctx.pb[0][:, :TC], AF.Silu),
                            ["psum0"], [u + "sil"], eng="scalar")
                        if use_cw:
                            chain(ctx, [
                                lambda v, o4=o4: v.tensor_mul(
                                    hsl[o4][:, :], sil[:, :], ctx.pb[1][:, :TC]),
                                lambda v, o4=o4, th=th: v.tensor_mul(
                                    hsl[o4][:, :], hsl[o4][:, :],
                                    cwb[:, th:th + TC]),
                            ], [u + "sil", "psum1", u + "cwb"], [f"{u}hsl{o4}"])
                        else:
                            vop(ctx, lambda v, o4=o4: v.tensor_mul(
                                hsl[o4][:, :], sil[:, :], ctx.pb[1][:, :TC]),
                                [u + "sil", "psum1"], [f"{u}hsl{o4}"])
                    for oc in range(NDC):
                        bank = 2 + oc % 2
                        kacc(ctx, ctx.pb[bank][:, :TC],
                             [dtiles[k][:, 128 * oc:128 * (oc + 1)]
                              for k in range(nhc)],
                             [hsl[k][:, :] for k in range(nhc)],
                             dres + [f"{u}hsl{k}" for k in range(nhc)],
                             [f"psum{bank}"])
                        vop(ctx, lambda v, oc=oc, th=th, bank=bank: v.tensor_add(
                            yac[oc][:, th:th + TC], yac[oc][:, th:th + TC],
                            ctx.pb[bank][:, :TC]),
                            [f"psum{bank}", f"{u}ya{oc}"], [f"{u}ya{oc}"])

            load_routed(0)
            for e in range(16):
                p = e % 2
                if e + 1 < 16:
                    load_routed(e + 1)
                gtiles = [[gwb[p][k][:, 128 * o4:128 * (o4 + 1)] for k in range(NDC)]
                          for o4 in range(4)]
                utiles = [[uwb[p][k][:, 128 * o4:128 * (o4 + 1)] for k in range(NDC)]
                          for o4 in range(4)]
                run_expert(p, 4, gtiles, utiles, dwb[p],
                           [f"{u}gwb{p}_{k}" for k in range(NDC)],
                           [f"{u}uwb{p}_{k}" for k in range(NDC)],
                           [f"{u}dwb{p}_{k}" for k in range(4)], True, e)
            for k in range(NDC):
                dma_in(ctx, gwb[0][k][:, :], W["f0_sg"][k][:, 0:512],
                       f"{u}gwb0_{k}", eng="gpsimd")
                dma_in(ctx, gwb[1][k][:, :], W["f0_sg"][k][:, 512:1024],
                       f"{u}gwb1_{k}", eng="gpsimd")
                dma_in(ctx, uwb[0][k][:, :], W["f0_su"][k][:, 0:512],
                       f"{u}uwb0_{k}", eng="gpsimd")
                dma_in(ctx, uwb[1][k][:, :], W["f0_su"][k][:, 512:1024],
                       f"{u}uwb1_{k}", eng="gpsimd")
            for k in range(4):
                dma_in(ctx, dwb[0][k][:, :], W["f0_sd"][k], f"{u}dwb0_{k}",
                       eng="gpsimd")
                dma_in(ctx, dwb[1][k][:, :], W["f0_sd"][4 + k], f"{u}dwb1_{k}",
                       eng="gpsimd")
            gtiles = [[gwb[o8 // 4][k][:, 128 * (o8 % 4):128 * (o8 % 4 + 1)]
                       for k in range(NDC)] for o8 in range(8)]
            utiles = [[uwb[o8 // 4][k][:, 128 * (o8 % 4):128 * (o8 % 4 + 1)]
                       for k in range(NDC)] for o8 in range(8)]
            dtiles = dwb[0] + dwb[1]
            run_expert(0, 8, gtiles, utiles, dtiles,
                       [f"{u}gwb{pp}_{k}" for pp in range(2) for k in range(NDC)],
                       [f"{u}uwb{pp}_{k}" for pp in range(2) for k in range(NDC)],
                       [f"{u}dwb{pp}_{k}" for pp in range(2) for k in range(4)],
                       False, 0)

            with scope(ctx) as sbf:
                xr = sbf(u + "fxr", [128, TC], F32)
                yv = sbf(u + "fyv", [128, TC], F32)
                for tcx in range(NHC):
                    t0 = hb + tcx * TC
                    th = tcx * TC
                    for i in range(NDC):
                        dma_in(ctx, xr[:, :], xin[i][:, t0:t0 + TC], u + "fxr")
                        vop(ctx, lambda v, i=i, th=th, xr=xr, yv=yv: v.tensor_add(
                            yv[:, :], xr[:, :], yac[i][:, th:th + TC]),
                            [u + "fxr", f"{u}ya{i}"], [u + "fyv"])
                        dma_out(ctx, xout[i][:, t0:t0 + TC], yv[:, :],
                                [u + "fyv"], u + "fxo")



def emit_l1_ffn(ctx, xin, xout, W):
    nc = ctx.nc
    u = f"1f{ctx.uid()}_"
    QT = 1024
    NQ = T_FULL // QT
    NQC = QT // TC
    RMS_EPS = 1.1920929e-07
    with scope(ctx) as sb:
        xrm = [sb(f"{u}xr{i}", [128, QT], BF16) for i in range(NDC)]
        xrmres = [f"{u}xr{i}" for i in range(NDC)]
        yac = [sb(f"{u}ya{i}", [128, QT], BF16) for i in range(NDC)]
        cwf = sb(u + "cwf", [14, QT], BF16)
        ln = LNTmp(ctx, sb, u)
        identf = sb(u + "idf", [128, 128], F32)
        rwt = [sb(f"{u}rwt{k}", [128, 14], F32) for k in range(NDC)]
        e14 = sb(u + "e14", [14, 14 * 128], BF16)
        gt = sb(u + "gt", [128, 16], F32)
        gtmp = sb(u + "gtmp", [128, 16], F32)
        gcnt = sb(u + "gcnt", [128, 16], F32)
        gcol = sb(u + "gcol", [128, 1], F32)
        gcol2 = sb(u + "gcol2", [128, 1], F32)
        lgt = sb(u + "lgt", [14, TC], F32)
        xs = [sb(f"{u}x{i}", [128, TC], F32) for i in range(NDC)]
        xnf = [sb(f"{u}xnf{i}", [128, TC], F32) for i in range(NDC)]
        sq = sb(u + "sq", [128, TC], BF16)
        r2r = sb(u + "r2r", [1, TC], F32)
        r2b = sb(u + "r2b", [1, TC], BF16)
        r2bc = sb(u + "r2bc", [128, TC], BF16)
        epst = sb(u + "epst", [1, 1], F32)
        w1t = [[sb(f"{u}w1t{p}_{k}", [128, TC], BF16) for k in range(NDC)]
               for p in range(2)]
        w2t = [[sb(f"{u}w2t{p}_{k}", [128, 1024], BF16) for k in range(8)]
               for p in range(2)]
        rwe = [sb(f"{u}rwe{p}", [128, 8], F32) for p in range(2)]
        hsl = [sb(f"{u}hsl{k}", [128, QT], BF16) for k in range(16)]
        cwb = sb(u + "cwb", [128, QT], BF16)
        sil = sb(u + "sil", [128, TC], F32)
        fxr = sb(u + "fxr", [128, TC], F32)
        fyv = sb(u + "fyv", [128, TC], F32)

        dma_in(ctx, identf[:, :], W["ident_f32"], u + "idf")
        dma_in(ctx, e14[:, :], W["E14"], u + "e14")
        vop(ctx, lambda v: v.memset(epst[:, :], RMS_EPS), [], [u + "epst"])
        for k in range(NDC):
            dma_in(ctx, rwt[k][:, :], W["f1_rw"][k], f"{u}rwt{k}")

        def wsrc(e):
            if e < 14:
                return W["f1_w1"][e], W["f1_w2"][e], W["f1_rms"][e]
            return W["f1_w1_s"][e - 14], W["f1_w2_s"][e - 14], W["f1_rms_s"][e - 14]

        sli = [0]

        def load_w1_slice(e, hc):
            p = sli[0] % 2
            sli[0] += 1
            w1d, _, _ = wsrc(e)
            for k in range(NDC):
                dma_in(ctx, w1t[p][k][:, :], w1d[k][:, TC * hc:TC * (hc + 1)],
                       f"{u}w1t{p}_{k}", eng="gpsimd")
                vop(ctx, lambda v, k=k, p=p, e=e: v.tensor_scalar(
                    w1t[p][k][:, :], w1t[p][k][:, :], rwe[e % 2][:, k:k + 1], None,
                    ALU.mult),
                    [f"{u}w1t{p}_{k}", f"{u}rwe{e % 2}"], [f"{u}w1t{p}_{k}"])
            return p

        w2i = [0]

        def load_w2_half(e, half):
            p = w2i[0] % 2
            w2i[0] += 1
            _, w2d, _ = wsrc(e)
            for k in range(8):
                dma_in(ctx, w2t[p][k][:, :], w2d[8 * half + k], f"{u}w2t{p}_{k}",
                       eng="gpsimd")
            return p

        def load_rms(e):
            _, _, rmsd = wsrc(e)
            dma_in(ctx, rwe[e % 2][:, :], rmsd, f"{u}rwe{e % 2}")

        for q in range(NQ):
            qb = q * QT
            for tcx in range(NQC):
                t0 = qb + tcx * TC
                th = tcx * TC
                rxl = [f"{u}x{i}" for i in range(NDC)]
                rnl = [f"{u}xnf{i}" for i in range(NDC)]
                for i in range(NDC):
                    dma_in(ctx, xs[i][:, :], xin[i][:, t0:t0 + TC], rxl[i])
                emit_ln(ctx, ln, [xs[i][:, :] for i in range(NDC)], rxl,
                        None, rxl, out_f32=[xnf[i][:, :] for i in range(NDC)],
                        out_f32_resl=rnl)
                for i in range(NDC):
                    vop(ctx, lambda sc, i=i: sc.activation(
                        sq[:, :], xnf[i][:, :], AF.Square),
                        [rnl[i]], [u + "sq"], eng="scalar")
                    mm(ctx, [(ctx.pb[6][0:1, :TC], ln.ones_bf[:, :], sq[:, :],
                              i == 0, i == NDC - 1)],
                       [u + "sq", u + "ln_ones"], ["psum6"])
                chain(ctx, [
                    lambda v: v.tensor_scalar_mul(r2r[0:1, :], ctx.pb[6][0:1, :TC],
                                                  1.0 / D),
                ], ["psum6"], [u + "r2r"])
                vop(ctx, lambda sc: sc.activation(r2r[0:1, :], r2r[0:1, :],
                                                  AF.Sqrt, bias=epst[0:1, 0:1]),
                    [u + "r2r", u + "epst"], [u + "r2r"], eng="scalar")
                chain(ctx, [
                    lambda v: v.reciprocal(r2r[0:1, :], r2r[0:1, :]),
                    lambda v: v.tensor_copy(r2b[0:1, :], r2r[0:1, :]),
                ], [u + "r2r"], [u + "r2b"])
                mm(ctx, [(ctx.pb[5][:, :TC], ln.ones1_bf[0:1, :], r2b[0:1, :],
                          True, True)],
                   [u + "r2b", u + "ln_ones1"], ["psum5"])
                vop(ctx, lambda v: v.tensor_copy(r2bc[:, :], ctx.pb[5][:, :TC]),
                    ["psum5"], [u + "r2bc"])
                for i in range(NDC):
                    vop(ctx, lambda v, i=i, th=th: v.tensor_mul(
                        xrm[i][:, th:th + TC], xnf[i][:, :], r2bc[:, :]),
                        [rnl[i], u + "r2bc"], [xrmres[i]])
                kacc(ctx, ctx.pb[0][0:14, :TC],
                     [rwt[k][:, :] for k in range(NDC)],
                     [xnf[k][:, :] for k in range(NDC)],
                     [f"{u}rwt{k}" for k in range(NDC)] + rnl, ["psum0"])
                vop(ctx, lambda v: v.tensor_copy(lgt[0:14, :], ctx.pb[0][0:14, :TC]),
                    ["psum0"], [u + "lgt"])
                for tt in range(TC // 128):
                    transpose(ctx, ctx.pb[4][0:128, 0:14],
                              lgt[0:14, 128 * tt:128 * (tt + 1)],
                              identf[0:14, 0:14], [u + "lgt", u + "idf"], ["psum4"])
                    chain(ctx, [
                        lambda v: v.reduce_max(gcol[:, :], ctx.pb[4][0:128, 0:14],
                                               axis=AX, negate=True),
                    ], ["psum4"], [u + "gcol"])
                    vop(ctx, lambda sc: sc.activation(
                        gt[:, 0:14], ctx.pb[4][0:128, 0:14], AF.Exp,
                        bias=gcol[:, 0:1], accum_out=gcol2[:, 0:1]),
                        ["psum4", u + "gcol"], [u + "gt", u + "gcol2"], eng="scalar")
                    fns = [
                        lambda v: v.reciprocal(gcol2[:, :], gcol2[:, :]),
                        lambda v: v.tensor_scalar(gt[:, 0:14], gt[:, 0:14],
                                                  gcol2[:, 0:1], None, ALU.mult),
                    ]
                    for e in range(14):
                        fns.append(lambda v, e=e: v.tensor_scalar(
                            gtmp[:, 0:14], gt[:, 0:14], gt[:, e:e + 1], None,
                            ALU.is_gt, ALU.add, accum_out=gcnt[:, e:e + 1]))
                    fns += [
                        lambda v: v.tensor_scalar(gcnt[:, 0:14], gcnt[:, 0:14], 2.0,
                                                  None, ALU.is_lt),
                        lambda v: v.tensor_mul(gtmp[:, 0:14], gt[:, 0:14],
                                               gcnt[:, 0:14]),
                        lambda v: v.reduce_sum(gcol[:, 0:1], gtmp[:, 0:14], axis=AX),
                        lambda v: v.tensor_scalar_add(gcol[:, 0:1], gcol[:, 0:1], 1e-9),
                        lambda v: v.reciprocal(gcol[:, :], gcol[:, :]),
                        lambda v: v.tensor_scalar(gtmp[:, 0:14], gtmp[:, 0:14],
                                                  gcol[:, 0:1], None, ALU.mult),
                    ]
                    chain(ctx, fns, [u + "gt", u + "gcol2"],
                          [u + "gtmp", u + "gcnt", u + "gcol", u + "gcol2", u + "gt"])
                    transpose(ctx, ctx.pb[4][0:16, 0:128], gtmp[:, 0:16],
                              identf[:, :], [u + "gtmp", u + "idf"], ["psum4"])
                    vop(ctx, lambda v, tcx=tcx, tt=tt: v.tensor_copy(
                        cwf[0:14, tcx * TC + 128 * tt:tcx * TC + 128 * (tt + 1)],
                        ctx.pb[4][0:14, 0:128]),
                        ["psum4"], [u + "cwf"])

            for i in range(NDC):
                vop(ctx, lambda v, i=i: v.memset(yac[i][:, :], 0.0), [],
                    [f"{u}ya{i}"])
            load_rms(0)
            p1 = load_w1_slice(0, 0)
            p2a = load_w2_half(0, 0)
            for e in range(16):
                routed = e < 14
                if e + 1 < 16:
                    load_rms(e + 1)
                if routed:
                    for tcx in range(NQC):
                        mm(ctx, [(ctx.pb[4][:, :TC], e14[:, 128 * e:128 * (e + 1)],
                                  cwf[0:14, tcx * TC:(tcx + 1) * TC], True, True)],
                           [u + "e14", u + "cwf"], ["psum4"])
                        vop(ctx, lambda v, tcx=tcx: v.tensor_copy(
                            cwb[:, tcx * TC:(tcx + 1) * TC], ctx.pb[4][:, :TC]),
                            ["psum4"], [u + "cwb"])
                for hc in range(4):
                    pcur = p1
                    if hc < 3:
                        p1 = load_w1_slice(e, hc + 1)
                    elif e + 1 < 16:
                        p1 = load_w1_slice(e + 1, 0)
                    for tcx in range(NQC):
                        th = tcx * TC
                        for o4 in range(4):
                            bank = o4 % 4
                            kacc(ctx, ctx.pb[bank][:, :TC],
                                 [w1t[pcur][k][:, 128 * o4:128 * (o4 + 1)]
                                  for k in range(NDC)],
                                 [xrm[k][:, th:th + TC] for k in range(NDC)],
                                 [f"{u}w1t{pcur}_{k}" for k in range(NDC)] + xrmres,
                                 [f"psum{bank}"])
                            hi = 4 * hc + o4
                            if routed:
                                vop(ctx, lambda sc, bank=bank: sc.activation(
                                    sil[:, :], ctx.pb[bank][:, :TC], AF.Silu),
                                    [f"psum{bank}"], [u + "sil"], eng="scalar")
                                vop(ctx, lambda v, hi=hi, th=th: v.tensor_mul(
                                    hsl[hi][:, th:th + TC], sil[:, :],
                                    cwb[:, th:th + TC]),
                                    [u + "sil", u + "cwb"], [f"{u}hsl{hi}"])
                            else:
                                vop(ctx, lambda sc, hi=hi, th=th, bank=bank:
                                    sc.activation(hsl[hi][:, th:th + TC],
                                                  ctx.pb[bank][:, :TC], AF.Silu),
                                    [f"psum{bank}"], [f"{u}hsl{hi}"], eng="scalar")
                p2b = load_w2_half(e, 1)
                for tcx in range(NQC):
                    th = tcx * TC
                    for oc in range(NDC):
                        kacc2(ctx, ctx.pb[oc][:, :TC],
                              [w2t[p2a][k][:, 128 * oc:128 * (oc + 1)]
                               for k in range(8)],
                              [hsl[k][:, th:th + TC] for k in range(8)],
                              [f"{u}w2t{p2a}_{k}" for k in range(8)] +
                              [f"{u}hsl{k}" for k in range(8)],
                              [f"psum{oc}"], start=True, stop=False)
                    if tcx == NQC - 1 and e + 1 < 16:
                        p2a_next = load_w2_half(e + 1, 0)
                    for oc in range(NDC):
                        kacc2(ctx, ctx.pb[oc][:, :TC],
                              [w2t[p2b][k][:, 128 * oc:128 * (oc + 1)]
                               for k in range(8)],
                              [hsl[8 + k][:, th:th + TC] for k in range(8)],
                              [f"{u}w2t{p2b}_{k}" for k in range(8)] +
                              [f"{u}hsl{8 + k}" for k in range(8)],
                              [f"psum{oc}"], start=False, stop=True)
                        vop(ctx, lambda v, oc=oc, th=th: v.tensor_add(
                            yac[oc][:, th:th + TC], yac[oc][:, th:th + TC],
                            ctx.pb[oc][:, :TC]),
                            [f"psum{oc}", f"{u}ya{oc}"], [f"{u}ya{oc}"])
                if e + 1 < 16:
                    p2a = p2a_next

            for tcx in range(NQC):
                t0 = qb + tcx * TC
                th = tcx * TC
                for i in range(NDC):
                    dma_in(ctx, fxr[:, :], xin[i][:, t0:t0 + TC], u + "fxr")
                    vop(ctx, lambda v, i=i, th=th: v.tensor_add(
                        fyv[:, :], fxr[:, :], yac[i][:, th:th + TC]),
                        [u + "fxr", f"{u}ya{i}"], [u + "fyv"])
                    dma_out(ctx, xout[i][:, t0:t0 + TC], fyv[:, :],
                            [u + "fyv"], u + "fxo")



def emit_l2_ffn(ctx, xin, xout, W):
    nc = ctx.nc
    u = f"2f{ctx.uid()}_"
    h_scr = nc.dram_tensor(u + "h", [32, 128, T_FULL], BF16, kind="Internal").ap()
    g_scr = nc.dram_tensor(u + "g", [32, 128, T_FULL], BF16, kind="Internal").ap()

    with scope(ctx) as sb:
        xnb = [sb(f"{u}xn{i}", [128, T_FULL], BF16) for i in range(NDC)]
        xnres = [f"{u}xn{i}" for i in range(NDC)]
        xs = [sb(f"{u}x{i}", [128, TC], F32) for i in range(NDC)]
        ln = LNTmp(ctx, sb, u)
        for tcx in range(NTC):
            t0 = tcx * TC
            rxl = [f"{u}x{i}" for i in range(NDC)]
            for i in range(NDC):
                dma_in(ctx, xs[i][:, :], xin[i][:, t0:t0 + TC], rxl[i])
            emit_ln(ctx, ln, [xs[i][:, :] for i in range(NDC)], rxl,
                    [xnb[i][:, t0:t0 + TC] for i in range(NDC)], xnres)

        w1g = [[[sb(f"{u}aw1{p}_{k}_{j}", [128, 128], BF16) for k in range(NDC)]
                for j in range(4)] for p in range(2)]
        w2g = [[[sb(f"{u}aw2{p}_{k}_{j}", [128, 128], BF16) for k in range(NDC)]
                for j in range(4)] for p in range(2)]
        s1 = sb(u + "s1", [128, 32], F32)
        s2 = sb(u + "s2", [128, 32], F32)
        sil = sb(u + "sil", [128, TC], F32)
        hbf = sb(u + "hbf", [128, TC], BF16)
        dma_in(ctx, s1[:, :], W["f2_s1"], u + "s1")
        dma_in(ctx, s2[:, :], W["f2_s2"], u + "s2")

        def load_a(og):
            p = og % 2
            for j in range(4):
                oc = og * 4 + j
                for k in range(NDC):
                    dma_in(ctx, w1g[p][j][k][:, :],
                           W["f2_w1"][k][:, 128 * oc:128 * (oc + 1)],
                           f"{u}aw1{p}_{k}_{j}", eng="gpsimd")
                    dma_in(ctx, w2g[p][j][k][:, :],
                           W["f2_w2"][k][:, 128 * oc:128 * (oc + 1)],
                           f"{u}aw2{p}_{k}_{j}", eng="gpsimd")

        load_a(0)
        for og in range(8):
            p = og % 2
            if og + 1 < 8:
                load_a(og + 1)
            for tcx in range(NTC):
                t0 = tcx * TC
                for j in range(4):
                    oc = og * 4 + j
                    ba, bb = (2 * j) % 4, (2 * j + 1) % 4
                    kacc(ctx, ctx.pb[ba][:, :TC],
                         [w1g[p][j][k][:, :] for k in range(NDC)],
                         [xnb[k][:, t0:t0 + TC] for k in range(NDC)],
                         [f"{u}aw1{p}_{k}_{j}" for k in range(NDC)] + xnres,
                         [f"psum{ba}"])
                    kacc(ctx, ctx.pb[bb][:, :TC],
                         [w2g[p][j][k][:, :] for k in range(NDC)],
                         [xnb[k][:, t0:t0 + TC] for k in range(NDC)],
                         [f"{u}aw2{p}_{k}_{j}" for k in range(NDC)] + xnres,
                         [f"psum{bb}"])
                    vop(ctx, lambda s_, oc=oc, ba=ba: s_.activation(
                        sil[:, :], ctx.pb[ba][:, :TC], AF.Silu,
                        scale=s1[:, oc:oc + 1]),
                        [f"psum{ba}", u + "s1"], [u + "sil"], eng="scalar")
                    vop(ctx, lambda v, oc=oc, bb=bb: v.scalar_tensor_tensor(
                        hbf[:, :], ctx.pb[bb][:, :TC], s2[:, oc:oc + 1],
                        sil[:, :], ALU.mult, ALU.mult),
                        [f"psum{bb}", u + "s2", u + "sil"], [u + "hbf"])
                    dma_out(ctx, h_scr[oc][:, t0:t0 + TC], hbf[:, :],
                            [u + "hbf"], u + "hscr")

    def gelu_stage(src_scr, dst_scr, w_key, ws_key, stag):
        with scope(ctx) as sb2:
            wgt = [[[sb2(f"{u}{stag}w{p}_{k}_{j}", [128, 128], BF16)
                     for k in range(32)] for j in range(8)] for p in range(2)]
            hs = sb2(u + stag + "hs", [128, 32], F32)
            htile = [sb2(f"{u}{stag}h{k}", [128, TC], BF16) for k in range(32)]
            obf = sb2(u + stag + "obf", [128, TC], BF16)
            dma_in(ctx, hs[:, :], W[ws_key], u + stag + "hs")

            def load_g(og):
                p = og % 2
                for j in range(8):
                    oc = og * 8 + j
                    for k in range(32):
                        dma_in(ctx, wgt[p][j][k][:, :],
                               W[w_key][k][:, 128 * oc:128 * (oc + 1)],
                               f"{u}{stag}w{p}_{k}_{j}", eng="gpsimd")

            load_g(0)
            for og in range(4):
                p = og % 2
                if og + 1 < 4:
                    load_g(og + 1)
                for tcx in range(NTC):
                    t0 = tcx * TC
                    hres = [f"{u}{stag}h{k}" for k in range(32)]
                    for k in range(32):
                        dma_in(ctx, htile[k][:, :], src_scr[k][:, t0:t0 + TC],
                               hres[k])
                    for j in range(8):
                        oc = og * 8 + j
                        bank = j % 4
                        kacc(ctx, ctx.pb[bank][:, :TC],
                             [wgt[p][j][k][:, :] for k in range(32)],
                             [htile[k][:, :] for k in range(32)],
                             [f"{u}{stag}w{p}_{k}_{j}" for k in range(32)] + hres,
                             [f"psum{bank}"])
                        vop(ctx, lambda s_, oc=oc, bank=bank: s_.activation(
                            obf[:, :], ctx.pb[bank][:, :TC], AF.Gelu,
                            scale=hs[:, oc:oc + 1]),
                            [f"psum{bank}", u + stag + "hs"], [u + stag + "obf"],
                            eng="scalar")
                        dma_out(ctx, dst_scr[oc][:, t0:t0 + TC], obf[:, :],
                                [u + stag + "obf"], u + stag + "dst")

    gelu_stage(h_scr, g_scr, "f2_h1", "f2_h1s", "b")
    gelu_stage(g_scr, h_scr, "f2_h2", "f2_h2s", "c")

    with scope(ctx) as sb2:
        w3t = [[sb2(f"{u}dw{k}_{j}", [128, 128], BF16) for k in range(32)]
               for j in range(8)]
        s3 = sb2(u + "s3", [128, 8], F32)
        htile = [sb2(f"{u}dh{k}", [128, TC], BF16) for k in range(32)]
        xr = sb2(u + "dxr", [128, TC], F32)
        yv = sb2(u + "yv", [128, TC], F32)
        dma_in(ctx, s3[:, :], W["f2_w3s"], u + "s3")
        for j in range(8):
            for k in range(32):
                dma_in(ctx, w3t[j][k][:, :], W["f2_w3"][k][:, 128 * j:128 * (j + 1)],
                       f"{u}dw{k}_{j}", eng="gpsimd")
        for tcx in range(NTC):
            t0 = tcx * TC
            hres = [f"{u}dh{k}" for k in range(32)]
            for k in range(32):
                dma_in(ctx, htile[k][:, :], h_scr[k][:, t0:t0 + TC], hres[k])
            for oc in range(8):
                bank = oc % 4
                kacc(ctx, ctx.pb[bank][:, :TC],
                     [w3t[oc][k][:, :] for k in range(32)],
                     [htile[k][:, :] for k in range(32)],
                     [f"{u}dw{k}_{oc}" for k in range(32)] + hres,
                     [f"psum{bank}"])
                dma_in(ctx, xr[:, :], xin[oc][:, t0:t0 + TC], u + "dxr")
                vop(ctx, lambda v, oc=oc, bank=bank: v.scalar_tensor_tensor(
                    yv[:, :], ctx.pb[bank][:, :TC], s3[:, oc:oc + 1],
                    xr[:, :], ALU.mult, ALU.add),
                    [f"psum{bank}", u + "s3", u + "dxr"], [u + "yv"])
                dma_out(ctx, xout[oc][:, t0:t0 + TC], yv[:, :],
                        [u + "yv"], u + "xo")



def emit_l3_ffn(ctx, xin, xout, W):
    nc = ctx.nc
    with contextlib.ExitStack() as bs:
        def sb(n, s, d):
            return bs.enter_context(nc.sbuf_tensor(n, s, d))
        tag = "f3_"

        wq = [sb(f"f3_w{i}", [128, 1024], F32) for i in range(8)]
        wdT = [sb(f"f3_wT{i}", [128, 1024], F32) for i in range(8)]
        ident = sb("f3_id", [128, 128], F32)
        bias = sb("f3_b", [128, 8], F32)
        xs = [sb(f"f3_x{i}", [128, TC], F32) for i in range(NDC)]
        xn = [sb(f"f3_xn{i}", [128, TC], F32) for i in range(NDC)]
        xnb = [sb(f"f3_xnb{i}", [128, TC], BF16) for i in range(NDC)]
        yout = [sb(f"f3_y{i}", [128, TC], F32) for i in range(NDC)]
        qt = QTmp(sb, tag, 1024)
        ln = LNTmp(ctx, sb, tag)

        dma_in(ctx, ident[:, :], W["ident_f32"], "f3_id")
        dma_in(ctx, bias[:, :], W["w3f_b"], "f3_b")
        for i in range(8):
            dma_in(ctx, wq[i][:, :], W["w3f_w"][i], f"f3_w{i}", eng="gpsimd")
            emit_quant(ctx, qt, wq[i][:, :], wq[i][:, :], 1024, "free",
                       [f"f3_w{i}"], [f"f3_w{i}"])
        for i in range(8):
            for j in range(8):
                transpose(ctx, ctx.pb[4][:, 0:128], wq[i][:, 128 * j:128 * (j + 1)],
                          ident[:, :], [f"f3_w{i}", "f3_id"], ["psum4"])
                vop(ctx, lambda v, i=i, j=j: v.tensor_copy(
                    wdT[j][:, 128 * i:128 * (i + 1)], ctx.pb[4][:, 0:128]),
                    ["psum4"], [f"f3_wT{j}"])

        for tcx in range(NTC):
            t0 = tcx * TC
            rxl = [f"f3_x{i}" for i in range(NDC)]
            rnl = [f"f3_xn{i}" for i in range(NDC)]
            for i in range(NDC):
                dma_in(ctx, xs[i][:, :], xin[i][:, t0:t0 + TC], rxl[i])
            emit_ln(ctx, ln, [xs[i][:, :] for i in range(NDC)], rxl,
                    [xnb[i][:, :] for i in range(NDC)], [f"f3_xnb{i}" for i in range(NDC)],
                    out_f32=[xn[i][:, :] for i in range(NDC)], out_f32_resl=rnl)
            for i in range(NDC):
                emit_quant(ctx, qt, xn[i][:, :], xn[i][:, :], TC, "part",
                           [rnl[i]], [rnl[i]])
            for oc in range(8):
                bank = oc % 4
                kacc(ctx, ctx.pb[bank][:, :TC],
                     [wdT[k][:, 128 * oc:128 * (oc + 1)] for k in range(8)],
                     [xn[k][:, :] for k in range(8)],
                     rnl + [f"f3_wT{k}" for k in range(8)],
                     [f"psum{bank}"])
                vop(ctx, lambda v, oc=oc, bank=bank: v.scalar_tensor_tensor(
                    yout[oc][:, :], ctx.pb[bank][:, :TC], bias[:, oc:oc + 1],
                    xs[oc][:, :], ALU.add, ALU.add),
                    [f"psum{bank}", "f3_b", rxl[oc]], [f"f3_y{oc}"])
                dma_out(ctx, xout[oc][:, t0:t0 + TC], yout[oc][:, :],
                        [f"f3_y{oc}"], f"f3_xo{tcx}")



def host_weight_layouts(inputs):
    W = {}
    f = np.ascontiguousarray
    W["a1_wq"] = f(inputs["L1_attn_W_q"].T.reshape(8, 128, 1024))
    W["a1_wk"] = f(inputs["L1_attn_W_k"].T.reshape(8, 128, 1024))
    W["a1_wv"] = f(inputs["L1_attn_W_v"].T.reshape(8, 128, 1024))
    W["a1_wo"] = f(inputs["L1_attn_W_o"].T.reshape(8, 128, 1024))
    _i = np.arange(32, dtype=np.float64)
    _t = np.minimum(np.arange(T_FULL), 2047).astype(np.float64)
    _ang = _t[None, :] * (10000.0 ** (-2.0 * _i / 64))[:, None]
    _c = np.cos(_ang)
    _s = np.sin(_ang)
    _p = np.arange(128)
    _iofp = (_p % 64) // 2
    cB = _c[_iofp]
    sB = _s[_iofp] * np.where(_p % 2 == 0, -1.0, 1.0)[:, None]
    W["a1_cos"] = f(cB.astype(np.float32).astype(bfdt))
    W["a1_sin"] = f(sB.astype(np.float32).astype(bfdt))
    Pswap = np.zeros((128, 128), dtype=np.float32)
    Pswap[np.arange(128), np.arange(128) ^ 1] = 1.0
    W["a1_pswap"] = f(Pswap.astype(bfdt))
    def _bd(m):
        out = np.zeros((8, 128, 128), dtype=np.float32)
        for p in range(8):
            out[p, 0:64, 0:64] = m[2 * p].T
            out[p, 64:128, 64:128] = m[2 * p + 1].T
        return out
    W["a3_wd"] = f(inputs["L3_attn_W_down"].reshape(8, 128, 1024))
    W["a3_tk"] = f(_bd(inputs["L3_attn_tokeys"]))
    W["a3_tq"] = f(_bd(inputs["L3_attn_toqueries"]))
    W["a3_tv"] = f(_bd(inputs["L3_attn_tovalues"]))
    W["a3_wu"] = f(inputs["L3_attn_W_unify"].reshape(8, 128, 1024))
    W["a3_bu"] = f(inputs["L3_attn_b_unify"].reshape(8, 128).T)
    W["a0_wqkv"] = f(inputs["L0_attn_w_qkv"].reshape(8, 128, 3072))
    W["a0_wg"] = f(inputs["L0_attn_w_gate"].reshape(8, 128, 1024))
    W["a0_wp"] = f(inputs["L0_attn_w_proj"].reshape(8, 128, 1024))
    W["a0_bqkv"] = f(inputs["L0_attn_b_qkv"].reshape(24, 128).T)
    W["a0_bg"] = f(inputs["L0_attn_b_gate"].reshape(8, 128).T)
    W["a0_bp"] = f(inputs["L0_attn_b_proj"].reshape(8, 128).T)
    W["a0_lg"] = f(inputs["L0_attn_ln_g"].reshape(8, 128).T)
    W["a0_lb"] = f(inputs["L0_attn_ln_b"].reshape(8, 128).T)
    s_ = np.arange(128)[:, None]
    t_ = np.arange(512)[None, :]
    W["maskA"] = f(np.concatenate([(128 * j + s_ <= t_) for j in range(4)],
                                  axis=1).astype(np.float32).astype(bfdt))
    W["f0_rw"] = f(inputs["L0_ffn_router_weight"].T.reshape(8, 128, 16))
    W["f0_eb"] = f(inputs["L0_ffn_e_bias"].reshape(16, 1))
    W["f0_gw"] = f(inputs["L0_ffn_gate_w"].transpose(0, 2, 1).reshape(16, 8, 128, 512))
    W["f0_uw"] = f(inputs["L0_ffn_up_w"].transpose(0, 2, 1).reshape(16, 8, 128, 512))
    W["f0_dw"] = f(inputs["L0_ffn_down_w"].transpose(0, 2, 1).reshape(16, 4, 128, 1024))
    W["f0_sg"] = f(inputs["L0_ffn_sh_gate_w"].T.reshape(8, 128, 1024))
    W["f0_su"] = f(inputs["L0_ffn_sh_up_w"].T.reshape(8, 128, 1024))
    W["f0_sd"] = f(inputs["L0_ffn_sh_down_w"].T.reshape(8, 128, 1024))
    E16 = np.zeros((16, 16 * 128), dtype=np.float32)
    for e in range(16):
        E16[e, 128 * e:128 * (e + 1)] = 1.0
    W["E16"] = f(E16.astype(bfdt))
    W["f1_rw"] = f(inputs["L1_ffn_router_w"].reshape(8, 128, 14))
    W["f1_rms"] = f(inputs["L1_ffn_rms_w"].reshape(14, 8, 128).transpose(0, 2, 1))
    W["f1_rms_s"] = f(inputs["L1_ffn_rms_w_s"].reshape(2, 8, 128).transpose(0, 2, 1))
    W["f1_w1"] = f(inputs["L1_ffn_w1"].reshape(14, 8, 128, 2048))
    W["f1_w2"] = f(inputs["L1_ffn_w2"].reshape(14, 16, 128, 1024))
    W["f1_w1_s"] = f(inputs["L1_ffn_w1_s"].reshape(2, 8, 128, 2048))
    W["f1_w2_s"] = f(inputs["L1_ffn_w2_s"].reshape(2, 16, 128, 1024))
    E14 = np.zeros((14, 14 * 128), dtype=np.float32)
    for e in range(14):
        E14[e, 128 * e:128 * (e + 1)] = 1.0
    W["E14"] = f(E14.astype(bfdt))
    W["a2_wk"] = f(inputs["L2_attn_W_K"].transpose(2, 0, 1).reshape(1024, 1024).reshape(8, 128, 1024))
    W["a2_wq"] = f(inputs["L2_attn_W_Q"].transpose(2, 0, 1).reshape(1024, 1024).reshape(8, 128, 1024))
    W["a2_wv"] = f(inputs["L2_attn_W_V"].transpose(2, 0, 1).reshape(1024, 1024).reshape(8, 128, 1024))
    q_ = np.arange(128)[:, None]
    k_ = np.arange(512)[None, :]
    W["maskQ"] = f(np.concatenate([(k_ > 128 * j + q_) * np.float32(-1e30)
                                   for j in range(4)], axis=1).astype(bfdt))
    kk_ = np.arange(128)[:, None]
    qq_ = np.arange(512)[None, :]
    W["maskT"] = f(np.concatenate([(qq_ < 128 * j + kk_) * np.float32(-1e30)
                                   for j in range(4)], axis=1).astype(bfdt))
    W["f2_w1"] = f(inputs["L2_ffn_w1_w"].T.reshape(8, 128, 4096))
    W["f2_w2"] = f(inputs["L2_ffn_w2_w"].T.reshape(8, 128, 4096))
    W["f2_h1"] = f(inputs["L2_ffn_h1_w"].T.reshape(32, 128, 4096))
    W["f2_h2"] = f(inputs["L2_ffn_h2_w"].T.reshape(32, 128, 4096))
    W["f2_w3"] = f(inputs["L2_ffn_w3_w"].T.reshape(32, 128, 1024))
    W["f2_s1"] = f(inputs["L2_ffn_w1_s"].reshape(32, 128).T)
    W["f2_s2"] = f(inputs["L2_ffn_w2_s"].reshape(32, 128).T)
    W["f2_h1s"] = f(inputs["L2_ffn_h1_s"].reshape(32, 128).T)
    W["f2_h2s"] = f(inputs["L2_ffn_h2_s"].reshape(32, 128).T)
    W["f2_w3s"] = f(inputs["L2_ffn_w3_s"].reshape(8, 128).T)
    W["w3f_w"] = f(inputs["L3_ffn_weight"].reshape(8, 128, 1024))
    W["w3f_b"] = f(inputs["L3_ffn_bias"].reshape(8, 128).T)
    W["ident_f32"] = np.eye(128, dtype=np.float32)
    W["ident_bf"] = np.eye(128, dtype=np.float32).astype(bfdt)
    return W


def declare_weights(nc, W):
    aps = {}
    for k, v in W.items():
        aps[k] = nc.dram_tensor(k, list(v.shape), mybir.dt.from_np(v.dtype),
                                kind="ExternalInput").ap()
    return aps



BLOCK_SEQ = ["l0_attn", "l0_ffn", "l1_attn", "l1_ffn",
             "l2_attn", "l2_ffn", "l3_attn", "l3_ffn"]
BLOCK_WKEYS = {
    "l0_attn": ["a0_wqkv", "a0_wg", "a0_wp", "a0_bqkv", "a0_bg", "a0_bp",
                "a0_lg", "a0_lb", "maskA", "ident_bf"],
    "l0_ffn": ["f0_rw", "f0_eb", "f0_gw", "f0_uw", "f0_dw", "f0_sg", "f0_su",
               "f0_sd", "E16", "ident_f32"],
    "l1_attn": ["a1_wq", "a1_wk", "a1_wv", "a1_wo", "a1_cos", "a1_sin",
                "a1_pswap", "maskQ", "maskT", "ident_bf", "ident_f32"],
    "l1_ffn": ["f1_rw", "f1_rms", "f1_rms_s", "f1_w1", "f1_w2", "f1_w1_s",
               "f1_w2_s", "E14", "ident_f32"],
    "l2_attn": ["a2_wk", "a2_wq", "a2_wv", "maskQ", "maskT", "ident_bf",
                "ident_f32"],
    "l2_ffn": ["f2_w1", "f2_w2", "f2_h1", "f2_h2", "f2_w3", "f2_s1", "f2_s2",
               "f2_h1s", "f2_h2s", "f2_w3s"],
    "l3_attn": ["a3_wd", "a3_tk", "a3_tq", "a3_tv", "a3_wu", "a3_bu", "maskQ",
                "maskT", "ident_bf", "ident_f32"],
    "l3_ffn": ["w3f_w", "w3f_b", "ident_f32"],
}
BLOCK_EMITTERS = {"l0_attn": emit_l0_attn, "l0_ffn": emit_l0_ffn,
                  "l1_attn": emit_l1_attn, "l2_attn": emit_l2_attn,
                  "l3_attn": emit_l3_attn, "l1_ffn": emit_l1_ffn,
                  "l2_ffn": emit_l2_ffn, "l3_ffn": emit_l3_ffn}


def build_single_block(block_name, W):
    nc = bass.Bass("TRN2", target_bir_lowering=False)
    xin_t = nc.dram_tensor("xin", [NDC, 128, T_FULL], F32, kind="ExternalInput")
    xout_t = nc.dram_tensor("xout", [NDC, 128, T_FULL], F32, kind="ExternalOutput")
    Wsub = {k: W[k] for k in BLOCK_WKEYS[block_name]}
    waps = declare_weights(nc, Wsub)
    with contextlib.ExitStack() as stack:
        ctx = Ctx(nc, stack)
        xin = [xin_t.ap()[i] for i in range(NDC)]
        xout = [xout_t.ap()[i] for i in range(NDC)]
        BLOCK_EMITTERS[block_name](ctx, xin, xout, waps)
        ctx.p.wait_all()
        ctx.p.replay()
    return nc


def build_fused(W):
    nc = bass.Bass("TRN2", target_bir_lowering=False)
    xin_t = nc.dram_tensor("xin", [NDC, 128, T_FULL], F32, kind="ExternalInput")
    xout_t = nc.dram_tensor("xout", [NDC, 128, T_FULL], F32, kind="ExternalOutput")
    xa_t = nc.dram_tensor("xscrA", [NDC, 128, T_FULL], F32, kind="Internal")
    xb_t = nc.dram_tensor("xscrB", [NDC, 128, T_FULL], F32, kind="Internal")
    waps = declare_weights(nc, W)
    hop = [xa_t.ap(), xb_t.ap()]
    with contextlib.ExitStack() as stack:
        ctx = Ctx(nc, stack)
        nblk = len(BLOCK_SEQ)
        for bi, name in enumerate(BLOCK_SEQ):
            src = xin_t.ap() if bi == 0 else hop[bi % 2]
            dst = xout_t.ap() if bi == nblk - 1 else hop[(bi + 1) % 2]
            xin = [src[i] for i in range(NDC)]
            xout = [dst[i] for i in range(NDC)]
            BLOCK_EMITTERS[name](ctx, xin, xout, waps)
        ctx.p.wait_all()
        ctx.p.replay()
    return nc


def kernel(**inputs):
    x = inputs["L0_attn_x"]
    W = host_weight_layouts(inputs)
    xs = [np.ascontiguousarray(x[b].T).reshape(NDC, 128, T_FULL) for b in range(8)]
    nc = build_fused(W)
    in_maps = [dict(W, xin=xs[b]) for b in range(8)]
    res = run_bass_kernel_spmd(nc, in_maps, core_ids=list(range(8)))
    out = np.stack([r["xout"].reshape(D, T_FULL).T for r in res.results])
    return out


def kernel_unfused(**inputs):
    x = inputs["L0_attn_x"]
    W = host_weight_layouts(inputs)
    xs = [np.ascontiguousarray(x[b].T).reshape(NDC, 128, T_FULL) for b in range(8)]
    cur = xs
    for name in BLOCK_SEQ:
        nc = build_single_block(name, W)
        Wsub = {k: W[k] for k in BLOCK_WKEYS[name]}
        in_maps = [dict(Wsub, xin=cur[b]) for b in range(8)]
        res = run_bass_kernel_spmd(nc, in_maps, core_ids=list(range(8)))
        cur = [r["xout"].reshape(NDC, 128, T_FULL) for r in res.results]
    out = np.stack([c.reshape(D, T_FULL).T for c in cur])
    return out


# version 05
# speedup vs baseline: 1.0140x; 1.0140x over previous
import contextlib
import numpy as np
import ml_dtypes
import concourse.bass as bass
import concourse.mybir as mybir
from concourse.bass_utils import run_bass_kernel_spmd

F32 = mybir.dt.float32
F32R = mybir.dt.float32r
BF16 = mybir.dt.bfloat16
U32 = mybir.dt.uint32
AF = mybir.ActivationFunctionType
ALU = mybir.AluOpType
AX = mybir.AxisListType.X
bfdt = ml_dtypes.bfloat16

ENG = ("sync", "scalar", "vector", "tensor", "gpsimd")
QSEM = {"sync": "dS", "scalar": "dA", "gpsimd": "dP"}
NSLOT = 6

T_FULL = 4096
D = 1024
NDC = 8
TC = 512
NTC = T_FULL // TC
MAGIC = 12582912.0


class P:

    def __init__(self, nc, stack):
        self.nc = nc
        self.stack = stack
        self.ops = {e: [] for e in ENG}
        self.sem = {}
        self.cnt = {}
        self.waited = {e: {} for e in ENG}
        self.res = {}
        self.dma_idx = {}
        for e in ENG:
            self._mk("s_" + e)
        for q in set(QSEM.values()):
            self.dma_idx[q] = 0
            for i in range(NSLOT):
                self._mk(f"{q}{i}")

    def _mk(self, name):
        self.sem[name] = self.stack.enter_context(self.nc.semaphore(name))
        self.cnt[name] = 0

    def _deps(self, e, reads, writes):
        m = {}
        for r in reads:
            d = self.res.get(r)
            if d and d["w"] is not None:
                s, v = d["w"]
                m[s] = max(m.get(s, 0), v)
        for w in writes:
            d = self.res.get(w)
            if d:
                for s, v in ([d["w"]] if d["w"] else []) + d["r"]:
                    m[s] = max(m.get(s, 0), v)
        own = "s_" + e
        out = []
        for s, v in m.items():
            if s == own and e == "tensor":
                continue
            if s == own and self.cnt[own] < v:
                raise RuntimeError("self-dep on not-yet-emitted token")
            if self.waited[e].get(s, 0) >= v:
                continue
            out.append((s, v))
        return out

    def emit(self, e, reads, writes, fn, dma=False):
        waits = self._deps(e, reads, writes)
        if dma:
            q = QSEM[e]
            sname = f"{q}{self.dma_idx[q] % NSLOT}"
            self.dma_idx[q] += 1
            amount = 16
            prev = self.cnt[sname]
            if prev > 0 and self.waited[e].get(sname, 0) < prev and \
                    all(s != sname for s, _ in waits):
                waits = waits + [(sname, prev)]
        else:
            sname, amount = "s_" + e, 1
        self.cnt[sname] += amount
        tok = (sname, self.cnt[sname])
        for s, v in waits:
            self.waited[e][s] = max(self.waited[e].get(s, 0), v)
        sem = self.sem[sname]

        def closure(eng, waits=waits, sem=sem, amount=amount, fn=fn):
            for s, v in waits:
                eng.wait_ge(self.sem[s], v)
            fn(eng).then_inc(sem, amount)

        self.ops[e].append(closure)
        for r in reads:
            self.res.setdefault(r, {"w": None, "r": []})["r"].append(tok)
        for w in writes:
            d = self.res.setdefault(w, {"w": None, "r": []})
            d["w"] = tok
            d["r"] = []
        return tok

    def barrier(self):
        snap = dict(self.cnt)
        for e in ENG:
            for s, v in snap.items():
                if v > 0 and self.waited[e].get(s, 0) < v:
                    self.ops[e].append(
                        lambda eng, s=s, v=v: eng.wait_ge(self.sem[s], v))
                    self.waited[e][s] = v
        for d in self.res.values():
            d["r"] = []

    def wait_all(self, e="sync"):
        for s in list(self.cnt):
            v = self.cnt[s]
            if v > 0 and self.waited[e].get(s, 0) < v:
                self.ops[e].append(lambda eng, s=s, v=v: eng.wait_ge(self.sem[s], v))
                self.waited[e][s] = v

    def replay(self):
        with self.nc.Block() as block:
            @block.sync
            def _(eng):
                for f in self.ops["sync"]:
                    f(eng)

            @block.scalar
            def _(eng):
                for f in self.ops["scalar"]:
                    f(eng)

            @block.vector
            def _(eng):
                for f in self.ops["vector"]:
                    f(eng)

            @block.tensor
            def _(eng):
                for f in self.ops["tensor"]:
                    f(eng)

            @block.gpsimd
            def _(eng):
                for f in self.ops["gpsimd"]:
                    f(eng)


class Ctx:
    def __init__(self, nc, stack):
        self.nc = nc
        self.stack = stack
        self.p = P(nc, stack)
        self.pb = [stack.enter_context(nc.psum_tensor(f"pbank{i}", [128, 512], F32))
                   for i in range(8)]
        self._uid = 0

    def uid(self):
        self._uid += 1
        return self._uid



def dma_in(ctx, dst_ap, src_ap, res_w, res_r=(), eng="sync"):
    return ctx.p.emit(eng, list(res_r), [res_w],
                      lambda g: g.dma_start(out=dst_ap, in_=src_ap), dma=True)


def dma_out(ctx, dst_ap, src_ap, res_r=(), res_w="dram_out", eng="scalar"):
    return ctx.p.emit(eng, list(res_r), [res_w],
                      lambda g: g.dma_start(out=dst_ap, in_=src_ap), dma=True)


def mm(ctx, plist, reads, writes):
    def fn(t):
        inst = None
        for (o, l, r, st, sp) in plist:
            inst = t.matmul(o, l, r, start=st, stop=sp)
        return inst
    return ctx.p.emit("tensor", reads, writes, fn)


def kacc(ctx, out_ap, lhsT_list, rhs_list, reads, writes):
    n = len(lhsT_list)
    return mm(ctx, [(out_ap, lhsT_list[i], rhs_list[i], i == 0, i == n - 1)
                    for i in range(n)], reads, writes)


def kacc2(ctx, out_ap, lhsT_list, rhs_list, reads, writes, start, stop):
    n = len(lhsT_list)
    return mm(ctx, [(out_ap, lhsT_list[i], rhs_list[i], start and i == 0,
                     stop and i == n - 1) for i in range(n)], reads, writes)


def transpose(ctx, out_psum_ap, in_ap, ident_ap, reads, writes):
    return ctx.p.emit("tensor", reads, writes,
                      lambda t: t.transpose(out_psum_ap, in_ap, ident_ap))


def vop(ctx, fn, reads, writes, eng="vector"):
    return ctx.p.emit(eng, reads, writes, fn)


@contextlib.contextmanager
def scope(ctx):
    with contextlib.ExitStack() as bs:
        yield lambda n, s, d: bs.enter_context(
            ctx.nc.sbuf_tensor(n, s, d, allow_name_mangling=True))
    ctx.p.barrier()


def chain(ctx, fns, reads, writes, eng="vector"):
    u = f"ch{ctx.uid()}"
    n = len(fns)
    tok = None
    for i, f in enumerate(fns):
        r = list(reads) + ([u] if i > 0 else [])
        w = ([u] if i < n - 1 else list(writes) + [u])
        tok = ctx.p.emit(eng, r, w, f)
    return tok



class LNTmp:
    def __init__(self, ctx, sb, tag, tc=TC):
        t = tag
        self.tag = t
        self.tc = tc
        self.xb = sb(t + "ln_xb", [128, tc], BF16)
        self.xsq = sb(t + "ln_xsq", [128, tc], BF16)
        self.rowS = sb(t + "ln_rowS", [1, tc], F32)[0:1, :]
        self.rowQ = sb(t + "ln_rowQ", [1, tc], F32)[0:1, :]
        self.rstd = sb(t + "ln_rstd", [1, tc], F32)[0:1, :]
        self.mneg = sb(t + "ln_mneg", [1, tc], F32)[0:1, :]
        self.rowbA = sb(t + "ln_rowbA", [1, tc], BF16)[0:1, :]
        self.rowbB = sb(t + "ln_rowbB", [1, tc], BF16)[0:1, :]
        self.rbc = sb(t + "ln_rbc", [128, 2 * tc], BF16)
        self.tmp = sb(t + "ln_tmp", [128, tc], F32)
        self.ones_bf = sb(t + "ln_ones", [128, 1], BF16)
        self.ones1_bf = sb(t + "ln_ones1", [1, 128], BF16)
        self.eps = sb(t + "ln_eps", [1, 1], F32)
        vop(ctx, lambda v: v.memset(self.ones_bf[:, :], 1.0), [], [t + "ln_ones"])
        vop(ctx, lambda v: v.memset(self.ones1_bf[:, :], 1.0), [], [t + "ln_ones1"])


def emit_ln(ctx, ln, x_chunks, x_resl, out_bf, out_resl, out_f32=None,
            out_f32_resl=None, eps=1e-5, affine=None, affine_res=()):
    tc = ln.tc
    t = ln.tag
    pass
    pbS, pbQ, pbB = ctx.pb[6], ctx.pb[7], ctx.pb[5]
    nd = len(x_chunks)
    for i in range(nd):
        vop(ctx, lambda v, i=i: v.tensor_copy(ln.xb[:, :tc], x_chunks[i]),
            [x_resl[i]], [t + "ln_xb"])
        vop(ctx, lambda s, i=i: s.activation(ln.xsq[:, :tc], x_chunks[i], AF.Square),
            [x_resl[i]], [t + "ln_xsq"], eng="scalar")
        mm(ctx, [(pbS[0:1, :tc], ln.ones_bf[:, :], ln.xb[:, :tc], i == 0, i == nd - 1)],
           [t + "ln_xb", t + "ln_ones"], ["psum6"])
        mm(ctx, [(pbQ[0:1, :tc], ln.ones_bf[:, :], ln.xsq[:, :tc], i == 0, i == nd - 1)],
           [t + "ln_xsq", t + "ln_ones"], ["psum7"])

    chain(ctx, [
        lambda v: v.tensor_scalar_mul(ln.rowS[0:1, :tc], pbS[0:1, :tc], 1.0 / (nd * 128)),
        lambda v: v.tensor_mul(ln.mneg[0:1, :tc], ln.rowS[0:1, :tc], ln.rowS[0:1, :tc]),
        lambda v: v.scalar_tensor_tensor(ln.rowQ[0:1, :tc], pbQ[0:1, :tc],
                                         1.0 / (nd * 128), ln.mneg[0:1, :tc],
                                         ALU.mult, ALU.subtract),
    ], ["psum6", "psum7"], [t + "ln_rows"])
    vop(ctx, lambda v: v.memset(ln.eps[:, :], float(eps)), [], [ln.tag + "ln_eps"])
    vop(ctx, lambda s: s.activation(ln.rowQ[0:1, :tc], ln.rowQ[0:1, :tc], AF.Sqrt,
                                    bias=ln.eps[0:1, 0:1]),
        [t + "ln_rows", ln.tag + "ln_eps"], [t + "ln_rows"], eng="scalar")
    chain(ctx, [
        lambda v: v.reciprocal(ln.rstd[0:1, :tc], ln.rowQ[0:1, :tc]),
        lambda v: v.scalar_tensor_tensor(ln.mneg[0:1, :tc], ln.rowS[0:1, :tc], -1.0,
                                         ln.rstd[0:1, :tc], ALU.mult, ALU.mult),
        lambda v: v.tensor_copy(ln.rowbA[0:1, :tc], ln.rstd[0:1, :tc]),
        lambda v: v.tensor_copy(ln.rowbB[0:1, :tc], ln.mneg[0:1, :tc]),
    ], [t + "ln_rows"], [t + "ln_r2b"])
    mm(ctx, [(pbB[:, :tc], ln.ones1_bf[0:1, :], ln.rowbA[0:1, :tc], True, True)],
       [t + "ln_r2b", t + "ln_ones1"], ["psum5"])
    vop(ctx, lambda v: v.tensor_copy(ln.rbc[:, 0:tc], pbB[:, :tc]),
        ["psum5"], [t + "ln_rbcF"])
    mm(ctx, [(pbB[:, :tc], ln.ones1_bf[0:1, :], ln.rowbB[0:1, :tc], True, True)],
       [t + "ln_r2b", "psum5"], ["psum5"])
    vop(ctx, lambda v: v.tensor_copy(ln.rbc[:, tc:2 * tc], pbB[:, :tc]),
        ["psum5"], [t + "ln_rbcF"])
    rb = ln.rbc[:, 0:tc]
    mb = ln.rbc[:, tc:2 * tc]
    for i in range(nd):
        wl = (([out_resl[i]] if out_bf is not None else [])
              + ([out_f32_resl[i]] if out_f32 is not None else []))
        fns = [lambda v, i=i: v.tensor_mul(ln.tmp[:, :tc], x_chunks[i], rb)]
        if affine is not None:
            ga, be = affine
            fns.append(lambda v, i=i: v.tensor_add(ln.tmp[:, :tc], ln.tmp[:, :tc],
                                                   mb))
            fns.append(lambda v, i=i, ga=ga, be=be: v.scalar_tensor_tensor(
                out_bf[i], ln.tmp[:, :tc], ga[i],
                be[i].broadcast_to([128, tc]), ALU.mult, ALU.add))
        elif out_f32 is not None:
            fns.append(lambda v, i=i: v.tensor_add(out_f32[i], ln.tmp[:, :tc], mb))
            if out_bf is not None:
                fns.append(lambda v, i=i: v.tensor_copy(out_bf[i], out_f32[i]))
        else:
            fns.append(lambda v, i=i: v.tensor_add(out_bf[i], ln.tmp[:, :tc], mb))
        chain(ctx, fns, [x_resl[i], t + "ln_rbcF", t + "ln_tmpres"] + list(affine_res),
              wl + [t + "ln_tmpres"])



class QTmp:
    def __init__(self, sb, tag, width):
        w = width
        t = tag
        self.w = w
        self.t1 = sb(t + "q_t1", [128, w], F32)
        self.t2 = sb(t + "q_t2", [128, w], F32)
        self.sc = sb(t + "q_sc", [128, max(w // 32, 32)], F32)
        self.inv = sb(t + "q_inv", [128, max(w // 32, 32)], F32)
        self.scE = sb(t + "q_scE", [128, w], F32)
        self.invE = sb(t + "q_invE", [128, w], F32)
        self.q8 = sb(t + "q_q8", [128, w], F32)
        self.q4 = sb(t + "q_q4", [128, w], F32)
        self.cm = sb(t + "q_cm", [128, w], F32)
        self.res = t + "qtmp"


def emit_quant(ctx, qt, x_ap, out_ap, width, mode, reads, writes):
    w = width
    nb = w // 32
    fns = []
    a = fns.append
    if mode == "part":
        a(lambda v: v.transpose(qt.t1[:, :w], x_ap))
        src = qt.t1[:, :w]
    else:
        src = x_ap
    a(lambda v: v.tensor_reduce(qt.sc[:, :nb], src.rearrange("p (b j) -> p b j", j=32),
                                axis=AX, op=ALU.max, apply_absolute_value=True))
    a(lambda v: v.tensor_scalar_mul(qt.sc[:, :nb], qt.sc[:, :nb], 1.0 / 7.5))
    a(lambda v: v.tensor_scalar_max(qt.sc[:, :nb], qt.sc[:, :nb], 1e-30))
    a(lambda v: v.reciprocal(qt.inv[:, :nb], qt.sc[:, :nb]))
    a(lambda v: v.tensor_copy(qt.scE[:, :w].rearrange("p (b j) -> p b j", j=32),
                              qt.sc[:, :nb].unsqueeze(2).broadcast_to([128, nb, 32])))
    a(lambda v: v.tensor_copy(qt.invE[:, :w].rearrange("p (b j) -> p b j", j=32),
                              qt.inv[:, :nb].unsqueeze(2).broadcast_to([128, nb, 32])))
    if mode == "part":
        a(lambda v: v.transpose(qt.t1[:, :w], qt.scE[:, :w]))
        a(lambda v: v.transpose(qt.t2[:, :w], qt.invE[:, :w]))
        a(lambda v: v.tensor_copy(qt.scE[:, :w], qt.t1[:, :w]))
        a(lambda v: v.tensor_copy(qt.invE[:, :w], qt.t2[:, :w]))
    a(lambda v: v.tensor_mul(qt.t1[:, :w], x_ap, qt.invE[:, :w]))
    for f, dst in ((8.0, qt.q8), (4.0, qt.q4), (2.0, qt.t2)):
        a(lambda v, f=f: v.tensor_scalar_mul(qt.cm[:, :w], qt.t1[:, :w], f))
        a(lambda v: v.tensor_scalar_add(qt.cm[:, :w], qt.cm[:, :w], MAGIC))
        a(lambda v: v.tensor_scalar_add(qt.cm[:, :w], qt.cm[:, :w], -MAGIC))
        a(lambda v, f=f, dst=dst: v.tensor_scalar_mul(dst[:, :w], qt.cm[:, :w], 1.0 / f))
    a(lambda v: v.tensor_mul(qt.cm[:, :w], qt.t1[:, :w], qt.t1[:, :w]))
    a(lambda v: v.tensor_scalar(qt.t1[:, :w], qt.cm[:, :w], 16.0, None, ALU.is_ge))
    a(lambda v: v.copy_predicated(qt.q4[:, :w], qt.t1[:, :w].bitcast(U32),
                                  qt.t2[:, :w]))
    a(lambda v: v.tensor_scalar(qt.t1[:, :w], qt.cm[:, :w], 4.0, None, ALU.is_ge))
    a(lambda v: v.copy_predicated(qt.q8[:, :w], qt.t1[:, :w].bitcast(U32),
                                  qt.q4[:, :w]))
    a(lambda v: v.tensor_mul(out_ap, qt.q8[:, :w], qt.scE[:, :w]))
    chain(ctx, fns, list(reads) + [qt.res], list(writes) + [qt.res])



def emit_l3_attn(ctx, xin, xout, W):
    return _emit_mha(ctx, xin, xout, W, wk="a3_tk", wq="a3_tq", wv="a3_tv",
                     rope=False, wo="a3_wu", tag="3a", causal=False,
                     pre_proj="a3_wd", wo_bias="a3_bu")


def emit_l1_attn(ctx, xin, xout, W):
    return _emit_mha(ctx, xin, xout, W, wk="a1_wk", wq="a1_wq", wv="a1_wv",
                     rope=True, wo="a1_wo", tag="1a")


def emit_l2_attn(ctx, xin, xout, W):
    return _emit_mha(ctx, xin, xout, W, wk="a2_wk", wq="a2_wq", wv="a2_wv",
                     rope=False, wo=None, tag="2a")


def _emit_mha(ctx, xin, xout, W, wk, wq, wv, rope, wo, tag, causal=True,
              pre_proj=None, wo_bias=None):
    nc = ctx.nc
    u = f"{tag}{ctx.uid()}_"
    NQT = T_FULL // 128
    NQG = T_FULL // TC
    ao_scr = None
    if wo is not None:
        ao_scr = nc.dram_tensor(u + "ao", [NDC, 128, T_FULL], BF16,
                                kind="Internal").ap()
    with scope(ctx) as sb:
        xnb = [sb(f"{u}xn{i}", [128, T_FULL], BF16) for i in range(NDC)]
        xnres = [f"{u}xn{i}" for i in range(NDC)]
        ln = LNTmp(ctx, sb, u)
        identb = sb(u + "idb", [128, 128], BF16)
        identf = sb(u + "idf", [128, 128], F32)
        maskQ = sb(u + "mq", [128, 4 * TC], BF16)
        maskT = sb(u + "mt", [128, 4 * TC], BF16)
        dma_in(ctx, identb[:, :], W["ident_bf"], u + "idb")
        dma_in(ctx, identf[:, :], W["ident_f32"], u + "idf")
        dma_in(ctx, maskQ[:, :], W["maskQ"], u + "mq")
        dma_in(ctx, maskT[:, :], W["maskT"], u + "mt")

        with scope(ctx) as sbx:
            xs = [sbx(f"{u}x{i}", [128, TC], F32) for i in range(NDC)]
            for tcx in range(NTC):
                t0 = tcx * TC
                rxl = [f"{u}x{i}" for i in range(NDC)]
                for i in range(NDC):
                    dma_in(ctx, xs[i][:, :], xin[i][:, t0:t0 + TC], rxl[i])
                emit_ln(ctx, ln, [xs[i][:, :] for i in range(NDC)], rxl,
                        [xnb[i][:, t0:t0 + TC] for i in range(NDC)], xnres)

        if pre_proj is not None:
            with scope(ctx) as sbd:
                wdt = [sbd(f"{u}wd{k}", [128, 1024], BF16) for k in range(NDC)]
                xdb = [sbd(f"{u}xd{i}", [128, T_FULL], BF16) for i in range(NDC)]
                for k in range(NDC):
                    dma_in(ctx, wdt[k][:, :], W[pre_proj][k], f"{u}wd{k}",
                           eng="gpsimd")
                for tg in range(NTC):
                    t0 = tg * TC
                    for oc in range(NDC):
                        bank = oc % 4
                        kacc(ctx, ctx.pb[bank][:, :TC],
                             [wdt[k][:, 128 * oc:128 * (oc + 1)] for k in range(NDC)],
                             [xnb[k][:, t0:t0 + TC] for k in range(NDC)],
                             [f"{u}wd{k}" for k in range(NDC)] + xnres,
                             [f"psum{bank}"])
                        vop(ctx, lambda v, oc=oc, t0=t0, bank=bank: v.tensor_copy(
                            xnb[oc][:, t0:t0 + TC] if False else xdb[oc][:, t0:t0 + TC],
                            ctx.pb[bank][:, :TC]),
                            [f"psum{bank}"], [f"{u}xd{oc}"])
                for oc in range(NDC):
                    vop(ctx, lambda v, oc=oc: v.tensor_copy(
                        xnb[oc][:, :], xdb[oc][:, :]),
                        [f"{u}xd{oc}"], [xnres[oc]])

        with scope(ctx) as sb2:
            qh = [sb2(f"{u}qh{h}", [128, T_FULL], BF16) for h in range(2)]
            vT = sb2(u + "vT", [128, TC], BF16)
            kx = [sb2(f"{u}kx{h}", [128, T_FULL], BF16) for h in range(2)]
            vtm = sb2(u + "vtm", [128, 32 * 256], BF16)
            qx = sb2(u + "qx", [128, TC], BF16)
            mrow = sb2(u + "mr", [65, T_FULL], BF16)
            mneg = sb2(u + "mn", [128, 1], F32)
            mcol = [sb2(f"{u}mc{h}", [128, 97], F32) for h in range(2)]
            rec = sb2(u + "rec", [1, TC], BF16)
            recb = sb2(u + "recb", [64, TC], F32)
            aoc = sb2(u + "aoc", [128, T_FULL], BF16)
            xr = sb2(u + "xr", [128, TC], F32)
            yv = sb2(u + "yv", [128, TC], F32)
            wpk2 = [[sb2(f"{u}wpk{p}_{k}", [128, 128], BF16) for k in range(NDC)]
                    for p in range(2)]
            wpq2 = [[sb2(f"{u}wpq{p}_{k}", [128, 128], BF16) for k in range(NDC)]
                    for p in range(2)]
            wpv2 = [[sb2(f"{u}wpv{p}_{k}", [128, 128], BF16) for k in range(NDC)]
                    for p in range(2)]
            if rope:
                cB = sb2(u + "cB", [128, T_FULL], BF16)
                sB = sb2(u + "sB", [128, T_FULL], BF16)
                psw = sb2(u + "psw", [128, 128], BF16)
                praw = sb2(u + "praw", [128, TC], BF16)
                rt1 = sb2(u + "rt1", [128, TC], F32)
                rt2 = sb2(u + "rt2", [128, TC], F32)
                dma_in(ctx, cB[:, :], W["a1_cos"], u + "cB")
                dma_in(ctx, sB[:, :], W["a1_sin"], u + "sB")
                dma_in(ctx, psw[:, :], W["a1_pswap"], u + "psw")
            vt3 = vtm[:, :].rearrange("p (k c) -> p k c", c=256)
            vop(ctx, lambda v: v.memset(vtm[:, :], 0.0), [], [u + "vtm"])
            vop(ctx, lambda v: v.memset(vt3[:, :, 64:65], 1.0), [], [u + "vtm"])
            vop(ctx, lambda v: v.memset(vt3[:, :, 192:193], 1.0), [], [u + "vtm"])
            for h in range(2):
                vop(ctx, lambda v, h=h: v.memset(qh[h][64:128, :], 0.0),
                    [], [f"{u}qh{h}"])
                vop(ctx, lambda v, h=h: v.memset(kx[h][64:128, :], 0.0),
                    [], [f"{u}kx{h}"])
            vop(ctx, lambda v: v.memset(qx[64:128, :], 0.0), [], [u + "qx"])

            def load_pair(pr):
                p = pr % 2
                if pre_proj is not None:
                    dma_in(ctx, wpk2[p][0][:, :], W[wk][pr], f"{u}wpk{p}_0",
                           eng="gpsimd")
                    dma_in(ctx, wpq2[p][0][:, :], W[wq][pr], f"{u}wpq{p}_0",
                           eng="gpsimd")
                    dma_in(ctx, wpv2[p][0][:, :], W[wv][pr], f"{u}wpv{p}_0",
                           eng="gpsimd")
                else:
                    for k in range(NDC):
                        dma_in(ctx, wpk2[p][k][:, :],
                               W[wk][k][:, 128 * pr:128 * (pr + 1)],
                               f"{u}wpk{p}_{k}", eng="gpsimd")
                        dma_in(ctx, wpq2[p][k][:, :],
                               W[wq][k][:, 128 * pr:128 * (pr + 1)],
                               f"{u}wpq{p}_{k}", eng="gpsimd")
                        dma_in(ctx, wpv2[p][k][:, :],
                               W[wv][k][:, 128 * pr:128 * (pr + 1)],
                               f"{u}wpv{p}_{k}", eng="gpsimd")

            load_pair(0)
            for pr in range(8):
                pp = pr % 2
                wpk, wpq, wpv = wpk2[pp], wpq2[pp], wpv2[pp]
                if pr + 1 < 8:
                    load_pair(pr + 1)
                for h in range(2):
                    vop(ctx, lambda v, h=h: v.memset(kx[h][64:65, :], 1.0),
                        [], [f"{u}kx{h}"])
                for tg in range(NTC):
                    t0 = tg * TC
                    if pre_proj is not None:
                        def pmm(bank, wg, wres, pr=pr, t0=t0):
                            mm(ctx, [(ctx.pb[bank][:, :TC], wg[0][:, :],
                                      xnb[pr][:, t0:t0 + TC], True, True)],
                               [wres + "_0", xnres[pr]], [f"psum{bank}"])
                    else:
                        def pmm(bank, wg, wres, t0=t0):
                            kacc(ctx, ctx.pb[bank][:, :TC],
                                 [wg[k][:, :] for k in range(NDC)],
                                 [xnb[k][:, t0:t0 + TC] for k in range(NDC)],
                                 [f"{wres}_{k}" for k in range(NDC)] + xnres,
                                 [f"psum{bank}"])
                    def evac_heads(bank, dsts, dres, drows, t0=t0):
                        if not rope:
                            for h in range(2):
                                vop(ctx, lambda v, h=h: v.tensor_copy(
                                    dsts[h][0:drows, t0:t0 + TC],
                                    ctx.pb[bank][64 * h:64 * h + 64, :TC]),
                                    [f"psum{bank}"], [dres[h]])
                            return
                        vop(ctx, lambda v: v.tensor_copy(
                            praw[:, :], ctx.pb[bank][:, :TC]),
                            [f"psum{bank}"], [u + "praw"])
                        mm(ctx, [(ctx.pb[4][:, :TC], psw[:, :], praw[:, :],
                                  True, True)],
                           [u + "psw", u + "praw"], ["psum4"])
                        chain(ctx, [
                            lambda v: v.tensor_mul(rt1[:, :], praw[:, :],
                                                   cB[:, t0:t0 + TC]),
                            lambda v: v.tensor_mul(rt2[:, :], ctx.pb[4][:, :TC],
                                                   sB[:, t0:t0 + TC]),
                        ], [u + "praw", u + "cB", u + "sB", "psum4"],
                            [u + "rt"])
                        for h in range(2):
                            vop(ctx, lambda v, h=h: v.tensor_add(
                                dsts[h][0:drows, t0:t0 + TC],
                                rt1[64 * h:64 * h + 64, :],
                                rt2[64 * h:64 * h + 64, :]),
                                [u + "rt"], [dres[h]])

                    pmm(0, wpk, f"{u}wpk{pp}")
                    evac_heads(0, kx, [f"{u}kx{h}" for h in range(2)], 64)
                    pmm(1, wpq, f"{u}wpq{pp}")
                    evac_heads(1, qh, [f"{u}qh{h}" for h in range(2)], 64)
                    pmm(2, wpv, f"{u}wpv{pp}")
                    vop(ctx, lambda v: v.tensor_copy(vT[:, :TC], ctx.pb[2][:, :TC]),
                        ["psum2"], [u + "vT"])
                    for j in range(4):
                        kk = 4 * tg + j
                        transpose(ctx, ctx.pb[4][:, 0:128].bitcast(BF16)[:, 0:128],
                                  vT[:, 128 * j:128 * (j + 1)], identb[:, :],
                                  [u + "vT", u + "idb"], ["psum4"])
                        chain(ctx, [
                            lambda v, kk=kk: v.tensor_copy(
                                vt3[:, kk, 0:64], ctx.pb[4][:, 0:128].bitcast(BF16)[:, 0:64]),
                            lambda v, kk=kk: v.tensor_copy(
                                vt3[:, kk, 128:192], ctx.pb[4][:, 0:128].bitcast(BF16)[:, 64:128]),
                        ], ["psum4"], [u + "vtm"])

                for h in range(2):
                    hq = qh[h][0:128, :]
                    hk = kx[h][0:128, :]
                    for qt in range(NQT):
                        nkc = (qt // 4 + 1) if causal else NQG
                        for c in range(nkc):
                            klen = min(TC, 128 * (qt + 1) - TC * c) if causal else TC
                            bank = c % 2
                            mm(ctx, [(ctx.pb[bank][:, :klen],
                                      hq[:, 128 * qt:128 * (qt + 1)],
                                      hk[:, TC * c:TC * c + klen], True, True)],
                               [f"{u}qh{h}", f"{u}kx{h}"], [f"psum{bank}"])
                            if causal and c == nkc - 1:
                                j = qt - 4 * c
                                vop(ctx, lambda v, bank=bank, j=j, klen=klen:
                                    v.tensor_tensor(ctx.pb[bank][:, :klen],
                                                    ctx.pb[bank][:, :klen],
                                                    maskQ[:, TC * j:TC * j + klen],
                                                    ALU.add),
                                    [u + "mq", f"psum{bank}"], [f"psum{bank}"])
                            if c == 0:
                                vop(ctx, lambda v, bank=bank, klen=klen: v.tensor_reduce(
                                    mneg[:, :], ctx.pb[bank][:, :klen], axis=AX,
                                    op=ALU.max, negate=True),
                                    [f"psum{bank}"], [u + "mn"])
                            else:
                                chain(ctx, [
                                    lambda v, bank=bank, klen=klen: v.tensor_reduce(
                                        ln.tmp[:, 0:1], ctx.pb[bank][:, :klen], axis=AX,
                                        op=ALU.max, negate=True),
                                    lambda v: v.tensor_tensor(
                                        mneg[:, :], mneg[:, :], ln.tmp[:, 0:1], ALU.min),
                                ], [f"psum{bank}", u + "mn"], [u + "mn", u + "ln_tmpres"])
                        vop(ctx, lambda v, qt=qt, h=h: v.tensor_copy(
                            mcol[h][:, 32 * (qt % 4):32 * (qt % 4) + 1], mneg[:, :]),
                            [u + "mn"], [f"{u}mc{h}"])
                        if qt % 4 == 3:
                            qg = qt // 4
                            transpose(ctx, ctx.pb[4][0:97, 0:128], mcol[h][:, :],
                                      identf[:, :], [f"{u}mc{h}", u + "idf"], ["psum4"])
                            for j in range(4):
                                vop(ctx, lambda v, h=h, qg=qg, j=j: v.tensor_copy(
                                    mrow[64 * h:64 * h + 1,
                                         TC * qg + 128 * j:TC * qg + 128 * (j + 1)],
                                    ctx.pb[4][32 * j:32 * j + 1, 0:128]),
                                    ["psum4"], [u + "mr"])

                for h in range(2):
                    obank = 2 + h
                    for qg in range(NQG):
                        chain(ctx, [
                            lambda v, h=h, qg=qg: v.tensor_copy(
                                qx[0:64, :], qh[h][0:64, TC * qg:TC * (qg + 1)]),
                            lambda v, h=h, qg=qg: v.tensor_copy(
                                qx[64:65, :],
                                mrow[64 * h:64 * h + 1, TC * qg:TC * (qg + 1)]),
                        ], [f"{u}qh{h}", u + "mr"], [u + "qx"])
                        nkc = 4 * (qg + 1) if causal else NQT
                        for kc in range(nkc):
                            bank = kc % 2
                            mm(ctx, [(ctx.pb[bank][:, :TC],
                                      kx[h][:, 128 * kc:128 * (kc + 1)],
                                      qx[:, :], True, True)],
                               [f"{u}kx{h}", u + "qx"], [f"psum{bank}"])
                            j = kc - 4 * qg
                            if causal and j >= 0:
                                vop(ctx, lambda v, bank=bank, j=j: v.tensor_tensor(
                                    ctx.pb[bank][:, :TC], ctx.pb[bank][:, :TC],
                                    maskT[:, TC * j:TC * (j + 1)], ALU.add),
                                    [u + "mt", f"psum{bank}"], [f"psum{bank}"])
                            vop(ctx, lambda s, bank=bank: s.activation(
                                ln.tmp[:, :].bitcast(BF16)[:, :TC], ctx.pb[bank][:, :TC],
                                AF.Exp, scale=0.125),
                                [f"psum{bank}"], [u + "ln_tmpres"], eng="scalar")
                            mm(ctx, [(ctx.pb[obank][0:128, :TC],
                                      vt3[:, kc, 128 * h:128 * (h + 1)],
                                      ln.tmp[:, :].bitcast(BF16)[:, :TC],
                                      kc == 0, kc == nkc - 1)],
                               [u + "vtm", u + "ln_tmpres"], [f"psum{obank}"])
                        chain(ctx, [
                            lambda v, obank=obank: v.reciprocal(
                                recb[0:1, :TC], ctx.pb[obank][64:65, :TC]),
                            lambda v: v.tensor_copy(rec[0:1, :TC], recb[0:1, :TC]),
                        ], [f"psum{obank}"], [u + "rec"])
                        mm(ctx, [(ctx.pb[4][0:64, :TC], ln.ones1_bf[0:1, 0:64],
                                  rec[0:1, :TC], True, True)],
                           [u + "rec", u + "ln_ones1"], ["psum4"])
                        vop(ctx, lambda v: v.tensor_copy(recb[:, :TC], ctx.pb[4][0:64, :TC]),
                            ["psum4"], [u + "recb"])
                        vop(ctx, lambda v, h=h, qg=qg, obank=obank: v.tensor_mul(
                            aoc[64 * h:64 * h + 64, TC * qg:TC * (qg + 1)],
                            ctx.pb[obank][0:64, :TC], recb[:, :TC]),
                            [f"psum{obank}", u + "recb"], [u + "aoc"])

                if wo is None:
                    for tcx in range(NTC):
                        t0 = tcx * TC
                        dma_in(ctx, xr[:, :], xin[pr][:, t0:t0 + TC], u + "xr")
                        vop(ctx, lambda v, t0=t0: v.tensor_add(
                            yv[:, :], xr[:, :], aoc[:, t0:t0 + TC]),
                            [u + "xr", u + "aoc"], [u + "yv"])
                        dma_out(ctx, xout[pr][:, t0:t0 + TC], yv[:, :],
                                [u + "yv"], u + "xo")
                else:
                    for tcx in range(NTC):
                        t0 = tcx * TC
                        dma_out(ctx, ao_scr[pr][:, t0:t0 + TC], aoc[:, t0:t0 + TC],
                                [u + "aoc"], u + "aos")

        if wo is not None:
            with scope(ctx) as sb3:
                wot = [sb3(f"{u}wo{k}", [128, 1024], BF16) for k in range(NDC)]
                aot = [sb3(f"{u}aot{k}", [128, TC], BF16) for k in range(NDC)]
                xr2 = sb3(u + "xr2", [128, TC], F32)
                yv2 = sb3(u + "yv2", [128, TC], F32)
                for k in range(NDC):
                    dma_in(ctx, wot[k][:, :], W[wo][k], f"{u}wo{k}", eng="gpsimd")
                if wo_bias is not None:
                    wob = sb3(u + "wob", [128, 8], F32)
                    dma_in(ctx, wob[:, :], W[wo_bias], u + "wob")
                for tcx in range(NTC):
                    t0 = tcx * TC
                    for k in range(NDC):
                        dma_in(ctx, aot[k][:, :], ao_scr[k][:, t0:t0 + TC],
                               f"{u}aot{k}")
                    for oc in range(NDC):
                        bank = oc % 4
                        kacc(ctx, ctx.pb[bank][:, :TC],
                             [wot[k][:, 128 * oc:128 * (oc + 1)] for k in range(NDC)],
                             [aot[k][:, :] for k in range(NDC)],
                             [f"{u}wo{k}" for k in range(NDC)] +
                             [f"{u}aot{k}" for k in range(NDC)],
                             [f"psum{bank}"])
                        dma_in(ctx, xr2[:, :], xin[oc][:, t0:t0 + TC], u + "xr2")
                        if wo_bias is not None:
                            chain(ctx, [
                                lambda v, bank=bank, oc=oc: v.scalar_tensor_tensor(
                                    yv2[:, :], ctx.pb[bank][:, :TC],
                                    wob[:, oc:oc + 1], xr2[:, :], ALU.add, ALU.add),
                            ], [f"psum{bank}", u + "xr2", u + "wob"], [u + "yv2"])
                        else:
                            vop(ctx, lambda v, bank=bank: v.tensor_add(
                                yv2[:, :], ctx.pb[bank][:, :TC], xr2[:, :]),
                                [f"psum{bank}", u + "xr2"], [u + "yv2"])
                        dma_out(ctx, xout[oc][:, t0:t0 + TC], yv2[:, :],
                                [u + "yv2"], u + "xo2")





def emit_l0_attn(ctx, xin, xout, W):
    nc = ctx.nc
    u = f"0a{ctx.uid()}_"
    DP = 1025
    with scope(ctx) as sb:
        ln = LNTmp(ctx, sb, u)
        identb = sb(u + "idb", [128, 128], BF16)
        maskA = sb(u + "mA", [128, 4 * TC], BF16)
        wq = [sb(f"{u}wq{k}", [128, 3072], BF16) for k in range(NDC)]
        wg = [sb(f"{u}wg{k}", [128, 1024], BF16) for k in range(NDC)]
        wp = [sb(f"{u}wp{k}", [128, 1024], BF16) for k in range(NDC)]
        bqkv = sb(u + "bqkv", [128, 24], F32)
        bg = sb(u + "bg", [128, 8], F32)
        bp = sb(u + "bp", [128, 8], F32)
        lg = sb(u + "lg", [128, 8], F32)
        lb = sb(u + "lb", [128, 8], F32)
        Sb = [sb(f"{u}Sb{k}", [128, DP], BF16) for k in range(NDC)]
        xs = [sb(f"{u}x{i}", [128, TC], F32) for i in range(NDC)]
        xn0 = [sb(f"{u}xn0{i}", [128, TC], BF16) for i in range(NDC)]
        xnin = [sb(f"{u}xi{i}", [128, TC], BF16) for i in range(NDC)]
        qT = [sb(f"{u}qT{i}", [128, TC], BF16) for i in range(NDC)]
        kT = [sb(f"{u}kT{i}", [128, TC], BF16) for i in range(NDC)]
        vT = [sb(f"{u}vT{i}", [128, TC], BF16) for i in range(NDC)]
        gatet = sb(u + "gatet", [128, TC], BF16)
        tmpa = sb(u + "tmpa", [128, TC], F32)
        tmpb = sb(u + "tmpb", [128, TC], BF16)
        tmpc = sb(u + "tmpc", [128, TC], BF16)
        vtm = sb(u + "vtm", [128, 4 * 1026], BF16)
        ktm = sb(u + "ktm", [128, 4 * 1024], BF16)
        aT = [sb(f"{u}aT{j}", [128, TC], BF16) for j in range(4)]
        oT = vT
        recb = sb(u + "recb", [128, TC], BF16)
        recr = sb(u + "recr", [1, TC], BF16)
        yv = sb(u + "yv", [128, TC], F32)

        dma_in(ctx, identb[:, :], W["ident_bf"], u + "idb")
        dma_in(ctx, maskA[:, :], W["maskA"], u + "mA")
        for k in range(NDC):
            dma_in(ctx, wq[k][:, :], W["a0_wqkv"][k], f"{u}wq{k}", eng="gpsimd")
            dma_in(ctx, wg[k][:, :], W["a0_wg"][k], f"{u}wg{k}", eng="gpsimd")
            dma_in(ctx, wp[k][:, :], W["a0_wp"][k], f"{u}wp{k}", eng="gpsimd")
        for nm, tile, src in (("bqkv", bqkv, "a0_bqkv"), ("bg", bg, "a0_bg"),
                              ("bp", bp, "a0_bp"), ("lg", lg, "a0_lg"),
                              ("lb", lb, "a0_lb")):
            dma_in(ctx, tile[:, :], W[src], u + nm)
        for k in range(NDC):
            vop(ctx, lambda v, k=k: v.memset(Sb[k][:, :], 0.0), [], [f"{u}Sb{k}"])
        vt4 = vtm[:, :].rearrange("p (j c) -> p j c", c=1026)
        vop(ctx, lambda v: v.memset(vt4[:, :, 1024:1025], 1.0), [], [u + "vtm"])

        xres = [f"{u}x{i}" for i in range(NDC)]
        n0res = [f"{u}xn0{i}" for i in range(NDC)]
        xires = [f"{u}xi{i}" for i in range(NDC)]

        for g in range(NTC):
            t0 = g * TC
            for i in range(NDC):
                dma_in(ctx, xs[i][:, :], xin[i][:, t0:t0 + TC], xres[i])
            emit_ln(ctx, ln, [xs[i][:, :] for i in range(NDC)], xres,
                    None, [r + "_d" for r in n0res],
                    out_f32=[xn0[i][:, :] for i in range(NDC)], out_f32_resl=n0res)
            emit_ln(ctx, ln, [xn0[i][:, :] for i in range(NDC)], n0res,
                    [xnin[i][:, :] for i in range(NDC)], xires,
                    affine=([lg[:, i:i + 1] for i in range(NDC)],
                            [lb[:, i:i + 1] for i in range(NDC)]),
                    affine_res=[u + "lg", u + "lb"])
            for i in range(NDC):
                kacc(ctx, ctx.pb[0][:, :TC],
                     [wg[k][:, 128 * i:128 * (i + 1)] for k in range(NDC)],
                     [xnin[k][:, :] for k in range(NDC)],
                     [f"{u}wg{k}" for k in range(NDC)] + xires, ["psum0"])
                vop(ctx, lambda sc, i=i: sc.activation(
                    gatet[:, :], ctx.pb[0][:, :TC], AF.Sigmoid,
                    bias=bg[:, i:i + 1]),
                    ["psum0", u + "bg"], [u + "gatet"], eng="scalar")
                kacc(ctx, ctx.pb[1][:, :TC],
                     [wq[k][:, 128 * i:128 * (i + 1)] for k in range(NDC)],
                     [xnin[k][:, :] for k in range(NDC)],
                     [f"{u}wq{k}" for k in range(NDC)] + xires, ["psum1"])
                zb = ln.eps[0:1, 0:1]
                chain(ctx, [
                    lambda v, i=i: v.tensor_scalar(tmpa[:, :], ctx.pb[1][:, :TC],
                                                   bqkv[:, i:i + 1], 0.0, ALU.add,
                                                   ALU.min),
                    lambda v, i=i: v.tensor_scalar_min(tmpa[:, :], tmpa[:, :], 0.0),
                ], ["psum1", u + "bqkv"], [u + "tmpa"])
                vop(ctx, lambda sc: sc.activation(tmpb[:, :], tmpa[:, :], AF.Exp),
                    [u + "tmpa"], [u + "tmpb"], eng="scalar")
                chain(ctx, [
                    lambda v, i=i: v.tensor_scalar(tmpc[:, :], ctx.pb[1][:, :TC],
                                                   bqkv[:, i:i + 1], 0.0, ALU.add,
                                                   ALU.max),
                    lambda v, i=i: v.tensor_scalar_max(tmpc[:, :], tmpc[:, :], 0.0),
                    lambda v, i=i: v.tensor_add(qT[i][:, :], tmpb[:, :], tmpc[:, :]),
                ], ["psum1", u + "bqkv", u + "tmpb"], [f"{u}qT{i}", u + "tmpc"])
                kacc(ctx, ctx.pb[2][:, :TC],
                     [wq[k][:, 1024 + 128 * i:1024 + 128 * (i + 1)] for k in range(NDC)],
                     [xnin[k][:, :] for k in range(NDC)],
                     [f"{u}wq{k}" for k in range(NDC)] + xires, ["psum2"])
                chain(ctx, [
                    lambda v, i=i: v.scalar_tensor_tensor(
                        tmpa[:, :], ctx.pb[2][:, :TC], bqkv[:, 8 + i:9 + i],
                        gatet[:, :], ALU.add, ALU.mult),
                    lambda v: v.tensor_scalar_min(tmpb[:, :], tmpa[:, :], 0.0),
                ], ["psum2", u + "bqkv", u + "gatet"], [u + "tmpa", u + "tmpb"])
                vop(ctx, lambda sc: sc.activation(tmpb[:, :], tmpb[:, :], AF.Exp),
                    [u + "tmpb"], [u + "tmpb"], eng="scalar")
                chain(ctx, [
                    lambda v: v.tensor_scalar_max(tmpc[:, :], tmpa[:, :], 0.0),
                    lambda v, i=i: v.tensor_add(kT[i][:, :], tmpb[:, :], tmpc[:, :]),
                ], [u + "tmpa", u + "tmpb"], [f"{u}kT{i}", u + "tmpc"])
                kacc(ctx, ctx.pb[3][:, :TC],
                     [wq[k][:, 2048 + 128 * i:2048 + 128 * (i + 1)] for k in range(NDC)],
                     [xnin[k][:, :] for k in range(NDC)],
                     [f"{u}wq{k}" for k in range(NDC)] + xires, ["psum3"])
                vop(ctx, lambda v, i=i: v.tensor_scalar(
                    vT[i][:, :], ctx.pb[3][:, :TC], bqkv[:, 16 + i:17 + i], None,
                    ALU.add),
                    ["psum3", u + "bqkv"], [f"{u}vT{i}"])
            for j in range(4):
                for dc in range(NDC):
                    transpose(ctx, ctx.pb[4][:, 0:128].bitcast(BF16)[:, 0:128],
                              vT[dc][:, 128 * j:128 * (j + 1)], identb[:, :],
                              [f"{u}vT{dc}", u + "idb"], ["psum4"])
                    vop(ctx, lambda v, j=j, dc=dc: v.tensor_copy(
                        vt4[:, j, 128 * dc:128 * (dc + 1)],
                        ctx.pb[4][:, 0:128].bitcast(BF16)[:, 0:128]),
                        ["psum4"], [u + "vtm"])
                    transpose(ctx, ctx.pb[4][:, 0:128].bitcast(BF16)[:, 0:128],
                              kT[dc][:, 128 * j:128 * (j + 1)], identb[:, :],
                              [f"{u}kT{dc}", u + "idb"], ["psum4"])
                    vop(ctx, lambda v, j=j, dc=dc: v.tensor_copy(
                        ktm[:, 1024 * j + 128 * dc:1024 * j + 128 * (dc + 1)],
                        ctx.pb[4][:, 0:128].bitcast(BF16)[:, 0:128]),
                        ["psum4"], [u + "ktm"])
            for j in range(4):
                kacc(ctx, ctx.pb[3][:, :TC],
                     [kT[dc][:, 128 * j:128 * (j + 1)] for dc in range(NDC)],
                     [qT[dc][:, :] for dc in range(NDC)],
                     [f"{u}kT{dc}" for dc in range(NDC)] +
                     [f"{u}qT{dc}" for dc in range(NDC)], ["psum3"])
                vop(ctx, lambda v, j=j: v.tensor_mul(
                    aT[j][:, :], ctx.pb[3][:, :TC], maskA[:, TC * j:TC * (j + 1)]),
                    ["psum3", u + "mA"], [f"{u}aT{j}"])
            for m in range(NDC):
                bank = m % 2
                plist = []
                for j in range(4):
                    plist.append((ctx.pb[bank][:, :TC],
                                  vt4[:, j, 128 * m:128 * (m + 1)], aT[j][:, :],
                                  j == 0, False))
                for dc in range(NDC):
                    plist.append((ctx.pb[bank][:, :TC],
                                  Sb[dc][:, 128 * m:128 * (m + 1)], qT[dc][:, :],
                                  False, dc == NDC - 1))
                mm(ctx, plist,
                   [u + "vtm"] + [f"{u}aT{j}" for j in range(4)] +
                   [f"{u}Sb{dc}" for dc in range(NDC)] +
                   [f"{u}qT{dc}" for dc in range(NDC)], [f"psum{bank}"])
                if m == 0:
                    plist = []
                    for j in range(4):
                        plist.append((ctx.pb[2][0:1, :TC],
                                      vt4[:, j, 1024:1025], aT[j][:, :],
                                      j == 0, False))
                    for dc in range(NDC):
                        plist.append((ctx.pb[2][0:1, :TC],
                                      Sb[dc][:, 1024:1025], qT[dc][:, :],
                                      False, dc == NDC - 1))
                    mm(ctx, plist,
                       [u + "vtm"] + [f"{u}aT{j}" for j in range(4)] +
                       [f"{u}Sb{dc}" for dc in range(NDC)] +
                       [f"{u}qT{dc}" for dc in range(NDC)], ["psum2"])
                    chain(ctx, [
                        lambda v: v.tensor_scalar_add(recr[0:1, :].bitcast(F32)[0:1, :TC // 2] if False else tmpa[0:1, :], ctx.pb[2][0:1, :TC], 1e-6),
                        lambda v: v.reciprocal(tmpa[0:1, :], tmpa[0:1, :]),
                        lambda v: v.tensor_copy(recr[0:1, :], tmpa[0:1, :]),
                    ], ["psum2"], [u + "recr", u + "tmpa"])
                    mm(ctx, [(ctx.pb[2][:, :TC], ln.ones1_bf[0:1, :], recr[0:1, :],
                              True, True)],
                       [u + "recr", u + "ln_ones1"], ["psum2"])
                    vop(ctx, lambda v: v.tensor_copy(recb[:, :], ctx.pb[2][:, :TC]),
                        ["psum2"], [u + "recb"])
                vop(ctx, lambda v, m=m, bank=bank: v.tensor_mul(
                    oT[m][:, :], ctx.pb[bank][:, :TC], recb[:, :]),
                    [f"psum{bank}", u + "recb"], [f"{u}vT{m}"])
            for oc in range(NDC):
                bank = oc % 2
                kacc(ctx, ctx.pb[bank][:, :TC],
                     [wp[k][:, 128 * oc:128 * (oc + 1)] for k in range(NDC)],
                     [oT[k][:, :] for k in range(NDC)],
                     [f"{u}wp{k}" for k in range(NDC)] +
                     [f"{u}vT{k}" for k in range(NDC)], [f"psum{bank}"])
                vop(ctx, lambda v, oc=oc, bank=bank: v.scalar_tensor_tensor(
                    yv[:, :], ctx.pb[bank][:, :TC], bp[:, oc:oc + 1], xs[oc][:, :],
                    ALU.add, ALU.add),
                    [f"psum{bank}", u + "bp", xres[oc]], [u + "yv"])
                dma_out(ctx, xout[oc][:, t0:t0 + TC], yv[:, :], [u + "yv"], u + "xo")
            for dc in range(NDC):
                for (c0, c1, bank) in ((0, 512, 4), (512, 1024, 5), (1024, 1025, 6)):
                    plist = []
                    for j in range(4):
                        plist.append((ctx.pb[bank][:, 0:c1 - c0],
                                      ktm[:, 1024 * j + 128 * dc:1024 * j + 128 * (dc + 1)],
                                      vt4[:, j, c0:c1], j == 0, j == 3))
                    mm(ctx, plist, [u + "ktm", u + "vtm"], [f"psum{bank}"])
                chain(ctx, [
                    lambda v, dc=dc: v.tensor_add(Sb[dc][:, 0:512], Sb[dc][:, 0:512],
                                                  ctx.pb[4][:, :TC]),
                    lambda v, dc=dc: v.tensor_add(Sb[dc][:, 512:1024],
                                                  Sb[dc][:, 512:1024],
                                                  ctx.pb[5][:, :TC]),
                    lambda v, dc=dc: v.tensor_add(Sb[dc][:, 1024:1025],
                                                  Sb[dc][:, 1024:1025],
                                                  ctx.pb[6][:, 0:1]),
                ], ["psum4", "psum5", "psum6"], [f"{u}Sb{dc}"])



def emit_l0_ffn(ctx, xin, xout, W):
    nc = ctx.nc
    u = f"0f{ctx.uid()}_"
    HT = T_FULL // 2
    NHC = HT // TC
    with scope(ctx) as sb:
        xnbh = [sb(f"{u}xh{i}", [128, HT], BF16) for i in range(NDC)]
        xnres = [f"{u}xh{i}" for i in range(NDC)]
        yac = [sb(f"{u}ya{i}", [128, HT], BF16) for i in range(NDC)]
        cwf = sb(u + "cwf", [16, HT], BF16)
        ln = LNTmp(ctx, sb, u)
        identf = sb(u + "idf", [128, 128], F32)
        rwt = [sb(f"{u}rwt{k}", [128, 16], F32) for k in range(NDC)]
        e16 = sb(u + "e16", [16, 16 * 128], BF16)
        ebias = sb(u + "eb", [16, 1], F32)
        scf = sb(u + "scf", [16, TC], F32)
        sbf_ = sb(u + "sbf", [16, TC], F32)
        sc_tm = sb(u + "sctm", [128, 16], F32)
        sr_tm = sb(u + "srtm", [128, 16], F32)
        gs = sb(u + "gs", [128, 4], F32)
        gt4 = sb(u + "gt4", [128, 4], F32)
        gcnt4 = sb(u + "gcnt4", [128, 4], F32)
        mt = sb(u + "mt16", [128, 16], F32)
        t16 = sb(u + "t16", [128, 16], F32)
        c1 = sb(u + "c1", [128, 16], F32)
        c2 = sb(u + "c2", [128, 16], F32)
        gcol = sb(u + "gcol", [128, 1], F32)
        gwb = [[sb(f"{u}gwb{p}_{k}", [128, 512], BF16) for k in range(NDC)]
               for p in range(2)]
        uwb = [[sb(f"{u}uwb{p}_{k}", [128, 512], BF16) for k in range(NDC)]
               for p in range(2)]
        dwb = [[sb(f"{u}dwb{p}_{k}", [128, 1024], BF16) for k in range(4)]
               for p in range(2)]
        hsl = [sb(f"{u}hsl{k}", [128, TC], BF16) for k in range(8)]
        cwb = sb(u + "cwb", [128, HT], BF16)
        sil = sb(u + "sil", [128, TC], F32)
        dma_in(ctx, identf[:, :], W["ident_f32"], u + "idf")
        dma_in(ctx, e16[:, :], W["E16"], u + "e16")
        dma_in(ctx, ebias[:, :], W["f0_eb"], u + "eb")
        for k in range(NDC):
            dma_in(ctx, rwt[k][:, :], W["f0_rw"][k], f"{u}rwt{k}")

        for half in range(2):
            hb = half * HT
            with scope(ctx) as sbr:
                xs = [sbr(f"{u}x{i}", [128, TC], F32) for i in range(NDC)]
                xnf = [sbr(f"{u}xnf{i}", [128, TC], F32) for i in range(NDC)]
                for tcx in range(NHC):
                    t0 = hb + tcx * TC
                    th = tcx * TC
                    rxl = [f"{u}x{i}" for i in range(NDC)]
                    rnl = [f"{u}xnf{i}" for i in range(NDC)]
                    for i in range(NDC):
                        dma_in(ctx, xs[i][:, :], xin[i][:, t0:t0 + TC], rxl[i])
                    emit_ln(ctx, ln, [xs[i][:, :] for i in range(NDC)], rxl,
                            [xnbh[i][:, th:th + TC] for i in range(NDC)], xnres,
                            out_f32=[xnf[i][:, :] for i in range(NDC)],
                            out_f32_resl=rnl)
                    kacc(ctx, ctx.pb[0][0:16, :TC],
                         [rwt[k][:, :] for k in range(NDC)],
                         [xnf[k][:, :] for k in range(NDC)],
                         [f"{u}rwt{k}" for k in range(NDC)] + rnl, ["psum0"])
                    chain(ctx, [
                        lambda sc: sc.activation(scf[0:16, :], ctx.pb[0][0:16, :TC],
                                                 AF.Sigmoid),
                    ], ["psum0"], [u + "scf"], eng="scalar")
                    vop(ctx, lambda v: v.tensor_scalar(
                        sbf_[0:16, :], scf[0:16, :], ebias[0:16, 0:1], None, ALU.add),
                        [u + "scf", u + "eb"], [u + "sbf"])
                    for tt in range(TC // 128):
                        transpose(ctx, ctx.pb[4][0:128, 0:16],
                                  scf[0:16, 128 * tt:128 * (tt + 1)],
                                  identf[0:16, 0:16], [u + "scf", u + "idf"], ["psum4"])
                        vop(ctx, lambda v: v.tensor_copy(sr_tm[:, :],
                                                         ctx.pb[4][0:128, 0:16]),
                            ["psum4"], [u + "srtm"])
                        transpose(ctx, ctx.pb[4][0:128, 0:16],
                                  sbf_[0:16, 128 * tt:128 * (tt + 1)],
                                  identf[0:16, 0:16],
                                  [u + "sbf", u + "idf", u + "srtm"], ["psum4"])
                        vop(ctx, lambda v: v.tensor_copy(sc_tm[:, :],
                                                         ctx.pb[4][0:128, 0:16]),
                            ["psum4"], [u + "sctm"])
                        sc3 = sc_tm[:, :].rearrange("p (g j) -> p g j", j=4)
                        fns = []
                        first = [True]
                        for a in range(4):
                            for b in range(a + 1, 4):
                                if first[0]:
                                    fns.append(lambda v, a=a, b=b: v.tensor_add(
                                        gs[:, :], sc3[:, :, a], sc3[:, :, b]))
                                    first[0] = False
                                else:
                                    fns.append(lambda v, a=a, b=b: v.tensor_add(
                                        gt4[:, :], sc3[:, :, a], sc3[:, :, b]))
                                    fns.append(lambda v: v.tensor_max(
                                        gs[:, :], gs[:, :], gt4[:, :]))
                        for g in range(4):
                            fns.append(lambda v, g=g: v.tensor_scalar(
                                gt4[:, :], gs[:, :], gs[:, g:g + 1], None,
                                ALU.is_gt, ALU.add, accum_out=gcnt4[:, g:g + 1]))
                        fns.append(lambda v: v.tensor_scalar(
                            gcnt4[:, :], gcnt4[:, :], 2.0, None, ALU.is_lt))
                        fns.append(lambda v: v.tensor_copy(
                            mt[:, :].rearrange("p (g j) -> p g j", j=4),
                            gcnt4[:, :].unsqueeze(2).broadcast_to([128, 4, 4])))
                        fns.append(lambda v: v.tensor_mul(mt[:, :], mt[:, :],
                                                          sc_tm[:, :]))
                        for e in range(16):
                            fns.append(lambda v, e=e: v.tensor_scalar(
                                t16[:, :], mt[:, :], mt[:, e:e + 1], None,
                                ALU.is_gt, ALU.add, accum_out=c1[:, e:e + 1]))
                        fns.append(lambda v: v.memset(c2[:, 0:1], 0.0))
                        for e in range(1, 16):
                            fns.append(lambda v, e=e: v.tensor_scalar(
                                t16[:, 0:e], mt[:, 0:e], mt[:, e:e + 1], None,
                                ALU.is_equal, ALU.add, accum_out=c2[:, e:e + 1]))
                        fns.append(lambda v: v.tensor_add(c1[:, :], c1[:, :],
                                                          c2[:, :]))
                        fns.append(lambda v: v.tensor_scalar(
                            c1[:, :], c1[:, :], 4.0, None, ALU.is_lt))
                        fns.append(lambda v: v.tensor_mul(t16[:, :], sr_tm[:, :],
                                                          c1[:, :]))
                        fns.append(lambda v: v.reduce_sum(gcol[:, 0:1], t16[:, :],
                                                          axis=AX))
                        fns.append(lambda v: v.tensor_scalar_add(gcol[:, 0:1],
                                                                 gcol[:, 0:1], 1e-20))
                        fns.append(lambda v: v.reciprocal(gcol[:, :], gcol[:, :]))
                        fns.append(lambda v: v.tensor_scalar_mul(gcol[:, 0:1],
                                                                 gcol[:, 0:1], 2.5))
                        fns.append(lambda v: v.tensor_scalar(
                            t16[:, :], t16[:, :], gcol[:, 0:1], None, ALU.mult))
                        chain(ctx, fns, [u + "sctm", u + "srtm"],
                              [u + "t16", u + "gcol", u + "c1", u + "c2",
                               u + "gs", u + "gt4", u + "gcnt4", u + "mt16"])
                        transpose(ctx, ctx.pb[4][0:16, 0:128], t16[:, :],
                                  identf[:, :], [u + "t16", u + "idf"], ["psum4"])
                        vop(ctx, lambda v, tcx=tcx, tt=tt: v.tensor_copy(
                            cwf[0:16, tcx * TC + 128 * tt:tcx * TC + 128 * (tt + 1)],
                            ctx.pb[4][0:16, 0:128]),
                            ["psum4"], [u + "cwf"])

            for i in range(NDC):
                vop(ctx, lambda v, i=i: v.memset(yac[i][:, :], 0.0), [],
                    [f"{u}ya{i}"])

            def load_routed(e):
                p = e % 2
                for k in range(NDC):
                    dma_in(ctx, gwb[p][k][:, :], W["f0_gw"][e][k], f"{u}gwb{p}_{k}",
                           eng="gpsimd")
                    dma_in(ctx, uwb[p][k][:, :], W["f0_uw"][e][k], f"{u}uwb{p}_{k}",
                           eng="gpsimd")
                for k in range(4):
                    dma_in(ctx, dwb[p][k][:, :], W["f0_dw"][e][k], f"{u}dwb{p}_{k}",
                           eng="gpsimd")

            def run_expert(p, nhc, gtiles, utiles, dtiles, gres, ures, dres,
                           use_cw, e):
                if use_cw:
                    for tcx in range(NHC):
                        mm(ctx, [(ctx.pb[4][:, :TC], e16[:, 128 * e:128 * (e + 1)],
                                  cwf[0:16, tcx * TC:(tcx + 1) * TC], True, True)],
                           [u + "e16", u + "cwf"], ["psum4"])
                        vop(ctx, lambda v, tcx=tcx: v.tensor_copy(
                            cwb[:, tcx * TC:(tcx + 1) * TC], ctx.pb[4][:, :TC]),
                            ["psum4"], [u + "cwb"])
                for tcx in range(NHC):
                    th = tcx * TC
                    for o4 in range(nhc):
                        kacc(ctx, ctx.pb[0][:, :TC],
                             [gtiles[o4][k] for k in range(NDC)],
                             [xnbh[k][:, th:th + TC] for k in range(NDC)],
                             gres + xnres, ["psum0"])
                        kacc(ctx, ctx.pb[1][:, :TC],
                             [utiles[o4][k] for k in range(NDC)],
                             [xnbh[k][:, th:th + TC] for k in range(NDC)],
                             ures + xnres, ["psum1"])
                        vop(ctx, lambda sc: sc.activation(
                            sil[:, :], ctx.pb[0][:, :TC], AF.Silu),
                            ["psum0"], [u + "sil"], eng="scalar")
                        if use_cw:
                            chain(ctx, [
                                lambda v, o4=o4: v.tensor_mul(
                                    hsl[o4][:, :], sil[:, :], ctx.pb[1][:, :TC]),
                                lambda v, o4=o4, th=th: v.tensor_mul(
                                    hsl[o4][:, :], hsl[o4][:, :],
                                    cwb[:, th:th + TC]),
                            ], [u + "sil", "psum1", u + "cwb"], [f"{u}hsl{o4}"])
                        else:
                            vop(ctx, lambda v, o4=o4: v.tensor_mul(
                                hsl[o4][:, :], sil[:, :], ctx.pb[1][:, :TC]),
                                [u + "sil", "psum1"], [f"{u}hsl{o4}"])
                    for oc in range(NDC):
                        bank = 2 + oc % 2
                        kacc(ctx, ctx.pb[bank][:, :TC],
                             [dtiles[k][:, 128 * oc:128 * (oc + 1)]
                              for k in range(nhc)],
                             [hsl[k][:, :] for k in range(nhc)],
                             dres + [f"{u}hsl{k}" for k in range(nhc)],
                             [f"psum{bank}"])
                        vop(ctx, lambda v, oc=oc, th=th, bank=bank: v.tensor_add(
                            yac[oc][:, th:th + TC], yac[oc][:, th:th + TC],
                            ctx.pb[bank][:, :TC]),
                            [f"psum{bank}", f"{u}ya{oc}"], [f"{u}ya{oc}"])

            load_routed(0)
            for e in range(16):
                p = e % 2
                if e + 1 < 16:
                    load_routed(e + 1)
                gtiles = [[gwb[p][k][:, 128 * o4:128 * (o4 + 1)] for k in range(NDC)]
                          for o4 in range(4)]
                utiles = [[uwb[p][k][:, 128 * o4:128 * (o4 + 1)] for k in range(NDC)]
                          for o4 in range(4)]
                run_expert(p, 4, gtiles, utiles, dwb[p],
                           [f"{u}gwb{p}_{k}" for k in range(NDC)],
                           [f"{u}uwb{p}_{k}" for k in range(NDC)],
                           [f"{u}dwb{p}_{k}" for k in range(4)], True, e)
            for k in range(NDC):
                dma_in(ctx, gwb[0][k][:, :], W["f0_sg"][k][:, 0:512],
                       f"{u}gwb0_{k}", eng="gpsimd")
                dma_in(ctx, gwb[1][k][:, :], W["f0_sg"][k][:, 512:1024],
                       f"{u}gwb1_{k}", eng="gpsimd")
                dma_in(ctx, uwb[0][k][:, :], W["f0_su"][k][:, 0:512],
                       f"{u}uwb0_{k}", eng="gpsimd")
                dma_in(ctx, uwb[1][k][:, :], W["f0_su"][k][:, 512:1024],
                       f"{u}uwb1_{k}", eng="gpsimd")
            for k in range(4):
                dma_in(ctx, dwb[0][k][:, :], W["f0_sd"][k], f"{u}dwb0_{k}",
                       eng="gpsimd")
                dma_in(ctx, dwb[1][k][:, :], W["f0_sd"][4 + k], f"{u}dwb1_{k}",
                       eng="gpsimd")
            gtiles = [[gwb[o8 // 4][k][:, 128 * (o8 % 4):128 * (o8 % 4 + 1)]
                       for k in range(NDC)] for o8 in range(8)]
            utiles = [[uwb[o8 // 4][k][:, 128 * (o8 % 4):128 * (o8 % 4 + 1)]
                       for k in range(NDC)] for o8 in range(8)]
            dtiles = dwb[0] + dwb[1]
            run_expert(0, 8, gtiles, utiles, dtiles,
                       [f"{u}gwb{pp}_{k}" for pp in range(2) for k in range(NDC)],
                       [f"{u}uwb{pp}_{k}" for pp in range(2) for k in range(NDC)],
                       [f"{u}dwb{pp}_{k}" for pp in range(2) for k in range(4)],
                       False, 0)

            with scope(ctx) as sbf:
                xr = sbf(u + "fxr", [128, TC], F32)
                yv = sbf(u + "fyv", [128, TC], F32)
                for tcx in range(NHC):
                    t0 = hb + tcx * TC
                    th = tcx * TC
                    for i in range(NDC):
                        dma_in(ctx, xr[:, :], xin[i][:, t0:t0 + TC], u + "fxr")
                        vop(ctx, lambda v, i=i, th=th, xr=xr, yv=yv: v.tensor_add(
                            yv[:, :], xr[:, :], yac[i][:, th:th + TC]),
                            [u + "fxr", f"{u}ya{i}"], [u + "fyv"])
                        dma_out(ctx, xout[i][:, t0:t0 + TC], yv[:, :],
                                [u + "fyv"], u + "fxo")



def emit_l1_ffn(ctx, xin, xout, W):
    nc = ctx.nc
    u = f"1f{ctx.uid()}_"
    QT = 1024
    NQ = T_FULL // QT
    NQC = QT // TC
    RMS_EPS = 1.1920929e-07
    with scope(ctx) as sb:
        xrm = [sb(f"{u}xr{i}", [128, QT], BF16) for i in range(NDC)]
        xrmres = [f"{u}xr{i}" for i in range(NDC)]
        yac = [sb(f"{u}ya{i}", [128, QT], BF16) for i in range(NDC)]
        cwf = sb(u + "cwf", [14, QT], BF16)
        ln = LNTmp(ctx, sb, u)
        identf = sb(u + "idf", [128, 128], F32)
        rwt = [sb(f"{u}rwt{k}", [128, 14], F32) for k in range(NDC)]
        e14 = sb(u + "e14", [14, 14 * 128], BF16)
        gt = sb(u + "gt", [128, 16], F32)
        gtmp = sb(u + "gtmp", [128, 16], F32)
        gcnt = sb(u + "gcnt", [128, 16], F32)
        gcol = sb(u + "gcol", [128, 1], F32)
        gcol2 = sb(u + "gcol2", [128, 1], F32)
        lgt = sb(u + "lgt", [14, TC], F32)
        xs = [sb(f"{u}x{i}", [128, TC], F32) for i in range(NDC)]
        xnf = [sb(f"{u}xnf{i}", [128, TC], F32) for i in range(NDC)]
        sq = sb(u + "sq", [128, TC], BF16)
        r2r = sb(u + "r2r", [1, TC], F32)
        r2b = sb(u + "r2b", [1, TC], BF16)
        r2bc = sb(u + "r2bc", [128, TC], BF16)
        epst = sb(u + "epst", [1, 1], F32)
        w1t = [[sb(f"{u}w1t{p}_{k}", [128, TC], BF16) for k in range(NDC)]
               for p in range(2)]
        w2t = [[sb(f"{u}w2t{p}_{k}", [128, 1024], BF16) for k in range(8)]
               for p in range(2)]
        rwe = [sb(f"{u}rwe{p}", [128, 8], F32) for p in range(2)]
        hsl = [sb(f"{u}hsl{k}", [128, QT], BF16) for k in range(16)]
        cwb = sb(u + "cwb", [128, QT], BF16)
        sil = sb(u + "sil", [128, TC], F32)
        fxr = sb(u + "fxr", [128, TC], F32)
        fyv = sb(u + "fyv", [128, TC], F32)

        dma_in(ctx, identf[:, :], W["ident_f32"], u + "idf")
        dma_in(ctx, e14[:, :], W["E14"], u + "e14")
        vop(ctx, lambda v: v.memset(epst[:, :], RMS_EPS), [], [u + "epst"])
        for k in range(NDC):
            dma_in(ctx, rwt[k][:, :], W["f1_rw"][k], f"{u}rwt{k}")

        def wsrc(e):
            if e < 14:
                return W["f1_w1"][e], W["f1_w2"][e], W["f1_rms"][e]
            return W["f1_w1_s"][e - 14], W["f1_w2_s"][e - 14], W["f1_rms_s"][e - 14]

        sli = [0]

        def load_w1_slice(e, hc):
            p = sli[0] % 2
            sli[0] += 1
            w1d, _, _ = wsrc(e)
            for k in range(NDC):
                dma_in(ctx, w1t[p][k][:, :], w1d[k][:, TC * hc:TC * (hc + 1)],
                       f"{u}w1t{p}_{k}", eng="gpsimd")
                vop(ctx, lambda v, k=k, p=p, e=e: v.tensor_scalar(
                    w1t[p][k][:, :], w1t[p][k][:, :], rwe[e % 2][:, k:k + 1], None,
                    ALU.mult),
                    [f"{u}w1t{p}_{k}", f"{u}rwe{e % 2}"], [f"{u}w1t{p}_{k}"])
            return p

        w2i = [0]

        def load_w2_half(e, half):
            p = w2i[0] % 2
            w2i[0] += 1
            _, w2d, _ = wsrc(e)
            for k in range(8):
                dma_in(ctx, w2t[p][k][:, :], w2d[8 * half + k], f"{u}w2t{p}_{k}",
                       eng="gpsimd")
            return p

        def load_rms(e):
            _, _, rmsd = wsrc(e)
            dma_in(ctx, rwe[e % 2][:, :], rmsd, f"{u}rwe{e % 2}")

        for q in range(NQ):
            qb = q * QT
            for tcx in range(NQC):
                t0 = qb + tcx * TC
                th = tcx * TC
                rxl = [f"{u}x{i}" for i in range(NDC)]
                rnl = [f"{u}xnf{i}" for i in range(NDC)]
                for i in range(NDC):
                    dma_in(ctx, xs[i][:, :], xin[i][:, t0:t0 + TC], rxl[i])
                emit_ln(ctx, ln, [xs[i][:, :] for i in range(NDC)], rxl,
                        None, rxl, out_f32=[xnf[i][:, :] for i in range(NDC)],
                        out_f32_resl=rnl)
                for i in range(NDC):
                    vop(ctx, lambda sc, i=i: sc.activation(
                        sq[:, :], xnf[i][:, :], AF.Square),
                        [rnl[i]], [u + "sq"], eng="scalar")
                    mm(ctx, [(ctx.pb[6][0:1, :TC], ln.ones_bf[:, :], sq[:, :],
                              i == 0, i == NDC - 1)],
                       [u + "sq", u + "ln_ones"], ["psum6"])
                chain(ctx, [
                    lambda v: v.tensor_scalar_mul(r2r[0:1, :], ctx.pb[6][0:1, :TC],
                                                  1.0 / D),
                ], ["psum6"], [u + "r2r"])
                vop(ctx, lambda sc: sc.activation(r2r[0:1, :], r2r[0:1, :],
                                                  AF.Sqrt, bias=epst[0:1, 0:1]),
                    [u + "r2r", u + "epst"], [u + "r2r"], eng="scalar")
                chain(ctx, [
                    lambda v: v.reciprocal(r2r[0:1, :], r2r[0:1, :]),
                    lambda v: v.tensor_copy(r2b[0:1, :], r2r[0:1, :]),
                ], [u + "r2r"], [u + "r2b"])
                mm(ctx, [(ctx.pb[5][:, :TC], ln.ones1_bf[0:1, :], r2b[0:1, :],
                          True, True)],
                   [u + "r2b", u + "ln_ones1"], ["psum5"])
                vop(ctx, lambda v: v.tensor_copy(r2bc[:, :], ctx.pb[5][:, :TC]),
                    ["psum5"], [u + "r2bc"])
                for i in range(NDC):
                    vop(ctx, lambda v, i=i, th=th: v.tensor_mul(
                        xrm[i][:, th:th + TC], xnf[i][:, :], r2bc[:, :]),
                        [rnl[i], u + "r2bc"], [xrmres[i]])
                kacc(ctx, ctx.pb[0][0:14, :TC],
                     [rwt[k][:, :] for k in range(NDC)],
                     [xnf[k][:, :] for k in range(NDC)],
                     [f"{u}rwt{k}" for k in range(NDC)] + rnl, ["psum0"])
                vop(ctx, lambda v: v.tensor_copy(lgt[0:14, :], ctx.pb[0][0:14, :TC]),
                    ["psum0"], [u + "lgt"])
                for tt in range(TC // 128):
                    transpose(ctx, ctx.pb[4][0:128, 0:14],
                              lgt[0:14, 128 * tt:128 * (tt + 1)],
                              identf[0:14, 0:14], [u + "lgt", u + "idf"], ["psum4"])
                    chain(ctx, [
                        lambda v: v.reduce_max(gcol[:, :], ctx.pb[4][0:128, 0:14],
                                               axis=AX, negate=True),
                    ], ["psum4"], [u + "gcol"])
                    vop(ctx, lambda sc: sc.activation(
                        gt[:, 0:14], ctx.pb[4][0:128, 0:14], AF.Exp,
                        bias=gcol[:, 0:1], accum_out=gcol2[:, 0:1]),
                        ["psum4", u + "gcol"], [u + "gt", u + "gcol2"], eng="scalar")
                    fns = [
                        lambda v: v.reciprocal(gcol2[:, :], gcol2[:, :]),
                        lambda v: v.tensor_scalar(gt[:, 0:14], gt[:, 0:14],
                                                  gcol2[:, 0:1], None, ALU.mult),
                    ]
                    for e in range(14):
                        fns.append(lambda v, e=e: v.tensor_scalar(
                            gtmp[:, 0:14], gt[:, 0:14], gt[:, e:e + 1], None,
                            ALU.is_gt, ALU.add, accum_out=gcnt[:, e:e + 1]))
                    fns += [
                        lambda v: v.tensor_scalar(gcnt[:, 0:14], gcnt[:, 0:14], 2.0,
                                                  None, ALU.is_lt),
                        lambda v: v.tensor_mul(gtmp[:, 0:14], gt[:, 0:14],
                                               gcnt[:, 0:14]),
                        lambda v: v.reduce_sum(gcol[:, 0:1], gtmp[:, 0:14], axis=AX),
                        lambda v: v.tensor_scalar_add(gcol[:, 0:1], gcol[:, 0:1], 1e-9),
                        lambda v: v.reciprocal(gcol[:, :], gcol[:, :]),
                        lambda v: v.tensor_scalar(gtmp[:, 0:14], gtmp[:, 0:14],
                                                  gcol[:, 0:1], None, ALU.mult),
                    ]
                    chain(ctx, fns, [u + "gt", u + "gcol2"],
                          [u + "gtmp", u + "gcnt", u + "gcol", u + "gcol2", u + "gt"])
                    transpose(ctx, ctx.pb[4][0:16, 0:128], gtmp[:, 0:16],
                              identf[:, :], [u + "gtmp", u + "idf"], ["psum4"])
                    vop(ctx, lambda v, tcx=tcx, tt=tt: v.tensor_copy(
                        cwf[0:14, tcx * TC + 128 * tt:tcx * TC + 128 * (tt + 1)],
                        ctx.pb[4][0:14, 0:128]),
                        ["psum4"], [u + "cwf"])

            for i in range(NDC):
                vop(ctx, lambda v, i=i: v.memset(yac[i][:, :], 0.0), [],
                    [f"{u}ya{i}"])
            load_rms(0)
            p1 = load_w1_slice(0, 0)
            p2a = load_w2_half(0, 0)
            for e in range(16):
                routed = e < 14
                if e + 1 < 16:
                    load_rms(e + 1)
                if routed:
                    for tcx in range(NQC):
                        mm(ctx, [(ctx.pb[4][:, :TC], e14[:, 128 * e:128 * (e + 1)],
                                  cwf[0:14, tcx * TC:(tcx + 1) * TC], True, True)],
                           [u + "e14", u + "cwf"], ["psum4"])
                        vop(ctx, lambda v, tcx=tcx: v.tensor_copy(
                            cwb[:, tcx * TC:(tcx + 1) * TC], ctx.pb[4][:, :TC]),
                            ["psum4"], [u + "cwb"])
                for hc in range(4):
                    pcur = p1
                    if hc < 3:
                        p1 = load_w1_slice(e, hc + 1)
                    elif e + 1 < 16:
                        p1 = load_w1_slice(e + 1, 0)
                    for tcx in range(NQC):
                        th = tcx * TC
                        for o4 in range(4):
                            bank = o4 % 4
                            kacc(ctx, ctx.pb[bank][:, :TC],
                                 [w1t[pcur][k][:, 128 * o4:128 * (o4 + 1)]
                                  for k in range(NDC)],
                                 [xrm[k][:, th:th + TC] for k in range(NDC)],
                                 [f"{u}w1t{pcur}_{k}" for k in range(NDC)] + xrmres,
                                 [f"psum{bank}"])
                            hi = 4 * hc + o4
                            if routed:
                                vop(ctx, lambda sc, bank=bank: sc.activation(
                                    sil[:, :], ctx.pb[bank][:, :TC], AF.Silu),
                                    [f"psum{bank}"], [u + "sil"], eng="scalar")
                                vop(ctx, lambda v, hi=hi, th=th: v.tensor_mul(
                                    hsl[hi][:, th:th + TC], sil[:, :],
                                    cwb[:, th:th + TC]),
                                    [u + "sil", u + "cwb"], [f"{u}hsl{hi}"])
                            else:
                                vop(ctx, lambda sc, hi=hi, th=th, bank=bank:
                                    sc.activation(hsl[hi][:, th:th + TC],
                                                  ctx.pb[bank][:, :TC], AF.Silu),
                                    [f"psum{bank}"], [f"{u}hsl{hi}"], eng="scalar")
                p2b = load_w2_half(e, 1)
                for tcx in range(NQC):
                    th = tcx * TC
                    for oc in range(NDC):
                        kacc2(ctx, ctx.pb[oc][:, :TC],
                              [w2t[p2a][k][:, 128 * oc:128 * (oc + 1)]
                               for k in range(8)],
                              [hsl[k][:, th:th + TC] for k in range(8)],
                              [f"{u}w2t{p2a}_{k}" for k in range(8)] +
                              [f"{u}hsl{k}" for k in range(8)],
                              [f"psum{oc}"], start=True, stop=False)
                    if tcx == NQC - 1 and e + 1 < 16:
                        p2a_next = load_w2_half(e + 1, 0)
                    for oc in range(NDC):
                        kacc2(ctx, ctx.pb[oc][:, :TC],
                              [w2t[p2b][k][:, 128 * oc:128 * (oc + 1)]
                               for k in range(8)],
                              [hsl[8 + k][:, th:th + TC] for k in range(8)],
                              [f"{u}w2t{p2b}_{k}" for k in range(8)] +
                              [f"{u}hsl{8 + k}" for k in range(8)],
                              [f"psum{oc}"], start=False, stop=True)
                        vop(ctx, lambda v, oc=oc, th=th: v.tensor_add(
                            yac[oc][:, th:th + TC], yac[oc][:, th:th + TC],
                            ctx.pb[oc][:, :TC]),
                            [f"psum{oc}", f"{u}ya{oc}"], [f"{u}ya{oc}"])
                if e + 1 < 16:
                    p2a = p2a_next

            for tcx in range(NQC):
                t0 = qb + tcx * TC
                th = tcx * TC
                for i in range(NDC):
                    dma_in(ctx, fxr[:, :], xin[i][:, t0:t0 + TC], u + "fxr")
                    vop(ctx, lambda v, i=i, th=th: v.tensor_add(
                        fyv[:, :], fxr[:, :], yac[i][:, th:th + TC]),
                        [u + "fxr", f"{u}ya{i}"], [u + "fyv"])
                    dma_out(ctx, xout[i][:, t0:t0 + TC], fyv[:, :],
                            [u + "fyv"], u + "fxo")



def emit_l2_ffn(ctx, xin, xout, W):
    nc = ctx.nc
    u = f"2f{ctx.uid()}_"
    h_scr = nc.dram_tensor(u + "h", [32, 128, T_FULL], BF16, kind="Internal").ap()
    g_scr = nc.dram_tensor(u + "g", [32, 128, T_FULL], BF16, kind="Internal").ap()

    with scope(ctx) as sb:
        xnb = [sb(f"{u}xn{i}", [128, T_FULL], BF16) for i in range(NDC)]
        xnres = [f"{u}xn{i}" for i in range(NDC)]
        xs = [sb(f"{u}x{i}", [128, TC], F32) for i in range(NDC)]
        ln = LNTmp(ctx, sb, u)
        for tcx in range(NTC):
            t0 = tcx * TC
            rxl = [f"{u}x{i}" for i in range(NDC)]
            for i in range(NDC):
                dma_in(ctx, xs[i][:, :], xin[i][:, t0:t0 + TC], rxl[i])
            emit_ln(ctx, ln, [xs[i][:, :] for i in range(NDC)], rxl,
                    [xnb[i][:, t0:t0 + TC] for i in range(NDC)], xnres)

        w1g = [[[sb(f"{u}aw1{p}_{k}_{j}", [128, 128], BF16) for k in range(NDC)]
                for j in range(4)] for p in range(2)]
        w2g = [[[sb(f"{u}aw2{p}_{k}_{j}", [128, 128], BF16) for k in range(NDC)]
                for j in range(4)] for p in range(2)]
        s1 = sb(u + "s1", [128, 32], F32)
        s2 = sb(u + "s2", [128, 32], F32)
        sil = sb(u + "sil", [128, TC], F32)
        hbf = sb(u + "hbf", [128, TC], BF16)
        dma_in(ctx, s1[:, :], W["f2_s1"], u + "s1")
        dma_in(ctx, s2[:, :], W["f2_s2"], u + "s2")

        def load_a(og):
            p = og % 2
            for j in range(4):
                oc = og * 4 + j
                for k in range(NDC):
                    dma_in(ctx, w1g[p][j][k][:, :],
                           W["f2_w1"][k][:, 128 * oc:128 * (oc + 1)],
                           f"{u}aw1{p}_{k}_{j}", eng="gpsimd")
                    dma_in(ctx, w2g[p][j][k][:, :],
                           W["f2_w2"][k][:, 128 * oc:128 * (oc + 1)],
                           f"{u}aw2{p}_{k}_{j}", eng="gpsimd")

        load_a(0)
        for og in range(8):
            p = og % 2
            if og + 1 < 8:
                load_a(og + 1)
            for tcx in range(NTC):
                t0 = tcx * TC
                for j in range(4):
                    oc = og * 4 + j
                    ba, bb = (2 * j) % 4, (2 * j + 1) % 4
                    kacc(ctx, ctx.pb[ba][:, :TC],
                         [w1g[p][j][k][:, :] for k in range(NDC)],
                         [xnb[k][:, t0:t0 + TC] for k in range(NDC)],
                         [f"{u}aw1{p}_{k}_{j}" for k in range(NDC)] + xnres,
                         [f"psum{ba}"])
                    kacc(ctx, ctx.pb[bb][:, :TC],
                         [w2g[p][j][k][:, :] for k in range(NDC)],
                         [xnb[k][:, t0:t0 + TC] for k in range(NDC)],
                         [f"{u}aw2{p}_{k}_{j}" for k in range(NDC)] + xnres,
                         [f"psum{bb}"])
                    vop(ctx, lambda s_, oc=oc, ba=ba: s_.activation(
                        sil[:, :], ctx.pb[ba][:, :TC], AF.Silu,
                        scale=s1[:, oc:oc + 1]),
                        [f"psum{ba}", u + "s1"], [u + "sil"], eng="scalar")
                    vop(ctx, lambda v, oc=oc, bb=bb: v.scalar_tensor_tensor(
                        hbf[:, :], ctx.pb[bb][:, :TC], s2[:, oc:oc + 1],
                        sil[:, :], ALU.mult, ALU.mult),
                        [f"psum{bb}", u + "s2", u + "sil"], [u + "hbf"])
                    dma_out(ctx, h_scr[oc][:, t0:t0 + TC], hbf[:, :],
                            [u + "hbf"], u + "hscr")

    def gelu_stage(src_scr, dst_scr, w_key, ws_key, stag):
        with scope(ctx) as sb2:
            wgt = [[[sb2(f"{u}{stag}w{p}_{k}_{j}", [128, 128], BF16)
                     for k in range(32)] for j in range(8)] for p in range(2)]
            hs = sb2(u + stag + "hs", [128, 32], F32)
            htile = [sb2(f"{u}{stag}h{k}", [128, TC], BF16) for k in range(32)]
            obf = sb2(u + stag + "obf", [128, TC], BF16)
            dma_in(ctx, hs[:, :], W[ws_key], u + stag + "hs")

            def load_g(og):
                p = og % 2
                for j in range(8):
                    oc = og * 8 + j
                    for k in range(32):
                        dma_in(ctx, wgt[p][j][k][:, :],
                               W[w_key][k][:, 128 * oc:128 * (oc + 1)],
                               f"{u}{stag}w{p}_{k}_{j}", eng="gpsimd")

            load_g(0)
            for og in range(4):
                p = og % 2
                if og + 1 < 4:
                    load_g(og + 1)
                for tcx in range(NTC):
                    t0 = tcx * TC
                    hres = [f"{u}{stag}h{k}" for k in range(32)]
                    for k in range(32):
                        dma_in(ctx, htile[k][:, :], src_scr[k][:, t0:t0 + TC],
                               hres[k])
                    for j in range(8):
                        oc = og * 8 + j
                        bank = j % 4
                        kacc(ctx, ctx.pb[bank][:, :TC],
                             [wgt[p][j][k][:, :] for k in range(32)],
                             [htile[k][:, :] for k in range(32)],
                             [f"{u}{stag}w{p}_{k}_{j}" for k in range(32)] + hres,
                             [f"psum{bank}"])
                        vop(ctx, lambda s_, oc=oc, bank=bank: s_.activation(
                            obf[:, :], ctx.pb[bank][:, :TC], AF.Gelu,
                            scale=hs[:, oc:oc + 1]),
                            [f"psum{bank}", u + stag + "hs"], [u + stag + "obf"],
                            eng="scalar")
                        dma_out(ctx, dst_scr[oc][:, t0:t0 + TC], obf[:, :],
                                [u + stag + "obf"], u + stag + "dst")

    gelu_stage(h_scr, g_scr, "f2_h1", "f2_h1s", "b")
    gelu_stage(g_scr, h_scr, "f2_h2", "f2_h2s", "c")

    with scope(ctx) as sb2:
        w3t = [[sb2(f"{u}dw{k}_{j}", [128, 128], BF16) for k in range(32)]
               for j in range(8)]
        s3 = sb2(u + "s3", [128, 8], F32)
        htile = [sb2(f"{u}dh{k}", [128, TC], BF16) for k in range(32)]
        xr = sb2(u + "dxr", [128, TC], F32)
        yv = sb2(u + "yv", [128, TC], F32)
        dma_in(ctx, s3[:, :], W["f2_w3s"], u + "s3")
        for j in range(8):
            for k in range(32):
                dma_in(ctx, w3t[j][k][:, :], W["f2_w3"][k][:, 128 * j:128 * (j + 1)],
                       f"{u}dw{k}_{j}", eng="gpsimd")
        for tcx in range(NTC):
            t0 = tcx * TC
            hres = [f"{u}dh{k}" for k in range(32)]
            for k in range(32):
                dma_in(ctx, htile[k][:, :], h_scr[k][:, t0:t0 + TC], hres[k])
            for oc in range(8):
                bank = oc % 4
                kacc(ctx, ctx.pb[bank][:, :TC],
                     [w3t[oc][k][:, :] for k in range(32)],
                     [htile[k][:, :] for k in range(32)],
                     [f"{u}dw{k}_{oc}" for k in range(32)] + hres,
                     [f"psum{bank}"])
                dma_in(ctx, xr[:, :], xin[oc][:, t0:t0 + TC], u + "dxr")
                vop(ctx, lambda v, oc=oc, bank=bank: v.scalar_tensor_tensor(
                    yv[:, :], ctx.pb[bank][:, :TC], s3[:, oc:oc + 1],
                    xr[:, :], ALU.mult, ALU.add),
                    [f"psum{bank}", u + "s3", u + "dxr"], [u + "yv"])
                dma_out(ctx, xout[oc][:, t0:t0 + TC], yv[:, :],
                        [u + "yv"], u + "xo")



def emit_l3_ffn(ctx, xin, xout, W):
    nc = ctx.nc
    with contextlib.ExitStack() as bs:
        def sb(n, s, d):
            return bs.enter_context(nc.sbuf_tensor(n, s, d))
        tag = "f3_"

        wq = [sb(f"f3_w{i}", [128, 1024], F32) for i in range(8)]
        wdT = [sb(f"f3_wT{i}", [128, 1024], F32) for i in range(8)]
        ident = sb("f3_id", [128, 128], F32)
        bias = sb("f3_b", [128, 8], F32)
        xs = [sb(f"f3_x{i}", [128, TC], F32) for i in range(NDC)]
        xn = [sb(f"f3_xn{i}", [128, TC], F32) for i in range(NDC)]
        xnb = [sb(f"f3_xnb{i}", [128, TC], BF16) for i in range(NDC)]
        yout = [sb(f"f3_y{i}", [128, TC], F32) for i in range(NDC)]
        qt = QTmp(sb, tag, 1024)
        ln = LNTmp(ctx, sb, tag)

        dma_in(ctx, ident[:, :], W["ident_f32"], "f3_id")
        dma_in(ctx, bias[:, :], W["w3f_b"], "f3_b")
        for i in range(8):
            dma_in(ctx, wq[i][:, :], W["w3f_w"][i], f"f3_w{i}", eng="gpsimd")
            emit_quant(ctx, qt, wq[i][:, :], wq[i][:, :], 1024, "free",
                       [f"f3_w{i}"], [f"f3_w{i}"])
        for i in range(8):
            for j in range(8):
                transpose(ctx, ctx.pb[4][:, 0:128], wq[i][:, 128 * j:128 * (j + 1)],
                          ident[:, :], [f"f3_w{i}", "f3_id"], ["psum4"])
                vop(ctx, lambda v, i=i, j=j: v.tensor_copy(
                    wdT[j][:, 128 * i:128 * (i + 1)], ctx.pb[4][:, 0:128]),
                    ["psum4"], [f"f3_wT{j}"])

        for tcx in range(NTC):
            t0 = tcx * TC
            rxl = [f"f3_x{i}" for i in range(NDC)]
            rnl = [f"f3_xn{i}" for i in range(NDC)]
            for i in range(NDC):
                dma_in(ctx, xs[i][:, :], xin[i][:, t0:t0 + TC], rxl[i])
            emit_ln(ctx, ln, [xs[i][:, :] for i in range(NDC)], rxl,
                    [xnb[i][:, :] for i in range(NDC)], [f"f3_xnb{i}" for i in range(NDC)],
                    out_f32=[xn[i][:, :] for i in range(NDC)], out_f32_resl=rnl)
            for k in range(NDC):
                emit_quant(ctx, qt, xn[k][:, :], xn[k][:, :], TC, "part",
                           [rnl[k]], [rnl[k]])
                for oc in range(8):
                    mm(ctx, [(ctx.pb[oc][:, :TC],
                              wdT[k][:, 128 * oc:128 * (oc + 1)], xn[k][:, :],
                              k == 0, k == NDC - 1)],
                       [rnl[k], f"f3_wT{k}"], [f"psum{oc}"])
            for oc in range(8):
                vop(ctx, lambda v, oc=oc: v.scalar_tensor_tensor(
                    yout[oc][:, :], ctx.pb[oc][:, :TC], bias[:, oc:oc + 1],
                    xs[oc][:, :], ALU.add, ALU.add),
                    [f"psum{oc}", "f3_b", rxl[oc]], [f"f3_y{oc}"])
                dma_out(ctx, xout[oc][:, t0:t0 + TC], yout[oc][:, :],
                        [f"f3_y{oc}"], f"f3_xo{tcx}")



def host_weight_layouts(inputs):
    W = {}
    f = np.ascontiguousarray
    W["a1_wq"] = f(inputs["L1_attn_W_q"].T.reshape(8, 128, 1024))
    W["a1_wk"] = f(inputs["L1_attn_W_k"].T.reshape(8, 128, 1024))
    W["a1_wv"] = f(inputs["L1_attn_W_v"].T.reshape(8, 128, 1024))
    W["a1_wo"] = f(inputs["L1_attn_W_o"].T.reshape(8, 128, 1024))
    _i = np.arange(32, dtype=np.float64)
    _t = np.minimum(np.arange(T_FULL), 2047).astype(np.float64)
    _ang = _t[None, :] * (10000.0 ** (-2.0 * _i / 64))[:, None]
    _c = np.cos(_ang)
    _s = np.sin(_ang)
    _p = np.arange(128)
    _iofp = (_p % 64) // 2
    cB = _c[_iofp]
    sB = _s[_iofp] * np.where(_p % 2 == 0, -1.0, 1.0)[:, None]
    W["a1_cos"] = f(cB.astype(np.float32).astype(bfdt))
    W["a1_sin"] = f(sB.astype(np.float32).astype(bfdt))
    Pswap = np.zeros((128, 128), dtype=np.float32)
    Pswap[np.arange(128), np.arange(128) ^ 1] = 1.0
    W["a1_pswap"] = f(Pswap.astype(bfdt))
    def _bd(m):
        out = np.zeros((8, 128, 128), dtype=np.float32)
        for p in range(8):
            out[p, 0:64, 0:64] = m[2 * p].T
            out[p, 64:128, 64:128] = m[2 * p + 1].T
        return out
    W["a3_wd"] = f(inputs["L3_attn_W_down"].reshape(8, 128, 1024))
    W["a3_tk"] = f(_bd(inputs["L3_attn_tokeys"]))
    W["a3_tq"] = f(_bd(inputs["L3_attn_toqueries"]))
    W["a3_tv"] = f(_bd(inputs["L3_attn_tovalues"]))
    W["a3_wu"] = f(inputs["L3_attn_W_unify"].reshape(8, 128, 1024))
    W["a3_bu"] = f(inputs["L3_attn_b_unify"].reshape(8, 128).T)
    W["a0_wqkv"] = f(inputs["L0_attn_w_qkv"].reshape(8, 128, 3072))
    W["a0_wg"] = f(inputs["L0_attn_w_gate"].reshape(8, 128, 1024))
    W["a0_wp"] = f(inputs["L0_attn_w_proj"].reshape(8, 128, 1024))
    W["a0_bqkv"] = f(inputs["L0_attn_b_qkv"].reshape(24, 128).T)
    W["a0_bg"] = f(inputs["L0_attn_b_gate"].reshape(8, 128).T)
    W["a0_bp"] = f(inputs["L0_attn_b_proj"].reshape(8, 128).T)
    W["a0_lg"] = f(inputs["L0_attn_ln_g"].reshape(8, 128).T)
    W["a0_lb"] = f(inputs["L0_attn_ln_b"].reshape(8, 128).T)
    s_ = np.arange(128)[:, None]
    t_ = np.arange(512)[None, :]
    W["maskA"] = f(np.concatenate([(128 * j + s_ <= t_) for j in range(4)],
                                  axis=1).astype(np.float32).astype(bfdt))
    W["f0_rw"] = f(inputs["L0_ffn_router_weight"].T.reshape(8, 128, 16))
    W["f0_eb"] = f(inputs["L0_ffn_e_bias"].reshape(16, 1))
    W["f0_gw"] = f(inputs["L0_ffn_gate_w"].transpose(0, 2, 1).reshape(16, 8, 128, 512))
    W["f0_uw"] = f(inputs["L0_ffn_up_w"].transpose(0, 2, 1).reshape(16, 8, 128, 512))
    W["f0_dw"] = f(inputs["L0_ffn_down_w"].transpose(0, 2, 1).reshape(16, 4, 128, 1024))
    W["f0_sg"] = f(inputs["L0_ffn_sh_gate_w"].T.reshape(8, 128, 1024))
    W["f0_su"] = f(inputs["L0_ffn_sh_up_w"].T.reshape(8, 128, 1024))
    W["f0_sd"] = f(inputs["L0_ffn_sh_down_w"].T.reshape(8, 128, 1024))
    E16 = np.zeros((16, 16 * 128), dtype=np.float32)
    for e in range(16):
        E16[e, 128 * e:128 * (e + 1)] = 1.0
    W["E16"] = f(E16.astype(bfdt))
    W["f1_rw"] = f(inputs["L1_ffn_router_w"].reshape(8, 128, 14))
    W["f1_rms"] = f(inputs["L1_ffn_rms_w"].reshape(14, 8, 128).transpose(0, 2, 1))
    W["f1_rms_s"] = f(inputs["L1_ffn_rms_w_s"].reshape(2, 8, 128).transpose(0, 2, 1))
    W["f1_w1"] = f(inputs["L1_ffn_w1"].reshape(14, 8, 128, 2048))
    W["f1_w2"] = f(inputs["L1_ffn_w2"].reshape(14, 16, 128, 1024))
    W["f1_w1_s"] = f(inputs["L1_ffn_w1_s"].reshape(2, 8, 128, 2048))
    W["f1_w2_s"] = f(inputs["L1_ffn_w2_s"].reshape(2, 16, 128, 1024))
    E14 = np.zeros((14, 14 * 128), dtype=np.float32)
    for e in range(14):
        E14[e, 128 * e:128 * (e + 1)] = 1.0
    W["E14"] = f(E14.astype(bfdt))
    W["a2_wk"] = f(inputs["L2_attn_W_K"].transpose(2, 0, 1).reshape(1024, 1024).reshape(8, 128, 1024))
    W["a2_wq"] = f(inputs["L2_attn_W_Q"].transpose(2, 0, 1).reshape(1024, 1024).reshape(8, 128, 1024))
    W["a2_wv"] = f(inputs["L2_attn_W_V"].transpose(2, 0, 1).reshape(1024, 1024).reshape(8, 128, 1024))
    q_ = np.arange(128)[:, None]
    k_ = np.arange(512)[None, :]
    W["maskQ"] = f(np.concatenate([(k_ > 128 * j + q_) * np.float32(-1e30)
                                   for j in range(4)], axis=1).astype(bfdt))
    kk_ = np.arange(128)[:, None]
    qq_ = np.arange(512)[None, :]
    W["maskT"] = f(np.concatenate([(qq_ < 128 * j + kk_) * np.float32(-1e30)
                                   for j in range(4)], axis=1).astype(bfdt))
    W["f2_w1"] = f(inputs["L2_ffn_w1_w"].T.reshape(8, 128, 4096))
    W["f2_w2"] = f(inputs["L2_ffn_w2_w"].T.reshape(8, 128, 4096))
    W["f2_h1"] = f(inputs["L2_ffn_h1_w"].T.reshape(32, 128, 4096))
    W["f2_h2"] = f(inputs["L2_ffn_h2_w"].T.reshape(32, 128, 4096))
    W["f2_w3"] = f(inputs["L2_ffn_w3_w"].T.reshape(32, 128, 1024))
    W["f2_s1"] = f(inputs["L2_ffn_w1_s"].reshape(32, 128).T)
    W["f2_s2"] = f(inputs["L2_ffn_w2_s"].reshape(32, 128).T)
    W["f2_h1s"] = f(inputs["L2_ffn_h1_s"].reshape(32, 128).T)
    W["f2_h2s"] = f(inputs["L2_ffn_h2_s"].reshape(32, 128).T)
    W["f2_w3s"] = f(inputs["L2_ffn_w3_s"].reshape(8, 128).T)
    W["w3f_w"] = f(inputs["L3_ffn_weight"].reshape(8, 128, 1024))
    W["w3f_b"] = f(inputs["L3_ffn_bias"].reshape(8, 128).T)
    W["ident_f32"] = np.eye(128, dtype=np.float32)
    W["ident_bf"] = np.eye(128, dtype=np.float32).astype(bfdt)
    return W


def declare_weights(nc, W):
    aps = {}
    for k, v in W.items():
        aps[k] = nc.dram_tensor(k, list(v.shape), mybir.dt.from_np(v.dtype),
                                kind="ExternalInput").ap()
    return aps



BLOCK_SEQ = ["l0_attn", "l0_ffn", "l1_attn", "l1_ffn",
             "l2_attn", "l2_ffn", "l3_attn", "l3_ffn"]
BLOCK_WKEYS = {
    "l0_attn": ["a0_wqkv", "a0_wg", "a0_wp", "a0_bqkv", "a0_bg", "a0_bp",
                "a0_lg", "a0_lb", "maskA", "ident_bf"],
    "l0_ffn": ["f0_rw", "f0_eb", "f0_gw", "f0_uw", "f0_dw", "f0_sg", "f0_su",
               "f0_sd", "E16", "ident_f32"],
    "l1_attn": ["a1_wq", "a1_wk", "a1_wv", "a1_wo", "a1_cos", "a1_sin",
                "a1_pswap", "maskQ", "maskT", "ident_bf", "ident_f32"],
    "l1_ffn": ["f1_rw", "f1_rms", "f1_rms_s", "f1_w1", "f1_w2", "f1_w1_s",
               "f1_w2_s", "E14", "ident_f32"],
    "l2_attn": ["a2_wk", "a2_wq", "a2_wv", "maskQ", "maskT", "ident_bf",
                "ident_f32"],
    "l2_ffn": ["f2_w1", "f2_w2", "f2_h1", "f2_h2", "f2_w3", "f2_s1", "f2_s2",
               "f2_h1s", "f2_h2s", "f2_w3s"],
    "l3_attn": ["a3_wd", "a3_tk", "a3_tq", "a3_tv", "a3_wu", "a3_bu", "maskQ",
                "maskT", "ident_bf", "ident_f32"],
    "l3_ffn": ["w3f_w", "w3f_b", "ident_f32"],
}
BLOCK_EMITTERS = {"l0_attn": emit_l0_attn, "l0_ffn": emit_l0_ffn,
                  "l1_attn": emit_l1_attn, "l2_attn": emit_l2_attn,
                  "l3_attn": emit_l3_attn, "l1_ffn": emit_l1_ffn,
                  "l2_ffn": emit_l2_ffn, "l3_ffn": emit_l3_ffn}


def build_single_block(block_name, W):
    nc = bass.Bass("TRN2", target_bir_lowering=False)
    xin_t = nc.dram_tensor("xin", [NDC, 128, T_FULL], F32, kind="ExternalInput")
    xout_t = nc.dram_tensor("xout", [NDC, 128, T_FULL], F32, kind="ExternalOutput")
    Wsub = {k: W[k] for k in BLOCK_WKEYS[block_name]}
    waps = declare_weights(nc, Wsub)
    with contextlib.ExitStack() as stack:
        ctx = Ctx(nc, stack)
        xin = [xin_t.ap()[i] for i in range(NDC)]
        xout = [xout_t.ap()[i] for i in range(NDC)]
        BLOCK_EMITTERS[block_name](ctx, xin, xout, waps)
        ctx.p.wait_all()
        ctx.p.replay()
    return nc


def build_fused(W):
    nc = bass.Bass("TRN2", target_bir_lowering=False)
    xin_t = nc.dram_tensor("xin", [NDC, 128, T_FULL], F32, kind="ExternalInput")
    xout_t = nc.dram_tensor("xout", [NDC, 128, T_FULL], F32, kind="ExternalOutput")
    xa_t = nc.dram_tensor("xscrA", [NDC, 128, T_FULL], F32, kind="Internal")
    xb_t = nc.dram_tensor("xscrB", [NDC, 128, T_FULL], F32, kind="Internal")
    waps = declare_weights(nc, W)
    hop = [xa_t.ap(), xb_t.ap()]
    with contextlib.ExitStack() as stack:
        ctx = Ctx(nc, stack)
        nblk = len(BLOCK_SEQ)
        for bi, name in enumerate(BLOCK_SEQ):
            src = xin_t.ap() if bi == 0 else hop[bi % 2]
            dst = xout_t.ap() if bi == nblk - 1 else hop[(bi + 1) % 2]
            xin = [src[i] for i in range(NDC)]
            xout = [dst[i] for i in range(NDC)]
            BLOCK_EMITTERS[name](ctx, xin, xout, waps)
        ctx.p.wait_all()
        ctx.p.replay()
    return nc


def kernel(**inputs):
    x = inputs["L0_attn_x"]
    W = host_weight_layouts(inputs)
    xs = [np.ascontiguousarray(x[b].T).reshape(NDC, 128, T_FULL) for b in range(8)]
    nc = build_fused(W)
    in_maps = [dict(W, xin=xs[b]) for b in range(8)]
    res = run_bass_kernel_spmd(nc, in_maps, core_ids=list(range(8)))
    out = np.stack([r["xout"].reshape(D, T_FULL).T for r in res.results])
    return out


def kernel_unfused(**inputs):
    x = inputs["L0_attn_x"]
    W = host_weight_layouts(inputs)
    xs = [np.ascontiguousarray(x[b].T).reshape(NDC, 128, T_FULL) for b in range(8)]
    cur = xs
    for name in BLOCK_SEQ:
        nc = build_single_block(name, W)
        Wsub = {k: W[k] for k in BLOCK_WKEYS[name]}
        in_maps = [dict(Wsub, xin=cur[b]) for b in range(8)]
        res = run_bass_kernel_spmd(nc, in_maps, core_ids=list(range(8)))
        cur = [r["xout"].reshape(NDC, 128, T_FULL) for r in res.results]
    out = np.stack([c.reshape(D, T_FULL).T for c in cur])
    return out


# version 07
# speedup vs baseline: 1.0413x; 1.0269x over previous
import contextlib
import numpy as np
import ml_dtypes
import concourse.bass as bass
import concourse.mybir as mybir
from concourse.bass_utils import run_bass_kernel_spmd

F32 = mybir.dt.float32
F32R = mybir.dt.float32r
BF16 = mybir.dt.bfloat16
U32 = mybir.dt.uint32
AF = mybir.ActivationFunctionType
ALU = mybir.AluOpType
AX = mybir.AxisListType.X
bfdt = ml_dtypes.bfloat16

ENG = ("sync", "scalar", "vector", "tensor", "gpsimd")
QSEM = {"sync": "dS", "scalar": "dA", "gpsimd": "dP"}
NSLOT = 6

T_FULL = 4096
D = 1024
NDC = 8
TC = 512
NTC = T_FULL // TC
MAGIC = 12582912.0


class P:

    def __init__(self, nc, stack):
        self.nc = nc
        self.stack = stack
        self.ops = {e: [] for e in ENG}
        self.sem = {}
        self.cnt = {}
        self.waited = {e: {} for e in ENG}
        self.res = {}
        self.dma_idx = {}
        for e in ENG:
            self._mk("s_" + e)
        for q in set(QSEM.values()):
            self.dma_idx[q] = 0
            for i in range(NSLOT):
                self._mk(f"{q}{i}")

    def _mk(self, name):
        self.sem[name] = self.stack.enter_context(self.nc.semaphore(name))
        self.cnt[name] = 0

    def _deps(self, e, reads, writes):
        m = {}
        for r in reads:
            d = self.res.get(r)
            if d and d["w"] is not None:
                s, v = d["w"]
                m[s] = max(m.get(s, 0), v)
        for w in writes:
            d = self.res.get(w)
            if d:
                for s, v in ([d["w"]] if d["w"] else []) + d["r"]:
                    m[s] = max(m.get(s, 0), v)
        own = "s_" + e
        out = []
        for s, v in m.items():
            if s == own and e == "tensor":
                continue
            if s == own and self.cnt[own] < v:
                raise RuntimeError("self-dep on not-yet-emitted token")
            if self.waited[e].get(s, 0) >= v:
                continue
            out.append((s, v))
        return out

    def emit(self, e, reads, writes, fn, dma=False):
        waits = self._deps(e, reads, writes)
        if dma:
            q = QSEM[e]
            sname = f"{q}{self.dma_idx[q] % NSLOT}"
            self.dma_idx[q] += 1
            amount = 16
            prev = self.cnt[sname]
            if prev > 0 and self.waited[e].get(sname, 0) < prev and \
                    all(s != sname for s, _ in waits):
                waits = waits + [(sname, prev)]
        else:
            sname, amount = "s_" + e, 1
        self.cnt[sname] += amount
        tok = (sname, self.cnt[sname])
        for s, v in waits:
            self.waited[e][s] = max(self.waited[e].get(s, 0), v)
        sem = self.sem[sname]

        def closure(eng, waits=waits, sem=sem, amount=amount, fn=fn):
            for s, v in waits:
                eng.wait_ge(self.sem[s], v)
            fn(eng).then_inc(sem, amount)

        self.ops[e].append(closure)
        for r in reads:
            self.res.setdefault(r, {"w": None, "r": []})["r"].append(tok)
        for w in writes:
            d = self.res.setdefault(w, {"w": None, "r": []})
            d["w"] = tok
            d["r"] = []
        return tok

    def barrier(self):
        snap = dict(self.cnt)
        for e in ENG:
            for s, v in snap.items():
                if v > 0 and self.waited[e].get(s, 0) < v:
                    self.ops[e].append(
                        lambda eng, s=s, v=v: eng.wait_ge(self.sem[s], v))
                    self.waited[e][s] = v
        for d in self.res.values():
            d["r"] = []

    def wait_all(self, e="sync"):
        for s in list(self.cnt):
            v = self.cnt[s]
            if v > 0 and self.waited[e].get(s, 0) < v:
                self.ops[e].append(lambda eng, s=s, v=v: eng.wait_ge(self.sem[s], v))
                self.waited[e][s] = v

    def replay(self):
        with self.nc.Block() as block:
            @block.sync
            def _(eng):
                for f in self.ops["sync"]:
                    f(eng)

            @block.scalar
            def _(eng):
                for f in self.ops["scalar"]:
                    f(eng)

            @block.vector
            def _(eng):
                for f in self.ops["vector"]:
                    f(eng)

            @block.tensor
            def _(eng):
                for f in self.ops["tensor"]:
                    f(eng)

            @block.gpsimd
            def _(eng):
                for f in self.ops["gpsimd"]:
                    f(eng)


class Ctx:
    def __init__(self, nc, stack):
        self.nc = nc
        self.stack = stack
        self.p = P(nc, stack)
        self.pb = [stack.enter_context(nc.psum_tensor(f"pbank{i}", [128, 512], F32))
                   for i in range(8)]
        self._uid = 0

    def uid(self):
        self._uid += 1
        return self._uid



def dma_in(ctx, dst_ap, src_ap, res_w, res_r=(), eng="sync"):
    return ctx.p.emit(eng, list(res_r), [res_w],
                      lambda g: g.dma_start(out=dst_ap, in_=src_ap), dma=True)


def dma_out(ctx, dst_ap, src_ap, res_r=(), res_w="dram_out", eng="scalar"):
    return ctx.p.emit(eng, list(res_r), [res_w],
                      lambda g: g.dma_start(out=dst_ap, in_=src_ap), dma=True)


def mm(ctx, plist, reads, writes):
    def fn(t):
        inst = None
        for (o, l, r, st, sp) in plist:
            inst = t.matmul(o, l, r, start=st, stop=sp)
        return inst
    return ctx.p.emit("tensor", reads, writes, fn)


def kacc(ctx, out_ap, lhsT_list, rhs_list, reads, writes):
    n = len(lhsT_list)
    return mm(ctx, [(out_ap, lhsT_list[i], rhs_list[i], i == 0, i == n - 1)
                    for i in range(n)], reads, writes)


def kacc2(ctx, out_ap, lhsT_list, rhs_list, reads, writes, start, stop):
    n = len(lhsT_list)
    return mm(ctx, [(out_ap, lhsT_list[i], rhs_list[i], start and i == 0,
                     stop and i == n - 1) for i in range(n)], reads, writes)


def transpose(ctx, out_psum_ap, in_ap, ident_ap, reads, writes):
    return ctx.p.emit("tensor", reads, writes,
                      lambda t: t.transpose(out_psum_ap, in_ap, ident_ap))


def vop(ctx, fn, reads, writes, eng="vector"):
    return ctx.p.emit(eng, reads, writes, fn)


@contextlib.contextmanager
def scope(ctx):
    with contextlib.ExitStack() as bs:
        yield lambda n, s, d: bs.enter_context(
            ctx.nc.sbuf_tensor(n, s, d, allow_name_mangling=True))
    ctx.p.barrier()


def chain(ctx, fns, reads, writes, eng="vector"):
    u = f"ch{ctx.uid()}"
    n = len(fns)
    tok = None
    for i, f in enumerate(fns):
        r = list(reads) + ([u] if i > 0 else [])
        w = ([u] if i < n - 1 else list(writes) + [u])
        tok = ctx.p.emit(eng, r, w, f)
    return tok



class LNTmp:
    def __init__(self, ctx, sb, tag, tc=TC):
        t = tag
        self.tag = t
        self.tc = tc
        self.xb = sb(t + "ln_xb", [128, tc], BF16)
        self.xsq = sb(t + "ln_xsq", [128, tc], BF16)
        self.rowS = sb(t + "ln_rowS", [1, tc], F32)[0:1, :]
        self.rowQ = sb(t + "ln_rowQ", [1, tc], F32)[0:1, :]
        self.rstd = sb(t + "ln_rstd", [1, tc], F32)[0:1, :]
        self.mneg = sb(t + "ln_mneg", [1, tc], F32)[0:1, :]
        self.rowbA = sb(t + "ln_rowbA", [1, tc], BF16)[0:1, :]
        self.rowbB = sb(t + "ln_rowbB", [1, tc], BF16)[0:1, :]
        self.rbc = sb(t + "ln_rbc", [128, 2 * tc], BF16)
        self.tmp = sb(t + "ln_tmp", [128, tc], F32)
        self.ones_bf = sb(t + "ln_ones", [128, 1], BF16)
        self.ones1_bf = sb(t + "ln_ones1", [1, 128], BF16)
        self.eps = sb(t + "ln_eps", [1, 1], F32)
        vop(ctx, lambda v: v.memset(self.ones_bf[:, :], 1.0), [], [t + "ln_ones"])
        vop(ctx, lambda v: v.memset(self.ones1_bf[:, :], 1.0), [], [t + "ln_ones1"])


def emit_ln(ctx, ln, x_chunks, x_resl, out_bf, out_resl, out_f32=None,
            out_f32_resl=None, eps=1e-5, affine=None, affine_res=()):
    tc = ln.tc
    t = ln.tag
    pass
    pbS, pbQ, pbB = ctx.pb[6], ctx.pb[7], ctx.pb[5]
    nd = len(x_chunks)
    for i in range(nd):
        vop(ctx, lambda v, i=i: v.tensor_copy(ln.xb[:, :tc], x_chunks[i]),
            [x_resl[i]], [t + "ln_xb"])
        vop(ctx, lambda s, i=i: s.activation(ln.xsq[:, :tc], x_chunks[i], AF.Square),
            [x_resl[i]], [t + "ln_xsq"], eng="scalar")
        mm(ctx, [(pbS[0:1, :tc], ln.ones_bf[:, :], ln.xb[:, :tc], i == 0, i == nd - 1)],
           [t + "ln_xb", t + "ln_ones"], ["psum6"])
        mm(ctx, [(pbQ[0:1, :tc], ln.ones_bf[:, :], ln.xsq[:, :tc], i == 0, i == nd - 1)],
           [t + "ln_xsq", t + "ln_ones"], ["psum7"])

    chain(ctx, [
        lambda v: v.tensor_scalar_mul(ln.rowS[0:1, :tc], pbS[0:1, :tc], 1.0 / (nd * 128)),
        lambda v: v.tensor_mul(ln.mneg[0:1, :tc], ln.rowS[0:1, :tc], ln.rowS[0:1, :tc]),
        lambda v: v.scalar_tensor_tensor(ln.rowQ[0:1, :tc], pbQ[0:1, :tc],
                                         1.0 / (nd * 128), ln.mneg[0:1, :tc],
                                         ALU.mult, ALU.subtract),
    ], ["psum6", "psum7"], [t + "ln_rows"])
    vop(ctx, lambda v: v.memset(ln.eps[:, :], float(eps)), [], [ln.tag + "ln_eps"])
    vop(ctx, lambda s: s.activation(ln.rowQ[0:1, :tc], ln.rowQ[0:1, :tc], AF.Sqrt,
                                    bias=ln.eps[0:1, 0:1]),
        [t + "ln_rows", ln.tag + "ln_eps"], [t + "ln_rows"], eng="scalar")
    chain(ctx, [
        lambda v: v.reciprocal(ln.rstd[0:1, :tc], ln.rowQ[0:1, :tc]),
        lambda v: v.scalar_tensor_tensor(ln.mneg[0:1, :tc], ln.rowS[0:1, :tc], -1.0,
                                         ln.rstd[0:1, :tc], ALU.mult, ALU.mult),
        lambda v: v.tensor_copy(ln.rowbA[0:1, :tc], ln.rstd[0:1, :tc]),
        lambda v: v.tensor_copy(ln.rowbB[0:1, :tc], ln.mneg[0:1, :tc]),
    ], [t + "ln_rows"], [t + "ln_r2b"])
    mm(ctx, [(pbB[:, :tc], ln.ones1_bf[0:1, :], ln.rowbA[0:1, :tc], True, True)],
       [t + "ln_r2b", t + "ln_ones1"], ["psum5"])
    vop(ctx, lambda v: v.tensor_copy(ln.rbc[:, 0:tc], pbB[:, :tc]),
        ["psum5"], [t + "ln_rbcF"])
    mm(ctx, [(pbB[:, :tc], ln.ones1_bf[0:1, :], ln.rowbB[0:1, :tc], True, True)],
       [t + "ln_r2b", "psum5"], ["psum5"])
    vop(ctx, lambda v: v.tensor_copy(ln.rbc[:, tc:2 * tc], pbB[:, :tc]),
        ["psum5"], [t + "ln_rbcF"])
    rb = ln.rbc[:, 0:tc]
    mb = ln.rbc[:, tc:2 * tc]
    for i in range(nd):
        wl = (([out_resl[i]] if out_bf is not None else [])
              + ([out_f32_resl[i]] if out_f32 is not None else []))
        fns = [lambda v, i=i: v.tensor_mul(ln.tmp[:, :tc], x_chunks[i], rb)]
        if affine is not None:
            ga, be = affine
            fns.append(lambda v, i=i: v.tensor_add(ln.tmp[:, :tc], ln.tmp[:, :tc],
                                                   mb))
            fns.append(lambda v, i=i, ga=ga, be=be: v.scalar_tensor_tensor(
                out_bf[i], ln.tmp[:, :tc], ga[i],
                be[i].broadcast_to([128, tc]), ALU.mult, ALU.add))
        elif out_f32 is not None:
            fns.append(lambda v, i=i: v.tensor_add(out_f32[i], ln.tmp[:, :tc], mb))
            if out_bf is not None:
                fns.append(lambda v, i=i: v.tensor_copy(out_bf[i], out_f32[i]))
        else:
            fns.append(lambda v, i=i: v.tensor_add(out_bf[i], ln.tmp[:, :tc], mb))
        chain(ctx, fns, [x_resl[i], t + "ln_rbcF", t + "ln_tmpres"] + list(affine_res),
              wl + [t + "ln_tmpres"])



class QTmp:
    def __init__(self, sb, tag, width):
        w = width
        t = tag
        self.w = w
        self.t1 = sb(t + "q_t1", [128, w], F32)
        self.t2 = sb(t + "q_t2", [128, w], F32)
        self.sc = sb(t + "q_sc", [128, max(w // 32, 32)], F32)
        self.inv = sb(t + "q_inv", [128, max(w // 32, 32)], F32)
        self.scE = sb(t + "q_scE", [128, w], F32)
        self.invE = sb(t + "q_invE", [128, w], F32)
        self.q8 = sb(t + "q_q8", [128, w], F32)
        self.q4 = sb(t + "q_q4", [128, w], F32)
        self.cm = sb(t + "q_cm", [128, w], F32)
        self.res = t + "qtmp"


def emit_quant(ctx, qt, x_ap, out_ap, width, mode, reads, writes):
    w = width
    nb = w // 32
    fns = []
    a = fns.append
    if mode == "part":
        a(lambda v: v.transpose(qt.t1[:, :w], x_ap))
        src = qt.t1[:, :w]
    else:
        src = x_ap
    a(lambda v: v.tensor_reduce(qt.sc[:, :nb], src.rearrange("p (b j) -> p b j", j=32),
                                axis=AX, op=ALU.max, apply_absolute_value=True))
    a(lambda v: v.tensor_scalar_mul(qt.sc[:, :nb], qt.sc[:, :nb], 1.0 / 7.5))
    a(lambda v: v.tensor_scalar_max(qt.sc[:, :nb], qt.sc[:, :nb], 1e-30))
    a(lambda v: v.reciprocal(qt.inv[:, :nb], qt.sc[:, :nb]))
    a(lambda v: v.tensor_copy(qt.scE[:, :w].rearrange("p (b j) -> p b j", j=32),
                              qt.sc[:, :nb].unsqueeze(2).broadcast_to([128, nb, 32])))
    a(lambda v: v.tensor_copy(qt.invE[:, :w].rearrange("p (b j) -> p b j", j=32),
                              qt.inv[:, :nb].unsqueeze(2).broadcast_to([128, nb, 32])))
    if mode == "part":
        a(lambda v: v.transpose(qt.t1[:, :w], qt.scE[:, :w]))
        a(lambda v: v.transpose(qt.t2[:, :w], qt.invE[:, :w]))
        a(lambda v: v.tensor_copy(qt.scE[:, :w], qt.t1[:, :w]))
        a(lambda v: v.tensor_copy(qt.invE[:, :w], qt.t2[:, :w]))
    a(lambda v: v.tensor_mul(qt.t1[:, :w], x_ap, qt.invE[:, :w]))
    for f, dst in ((8.0, qt.q8), (4.0, qt.q4), (2.0, qt.t2)):
        a(lambda v, f=f: v.tensor_scalar_mul(qt.cm[:, :w], qt.t1[:, :w], f))
        a(lambda v: v.tensor_scalar_add(qt.cm[:, :w], qt.cm[:, :w], MAGIC))
        a(lambda v: v.tensor_scalar_add(qt.cm[:, :w], qt.cm[:, :w], -MAGIC))
        a(lambda v, f=f, dst=dst: v.tensor_scalar_mul(dst[:, :w], qt.cm[:, :w], 1.0 / f))
    a(lambda v: v.tensor_mul(qt.cm[:, :w], qt.t1[:, :w], qt.t1[:, :w]))
    a(lambda v: v.tensor_scalar(qt.t1[:, :w], qt.cm[:, :w], 16.0, None, ALU.is_ge))
    a(lambda v: v.copy_predicated(qt.q4[:, :w], qt.t1[:, :w].bitcast(U32),
                                  qt.t2[:, :w]))
    a(lambda v: v.tensor_scalar(qt.t1[:, :w], qt.cm[:, :w], 4.0, None, ALU.is_ge))
    a(lambda v: v.copy_predicated(qt.q8[:, :w], qt.t1[:, :w].bitcast(U32),
                                  qt.q4[:, :w]))
    a(lambda v: v.tensor_mul(out_ap, qt.q8[:, :w], qt.scE[:, :w]))
    chain(ctx, fns, list(reads) + [qt.res], list(writes) + [qt.res])



def emit_l3_attn(ctx, xin, xout, W):
    return _emit_mha(ctx, xin, xout, W, wk="a3_tk", wq="a3_tq", wv="a3_tv",
                     rope=False, wo="a3_wu", tag="3a", causal=False,
                     pre_proj="a3_wd", wo_bias="a3_bu")


def emit_l1_attn(ctx, xin, xout, W):
    return _emit_mha(ctx, xin, xout, W, wk="a1_wk", wq="a1_wq", wv="a1_wv",
                     rope=True, wo="a1_wo", tag="1a")


def emit_l2_attn(ctx, xin, xout, W):
    return _emit_mha(ctx, xin, xout, W, wk="a2_wk", wq="a2_wq", wv="a2_wv",
                     rope=False, wo=None, tag="2a")


def _emit_mha(ctx, xin, xout, W, wk, wq, wv, rope, wo, tag, causal=True,
              pre_proj=None, wo_bias=None):
    nc = ctx.nc
    u = f"{tag}{ctx.uid()}_"
    NQT = T_FULL // 128
    NQG = T_FULL // TC
    ao_scr = None
    if wo is not None:
        ao_scr = nc.dram_tensor(u + "ao", [NDC, 128, T_FULL], BF16,
                                kind="Internal").ap()
    with scope(ctx) as sb:
        xnb = [sb(f"{u}xn{i}", [128, T_FULL], BF16) for i in range(NDC)]
        xnres = [f"{u}xn{i}" for i in range(NDC)]
        ln = LNTmp(ctx, sb, u)
        identb = sb(u + "idb", [128, 128], BF16)
        identf = sb(u + "idf", [128, 128], F32)
        maskQ = sb(u + "mq", [128, 4 * TC], BF16)
        maskT = sb(u + "mt", [128, 4 * TC], BF16)
        dma_in(ctx, identb[:, :], W["ident_bf"], u + "idb")
        dma_in(ctx, identf[:, :], W["ident_f32"], u + "idf")
        dma_in(ctx, maskQ[:, :], W["maskQ"], u + "mq")
        dma_in(ctx, maskT[:, :], W["maskT"], u + "mt")

        with scope(ctx) as sbx:
            xs = [sbx(f"{u}x{i}", [128, TC], F32) for i in range(NDC)]
            for tcx in range(NTC):
                t0 = tcx * TC
                rxl = [f"{u}x{i}" for i in range(NDC)]
                for i in range(NDC):
                    dma_in(ctx, xs[i][:, :], xin[i][:, t0:t0 + TC], rxl[i])
                emit_ln(ctx, ln, [xs[i][:, :] for i in range(NDC)], rxl,
                        [xnb[i][:, t0:t0 + TC] for i in range(NDC)], xnres)

        if pre_proj is not None:
            with scope(ctx) as sbd:
                wdt = [sbd(f"{u}wd{k}", [128, 1024], BF16) for k in range(NDC)]
                xdb = [sbd(f"{u}xd{i}", [128, T_FULL], BF16) for i in range(NDC)]
                for k in range(NDC):
                    dma_in(ctx, wdt[k][:, :], W[pre_proj][k], f"{u}wd{k}",
                           eng="gpsimd")
                for tg in range(NTC):
                    t0 = tg * TC
                    for oc in range(NDC):
                        bank = oc % 4
                        kacc(ctx, ctx.pb[bank][:, :TC],
                             [wdt[k][:, 128 * oc:128 * (oc + 1)] for k in range(NDC)],
                             [xnb[k][:, t0:t0 + TC] for k in range(NDC)],
                             [f"{u}wd{k}" for k in range(NDC)] + xnres,
                             [f"psum{bank}"])
                        vop(ctx, lambda v, oc=oc, t0=t0, bank=bank: v.tensor_copy(
                            xnb[oc][:, t0:t0 + TC] if False else xdb[oc][:, t0:t0 + TC],
                            ctx.pb[bank][:, :TC]),
                            [f"psum{bank}"], [f"{u}xd{oc}"])
                for oc in range(NDC):
                    vop(ctx, lambda v, oc=oc: v.tensor_copy(
                        xnb[oc][:, :], xdb[oc][:, :]),
                        [f"{u}xd{oc}"], [xnres[oc]])

        with scope(ctx) as sb2:
            qh = [sb2(f"{u}qh{h}", [128, T_FULL], BF16) for h in range(2)]
            vT = sb2(u + "vT", [128, TC], BF16)
            kx = [sb2(f"{u}kx{h}", [128, T_FULL], BF16) for h in range(2)]
            vtm = sb2(u + "vtm", [128, 32 * 256], BF16)
            qx = sb2(u + "qx", [128, TC], BF16)
            mrow = sb2(u + "mr", [65, T_FULL], BF16)
            mneg = sb2(u + "mn", [128, 1], F32)
            mcol = [sb2(f"{u}mc{h}", [128, 97], F32) for h in range(2)]
            rec = sb2(u + "rec", [1, TC], BF16)
            expt = [sb2(f"{u}expt{i}", [128, TC], BF16) for i in range(3)]
            recb = sb2(u + "recb", [64, TC], F32)
            aoc = sb2(u + "aoc", [128, T_FULL], BF16)
            xr = sb2(u + "xr", [128, TC], F32)
            yv = sb2(u + "yv", [128, TC], F32)
            wpk2 = [[sb2(f"{u}wpk{p}_{k}", [128, 128], BF16) for k in range(NDC)]
                    for p in range(2)]
            wpq2 = [[sb2(f"{u}wpq{p}_{k}", [128, 128], BF16) for k in range(NDC)]
                    for p in range(2)]
            wpv2 = [[sb2(f"{u}wpv{p}_{k}", [128, 128], BF16) for k in range(NDC)]
                    for p in range(2)]
            if rope:
                cB = sb2(u + "cB", [128, T_FULL], BF16)
                sB = sb2(u + "sB", [128, T_FULL], BF16)
                psw = sb2(u + "psw", [128, 128], BF16)
                praw = sb2(u + "praw", [128, TC], BF16)
                rt1 = sb2(u + "rt1", [128, TC], F32)
                rt2 = sb2(u + "rt2", [128, TC], F32)
                dma_in(ctx, cB[:, :], W["a1_cos"], u + "cB")
                dma_in(ctx, sB[:, :], W["a1_sin"], u + "sB")
                dma_in(ctx, psw[:, :], W["a1_pswap"], u + "psw")
            vt3 = vtm[:, :].rearrange("p (k c) -> p k c", c=256)
            vop(ctx, lambda v: v.memset(vtm[:, :], 0.0), [], [u + "vtm"])
            vop(ctx, lambda v: v.memset(vt3[:, :, 64:65], 1.0), [], [u + "vtm"])
            vop(ctx, lambda v: v.memset(vt3[:, :, 192:193], 1.0), [], [u + "vtm"])
            for h in range(2):
                vop(ctx, lambda v, h=h: v.memset(qh[h][64:128, :], 0.0),
                    [], [f"{u}qh{h}"])
                vop(ctx, lambda v, h=h: v.memset(kx[h][64:128, :], 0.0),
                    [], [f"{u}kx{h}"])
            vop(ctx, lambda v: v.memset(qx[64:128, :], 0.0), [], [u + "qx"])

            def load_pair(pr):
                p = pr % 2
                if pre_proj is not None:
                    dma_in(ctx, wpk2[p][0][:, :], W[wk][pr], f"{u}wpk{p}_0",
                           eng="gpsimd")
                    dma_in(ctx, wpq2[p][0][:, :], W[wq][pr], f"{u}wpq{p}_0",
                           eng="gpsimd")
                    dma_in(ctx, wpv2[p][0][:, :], W[wv][pr], f"{u}wpv{p}_0",
                           eng="gpsimd")
                else:
                    for k in range(NDC):
                        dma_in(ctx, wpk2[p][k][:, :],
                               W[wk][k][:, 128 * pr:128 * (pr + 1)],
                               f"{u}wpk{p}_{k}", eng="gpsimd")
                        dma_in(ctx, wpq2[p][k][:, :],
                               W[wq][k][:, 128 * pr:128 * (pr + 1)],
                               f"{u}wpq{p}_{k}", eng="gpsimd")
                        dma_in(ctx, wpv2[p][k][:, :],
                               W[wv][k][:, 128 * pr:128 * (pr + 1)],
                               f"{u}wpv{p}_{k}", eng="gpsimd")

            load_pair(0)
            for pr in range(8):
                pp = pr % 2
                wpk, wpq, wpv = wpk2[pp], wpq2[pp], wpv2[pp]
                if pr + 1 < 8:
                    load_pair(pr + 1)
                for h in range(2):
                    vop(ctx, lambda v, h=h: v.memset(kx[h][64:65, :], 1.0),
                        [], [f"{u}kx{h}"])
                for tg in range(NTC):
                    t0 = tg * TC
                    if pre_proj is not None:
                        def pmm(bank, wg, wres, pr=pr, t0=t0):
                            mm(ctx, [(ctx.pb[bank][:, :TC], wg[0][:, :],
                                      xnb[pr][:, t0:t0 + TC], True, True)],
                               [wres + "_0", xnres[pr]], [f"psum{bank}"])
                    else:
                        def pmm(bank, wg, wres, t0=t0):
                            kacc(ctx, ctx.pb[bank][:, :TC],
                                 [wg[k][:, :] for k in range(NDC)],
                                 [xnb[k][:, t0:t0 + TC] for k in range(NDC)],
                                 [f"{wres}_{k}" for k in range(NDC)] + xnres,
                                 [f"psum{bank}"])
                    def evac_heads(bank, dsts, dres, drows, t0=t0):
                        if not rope:
                            for h in range(2):
                                vop(ctx, lambda v, h=h: v.tensor_copy(
                                    dsts[h][0:drows, t0:t0 + TC],
                                    ctx.pb[bank][64 * h:64 * h + 64, :TC]),
                                    [f"psum{bank}"], [dres[h]])
                            return
                        vop(ctx, lambda v: v.tensor_copy(
                            praw[:, :], ctx.pb[bank][:, :TC]),
                            [f"psum{bank}"], [u + "praw"])
                        mm(ctx, [(ctx.pb[4][:, :TC], psw[:, :], praw[:, :],
                                  True, True)],
                           [u + "psw", u + "praw"], ["psum4"])
                        chain(ctx, [
                            lambda v: v.tensor_mul(rt1[:, :], praw[:, :],
                                                   cB[:, t0:t0 + TC]),
                            lambda v: v.tensor_mul(rt2[:, :], ctx.pb[4][:, :TC],
                                                   sB[:, t0:t0 + TC]),
                        ], [u + "praw", u + "cB", u + "sB", "psum4"],
                            [u + "rt"])
                        for h in range(2):
                            vop(ctx, lambda v, h=h: v.tensor_add(
                                dsts[h][0:drows, t0:t0 + TC],
                                rt1[64 * h:64 * h + 64, :],
                                rt2[64 * h:64 * h + 64, :]),
                                [u + "rt"], [dres[h]])

                    pmm(0, wpk, f"{u}wpk{pp}")
                    evac_heads(0, kx, [f"{u}kx{h}" for h in range(2)], 64)
                    pmm(1, wpq, f"{u}wpq{pp}")
                    evac_heads(1, qh, [f"{u}qh{h}" for h in range(2)], 64)
                    pmm(2, wpv, f"{u}wpv{pp}")
                    vop(ctx, lambda v: v.tensor_copy(vT[:, :TC], ctx.pb[2][:, :TC]),
                        ["psum2"], [u + "vT"])
                    for j in range(4):
                        kk = 4 * tg + j
                        transpose(ctx, ctx.pb[4][:, 0:128].bitcast(BF16)[:, 0:128],
                                  vT[:, 128 * j:128 * (j + 1)], identb[:, :],
                                  [u + "vT", u + "idb"], ["psum4"])
                        chain(ctx, [
                            lambda v, kk=kk: v.tensor_copy(
                                vt3[:, kk, 0:64], ctx.pb[4][:, 0:128].bitcast(BF16)[:, 0:64]),
                            lambda v, kk=kk: v.tensor_copy(
                                vt3[:, kk, 128:192], ctx.pb[4][:, 0:128].bitcast(BF16)[:, 64:128]),
                        ], ["psum4"], [u + "vtm"])

                def p1(h, qg):
                    hq = qh[h][0:128, :]
                    hk = kx[h][0:128, :]
                    for qt in range(4 * qg, 4 * qg + 4):
                        nkc = (qt // 4 + 1) if causal else NQG
                        for c in range(nkc):
                            klen = min(TC, 128 * (qt + 1) - TC * c) if causal else TC
                            bank = c % 2
                            mm(ctx, [(ctx.pb[bank][:, :klen],
                                      hq[:, 128 * qt:128 * (qt + 1)],
                                      hk[:, TC * c:TC * c + klen], True, True)],
                               [f"{u}qh{h}", f"{u}kx{h}"], [f"psum{bank}"])
                            if causal and c == nkc - 1:
                                j = qt - 4 * c
                                vop(ctx, lambda v, bank=bank, j=j, klen=klen:
                                    v.tensor_tensor(ctx.pb[bank][:, :klen],
                                                    ctx.pb[bank][:, :klen],
                                                    maskQ[:, TC * j:TC * j + klen],
                                                    ALU.add),
                                    [u + "mq", f"psum{bank}"], [f"psum{bank}"])
                            if c == 0:
                                vop(ctx, lambda v, bank=bank, klen=klen: v.tensor_reduce(
                                    mneg[:, :], ctx.pb[bank][:, :klen], axis=AX,
                                    op=ALU.max, negate=True),
                                    [f"psum{bank}"], [u + "mn"])
                            else:
                                chain(ctx, [
                                    lambda v, bank=bank, klen=klen: v.tensor_reduce(
                                        ln.tmp[:, 0:1], ctx.pb[bank][:, :klen], axis=AX,
                                        op=ALU.max, negate=True),
                                    lambda v: v.tensor_tensor(
                                        mneg[:, :], mneg[:, :], ln.tmp[:, 0:1], ALU.min),
                                ], [f"psum{bank}", u + "mn"], [u + "mn", u + "ln_tmpres"])
                        vop(ctx, lambda v, qt=qt, h=h: v.tensor_copy(
                            mcol[h][:, 32 * (qt % 4):32 * (qt % 4) + 1], mneg[:, :]),
                            [u + "mn"], [f"{u}mc{h}"])
                    transpose(ctx, ctx.pb[4][0:97, 0:128], mcol[h][:, :],
                              identf[:, :], [f"{u}mc{h}", u + "idf"], ["psum4"])
                    for j in range(4):
                        vop(ctx, lambda v, h=h, qg=qg, j=j: v.tensor_copy(
                            mrow[64 * h:64 * h + 1,
                                 TC * qg + 128 * j:TC * qg + 128 * (j + 1)],
                            ctx.pb[4][32 * j:32 * j + 1, 0:128]),
                            ["psum4"], [f"{u}mr{h}_{qg}"])

                def p2(h, qg):
                    obank = 5
                    chain(ctx, [
                        lambda v, h=h, qg=qg: v.tensor_copy(
                            qx[0:64, :], qh[h][0:64, TC * qg:TC * (qg + 1)]),
                        lambda v, h=h, qg=qg: v.tensor_copy(
                            qx[64:65, :],
                            mrow[64 * h:64 * h + 1, TC * qg:TC * (qg + 1)]),
                    ], [f"{u}qh{h}", f"{u}mr{h}_{qg}"], [u + "qx"])
                    nkc = 4 * (qg + 1) if causal else NQT
                    def emit_v(kc):
                        mm(ctx, [(ctx.pb[obank][0:128, :TC],
                                  vt3[:, kc, 128 * h:128 * (h + 1)],
                                  expt[kc % 3][:, :TC],
                                  kc == 0, kc == nkc - 1)],
                           [u + "vtm", f"{u}expt{kc % 3}"], [f"psum{obank}"])
                    for kc in range(nkc):
                        bank = 2 + kc % 2
                        mm(ctx, [(ctx.pb[bank][:, :TC],
                                  kx[h][:, 128 * kc:128 * (kc + 1)],
                                  qx[:, :], True, True)],
                           [f"{u}kx{h}", u + "qx"], [f"psum{bank}"])
                        j = kc - 4 * qg
                        if causal and j >= 0:
                            vop(ctx, lambda v, bank=bank, j=j: v.tensor_tensor(
                                ctx.pb[bank][:, :TC], ctx.pb[bank][:, :TC],
                                maskT[:, TC * j:TC * (j + 1)], ALU.add),
                                [u + "mt", f"psum{bank}"], [f"psum{bank}"])
                        vop(ctx, lambda s, bank=bank, kc=kc: s.activation(
                            expt[kc % 3][:, :TC], ctx.pb[bank][:, :TC],
                            AF.Exp, scale=0.125),
                            [f"psum{bank}"], [f"{u}expt{kc % 3}"], eng="scalar")
                        if kc >= 2:
                            emit_v(kc - 2)
                    for kc in range(max(nkc - 2, 0), nkc):
                        emit_v(kc)
                    chain(ctx, [
                        lambda v, obank=obank: v.reciprocal(
                            recb[0:1, :TC], ctx.pb[obank][64:65, :TC]),
                        lambda v: v.tensor_copy(rec[0:1, :TC], recb[0:1, :TC]),
                    ], [f"psum{obank}"], [u + "rec"])
                    mm(ctx, [(ctx.pb[6][0:64, :TC], ln.ones1_bf[0:1, 0:64],
                              rec[0:1, :TC], True, True)],
                       [u + "rec", u + "ln_ones1"], ["psum6"])
                    vop(ctx, lambda v: v.tensor_copy(recb[:, :TC],
                                                     ctx.pb[6][0:64, :TC]),
                        ["psum6"], [u + "recb"])
                    vop(ctx, lambda v, h=h, qg=qg, obank=obank: v.tensor_mul(
                        aoc[64 * h:64 * h + 64, TC * qg:TC * (qg + 1)],
                        ctx.pb[obank][0:64, :TC], recb[:, :TC]),
                        [f"psum{obank}", u + "recb"], [u + "aoc"])

                p1(0, 0)
                p1(1, 0)
                for qg in range(NQG):
                    if qg + 1 < NQG:
                        p1(0, qg + 1)
                        p1(1, qg + 1)
                    p2(0, qg)
                    p2(1, qg)

                if wo is None:
                    for tcx in range(NTC):
                        t0 = tcx * TC
                        dma_in(ctx, xr[:, :], xin[pr][:, t0:t0 + TC], u + "xr")
                        vop(ctx, lambda v, t0=t0: v.tensor_add(
                            yv[:, :], xr[:, :], aoc[:, t0:t0 + TC]),
                            [u + "xr", u + "aoc"], [u + "yv"])
                        dma_out(ctx, xout[pr][:, t0:t0 + TC], yv[:, :],
                                [u + "yv"], u + "xo")
                else:
                    for tcx in range(NTC):
                        t0 = tcx * TC
                        dma_out(ctx, ao_scr[pr][:, t0:t0 + TC], aoc[:, t0:t0 + TC],
                                [u + "aoc"], u + "aos")

        if wo is not None:
            with scope(ctx) as sb3:
                wot = [sb3(f"{u}wo{k}", [128, 1024], BF16) for k in range(NDC)]
                aot = [sb3(f"{u}aot{k}", [128, TC], BF16) for k in range(NDC)]
                xr2 = sb3(u + "xr2", [128, TC], F32)
                yv2 = sb3(u + "yv2", [128, TC], F32)
                for k in range(NDC):
                    dma_in(ctx, wot[k][:, :], W[wo][k], f"{u}wo{k}", eng="gpsimd")
                if wo_bias is not None:
                    wob = sb3(u + "wob", [128, 8], F32)
                    dma_in(ctx, wob[:, :], W[wo_bias], u + "wob")
                for tcx in range(NTC):
                    t0 = tcx * TC
                    for k in range(NDC):
                        dma_in(ctx, aot[k][:, :], ao_scr[k][:, t0:t0 + TC],
                               f"{u}aot{k}")
                    for oc in range(NDC):
                        bank = oc % 4
                        kacc(ctx, ctx.pb[bank][:, :TC],
                             [wot[k][:, 128 * oc:128 * (oc + 1)] for k in range(NDC)],
                             [aot[k][:, :] for k in range(NDC)],
                             [f"{u}wo{k}" for k in range(NDC)] +
                             [f"{u}aot{k}" for k in range(NDC)],
                             [f"psum{bank}"])
                        dma_in(ctx, xr2[:, :], xin[oc][:, t0:t0 + TC], u + "xr2")
                        if wo_bias is not None:
                            chain(ctx, [
                                lambda v, bank=bank, oc=oc: v.scalar_tensor_tensor(
                                    yv2[:, :], ctx.pb[bank][:, :TC],
                                    wob[:, oc:oc + 1], xr2[:, :], ALU.add, ALU.add),
                            ], [f"psum{bank}", u + "xr2", u + "wob"], [u + "yv2"])
                        else:
                            vop(ctx, lambda v, bank=bank: v.tensor_add(
                                yv2[:, :], ctx.pb[bank][:, :TC], xr2[:, :]),
                                [f"psum{bank}", u + "xr2"], [u + "yv2"])
                        dma_out(ctx, xout[oc][:, t0:t0 + TC], yv2[:, :],
                                [u + "yv2"], u + "xo2")





def emit_l0_attn(ctx, xin, xout, W):
    nc = ctx.nc
    u = f"0a{ctx.uid()}_"
    DP = 1025
    with scope(ctx) as sb:
        ln = LNTmp(ctx, sb, u)
        identb = sb(u + "idb", [128, 128], BF16)
        maskA = sb(u + "mA", [128, 4 * TC], BF16)
        wq = [sb(f"{u}wq{k}", [128, 3072], BF16) for k in range(NDC)]
        wg = [sb(f"{u}wg{k}", [128, 1024], BF16) for k in range(NDC)]
        wp = [sb(f"{u}wp{k}", [128, 1024], BF16) for k in range(NDC)]
        bqkv = sb(u + "bqkv", [128, 24], F32)
        bg = sb(u + "bg", [128, 8], F32)
        bp = sb(u + "bp", [128, 8], F32)
        lg = sb(u + "lg", [128, 8], F32)
        lb = sb(u + "lb", [128, 8], F32)
        Sb = [sb(f"{u}Sb{k}", [128, DP], BF16) for k in range(NDC)]
        xs = [sb(f"{u}x{i}", [128, TC], F32) for i in range(NDC)]
        xn0 = [sb(f"{u}xn0{i}", [128, TC], BF16) for i in range(NDC)]
        xnin = [sb(f"{u}xi{i}", [128, TC], BF16) for i in range(NDC)]
        qT = [sb(f"{u}qT{i}", [128, TC], BF16) for i in range(NDC)]
        kT = [sb(f"{u}kT{i}", [128, TC], BF16) for i in range(NDC)]
        vT = [sb(f"{u}vT{i}", [128, TC], BF16) for i in range(NDC)]
        gatet = sb(u + "gatet", [128, TC], BF16)
        tmpa = sb(u + "tmpa", [128, TC], F32)
        tmpb = sb(u + "tmpb", [128, TC], BF16)
        tmpc = sb(u + "tmpc", [128, TC], BF16)
        vtm = sb(u + "vtm", [128, 4 * 1026], BF16)
        ktm = sb(u + "ktm", [128, 4 * 1024], BF16)
        aT = [sb(f"{u}aT{j}", [128, TC], BF16) for j in range(4)]
        oT = vT
        recb = sb(u + "recb", [128, TC], BF16)
        recr = sb(u + "recr", [1, TC], BF16)
        yv = sb(u + "yv", [128, TC], F32)

        dma_in(ctx, identb[:, :], W["ident_bf"], u + "idb")
        dma_in(ctx, maskA[:, :], W["maskA"], u + "mA")
        for k in range(NDC):
            dma_in(ctx, wq[k][:, :], W["a0_wqkv"][k], f"{u}wq{k}", eng="gpsimd")
            dma_in(ctx, wg[k][:, :], W["a0_wg"][k], f"{u}wg{k}", eng="gpsimd")
            dma_in(ctx, wp[k][:, :], W["a0_wp"][k], f"{u}wp{k}", eng="gpsimd")
        for nm, tile, src in (("bqkv", bqkv, "a0_bqkv"), ("bg", bg, "a0_bg"),
                              ("bp", bp, "a0_bp"), ("lg", lg, "a0_lg"),
                              ("lb", lb, "a0_lb")):
            dma_in(ctx, tile[:, :], W[src], u + nm)
        for k in range(NDC):
            vop(ctx, lambda v, k=k: v.memset(Sb[k][:, :], 0.0), [], [f"{u}Sb{k}"])
        vt4 = vtm[:, :].rearrange("p (j c) -> p j c", c=1026)
        vop(ctx, lambda v: v.memset(vt4[:, :, 1024:1025], 1.0), [], [u + "vtm"])

        xres = [f"{u}x{i}" for i in range(NDC)]
        n0res = [f"{u}xn0{i}" for i in range(NDC)]
        xires = [f"{u}xi{i}" for i in range(NDC)]

        for g in range(NTC):
            t0 = g * TC
            for i in range(NDC):
                dma_in(ctx, xs[i][:, :], xin[i][:, t0:t0 + TC], xres[i])
            emit_ln(ctx, ln, [xs[i][:, :] for i in range(NDC)], xres,
                    None, [r + "_d" for r in n0res],
                    out_f32=[xn0[i][:, :] for i in range(NDC)], out_f32_resl=n0res)
            emit_ln(ctx, ln, [xn0[i][:, :] for i in range(NDC)], n0res,
                    [xnin[i][:, :] for i in range(NDC)], xires,
                    affine=([lg[:, i:i + 1] for i in range(NDC)],
                            [lb[:, i:i + 1] for i in range(NDC)]),
                    affine_res=[u + "lg", u + "lb"])
            for i in range(NDC):
                kacc(ctx, ctx.pb[0][:, :TC],
                     [wg[k][:, 128 * i:128 * (i + 1)] for k in range(NDC)],
                     [xnin[k][:, :] for k in range(NDC)],
                     [f"{u}wg{k}" for k in range(NDC)] + xires, ["psum0"])
                vop(ctx, lambda sc, i=i: sc.activation(
                    gatet[:, :], ctx.pb[0][:, :TC], AF.Sigmoid,
                    bias=bg[:, i:i + 1]),
                    ["psum0", u + "bg"], [u + "gatet"], eng="scalar")
                kacc(ctx, ctx.pb[1][:, :TC],
                     [wq[k][:, 128 * i:128 * (i + 1)] for k in range(NDC)],
                     [xnin[k][:, :] for k in range(NDC)],
                     [f"{u}wq{k}" for k in range(NDC)] + xires, ["psum1"])
                zb = ln.eps[0:1, 0:1]
                chain(ctx, [
                    lambda v, i=i: v.tensor_scalar(tmpa[:, :], ctx.pb[1][:, :TC],
                                                   bqkv[:, i:i + 1], 0.0, ALU.add,
                                                   ALU.min),
                    lambda v, i=i: v.tensor_scalar_min(tmpa[:, :], tmpa[:, :], 0.0),
                ], ["psum1", u + "bqkv"], [u + "tmpa"])
                vop(ctx, lambda sc: sc.activation(tmpb[:, :], tmpa[:, :], AF.Exp),
                    [u + "tmpa"], [u + "tmpb"], eng="scalar")
                chain(ctx, [
                    lambda v, i=i: v.tensor_scalar(tmpc[:, :], ctx.pb[1][:, :TC],
                                                   bqkv[:, i:i + 1], 0.0, ALU.add,
                                                   ALU.max),
                    lambda v, i=i: v.tensor_scalar_max(tmpc[:, :], tmpc[:, :], 0.0),
                    lambda v, i=i: v.tensor_add(qT[i][:, :], tmpb[:, :], tmpc[:, :]),
                ], ["psum1", u + "bqkv", u + "tmpb"], [f"{u}qT{i}", u + "tmpc"])
                kacc(ctx, ctx.pb[2][:, :TC],
                     [wq[k][:, 1024 + 128 * i:1024 + 128 * (i + 1)] for k in range(NDC)],
                     [xnin[k][:, :] for k in range(NDC)],
                     [f"{u}wq{k}" for k in range(NDC)] + xires, ["psum2"])
                chain(ctx, [
                    lambda v, i=i: v.scalar_tensor_tensor(
                        tmpa[:, :], ctx.pb[2][:, :TC], bqkv[:, 8 + i:9 + i],
                        gatet[:, :], ALU.add, ALU.mult),
                    lambda v: v.tensor_scalar_min(tmpb[:, :], tmpa[:, :], 0.0),
                ], ["psum2", u + "bqkv", u + "gatet"], [u + "tmpa", u + "tmpb"])
                vop(ctx, lambda sc: sc.activation(tmpb[:, :], tmpb[:, :], AF.Exp),
                    [u + "tmpb"], [u + "tmpb"], eng="scalar")
                chain(ctx, [
                    lambda v: v.tensor_scalar_max(tmpc[:, :], tmpa[:, :], 0.0),
                    lambda v, i=i: v.tensor_add(kT[i][:, :], tmpb[:, :], tmpc[:, :]),
                ], [u + "tmpa", u + "tmpb"], [f"{u}kT{i}", u + "tmpc"])
                kacc(ctx, ctx.pb[3][:, :TC],
                     [wq[k][:, 2048 + 128 * i:2048 + 128 * (i + 1)] for k in range(NDC)],
                     [xnin[k][:, :] for k in range(NDC)],
                     [f"{u}wq{k}" for k in range(NDC)] + xires, ["psum3"])
                vop(ctx, lambda v, i=i: v.tensor_scalar(
                    vT[i][:, :], ctx.pb[3][:, :TC], bqkv[:, 16 + i:17 + i], None,
                    ALU.add),
                    ["psum3", u + "bqkv"], [f"{u}vT{i}"])
            for j in range(4):
                for dc in range(NDC):
                    transpose(ctx, ctx.pb[4][:, 0:128].bitcast(BF16)[:, 0:128],
                              vT[dc][:, 128 * j:128 * (j + 1)], identb[:, :],
                              [f"{u}vT{dc}", u + "idb"], ["psum4"])
                    vop(ctx, lambda v, j=j, dc=dc: v.tensor_copy(
                        vt4[:, j, 128 * dc:128 * (dc + 1)],
                        ctx.pb[4][:, 0:128].bitcast(BF16)[:, 0:128]),
                        ["psum4"], [u + "vtm"])
                    transpose(ctx, ctx.pb[4][:, 0:128].bitcast(BF16)[:, 0:128],
                              kT[dc][:, 128 * j:128 * (j + 1)], identb[:, :],
                              [f"{u}kT{dc}", u + "idb"], ["psum4"])
                    vop(ctx, lambda v, j=j, dc=dc: v.tensor_copy(
                        ktm[:, 1024 * j + 128 * dc:1024 * j + 128 * (dc + 1)],
                        ctx.pb[4][:, 0:128].bitcast(BF16)[:, 0:128]),
                        ["psum4"], [u + "ktm"])
            for j in range(4):
                kacc(ctx, ctx.pb[3][:, :TC],
                     [kT[dc][:, 128 * j:128 * (j + 1)] for dc in range(NDC)],
                     [qT[dc][:, :] for dc in range(NDC)],
                     [f"{u}kT{dc}" for dc in range(NDC)] +
                     [f"{u}qT{dc}" for dc in range(NDC)], ["psum3"])
                vop(ctx, lambda v, j=j: v.tensor_mul(
                    aT[j][:, :], ctx.pb[3][:, :TC], maskA[:, TC * j:TC * (j + 1)]),
                    ["psum3", u + "mA"], [f"{u}aT{j}"])
            for m in range(NDC):
                bank = m % 2
                plist = []
                for j in range(4):
                    plist.append((ctx.pb[bank][:, :TC],
                                  vt4[:, j, 128 * m:128 * (m + 1)], aT[j][:, :],
                                  j == 0, False))
                for dc in range(NDC):
                    plist.append((ctx.pb[bank][:, :TC],
                                  Sb[dc][:, 128 * m:128 * (m + 1)], qT[dc][:, :],
                                  False, dc == NDC - 1))
                mm(ctx, plist,
                   [u + "vtm"] + [f"{u}aT{j}" for j in range(4)] +
                   [f"{u}Sb{dc}" for dc in range(NDC)] +
                   [f"{u}qT{dc}" for dc in range(NDC)], [f"psum{bank}"])
                if m == 0:
                    plist = []
                    for j in range(4):
                        plist.append((ctx.pb[2][0:1, :TC],
                                      vt4[:, j, 1024:1025], aT[j][:, :],
                                      j == 0, False))
                    for dc in range(NDC):
                        plist.append((ctx.pb[2][0:1, :TC],
                                      Sb[dc][:, 1024:1025], qT[dc][:, :],
                                      False, dc == NDC - 1))
                    mm(ctx, plist,
                       [u + "vtm"] + [f"{u}aT{j}" for j in range(4)] +
                       [f"{u}Sb{dc}" for dc in range(NDC)] +
                       [f"{u}qT{dc}" for dc in range(NDC)], ["psum2"])
                    chain(ctx, [
                        lambda v: v.tensor_scalar_add(recr[0:1, :].bitcast(F32)[0:1, :TC // 2] if False else tmpa[0:1, :], ctx.pb[2][0:1, :TC], 1e-6),
                        lambda v: v.reciprocal(tmpa[0:1, :], tmpa[0:1, :]),
                        lambda v: v.tensor_copy(recr[0:1, :], tmpa[0:1, :]),
                    ], ["psum2"], [u + "recr", u + "tmpa"])
                    mm(ctx, [(ctx.pb[2][:, :TC], ln.ones1_bf[0:1, :], recr[0:1, :],
                              True, True)],
                       [u + "recr", u + "ln_ones1"], ["psum2"])
                    vop(ctx, lambda v: v.tensor_copy(recb[:, :], ctx.pb[2][:, :TC]),
                        ["psum2"], [u + "recb"])
                vop(ctx, lambda v, m=m, bank=bank: v.tensor_mul(
                    oT[m][:, :], ctx.pb[bank][:, :TC], recb[:, :]),
                    [f"psum{bank}", u + "recb"], [f"{u}vT{m}"])
            for oc in range(NDC):
                bank = oc % 2
                kacc(ctx, ctx.pb[bank][:, :TC],
                     [wp[k][:, 128 * oc:128 * (oc + 1)] for k in range(NDC)],
                     [oT[k][:, :] for k in range(NDC)],
                     [f"{u}wp{k}" for k in range(NDC)] +
                     [f"{u}vT{k}" for k in range(NDC)], [f"psum{bank}"])
                vop(ctx, lambda v, oc=oc, bank=bank: v.scalar_tensor_tensor(
                    yv[:, :], ctx.pb[bank][:, :TC], bp[:, oc:oc + 1], xs[oc][:, :],
                    ALU.add, ALU.add),
                    [f"psum{bank}", u + "bp", xres[oc]], [u + "yv"])
                dma_out(ctx, xout[oc][:, t0:t0 + TC], yv[:, :], [u + "yv"], u + "xo")
            for dc in range(NDC):
                for (c0, c1, bank) in ((0, 512, 4), (512, 1024, 5), (1024, 1025, 6)):
                    plist = []
                    for j in range(4):
                        plist.append((ctx.pb[bank][:, 0:c1 - c0],
                                      ktm[:, 1024 * j + 128 * dc:1024 * j + 128 * (dc + 1)],
                                      vt4[:, j, c0:c1], j == 0, j == 3))
                    mm(ctx, plist, [u + "ktm", u + "vtm"], [f"psum{bank}"])
                chain(ctx, [
                    lambda v, dc=dc: v.tensor_add(Sb[dc][:, 0:512], Sb[dc][:, 0:512],
                                                  ctx.pb[4][:, :TC]),
                    lambda v, dc=dc: v.tensor_add(Sb[dc][:, 512:1024],
                                                  Sb[dc][:, 512:1024],
                                                  ctx.pb[5][:, :TC]),
                    lambda v, dc=dc: v.tensor_add(Sb[dc][:, 1024:1025],
                                                  Sb[dc][:, 1024:1025],
                                                  ctx.pb[6][:, 0:1]),
                ], ["psum4", "psum5", "psum6"], [f"{u}Sb{dc}"])



def emit_l0_ffn(ctx, xin, xout, W):
    nc = ctx.nc
    u = f"0f{ctx.uid()}_"
    HT = T_FULL // 2
    NHC = HT // TC
    with scope(ctx) as sb:
        xnbh = [sb(f"{u}xh{i}", [128, HT], BF16) for i in range(NDC)]
        xnres = [f"{u}xh{i}" for i in range(NDC)]
        yac = [sb(f"{u}ya{i}", [128, HT], BF16) for i in range(NDC)]
        cwf = sb(u + "cwf", [16, HT], BF16)
        ln = LNTmp(ctx, sb, u)
        identf = sb(u + "idf", [128, 128], F32)
        rwt = [sb(f"{u}rwt{k}", [128, 16], F32) for k in range(NDC)]
        e16 = sb(u + "e16", [16, 16 * 128], BF16)
        ebias = sb(u + "eb", [16, 1], F32)
        scf = sb(u + "scf", [16, TC], F32)
        sbf_ = sb(u + "sbf", [16, TC], F32)
        sc_tm = sb(u + "sctm", [128, 16], F32)
        sr_tm = sb(u + "srtm", [128, 16], F32)
        gs = sb(u + "gs", [128, 4], F32)
        gt4 = sb(u + "gt4", [128, 4], F32)
        gcnt4 = sb(u + "gcnt4", [128, 4], F32)
        mt = sb(u + "mt16", [128, 16], F32)
        t16 = sb(u + "t16", [128, 16], F32)
        c1 = sb(u + "c1", [128, 16], F32)
        c2 = sb(u + "c2", [128, 16], F32)
        gcol = sb(u + "gcol", [128, 1], F32)
        gwb = [[sb(f"{u}gwb{p}_{k}", [128, 512], BF16) for k in range(NDC)]
               for p in range(2)]
        uwb = [[sb(f"{u}uwb{p}_{k}", [128, 512], BF16) for k in range(NDC)]
               for p in range(2)]
        dwb = [[sb(f"{u}dwb{p}_{k}", [128, 1024], BF16) for k in range(4)]
               for p in range(2)]
        hsl = [sb(f"{u}hsl{k}", [128, TC], BF16) for k in range(8)]
        cwb = sb(u + "cwb", [128, HT], BF16)
        sil = sb(u + "sil", [128, TC], F32)
        dma_in(ctx, identf[:, :], W["ident_f32"], u + "idf")
        dma_in(ctx, e16[:, :], W["E16"], u + "e16")
        dma_in(ctx, ebias[:, :], W["f0_eb"], u + "eb")
        for k in range(NDC):
            dma_in(ctx, rwt[k][:, :], W["f0_rw"][k], f"{u}rwt{k}")

        for half in range(2):
            hb = half * HT
            with scope(ctx) as sbr:
                xs = [sbr(f"{u}x{i}", [128, TC], F32) for i in range(NDC)]
                xnf = [sbr(f"{u}xnf{i}", [128, TC], F32) for i in range(NDC)]
                for tcx in range(NHC):
                    t0 = hb + tcx * TC
                    th = tcx * TC
                    rxl = [f"{u}x{i}" for i in range(NDC)]
                    rnl = [f"{u}xnf{i}" for i in range(NDC)]
                    for i in range(NDC):
                        dma_in(ctx, xs[i][:, :], xin[i][:, t0:t0 + TC], rxl[i])
                    emit_ln(ctx, ln, [xs[i][:, :] for i in range(NDC)], rxl,
                            [xnbh[i][:, th:th + TC] for i in range(NDC)], xnres,
                            out_f32=[xnf[i][:, :] for i in range(NDC)],
                            out_f32_resl=rnl)
                    kacc(ctx, ctx.pb[0][0:16, :TC],
                         [rwt[k][:, :] for k in range(NDC)],
                         [xnf[k][:, :] for k in range(NDC)],
                         [f"{u}rwt{k}" for k in range(NDC)] + rnl, ["psum0"])
                    chain(ctx, [
                        lambda sc: sc.activation(scf[0:16, :], ctx.pb[0][0:16, :TC],
                                                 AF.Sigmoid),
                    ], ["psum0"], [u + "scf"], eng="scalar")
                    vop(ctx, lambda v: v.tensor_scalar(
                        sbf_[0:16, :], scf[0:16, :], ebias[0:16, 0:1], None, ALU.add),
                        [u + "scf", u + "eb"], [u + "sbf"])
                    for tt in range(TC // 128):
                        transpose(ctx, ctx.pb[4][0:128, 0:16],
                                  scf[0:16, 128 * tt:128 * (tt + 1)],
                                  identf[0:16, 0:16], [u + "scf", u + "idf"], ["psum4"])
                        vop(ctx, lambda v: v.tensor_copy(sr_tm[:, :],
                                                         ctx.pb[4][0:128, 0:16]),
                            ["psum4"], [u + "srtm"])
                        transpose(ctx, ctx.pb[4][0:128, 0:16],
                                  sbf_[0:16, 128 * tt:128 * (tt + 1)],
                                  identf[0:16, 0:16],
                                  [u + "sbf", u + "idf", u + "srtm"], ["psum4"])
                        vop(ctx, lambda v: v.tensor_copy(sc_tm[:, :],
                                                         ctx.pb[4][0:128, 0:16]),
                            ["psum4"], [u + "sctm"])
                        sc3 = sc_tm[:, :].rearrange("p (g j) -> p g j", j=4)
                        fns = []
                        first = [True]
                        for a in range(4):
                            for b in range(a + 1, 4):
                                if first[0]:
                                    fns.append(lambda v, a=a, b=b: v.tensor_add(
                                        gs[:, :], sc3[:, :, a], sc3[:, :, b]))
                                    first[0] = False
                                else:
                                    fns.append(lambda v, a=a, b=b: v.tensor_add(
                                        gt4[:, :], sc3[:, :, a], sc3[:, :, b]))
                                    fns.append(lambda v: v.tensor_max(
                                        gs[:, :], gs[:, :], gt4[:, :]))
                        for g in range(4):
                            fns.append(lambda v, g=g: v.tensor_scalar(
                                gt4[:, :], gs[:, :], gs[:, g:g + 1], None,
                                ALU.is_gt, ALU.add, accum_out=gcnt4[:, g:g + 1]))
                        fns.append(lambda v: v.tensor_scalar(
                            gcnt4[:, :], gcnt4[:, :], 2.0, None, ALU.is_lt))
                        fns.append(lambda v: v.tensor_copy(
                            mt[:, :].rearrange("p (g j) -> p g j", j=4),
                            gcnt4[:, :].unsqueeze(2).broadcast_to([128, 4, 4])))
                        fns.append(lambda v: v.tensor_mul(mt[:, :], mt[:, :],
                                                          sc_tm[:, :]))
                        for e in range(16):
                            fns.append(lambda v, e=e: v.tensor_scalar(
                                t16[:, :], mt[:, :], mt[:, e:e + 1], None,
                                ALU.is_gt, ALU.add, accum_out=c1[:, e:e + 1]))
                        fns.append(lambda v: v.memset(c2[:, 0:1], 0.0))
                        for e in range(1, 16):
                            fns.append(lambda v, e=e: v.tensor_scalar(
                                t16[:, 0:e], mt[:, 0:e], mt[:, e:e + 1], None,
                                ALU.is_equal, ALU.add, accum_out=c2[:, e:e + 1]))
                        fns.append(lambda v: v.tensor_add(c1[:, :], c1[:, :],
                                                          c2[:, :]))
                        fns.append(lambda v: v.tensor_scalar(
                            c1[:, :], c1[:, :], 4.0, None, ALU.is_lt))
                        fns.append(lambda v: v.tensor_mul(t16[:, :], sr_tm[:, :],
                                                          c1[:, :]))
                        fns.append(lambda v: v.reduce_sum(gcol[:, 0:1], t16[:, :],
                                                          axis=AX))
                        fns.append(lambda v: v.tensor_scalar_add(gcol[:, 0:1],
                                                                 gcol[:, 0:1], 1e-20))
                        fns.append(lambda v: v.reciprocal(gcol[:, :], gcol[:, :]))
                        fns.append(lambda v: v.tensor_scalar_mul(gcol[:, 0:1],
                                                                 gcol[:, 0:1], 2.5))
                        fns.append(lambda v: v.tensor_scalar(
                            t16[:, :], t16[:, :], gcol[:, 0:1], None, ALU.mult))
                        chain(ctx, fns, [u + "sctm", u + "srtm"],
                              [u + "t16", u + "gcol", u + "c1", u + "c2",
                               u + "gs", u + "gt4", u + "gcnt4", u + "mt16"])
                        transpose(ctx, ctx.pb[4][0:16, 0:128], t16[:, :],
                                  identf[:, :], [u + "t16", u + "idf"], ["psum4"])
                        vop(ctx, lambda v, tcx=tcx, tt=tt: v.tensor_copy(
                            cwf[0:16, tcx * TC + 128 * tt:tcx * TC + 128 * (tt + 1)],
                            ctx.pb[4][0:16, 0:128]),
                            ["psum4"], [u + "cwf"])

            for i in range(NDC):
                vop(ctx, lambda v, i=i: v.memset(yac[i][:, :], 0.0), [],
                    [f"{u}ya{i}"])

            def load_routed(e):
                p = e % 2
                for k in range(NDC):
                    dma_in(ctx, gwb[p][k][:, :], W["f0_gw"][e][k], f"{u}gwb{p}_{k}",
                           eng="gpsimd")
                    dma_in(ctx, uwb[p][k][:, :], W["f0_uw"][e][k], f"{u}uwb{p}_{k}",
                           eng="gpsimd")
                for k in range(4):
                    dma_in(ctx, dwb[p][k][:, :], W["f0_dw"][e][k], f"{u}dwb{p}_{k}",
                           eng="gpsimd")

            def run_expert(p, nhc, gtiles, utiles, dtiles, gres, ures, dres,
                           use_cw, e):
                if use_cw:
                    for tcx in range(NHC):
                        mm(ctx, [(ctx.pb[4][:, :TC], e16[:, 128 * e:128 * (e + 1)],
                                  cwf[0:16, tcx * TC:(tcx + 1) * TC], True, True)],
                           [u + "e16", u + "cwf"], ["psum4"])
                        vop(ctx, lambda v, tcx=tcx: v.tensor_copy(
                            cwb[:, tcx * TC:(tcx + 1) * TC], ctx.pb[4][:, :TC]),
                            ["psum4"], [u + "cwb"])
                for tcx in range(NHC):
                    th = tcx * TC
                    for o4 in range(nhc):
                        kacc(ctx, ctx.pb[0][:, :TC],
                             [gtiles[o4][k] for k in range(NDC)],
                             [xnbh[k][:, th:th + TC] for k in range(NDC)],
                             gres + xnres, ["psum0"])
                        kacc(ctx, ctx.pb[1][:, :TC],
                             [utiles[o4][k] for k in range(NDC)],
                             [xnbh[k][:, th:th + TC] for k in range(NDC)],
                             ures + xnres, ["psum1"])
                        vop(ctx, lambda sc: sc.activation(
                            sil[:, :], ctx.pb[0][:, :TC], AF.Silu),
                            ["psum0"], [u + "sil"], eng="scalar")
                        if use_cw:
                            chain(ctx, [
                                lambda v, o4=o4: v.tensor_mul(
                                    hsl[o4][:, :], sil[:, :], ctx.pb[1][:, :TC]),
                                lambda v, o4=o4, th=th: v.tensor_mul(
                                    hsl[o4][:, :], hsl[o4][:, :],
                                    cwb[:, th:th + TC]),
                            ], [u + "sil", "psum1", u + "cwb"], [f"{u}hsl{o4}"])
                        else:
                            vop(ctx, lambda v, o4=o4: v.tensor_mul(
                                hsl[o4][:, :], sil[:, :], ctx.pb[1][:, :TC]),
                                [u + "sil", "psum1"], [f"{u}hsl{o4}"])
                    for oc in range(NDC):
                        bank = 2 + oc % 2
                        kacc(ctx, ctx.pb[bank][:, :TC],
                             [dtiles[k][:, 128 * oc:128 * (oc + 1)]
                              for k in range(nhc)],
                             [hsl[k][:, :] for k in range(nhc)],
                             dres + [f"{u}hsl{k}" for k in range(nhc)],
                             [f"psum{bank}"])
                        vop(ctx, lambda v, oc=oc, th=th, bank=bank: v.tensor_add(
                            yac[oc][:, th:th + TC], yac[oc][:, th:th + TC],
                            ctx.pb[bank][:, :TC]),
                            [f"psum{bank}", f"{u}ya{oc}"], [f"{u}ya{oc}"])

            load_routed(0)
            for e in range(16):
                p = e % 2
                if e + 1 < 16:
                    load_routed(e + 1)
                gtiles = [[gwb[p][k][:, 128 * o4:128 * (o4 + 1)] for k in range(NDC)]
                          for o4 in range(4)]
                utiles = [[uwb[p][k][:, 128 * o4:128 * (o4 + 1)] for k in range(NDC)]
                          for o4 in range(4)]
                run_expert(p, 4, gtiles, utiles, dwb[p],
                           [f"{u}gwb{p}_{k}" for k in range(NDC)],
                           [f"{u}uwb{p}_{k}" for k in range(NDC)],
                           [f"{u}dwb{p}_{k}" for k in range(4)], True, e)
            for k in range(NDC):
                dma_in(ctx, gwb[0][k][:, :], W["f0_sg"][k][:, 0:512],
                       f"{u}gwb0_{k}", eng="gpsimd")
                dma_in(ctx, gwb[1][k][:, :], W["f0_sg"][k][:, 512:1024],
                       f"{u}gwb1_{k}", eng="gpsimd")
                dma_in(ctx, uwb[0][k][:, :], W["f0_su"][k][:, 0:512],
                       f"{u}uwb0_{k}", eng="gpsimd")
                dma_in(ctx, uwb[1][k][:, :], W["f0_su"][k][:, 512:1024],
                       f"{u}uwb1_{k}", eng="gpsimd")
            for k in range(4):
                dma_in(ctx, dwb[0][k][:, :], W["f0_sd"][k], f"{u}dwb0_{k}",
                       eng="gpsimd")
                dma_in(ctx, dwb[1][k][:, :], W["f0_sd"][4 + k], f"{u}dwb1_{k}",
                       eng="gpsimd")
            gtiles = [[gwb[o8 // 4][k][:, 128 * (o8 % 4):128 * (o8 % 4 + 1)]
                       for k in range(NDC)] for o8 in range(8)]
            utiles = [[uwb[o8 // 4][k][:, 128 * (o8 % 4):128 * (o8 % 4 + 1)]
                       for k in range(NDC)] for o8 in range(8)]
            dtiles = dwb[0] + dwb[1]
            run_expert(0, 8, gtiles, utiles, dtiles,
                       [f"{u}gwb{pp}_{k}" for pp in range(2) for k in range(NDC)],
                       [f"{u}uwb{pp}_{k}" for pp in range(2) for k in range(NDC)],
                       [f"{u}dwb{pp}_{k}" for pp in range(2) for k in range(4)],
                       False, 0)

            with scope(ctx) as sbf:
                xr = sbf(u + "fxr", [128, TC], F32)
                yv = sbf(u + "fyv", [128, TC], F32)
                for tcx in range(NHC):
                    t0 = hb + tcx * TC
                    th = tcx * TC
                    for i in range(NDC):
                        dma_in(ctx, xr[:, :], xin[i][:, t0:t0 + TC], u + "fxr")
                        vop(ctx, lambda v, i=i, th=th, xr=xr, yv=yv: v.tensor_add(
                            yv[:, :], xr[:, :], yac[i][:, th:th + TC]),
                            [u + "fxr", f"{u}ya{i}"], [u + "fyv"])
                        dma_out(ctx, xout[i][:, t0:t0 + TC], yv[:, :],
                                [u + "fyv"], u + "fxo")



def emit_l1_ffn(ctx, xin, xout, W):
    nc = ctx.nc
    u = f"1f{ctx.uid()}_"
    QT = 1024
    NQ = T_FULL // QT
    NQC = QT // TC
    RMS_EPS = 1.1920929e-07
    with scope(ctx) as sb:
        xrm = [sb(f"{u}xr{i}", [128, QT], BF16) for i in range(NDC)]
        xrmres = [f"{u}xr{i}" for i in range(NDC)]
        yac = [sb(f"{u}ya{i}", [128, QT], BF16) for i in range(NDC)]
        cwf = sb(u + "cwf", [14, QT], BF16)
        ln = LNTmp(ctx, sb, u)
        identf = sb(u + "idf", [128, 128], F32)
        rwt = [sb(f"{u}rwt{k}", [128, 14], F32) for k in range(NDC)]
        e14 = sb(u + "e14", [14, 14 * 128], BF16)
        gt = sb(u + "gt", [128, 16], F32)
        gtmp = sb(u + "gtmp", [128, 16], F32)
        gcnt = sb(u + "gcnt", [128, 16], F32)
        gcol = sb(u + "gcol", [128, 1], F32)
        gcol2 = sb(u + "gcol2", [128, 1], F32)
        lgt = sb(u + "lgt", [14, TC], F32)
        xs = [sb(f"{u}x{i}", [128, TC], F32) for i in range(NDC)]
        xnf = [sb(f"{u}xnf{i}", [128, TC], F32) for i in range(NDC)]
        sq = sb(u + "sq", [128, TC], BF16)
        r2r = sb(u + "r2r", [1, TC], F32)
        r2b = sb(u + "r2b", [1, TC], BF16)
        r2bc = sb(u + "r2bc", [128, TC], BF16)
        epst = sb(u + "epst", [1, 1], F32)
        w1t = [[sb(f"{u}w1t{p}_{k}", [128, TC], BF16) for k in range(NDC)]
               for p in range(2)]
        w2t = [[sb(f"{u}w2t{p}_{k}", [128, 1024], BF16) for k in range(8)]
               for p in range(2)]
        rwe = [sb(f"{u}rwe{p}", [128, 8], F32) for p in range(2)]
        hsl = [sb(f"{u}hsl{k}", [128, QT], BF16) for k in range(16)]
        cwb = sb(u + "cwb", [128, QT], BF16)
        sil = sb(u + "sil", [128, TC], F32)
        fxr = sb(u + "fxr", [128, TC], F32)
        fyv = sb(u + "fyv", [128, TC], F32)

        dma_in(ctx, identf[:, :], W["ident_f32"], u + "idf")
        dma_in(ctx, e14[:, :], W["E14"], u + "e14")
        vop(ctx, lambda v: v.memset(epst[:, :], RMS_EPS), [], [u + "epst"])
        for k in range(NDC):
            dma_in(ctx, rwt[k][:, :], W["f1_rw"][k], f"{u}rwt{k}")

        def wsrc(e):
            if e < 14:
                return W["f1_w1"][e], W["f1_w2"][e], W["f1_rms"][e]
            return W["f1_w1_s"][e - 14], W["f1_w2_s"][e - 14], W["f1_rms_s"][e - 14]

        sli = [0]

        def load_w1_slice(e, hc):
            p = sli[0] % 2
            sli[0] += 1
            w1d, _, _ = wsrc(e)
            for k in range(NDC):
                dma_in(ctx, w1t[p][k][:, :], w1d[k][:, TC * hc:TC * (hc + 1)],
                       f"{u}w1t{p}_{k}", eng="gpsimd")
                vop(ctx, lambda v, k=k, p=p, e=e: v.tensor_scalar(
                    w1t[p][k][:, :], w1t[p][k][:, :], rwe[e % 2][:, k:k + 1], None,
                    ALU.mult),
                    [f"{u}w1t{p}_{k}", f"{u}rwe{e % 2}"], [f"{u}w1t{p}_{k}"])
            return p

        w2i = [0]

        def load_w2_half(e, half):
            p = w2i[0] % 2
            w2i[0] += 1
            _, w2d, _ = wsrc(e)
            for k in range(8):
                dma_in(ctx, w2t[p][k][:, :], w2d[8 * half + k], f"{u}w2t{p}_{k}",
                       eng="gpsimd")
            return p

        def load_rms(e):
            _, _, rmsd = wsrc(e)
            dma_in(ctx, rwe[e % 2][:, :], rmsd, f"{u}rwe{e % 2}")

        for q in range(NQ):
            qb = q * QT
            for tcx in range(NQC):
                t0 = qb + tcx * TC
                th = tcx * TC
                rxl = [f"{u}x{i}" for i in range(NDC)]
                rnl = [f"{u}xnf{i}" for i in range(NDC)]
                for i in range(NDC):
                    dma_in(ctx, xs[i][:, :], xin[i][:, t0:t0 + TC], rxl[i])
                emit_ln(ctx, ln, [xs[i][:, :] for i in range(NDC)], rxl,
                        None, rxl, out_f32=[xnf[i][:, :] for i in range(NDC)],
                        out_f32_resl=rnl)
                for i in range(NDC):
                    vop(ctx, lambda sc, i=i: sc.activation(
                        sq[:, :], xnf[i][:, :], AF.Square),
                        [rnl[i]], [u + "sq"], eng="scalar")
                    mm(ctx, [(ctx.pb[6][0:1, :TC], ln.ones_bf[:, :], sq[:, :],
                              i == 0, i == NDC - 1)],
                       [u + "sq", u + "ln_ones"], ["psum6"])
                chain(ctx, [
                    lambda v: v.tensor_scalar_mul(r2r[0:1, :], ctx.pb[6][0:1, :TC],
                                                  1.0 / D),
                ], ["psum6"], [u + "r2r"])
                vop(ctx, lambda sc: sc.activation(r2r[0:1, :], r2r[0:1, :],
                                                  AF.Sqrt, bias=epst[0:1, 0:1]),
                    [u + "r2r", u + "epst"], [u + "r2r"], eng="scalar")
                chain(ctx, [
                    lambda v: v.reciprocal(r2r[0:1, :], r2r[0:1, :]),
                    lambda v: v.tensor_copy(r2b[0:1, :], r2r[0:1, :]),
                ], [u + "r2r"], [u + "r2b"])
                mm(ctx, [(ctx.pb[5][:, :TC], ln.ones1_bf[0:1, :], r2b[0:1, :],
                          True, True)],
                   [u + "r2b", u + "ln_ones1"], ["psum5"])
                vop(ctx, lambda v: v.tensor_copy(r2bc[:, :], ctx.pb[5][:, :TC]),
                    ["psum5"], [u + "r2bc"])
                for i in range(NDC):
                    vop(ctx, lambda v, i=i, th=th: v.tensor_mul(
                        xrm[i][:, th:th + TC], xnf[i][:, :], r2bc[:, :]),
                        [rnl[i], u + "r2bc"], [xrmres[i]])
                kacc(ctx, ctx.pb[0][0:14, :TC],
                     [rwt[k][:, :] for k in range(NDC)],
                     [xnf[k][:, :] for k in range(NDC)],
                     [f"{u}rwt{k}" for k in range(NDC)] + rnl, ["psum0"])
                vop(ctx, lambda v: v.tensor_copy(lgt[0:14, :], ctx.pb[0][0:14, :TC]),
                    ["psum0"], [u + "lgt"])
                for tt in range(TC // 128):
                    transpose(ctx, ctx.pb[4][0:128, 0:14],
                              lgt[0:14, 128 * tt:128 * (tt + 1)],
                              identf[0:14, 0:14], [u + "lgt", u + "idf"], ["psum4"])
                    chain(ctx, [
                        lambda v: v.reduce_max(gcol[:, :], ctx.pb[4][0:128, 0:14],
                                               axis=AX, negate=True),
                    ], ["psum4"], [u + "gcol"])
                    vop(ctx, lambda sc: sc.activation(
                        gt[:, 0:14], ctx.pb[4][0:128, 0:14], AF.Exp,
                        bias=gcol[:, 0:1], accum_out=gcol2[:, 0:1]),
                        ["psum4", u + "gcol"], [u + "gt", u + "gcol2"], eng="scalar")
                    fns = [
                        lambda v: v.reciprocal(gcol2[:, :], gcol2[:, :]),
                        lambda v: v.tensor_scalar(gt[:, 0:14], gt[:, 0:14],
                                                  gcol2[:, 0:1], None, ALU.mult),
                    ]
                    for e in range(14):
                        fns.append(lambda v, e=e: v.tensor_scalar(
                            gtmp[:, 0:14], gt[:, 0:14], gt[:, e:e + 1], None,
                            ALU.is_gt, ALU.add, accum_out=gcnt[:, e:e + 1]))
                    fns += [
                        lambda v: v.tensor_scalar(gcnt[:, 0:14], gcnt[:, 0:14], 2.0,
                                                  None, ALU.is_lt),
                        lambda v: v.tensor_mul(gtmp[:, 0:14], gt[:, 0:14],
                                               gcnt[:, 0:14]),
                        lambda v: v.reduce_sum(gcol[:, 0:1], gtmp[:, 0:14], axis=AX),
                        lambda v: v.tensor_scalar_add(gcol[:, 0:1], gcol[:, 0:1], 1e-9),
                        lambda v: v.reciprocal(gcol[:, :], gcol[:, :]),
                        lambda v: v.tensor_scalar(gtmp[:, 0:14], gtmp[:, 0:14],
                                                  gcol[:, 0:1], None, ALU.mult),
                    ]
                    chain(ctx, fns, [u + "gt", u + "gcol2"],
                          [u + "gtmp", u + "gcnt", u + "gcol", u + "gcol2", u + "gt"])
                    transpose(ctx, ctx.pb[4][0:16, 0:128], gtmp[:, 0:16],
                              identf[:, :], [u + "gtmp", u + "idf"], ["psum4"])
                    vop(ctx, lambda v, tcx=tcx, tt=tt: v.tensor_copy(
                        cwf[0:14, tcx * TC + 128 * tt:tcx * TC + 128 * (tt + 1)],
                        ctx.pb[4][0:14, 0:128]),
                        ["psum4"], [u + "cwf"])

            for i in range(NDC):
                vop(ctx, lambda v, i=i: v.memset(yac[i][:, :], 0.0), [],
                    [f"{u}ya{i}"])
            load_rms(0)
            p1 = load_w1_slice(0, 0)
            p2a = load_w2_half(0, 0)
            for e in range(16):
                routed = e < 14
                if e + 1 < 16:
                    load_rms(e + 1)
                if routed:
                    for tcx in range(NQC):
                        mm(ctx, [(ctx.pb[4][:, :TC], e14[:, 128 * e:128 * (e + 1)],
                                  cwf[0:14, tcx * TC:(tcx + 1) * TC], True, True)],
                           [u + "e14", u + "cwf"], ["psum4"])
                        vop(ctx, lambda v, tcx=tcx: v.tensor_copy(
                            cwb[:, tcx * TC:(tcx + 1) * TC], ctx.pb[4][:, :TC]),
                            ["psum4"], [u + "cwb"])
                for hc in range(4):
                    pcur = p1
                    if hc < 3:
                        p1 = load_w1_slice(e, hc + 1)
                    elif e + 1 < 16:
                        p1 = load_w1_slice(e + 1, 0)
                    for tcx in range(NQC):
                        th = tcx * TC
                        for o4 in range(4):
                            bank = o4 % 4
                            kacc(ctx, ctx.pb[bank][:, :TC],
                                 [w1t[pcur][k][:, 128 * o4:128 * (o4 + 1)]
                                  for k in range(NDC)],
                                 [xrm[k][:, th:th + TC] for k in range(NDC)],
                                 [f"{u}w1t{pcur}_{k}" for k in range(NDC)] + xrmres,
                                 [f"psum{bank}"])
                            hi = 4 * hc + o4
                            if routed:
                                vop(ctx, lambda sc, bank=bank: sc.activation(
                                    sil[:, :], ctx.pb[bank][:, :TC], AF.Silu),
                                    [f"psum{bank}"], [u + "sil"], eng="scalar")
                                vop(ctx, lambda v, hi=hi, th=th: v.tensor_mul(
                                    hsl[hi][:, th:th + TC], sil[:, :],
                                    cwb[:, th:th + TC]),
                                    [u + "sil", u + "cwb"], [f"{u}hsl{hi}"])
                            else:
                                vop(ctx, lambda sc, hi=hi, th=th, bank=bank:
                                    sc.activation(hsl[hi][:, th:th + TC],
                                                  ctx.pb[bank][:, :TC], AF.Silu),
                                    [f"psum{bank}"], [f"{u}hsl{hi}"], eng="scalar")
                p2b = load_w2_half(e, 1)
                for tcx in range(NQC):
                    th = tcx * TC
                    for oc in range(NDC):
                        kacc2(ctx, ctx.pb[oc][:, :TC],
                              [w2t[p2a][k][:, 128 * oc:128 * (oc + 1)]
                               for k in range(8)],
                              [hsl[k][:, th:th + TC] for k in range(8)],
                              [f"{u}w2t{p2a}_{k}" for k in range(8)] +
                              [f"{u}hsl{k}" for k in range(8)],
                              [f"psum{oc}"], start=True, stop=False)
                    if tcx == NQC - 1 and e + 1 < 16:
                        p2a_next = load_w2_half(e + 1, 0)
                    for oc in range(NDC):
                        kacc2(ctx, ctx.pb[oc][:, :TC],
                              [w2t[p2b][k][:, 128 * oc:128 * (oc + 1)]
                               for k in range(8)],
                              [hsl[8 + k][:, th:th + TC] for k in range(8)],
                              [f"{u}w2t{p2b}_{k}" for k in range(8)] +
                              [f"{u}hsl{8 + k}" for k in range(8)],
                              [f"psum{oc}"], start=False, stop=True)
                        vop(ctx, lambda v, oc=oc, th=th: v.tensor_add(
                            yac[oc][:, th:th + TC], yac[oc][:, th:th + TC],
                            ctx.pb[oc][:, :TC]),
                            [f"psum{oc}", f"{u}ya{oc}"], [f"{u}ya{oc}"])
                if e + 1 < 16:
                    p2a = p2a_next

            for tcx in range(NQC):
                t0 = qb + tcx * TC
                th = tcx * TC
                for i in range(NDC):
                    dma_in(ctx, fxr[:, :], xin[i][:, t0:t0 + TC], u + "fxr")
                    vop(ctx, lambda v, i=i, th=th: v.tensor_add(
                        fyv[:, :], fxr[:, :], yac[i][:, th:th + TC]),
                        [u + "fxr", f"{u}ya{i}"], [u + "fyv"])
                    dma_out(ctx, xout[i][:, t0:t0 + TC], fyv[:, :],
                            [u + "fyv"], u + "fxo")



def emit_l2_ffn(ctx, xin, xout, W):
    nc = ctx.nc
    u = f"2f{ctx.uid()}_"
    h_scr = nc.dram_tensor(u + "h", [32, 128, T_FULL], BF16, kind="Internal").ap()
    g_scr = nc.dram_tensor(u + "g", [32, 128, T_FULL], BF16, kind="Internal").ap()

    with scope(ctx) as sb:
        xnb = [sb(f"{u}xn{i}", [128, T_FULL], BF16) for i in range(NDC)]
        xnres = [f"{u}xn{i}" for i in range(NDC)]
        xs = [sb(f"{u}x{i}", [128, TC], F32) for i in range(NDC)]
        ln = LNTmp(ctx, sb, u)
        for tcx in range(NTC):
            t0 = tcx * TC
            rxl = [f"{u}x{i}" for i in range(NDC)]
            for i in range(NDC):
                dma_in(ctx, xs[i][:, :], xin[i][:, t0:t0 + TC], rxl[i])
            emit_ln(ctx, ln, [xs[i][:, :] for i in range(NDC)], rxl,
                    [xnb[i][:, t0:t0 + TC] for i in range(NDC)], xnres)

        w1g = [[[sb(f"{u}aw1{p}_{k}_{j}", [128, 128], BF16) for k in range(NDC)]
                for j in range(4)] for p in range(2)]
        w2g = [[[sb(f"{u}aw2{p}_{k}_{j}", [128, 128], BF16) for k in range(NDC)]
                for j in range(4)] for p in range(2)]
        s1 = sb(u + "s1", [128, 32], F32)
        s2 = sb(u + "s2", [128, 32], F32)
        sil = sb(u + "sil", [128, TC], F32)
        hbf = sb(u + "hbf", [128, TC], BF16)
        dma_in(ctx, s1[:, :], W["f2_s1"], u + "s1")
        dma_in(ctx, s2[:, :], W["f2_s2"], u + "s2")

        def load_a(og):
            p = og % 2
            for j in range(4):
                oc = og * 4 + j
                for k in range(NDC):
                    dma_in(ctx, w1g[p][j][k][:, :],
                           W["f2_w1"][k][:, 128 * oc:128 * (oc + 1)],
                           f"{u}aw1{p}_{k}_{j}", eng="gpsimd")
                    dma_in(ctx, w2g[p][j][k][:, :],
                           W["f2_w2"][k][:, 128 * oc:128 * (oc + 1)],
                           f"{u}aw2{p}_{k}_{j}", eng="gpsimd")

        load_a(0)
        for og in range(8):
            p = og % 2
            if og + 1 < 8:
                load_a(og + 1)
            for tcx in range(NTC):
                t0 = tcx * TC
                for j in range(4):
                    oc = og * 4 + j
                    ba, bb = (2 * j) % 4, (2 * j + 1) % 4
                    kacc(ctx, ctx.pb[ba][:, :TC],
                         [w1g[p][j][k][:, :] for k in range(NDC)],
                         [xnb[k][:, t0:t0 + TC] for k in range(NDC)],
                         [f"{u}aw1{p}_{k}_{j}" for k in range(NDC)] + xnres,
                         [f"psum{ba}"])
                    kacc(ctx, ctx.pb[bb][:, :TC],
                         [w2g[p][j][k][:, :] for k in range(NDC)],
                         [xnb[k][:, t0:t0 + TC] for k in range(NDC)],
                         [f"{u}aw2{p}_{k}_{j}" for k in range(NDC)] + xnres,
                         [f"psum{bb}"])
                    vop(ctx, lambda s_, oc=oc, ba=ba: s_.activation(
                        sil[:, :], ctx.pb[ba][:, :TC], AF.Silu,
                        scale=s1[:, oc:oc + 1]),
                        [f"psum{ba}", u + "s1"], [u + "sil"], eng="scalar")
                    vop(ctx, lambda v, oc=oc, bb=bb: v.scalar_tensor_tensor(
                        hbf[:, :], ctx.pb[bb][:, :TC], s2[:, oc:oc + 1],
                        sil[:, :], ALU.mult, ALU.mult),
                        [f"psum{bb}", u + "s2", u + "sil"], [u + "hbf"])
                    dma_out(ctx, h_scr[oc][:, t0:t0 + TC], hbf[:, :],
                            [u + "hbf"], u + "hscr")

    def gelu_stage(src_scr, dst_scr, w_key, ws_key, stag):
        with scope(ctx) as sb2:
            wgt = [[[sb2(f"{u}{stag}w{p}_{k}_{j}", [128, 128], BF16)
                     for k in range(32)] for j in range(8)] for p in range(2)]
            hs = sb2(u + stag + "hs", [128, 32], F32)
            htile = [sb2(f"{u}{stag}h{k}", [128, TC], BF16) for k in range(32)]
            obf = sb2(u + stag + "obf", [128, TC], BF16)
            dma_in(ctx, hs[:, :], W[ws_key], u + stag + "hs")

            def load_g(og):
                p = og % 2
                for j in range(8):
                    oc = og * 8 + j
                    for k in range(32):
                        dma_in(ctx, wgt[p][j][k][:, :],
                               W[w_key][k][:, 128 * oc:128 * (oc + 1)],
                               f"{u}{stag}w{p}_{k}_{j}", eng="gpsimd")

            load_g(0)
            for og in range(4):
                p = og % 2
                if og + 1 < 4:
                    load_g(og + 1)
                for tcx in range(NTC):
                    t0 = tcx * TC
                    hres = [f"{u}{stag}h{k}" for k in range(32)]
                    for k in range(32):
                        dma_in(ctx, htile[k][:, :], src_scr[k][:, t0:t0 + TC],
                               hres[k])
                    for j in range(8):
                        oc = og * 8 + j
                        bank = j % 4
                        kacc(ctx, ctx.pb[bank][:, :TC],
                             [wgt[p][j][k][:, :] for k in range(32)],
                             [htile[k][:, :] for k in range(32)],
                             [f"{u}{stag}w{p}_{k}_{j}" for k in range(32)] + hres,
                             [f"psum{bank}"])
                        vop(ctx, lambda s_, oc=oc, bank=bank: s_.activation(
                            obf[:, :], ctx.pb[bank][:, :TC], AF.Gelu,
                            scale=hs[:, oc:oc + 1]),
                            [f"psum{bank}", u + stag + "hs"], [u + stag + "obf"],
                            eng="scalar")
                        dma_out(ctx, dst_scr[oc][:, t0:t0 + TC], obf[:, :],
                                [u + stag + "obf"], u + stag + "dst")

    gelu_stage(h_scr, g_scr, "f2_h1", "f2_h1s", "b")
    gelu_stage(g_scr, h_scr, "f2_h2", "f2_h2s", "c")

    with scope(ctx) as sb2:
        w3t = [[sb2(f"{u}dw{k}_{j}", [128, 128], BF16) for k in range(32)]
               for j in range(8)]
        s3 = sb2(u + "s3", [128, 8], F32)
        htile = [sb2(f"{u}dh{k}", [128, TC], BF16) for k in range(32)]
        xr = sb2(u + "dxr", [128, TC], F32)
        yv = sb2(u + "yv", [128, TC], F32)
        dma_in(ctx, s3[:, :], W["f2_w3s"], u + "s3")
        for j in range(8):
            for k in range(32):
                dma_in(ctx, w3t[j][k][:, :], W["f2_w3"][k][:, 128 * j:128 * (j + 1)],
                       f"{u}dw{k}_{j}", eng="gpsimd")
        for tcx in range(NTC):
            t0 = tcx * TC
            hres = [f"{u}dh{k}" for k in range(32)]
            for k in range(32):
                dma_in(ctx, htile[k][:, :], h_scr[k][:, t0:t0 + TC], hres[k])
            for oc in range(8):
                bank = oc % 4
                kacc(ctx, ctx.pb[bank][:, :TC],
                     [w3t[oc][k][:, :] for k in range(32)],
                     [htile[k][:, :] for k in range(32)],
                     [f"{u}dw{k}_{oc}" for k in range(32)] + hres,
                     [f"psum{bank}"])
                dma_in(ctx, xr[:, :], xin[oc][:, t0:t0 + TC], u + "dxr")
                vop(ctx, lambda v, oc=oc, bank=bank: v.scalar_tensor_tensor(
                    yv[:, :], ctx.pb[bank][:, :TC], s3[:, oc:oc + 1],
                    xr[:, :], ALU.mult, ALU.add),
                    [f"psum{bank}", u + "s3", u + "dxr"], [u + "yv"])
                dma_out(ctx, xout[oc][:, t0:t0 + TC], yv[:, :],
                        [u + "yv"], u + "xo")



def emit_l3_ffn(ctx, xin, xout, W):
    nc = ctx.nc
    with contextlib.ExitStack() as bs:
        def sb(n, s, d):
            return bs.enter_context(nc.sbuf_tensor(n, s, d))
        tag = "f3_"

        wq = [sb(f"f3_w{i}", [128, 1024], F32) for i in range(8)]
        wdT = [sb(f"f3_wT{i}", [128, 1024], F32) for i in range(8)]
        ident = sb("f3_id", [128, 128], F32)
        bias = sb("f3_b", [128, 8], F32)
        xs = [sb(f"f3_x{i}", [128, TC], F32) for i in range(NDC)]
        xn = [sb(f"f3_xn{i}", [128, TC], F32) for i in range(NDC)]
        xnb = [sb(f"f3_xnb{i}", [128, TC], BF16) for i in range(NDC)]
        yout = [sb(f"f3_y{i}", [128, TC], F32) for i in range(NDC)]
        qt = QTmp(sb, tag, 1024)
        ln = LNTmp(ctx, sb, tag)

        dma_in(ctx, ident[:, :], W["ident_f32"], "f3_id")
        dma_in(ctx, bias[:, :], W["w3f_b"], "f3_b")
        for i in range(8):
            dma_in(ctx, wq[i][:, :], W["w3f_w"][i], f"f3_w{i}", eng="gpsimd")
            emit_quant(ctx, qt, wq[i][:, :], wq[i][:, :], 1024, "free",
                       [f"f3_w{i}"], [f"f3_w{i}"])
        for i in range(8):
            for j in range(8):
                transpose(ctx, ctx.pb[4][:, 0:128], wq[i][:, 128 * j:128 * (j + 1)],
                          ident[:, :], [f"f3_w{i}", "f3_id"], ["psum4"])
                vop(ctx, lambda v, i=i, j=j: v.tensor_copy(
                    wdT[j][:, 128 * i:128 * (i + 1)], ctx.pb[4][:, 0:128]),
                    ["psum4"], [f"f3_wT{j}"])

        for tcx in range(NTC):
            t0 = tcx * TC
            rxl = [f"f3_x{i}" for i in range(NDC)]
            rnl = [f"f3_xn{i}" for i in range(NDC)]
            for i in range(NDC):
                dma_in(ctx, xs[i][:, :], xin[i][:, t0:t0 + TC], rxl[i])
            emit_ln(ctx, ln, [xs[i][:, :] for i in range(NDC)], rxl,
                    [xnb[i][:, :] for i in range(NDC)], [f"f3_xnb{i}" for i in range(NDC)],
                    out_f32=[xn[i][:, :] for i in range(NDC)], out_f32_resl=rnl)
            for k in range(NDC):
                emit_quant(ctx, qt, xn[k][:, :], xn[k][:, :], TC, "part",
                           [rnl[k]], [rnl[k]])
                for oc in range(8):
                    mm(ctx, [(ctx.pb[oc][:, :TC],
                              wdT[k][:, 128 * oc:128 * (oc + 1)], xn[k][:, :],
                              k == 0, k == NDC - 1)],
                       [rnl[k], f"f3_wT{k}"], [f"psum{oc}"])
            for oc in range(8):
                vop(ctx, lambda v, oc=oc: v.scalar_tensor_tensor(
                    yout[oc][:, :], ctx.pb[oc][:, :TC], bias[:, oc:oc + 1],
                    xs[oc][:, :], ALU.add, ALU.add),
                    [f"psum{oc}", "f3_b", rxl[oc]], [f"f3_y{oc}"])
                dma_out(ctx, xout[oc][:, t0:t0 + TC], yout[oc][:, :],
                        [f"f3_y{oc}"], f"f3_xo{tcx}")



def host_weight_layouts(inputs):
    W = {}
    f = np.ascontiguousarray
    W["a1_wq"] = f(inputs["L1_attn_W_q"].T.reshape(8, 128, 1024))
    W["a1_wk"] = f(inputs["L1_attn_W_k"].T.reshape(8, 128, 1024))
    W["a1_wv"] = f(inputs["L1_attn_W_v"].T.reshape(8, 128, 1024))
    W["a1_wo"] = f(inputs["L1_attn_W_o"].T.reshape(8, 128, 1024))
    _i = np.arange(32, dtype=np.float64)
    _t = np.minimum(np.arange(T_FULL), 2047).astype(np.float64)
    _ang = _t[None, :] * (10000.0 ** (-2.0 * _i / 64))[:, None]
    _c = np.cos(_ang)
    _s = np.sin(_ang)
    _p = np.arange(128)
    _iofp = (_p % 64) // 2
    cB = _c[_iofp]
    sB = _s[_iofp] * np.where(_p % 2 == 0, -1.0, 1.0)[:, None]
    W["a1_cos"] = f(cB.astype(np.float32).astype(bfdt))
    W["a1_sin"] = f(sB.astype(np.float32).astype(bfdt))
    Pswap = np.zeros((128, 128), dtype=np.float32)
    Pswap[np.arange(128), np.arange(128) ^ 1] = 1.0
    W["a1_pswap"] = f(Pswap.astype(bfdt))
    def _bd(m):
        out = np.zeros((8, 128, 128), dtype=np.float32)
        for p in range(8):
            out[p, 0:64, 0:64] = m[2 * p].T
            out[p, 64:128, 64:128] = m[2 * p + 1].T
        return out
    W["a3_wd"] = f(inputs["L3_attn_W_down"].reshape(8, 128, 1024))
    W["a3_tk"] = f(_bd(inputs["L3_attn_tokeys"]))
    W["a3_tq"] = f(_bd(inputs["L3_attn_toqueries"]))
    W["a3_tv"] = f(_bd(inputs["L3_attn_tovalues"]))
    W["a3_wu"] = f(inputs["L3_attn_W_unify"].reshape(8, 128, 1024))
    W["a3_bu"] = f(inputs["L3_attn_b_unify"].reshape(8, 128).T)
    W["a0_wqkv"] = f(inputs["L0_attn_w_qkv"].reshape(8, 128, 3072))
    W["a0_wg"] = f(inputs["L0_attn_w_gate"].reshape(8, 128, 1024))
    W["a0_wp"] = f(inputs["L0_attn_w_proj"].reshape(8, 128, 1024))
    W["a0_bqkv"] = f(inputs["L0_attn_b_qkv"].reshape(24, 128).T)
    W["a0_bg"] = f(inputs["L0_attn_b_gate"].reshape(8, 128).T)
    W["a0_bp"] = f(inputs["L0_attn_b_proj"].reshape(8, 128).T)
    W["a0_lg"] = f(inputs["L0_attn_ln_g"].reshape(8, 128).T)
    W["a0_lb"] = f(inputs["L0_attn_ln_b"].reshape(8, 128).T)
    s_ = np.arange(128)[:, None]
    t_ = np.arange(512)[None, :]
    W["maskA"] = f(np.concatenate([(128 * j + s_ <= t_) for j in range(4)],
                                  axis=1).astype(np.float32).astype(bfdt))
    W["f0_rw"] = f(inputs["L0_ffn_router_weight"].T.reshape(8, 128, 16))
    W["f0_eb"] = f(inputs["L0_ffn_e_bias"].reshape(16, 1))
    W["f0_gw"] = f(inputs["L0_ffn_gate_w"].transpose(0, 2, 1).reshape(16, 8, 128, 512))
    W["f0_uw"] = f(inputs["L0_ffn_up_w"].transpose(0, 2, 1).reshape(16, 8, 128, 512))
    W["f0_dw"] = f(inputs["L0_ffn_down_w"].transpose(0, 2, 1).reshape(16, 4, 128, 1024))
    W["f0_sg"] = f(inputs["L0_ffn_sh_gate_w"].T.reshape(8, 128, 1024))
    W["f0_su"] = f(inputs["L0_ffn_sh_up_w"].T.reshape(8, 128, 1024))
    W["f0_sd"] = f(inputs["L0_ffn_sh_down_w"].T.reshape(8, 128, 1024))
    E16 = np.zeros((16, 16 * 128), dtype=np.float32)
    for e in range(16):
        E16[e, 128 * e:128 * (e + 1)] = 1.0
    W["E16"] = f(E16.astype(bfdt))
    W["f1_rw"] = f(inputs["L1_ffn_router_w"].reshape(8, 128, 14))
    W["f1_rms"] = f(inputs["L1_ffn_rms_w"].reshape(14, 8, 128).transpose(0, 2, 1))
    W["f1_rms_s"] = f(inputs["L1_ffn_rms_w_s"].reshape(2, 8, 128).transpose(0, 2, 1))
    W["f1_w1"] = f(inputs["L1_ffn_w1"].reshape(14, 8, 128, 2048))
    W["f1_w2"] = f(inputs["L1_ffn_w2"].reshape(14, 16, 128, 1024))
    W["f1_w1_s"] = f(inputs["L1_ffn_w1_s"].reshape(2, 8, 128, 2048))
    W["f1_w2_s"] = f(inputs["L1_ffn_w2_s"].reshape(2, 16, 128, 1024))
    E14 = np.zeros((14, 14 * 128), dtype=np.float32)
    for e in range(14):
        E14[e, 128 * e:128 * (e + 1)] = 1.0
    W["E14"] = f(E14.astype(bfdt))
    W["a2_wk"] = f(inputs["L2_attn_W_K"].transpose(2, 0, 1).reshape(1024, 1024).reshape(8, 128, 1024))
    W["a2_wq"] = f(inputs["L2_attn_W_Q"].transpose(2, 0, 1).reshape(1024, 1024).reshape(8, 128, 1024))
    W["a2_wv"] = f(inputs["L2_attn_W_V"].transpose(2, 0, 1).reshape(1024, 1024).reshape(8, 128, 1024))
    q_ = np.arange(128)[:, None]
    k_ = np.arange(512)[None, :]
    W["maskQ"] = f(np.concatenate([(k_ > 128 * j + q_) * np.float32(-1e30)
                                   for j in range(4)], axis=1).astype(bfdt))
    kk_ = np.arange(128)[:, None]
    qq_ = np.arange(512)[None, :]
    W["maskT"] = f(np.concatenate([(qq_ < 128 * j + kk_) * np.float32(-1e30)
                                   for j in range(4)], axis=1).astype(bfdt))
    W["f2_w1"] = f(inputs["L2_ffn_w1_w"].T.reshape(8, 128, 4096))
    W["f2_w2"] = f(inputs["L2_ffn_w2_w"].T.reshape(8, 128, 4096))
    W["f2_h1"] = f(inputs["L2_ffn_h1_w"].T.reshape(32, 128, 4096))
    W["f2_h2"] = f(inputs["L2_ffn_h2_w"].T.reshape(32, 128, 4096))
    W["f2_w3"] = f(inputs["L2_ffn_w3_w"].T.reshape(32, 128, 1024))
    W["f2_s1"] = f(inputs["L2_ffn_w1_s"].reshape(32, 128).T)
    W["f2_s2"] = f(inputs["L2_ffn_w2_s"].reshape(32, 128).T)
    W["f2_h1s"] = f(inputs["L2_ffn_h1_s"].reshape(32, 128).T)
    W["f2_h2s"] = f(inputs["L2_ffn_h2_s"].reshape(32, 128).T)
    W["f2_w3s"] = f(inputs["L2_ffn_w3_s"].reshape(8, 128).T)
    W["w3f_w"] = f(inputs["L3_ffn_weight"].reshape(8, 128, 1024))
    W["w3f_b"] = f(inputs["L3_ffn_bias"].reshape(8, 128).T)
    W["ident_f32"] = np.eye(128, dtype=np.float32)
    W["ident_bf"] = np.eye(128, dtype=np.float32).astype(bfdt)
    return W


def declare_weights(nc, W):
    aps = {}
    for k, v in W.items():
        aps[k] = nc.dram_tensor(k, list(v.shape), mybir.dt.from_np(v.dtype),
                                kind="ExternalInput").ap()
    return aps



BLOCK_SEQ = ["l0_attn", "l0_ffn", "l1_attn", "l1_ffn",
             "l2_attn", "l2_ffn", "l3_attn", "l3_ffn"]
BLOCK_WKEYS = {
    "l0_attn": ["a0_wqkv", "a0_wg", "a0_wp", "a0_bqkv", "a0_bg", "a0_bp",
                "a0_lg", "a0_lb", "maskA", "ident_bf"],
    "l0_ffn": ["f0_rw", "f0_eb", "f0_gw", "f0_uw", "f0_dw", "f0_sg", "f0_su",
               "f0_sd", "E16", "ident_f32"],
    "l1_attn": ["a1_wq", "a1_wk", "a1_wv", "a1_wo", "a1_cos", "a1_sin",
                "a1_pswap", "maskQ", "maskT", "ident_bf", "ident_f32"],
    "l1_ffn": ["f1_rw", "f1_rms", "f1_rms_s", "f1_w1", "f1_w2", "f1_w1_s",
               "f1_w2_s", "E14", "ident_f32"],
    "l2_attn": ["a2_wk", "a2_wq", "a2_wv", "maskQ", "maskT", "ident_bf",
                "ident_f32"],
    "l2_ffn": ["f2_w1", "f2_w2", "f2_h1", "f2_h2", "f2_w3", "f2_s1", "f2_s2",
               "f2_h1s", "f2_h2s", "f2_w3s"],
    "l3_attn": ["a3_wd", "a3_tk", "a3_tq", "a3_tv", "a3_wu", "a3_bu", "maskQ",
                "maskT", "ident_bf", "ident_f32"],
    "l3_ffn": ["w3f_w", "w3f_b", "ident_f32"],
}
BLOCK_EMITTERS = {"l0_attn": emit_l0_attn, "l0_ffn": emit_l0_ffn,
                  "l1_attn": emit_l1_attn, "l2_attn": emit_l2_attn,
                  "l3_attn": emit_l3_attn, "l1_ffn": emit_l1_ffn,
                  "l2_ffn": emit_l2_ffn, "l3_ffn": emit_l3_ffn}


def build_single_block(block_name, W):
    nc = bass.Bass("TRN2", target_bir_lowering=False)
    xin_t = nc.dram_tensor("xin", [NDC, 128, T_FULL], F32, kind="ExternalInput")
    xout_t = nc.dram_tensor("xout", [NDC, 128, T_FULL], F32, kind="ExternalOutput")
    Wsub = {k: W[k] for k in BLOCK_WKEYS[block_name]}
    waps = declare_weights(nc, Wsub)
    with contextlib.ExitStack() as stack:
        ctx = Ctx(nc, stack)
        xin = [xin_t.ap()[i] for i in range(NDC)]
        xout = [xout_t.ap()[i] for i in range(NDC)]
        BLOCK_EMITTERS[block_name](ctx, xin, xout, waps)
        ctx.p.wait_all()
        ctx.p.replay()
    return nc


def build_fused(W):
    nc = bass.Bass("TRN2", target_bir_lowering=False)
    xin_t = nc.dram_tensor("xin", [NDC, 128, T_FULL], F32, kind="ExternalInput")
    xout_t = nc.dram_tensor("xout", [NDC, 128, T_FULL], F32, kind="ExternalOutput")
    xa_t = nc.dram_tensor("xscrA", [NDC, 128, T_FULL], F32, kind="Internal")
    xb_t = nc.dram_tensor("xscrB", [NDC, 128, T_FULL], F32, kind="Internal")
    waps = declare_weights(nc, W)
    hop = [xa_t.ap(), xb_t.ap()]
    with contextlib.ExitStack() as stack:
        ctx = Ctx(nc, stack)
        nblk = len(BLOCK_SEQ)
        for bi, name in enumerate(BLOCK_SEQ):
            src = xin_t.ap() if bi == 0 else hop[bi % 2]
            dst = xout_t.ap() if bi == nblk - 1 else hop[(bi + 1) % 2]
            xin = [src[i] for i in range(NDC)]
            xout = [dst[i] for i in range(NDC)]
            BLOCK_EMITTERS[name](ctx, xin, xout, waps)
        ctx.p.wait_all()
        ctx.p.replay()
    return nc


def kernel(**inputs):
    x = inputs["L0_attn_x"]
    W = host_weight_layouts(inputs)
    xs = [np.ascontiguousarray(x[b].T).reshape(NDC, 128, T_FULL) for b in range(8)]
    nc = build_fused(W)
    in_maps = [dict(W, xin=xs[b]) for b in range(8)]
    res = run_bass_kernel_spmd(nc, in_maps, core_ids=list(range(8)))
    out = np.stack([r["xout"].reshape(D, T_FULL).T for r in res.results])
    return out


def kernel_unfused(**inputs):
    x = inputs["L0_attn_x"]
    W = host_weight_layouts(inputs)
    xs = [np.ascontiguousarray(x[b].T).reshape(NDC, 128, T_FULL) for b in range(8)]
    cur = xs
    for name in BLOCK_SEQ:
        nc = build_single_block(name, W)
        Wsub = {k: W[k] for k in BLOCK_WKEYS[name]}
        in_maps = [dict(Wsub, xin=cur[b]) for b in range(8)]
        res = run_bass_kernel_spmd(nc, in_maps, core_ids=list(range(8)))
        cur = [r["xout"].reshape(NDC, 128, T_FULL) for r in res.results]
    out = np.stack([c.reshape(D, T_FULL).T for c in cur])
    return out


# version 08
# speedup vs baseline: 1.2506x; 1.2011x over previous
import contextlib
import numpy as np
import ml_dtypes
import concourse.bass as bass
import concourse.mybir as mybir
from concourse.bass_utils import run_bass_kernel_spmd

F32 = mybir.dt.float32
F32R = mybir.dt.float32r
BF16 = mybir.dt.bfloat16
U32 = mybir.dt.uint32
AF = mybir.ActivationFunctionType
ALU = mybir.AluOpType
AX = mybir.AxisListType.X
bfdt = ml_dtypes.bfloat16

ENG = ("sync", "scalar", "vector", "tensor", "gpsimd")
QSEM = {"sync": "dS", "scalar": "dA", "gpsimd": "dP"}
NSLOT = 6

T_FULL = 4096
D = 1024
NDC = 8
TC = 512
NTC = T_FULL // TC
MAGIC = 12582912.0


class P:

    def __init__(self, nc, stack):
        self.nc = nc
        self.stack = stack
        self.ops = {e: [] for e in ENG}
        self.sem = {}
        self.cnt = {}
        self.waited = {e: {} for e in ENG}
        self.res = {}
        self.dma_idx = {}
        for e in ENG:
            self._mk("s_" + e)
        for q in set(QSEM.values()):
            self.dma_idx[q] = 0
            for i in range(NSLOT):
                self._mk(f"{q}{i}")

    def _mk(self, name):
        self.sem[name] = self.stack.enter_context(self.nc.semaphore(name))
        self.cnt[name] = 0

    def _deps(self, e, reads, writes):
        m = {}
        for r in reads:
            d = self.res.get(r)
            if d and d["w"] is not None:
                s, v = d["w"]
                m[s] = max(m.get(s, 0), v)
        for w in writes:
            d = self.res.get(w)
            if d:
                for s, v in ([d["w"]] if d["w"] else []) + d["r"]:
                    m[s] = max(m.get(s, 0), v)
        own = "s_" + e
        out = []
        for s, v in m.items():
            if s == own and e == "tensor":
                continue
            if s == own and self.cnt[own] < v:
                raise RuntimeError("self-dep on not-yet-emitted token")
            if self.waited[e].get(s, 0) >= v:
                continue
            out.append((s, v))
        return out

    def emit(self, e, reads, writes, fn, dma=False):
        waits = self._deps(e, reads, writes)
        if dma:
            q = QSEM[e]
            sname = f"{q}{self.dma_idx[q] % NSLOT}"
            self.dma_idx[q] += 1
            amount = 16
            prev = self.cnt[sname]
            if prev > 0 and self.waited[e].get(sname, 0) < prev and \
                    all(s != sname for s, _ in waits):
                waits = waits + [(sname, prev)]
        else:
            sname, amount = "s_" + e, 1
        self.cnt[sname] += amount
        tok = (sname, self.cnt[sname])
        for s, v in waits:
            self.waited[e][s] = max(self.waited[e].get(s, 0), v)
        sem = self.sem[sname]

        def closure(eng, waits=waits, sem=sem, amount=amount, fn=fn):
            for s, v in waits:
                eng.wait_ge(self.sem[s], v)
            fn(eng).then_inc(sem, amount)

        self.ops[e].append(closure)
        for r in reads:
            self.res.setdefault(r, {"w": None, "r": []})["r"].append(tok)
        for w in writes:
            d = self.res.setdefault(w, {"w": None, "r": []})
            d["w"] = tok
            d["r"] = []
        return tok

    def barrier(self):
        snap = dict(self.cnt)
        for e in ENG:
            for s, v in snap.items():
                if v > 0 and self.waited[e].get(s, 0) < v:
                    self.ops[e].append(
                        lambda eng, s=s, v=v: eng.wait_ge(self.sem[s], v))
                    self.waited[e][s] = v
        for d in self.res.values():
            d["r"] = []

    def wait_all(self, e="sync"):
        for s in list(self.cnt):
            v = self.cnt[s]
            if v > 0 and self.waited[e].get(s, 0) < v:
                self.ops[e].append(lambda eng, s=s, v=v: eng.wait_ge(self.sem[s], v))
                self.waited[e][s] = v

    def replay(self):
        with self.nc.Block() as block:
            @block.sync
            def _(eng):
                for f in self.ops["sync"]:
                    f(eng)

            @block.scalar
            def _(eng):
                for f in self.ops["scalar"]:
                    f(eng)

            @block.vector
            def _(eng):
                for f in self.ops["vector"]:
                    f(eng)

            @block.tensor
            def _(eng):
                for f in self.ops["tensor"]:
                    f(eng)

            @block.gpsimd
            def _(eng):
                for f in self.ops["gpsimd"]:
                    f(eng)


class Ctx:
    def __init__(self, nc, stack):
        self.nc = nc
        self.stack = stack
        self.p = P(nc, stack)
        self.pb = [stack.enter_context(nc.psum_tensor(f"pbank{i}", [128, 512], F32))
                   for i in range(8)]
        self._uid = 0

    def uid(self):
        self._uid += 1
        return self._uid



def dma_in(ctx, dst_ap, src_ap, res_w, res_r=(), eng="sync"):
    return ctx.p.emit(eng, list(res_r), [res_w],
                      lambda g: g.dma_start(out=dst_ap, in_=src_ap), dma=True)


def dma_out(ctx, dst_ap, src_ap, res_r=(), res_w="dram_out", eng="scalar"):
    return ctx.p.emit(eng, list(res_r), [res_w],
                      lambda g: g.dma_start(out=dst_ap, in_=src_ap), dma=True)


def mm(ctx, plist, reads, writes):
    def fn(t):
        inst = None
        for (o, l, r, st, sp) in plist:
            inst = t.matmul(o, l, r, start=st, stop=sp)
        return inst
    return ctx.p.emit("tensor", reads, writes, fn)


def kacc(ctx, out_ap, lhsT_list, rhs_list, reads, writes):
    n = len(lhsT_list)
    return mm(ctx, [(out_ap, lhsT_list[i], rhs_list[i], i == 0, i == n - 1)
                    for i in range(n)], reads, writes)


def kacc2(ctx, out_ap, lhsT_list, rhs_list, reads, writes, start, stop):
    n = len(lhsT_list)
    return mm(ctx, [(out_ap, lhsT_list[i], rhs_list[i], start and i == 0,
                     stop and i == n - 1) for i in range(n)], reads, writes)


def transpose(ctx, out_psum_ap, in_ap, ident_ap, reads, writes):
    return ctx.p.emit("tensor", reads, writes,
                      lambda t: t.transpose(out_psum_ap, in_ap, ident_ap))


def vop(ctx, fn, reads, writes, eng="vector"):
    return ctx.p.emit(eng, reads, writes, fn)


@contextlib.contextmanager
def scope(ctx):
    with contextlib.ExitStack() as bs:
        yield lambda n, s, d: bs.enter_context(
            ctx.nc.sbuf_tensor(n, s, d, allow_name_mangling=True))
    ctx.p.barrier()


def chain(ctx, fns, reads, writes, eng="vector"):
    u = f"ch{ctx.uid()}"
    n = len(fns)
    tok = None
    for i, f in enumerate(fns):
        r = list(reads) + ([u] if i > 0 else [])
        w = ([u] if i < n - 1 else list(writes) + [u])
        tok = ctx.p.emit(eng, r, w, f)
    return tok



class LNTmp:
    def __init__(self, ctx, sb, tag, tc=TC):
        t = tag
        self.tag = t
        self.tc = tc
        self.xb = sb(t + "ln_xb", [128, tc], BF16)
        self.xsq = sb(t + "ln_xsq", [128, tc], BF16)
        self.rowS = sb(t + "ln_rowS", [1, tc], F32)[0:1, :]
        self.rowQ = sb(t + "ln_rowQ", [1, tc], F32)[0:1, :]
        self.rstd = sb(t + "ln_rstd", [1, tc], F32)[0:1, :]
        self.mneg = sb(t + "ln_mneg", [1, tc], F32)[0:1, :]
        self.rowbA = sb(t + "ln_rowbA", [1, tc], BF16)[0:1, :]
        self.rowbB = sb(t + "ln_rowbB", [1, tc], BF16)[0:1, :]
        self.rbc = sb(t + "ln_rbc", [128, 2 * tc], BF16)
        self.tmp = sb(t + "ln_tmp", [128, tc], F32)
        self.ones_bf = sb(t + "ln_ones", [128, 1], BF16)
        self.ones1_bf = sb(t + "ln_ones1", [1, 128], BF16)
        self.eps = sb(t + "ln_eps", [1, 1], F32)
        vop(ctx, lambda v: v.memset(self.ones_bf[:, :], 1.0), [], [t + "ln_ones"])
        vop(ctx, lambda v: v.memset(self.ones1_bf[:, :], 1.0), [], [t + "ln_ones1"])


def emit_ln(ctx, ln, x_chunks, x_resl, out_bf, out_resl, out_f32=None,
            out_f32_resl=None, eps=1e-5, affine=None, affine_res=()):
    tc = ln.tc
    t = ln.tag
    pass
    pbS, pbQ, pbB = ctx.pb[6], ctx.pb[7], ctx.pb[5]
    nd = len(x_chunks)
    for i in range(nd):
        vop(ctx, lambda v, i=i: v.tensor_copy(ln.xb[:, :tc], x_chunks[i]),
            [x_resl[i]], [t + "ln_xb"])
        vop(ctx, lambda s, i=i: s.activation(ln.xsq[:, :tc], x_chunks[i], AF.Square),
            [x_resl[i]], [t + "ln_xsq"], eng="scalar")
        mm(ctx, [(pbS[0:1, :tc], ln.ones_bf[:, :], ln.xb[:, :tc], i == 0, i == nd - 1)],
           [t + "ln_xb", t + "ln_ones"], ["psum6"])
        mm(ctx, [(pbQ[0:1, :tc], ln.ones_bf[:, :], ln.xsq[:, :tc], i == 0, i == nd - 1)],
           [t + "ln_xsq", t + "ln_ones"], ["psum7"])

    chain(ctx, [
        lambda v: v.tensor_scalar_mul(ln.rowS[0:1, :tc], pbS[0:1, :tc], 1.0 / (nd * 128)),
        lambda v: v.tensor_mul(ln.mneg[0:1, :tc], ln.rowS[0:1, :tc], ln.rowS[0:1, :tc]),
        lambda v: v.scalar_tensor_tensor(ln.rowQ[0:1, :tc], pbQ[0:1, :tc],
                                         1.0 / (nd * 128), ln.mneg[0:1, :tc],
                                         ALU.mult, ALU.subtract),
    ], ["psum6", "psum7"], [t + "ln_rows"])
    vop(ctx, lambda v: v.memset(ln.eps[:, :], float(eps)), [], [ln.tag + "ln_eps"])
    vop(ctx, lambda s: s.activation(ln.rowQ[0:1, :tc], ln.rowQ[0:1, :tc], AF.Sqrt,
                                    bias=ln.eps[0:1, 0:1]),
        [t + "ln_rows", ln.tag + "ln_eps"], [t + "ln_rows"], eng="scalar")
    chain(ctx, [
        lambda v: v.reciprocal(ln.rstd[0:1, :tc], ln.rowQ[0:1, :tc]),
        lambda v: v.scalar_tensor_tensor(ln.mneg[0:1, :tc], ln.rowS[0:1, :tc], -1.0,
                                         ln.rstd[0:1, :tc], ALU.mult, ALU.mult),
        lambda v: v.tensor_copy(ln.rowbA[0:1, :tc], ln.rstd[0:1, :tc]),
        lambda v: v.tensor_copy(ln.rowbB[0:1, :tc], ln.mneg[0:1, :tc]),
    ], [t + "ln_rows"], [t + "ln_r2b"])
    mm(ctx, [(pbB[:, :tc], ln.ones1_bf[0:1, :], ln.rowbA[0:1, :tc], True, True)],
       [t + "ln_r2b", t + "ln_ones1"], ["psum5"])
    vop(ctx, lambda v: v.tensor_copy(ln.rbc[:, 0:tc], pbB[:, :tc]),
        ["psum5"], [t + "ln_rbcF"])
    mm(ctx, [(pbB[:, :tc], ln.ones1_bf[0:1, :], ln.rowbB[0:1, :tc], True, True)],
       [t + "ln_r2b", "psum5"], ["psum5"])
    vop(ctx, lambda v: v.tensor_copy(ln.rbc[:, tc:2 * tc], pbB[:, :tc]),
        ["psum5"], [t + "ln_rbcF"])
    rb = ln.rbc[:, 0:tc]
    mb = ln.rbc[:, tc:2 * tc]
    for i in range(nd):
        wl = (([out_resl[i]] if out_bf is not None else [])
              + ([out_f32_resl[i]] if out_f32 is not None else []))
        fns = [lambda v, i=i: v.tensor_mul(ln.tmp[:, :tc], x_chunks[i], rb)]
        if affine is not None:
            ga, be = affine
            fns.append(lambda v, i=i: v.tensor_add(ln.tmp[:, :tc], ln.tmp[:, :tc],
                                                   mb))
            fns.append(lambda v, i=i, ga=ga, be=be: v.scalar_tensor_tensor(
                out_bf[i], ln.tmp[:, :tc], ga[i],
                be[i].broadcast_to([128, tc]), ALU.mult, ALU.add))
        elif out_f32 is not None:
            fns.append(lambda v, i=i: v.tensor_add(out_f32[i], ln.tmp[:, :tc], mb))
            if out_bf is not None:
                fns.append(lambda v, i=i: v.tensor_copy(out_bf[i], out_f32[i]))
        else:
            fns.append(lambda v, i=i: v.tensor_add(out_bf[i], ln.tmp[:, :tc], mb))
        chain(ctx, fns, [x_resl[i], t + "ln_rbcF", t + "ln_tmpres"] + list(affine_res),
              wl + [t + "ln_tmpres"])



class QTmp:
    def __init__(self, sb, tag, width):
        w = width
        t = tag
        self.w = w
        self.t1 = sb(t + "q_t1", [128, w], F32)
        self.t2 = sb(t + "q_t2", [128, w], F32)
        self.sc = sb(t + "q_sc", [128, max(w // 32, 32)], F32)
        self.inv = sb(t + "q_inv", [128, max(w // 32, 32)], F32)
        self.scE = sb(t + "q_scE", [128, w], F32)
        self.invE = sb(t + "q_invE", [128, w], F32)
        self.q8 = sb(t + "q_q8", [128, w], F32)
        self.q4 = sb(t + "q_q4", [128, w], F32)
        self.cm = sb(t + "q_cm", [128, w], F32)
        self.res = t + "qtmp"


def emit_quant(ctx, qt, x_ap, out_ap, width, mode, reads, writes):
    w = width
    nb = w // 32
    fns = []
    a = fns.append
    if mode == "part":
        a(lambda v: v.transpose(qt.t1[:, :w], x_ap))
        src = qt.t1[:, :w]
    else:
        src = x_ap
    a(lambda v: v.tensor_reduce(qt.sc[:, :nb], src.rearrange("p (b j) -> p b j", j=32),
                                axis=AX, op=ALU.max, apply_absolute_value=True))
    a(lambda v: v.tensor_scalar_mul(qt.sc[:, :nb], qt.sc[:, :nb], 1.0 / 7.5))
    a(lambda v: v.tensor_scalar_max(qt.sc[:, :nb], qt.sc[:, :nb], 1e-30))
    a(lambda v: v.reciprocal(qt.inv[:, :nb], qt.sc[:, :nb]))
    a(lambda v: v.tensor_copy(qt.scE[:, :w].rearrange("p (b j) -> p b j", j=32),
                              qt.sc[:, :nb].unsqueeze(2).broadcast_to([128, nb, 32])))
    a(lambda v: v.tensor_copy(qt.invE[:, :w].rearrange("p (b j) -> p b j", j=32),
                              qt.inv[:, :nb].unsqueeze(2).broadcast_to([128, nb, 32])))
    if mode == "part":
        a(lambda v: v.transpose(qt.t1[:, :w], qt.scE[:, :w]))
        a(lambda v: v.transpose(qt.t2[:, :w], qt.invE[:, :w]))
        a(lambda v: v.tensor_copy(qt.scE[:, :w], qt.t1[:, :w]))
        a(lambda v: v.tensor_copy(qt.invE[:, :w], qt.t2[:, :w]))
    a(lambda v: v.tensor_mul(qt.t1[:, :w], x_ap, qt.invE[:, :w]))
    for f, dst in ((8.0, qt.q8), (4.0, qt.q4), (2.0, qt.t2)):
        a(lambda v, f=f: v.tensor_scalar_mul(qt.cm[:, :w], qt.t1[:, :w], f))
        a(lambda v: v.tensor_scalar_add(qt.cm[:, :w], qt.cm[:, :w], MAGIC))
        a(lambda v: v.tensor_scalar_add(qt.cm[:, :w], qt.cm[:, :w], -MAGIC))
        a(lambda v, f=f, dst=dst: v.tensor_scalar_mul(dst[:, :w], qt.cm[:, :w], 1.0 / f))
    a(lambda v: v.tensor_mul(qt.cm[:, :w], qt.t1[:, :w], qt.t1[:, :w]))
    a(lambda v: v.tensor_scalar(qt.t1[:, :w], qt.cm[:, :w], 16.0, None, ALU.is_ge))
    a(lambda v: v.copy_predicated(qt.q4[:, :w], qt.t1[:, :w].bitcast(U32),
                                  qt.t2[:, :w]))
    a(lambda v: v.tensor_scalar(qt.t1[:, :w], qt.cm[:, :w], 4.0, None, ALU.is_ge))
    a(lambda v: v.copy_predicated(qt.q8[:, :w], qt.t1[:, :w].bitcast(U32),
                                  qt.q4[:, :w]))
    a(lambda v: v.tensor_mul(out_ap, qt.q8[:, :w], qt.scE[:, :w]))
    chain(ctx, fns, list(reads) + [qt.res], list(writes) + [qt.res])



def emit_l3_attn(ctx, xin, xout, W):
    return _emit_mha(ctx, xin, xout, W, wk="a3_tk", wq="a3_tq", wv="a3_tv",
                     rope=False, wo="a3_wu", tag="3a", causal=False,
                     pre_proj="a3_wd", wo_bias="a3_bu")


def emit_l1_attn(ctx, xin, xout, W):
    return _emit_mha(ctx, xin, xout, W, wk="a1_wk", wq="a1_wq", wv="a1_wv",
                     rope=True, wo="a1_wo", tag="1a")


def emit_l2_attn(ctx, xin, xout, W):
    return _emit_mha(ctx, xin, xout, W, wk="a2_wk", wq="a2_wq", wv="a2_wv",
                     rope=False, wo=None, tag="2a")


def _emit_mha(ctx, xin, xout, W, wk, wq, wv, rope, wo, tag, causal=True,
              pre_proj=None, wo_bias=None):
    nc = ctx.nc
    u = f"{tag}{ctx.uid()}_"
    NQT = T_FULL // 128
    NQG = T_FULL // TC
    ao_scr = None
    if wo is not None:
        ao_scr = nc.dram_tensor(u + "ao", [NDC, 128, T_FULL], BF16,
                                kind="Internal").ap()
    with scope(ctx) as sb:
        xnb = [sb(f"{u}xn{i}", [128, T_FULL], BF16) for i in range(NDC)]
        xnres = [f"{u}xn{i}" for i in range(NDC)]
        ln = LNTmp(ctx, sb, u)
        identb = sb(u + "idb", [128, 128], BF16)
        identf = sb(u + "idf", [128, 128], F32)
        maskQ = sb(u + "mq", [128, 4 * TC], BF16)
        maskT = sb(u + "mt", [128, 4 * TC], BF16)
        dma_in(ctx, identb[:, :], W["ident_bf"], u + "idb")
        dma_in(ctx, identf[:, :], W["ident_f32"], u + "idf")
        dma_in(ctx, maskQ[:, :], W["maskQ"], u + "mq")
        dma_in(ctx, maskT[:, :], W["maskT"], u + "mt")

        with scope(ctx) as sbx:
            xs = [sbx(f"{u}x{i}", [128, TC], F32) for i in range(NDC)]
            for tcx in range(NTC):
                t0 = tcx * TC
                rxl = [f"{u}x{i}" for i in range(NDC)]
                for i in range(NDC):
                    dma_in(ctx, xs[i][:, :], xin[i][:, t0:t0 + TC], rxl[i])
                emit_ln(ctx, ln, [xs[i][:, :] for i in range(NDC)], rxl,
                        [xnb[i][:, t0:t0 + TC] for i in range(NDC)], xnres)

        if pre_proj is not None:
            with scope(ctx) as sbd:
                wdt = [sbd(f"{u}wd{k}", [128, 1024], BF16) for k in range(NDC)]
                xdb = [sbd(f"{u}xd{i}", [128, T_FULL], BF16) for i in range(NDC)]
                for k in range(NDC):
                    dma_in(ctx, wdt[k][:, :], W[pre_proj][k], f"{u}wd{k}",
                           eng="gpsimd")
                for tg in range(NTC):
                    t0 = tg * TC
                    for oc in range(NDC):
                        bank = oc % 4
                        kacc(ctx, ctx.pb[bank][:, :TC],
                             [wdt[k][:, 128 * oc:128 * (oc + 1)] for k in range(NDC)],
                             [xnb[k][:, t0:t0 + TC] for k in range(NDC)],
                             [f"{u}wd{k}" for k in range(NDC)] + xnres,
                             [f"psum{bank}"])
                        vop(ctx, lambda v, oc=oc, t0=t0, bank=bank: v.tensor_copy(
                            xnb[oc][:, t0:t0 + TC] if False else xdb[oc][:, t0:t0 + TC],
                            ctx.pb[bank][:, :TC]),
                            [f"psum{bank}"], [f"{u}xd{oc}"])
                for oc in range(NDC):
                    vop(ctx, lambda v, oc=oc: v.tensor_copy(
                        xnb[oc][:, :], xdb[oc][:, :]),
                        [f"{u}xd{oc}"], [xnres[oc]])

        with scope(ctx) as sb2:
            qh = [sb2(f"{u}qh{h}", [128, T_FULL], BF16) for h in range(2)]
            vT = sb2(u + "vT", [128, TC], BF16)
            kx = [sb2(f"{u}kx{h}", [128, T_FULL], BF16) for h in range(2)]
            vtm = sb2(u + "vtm", [128, 32 * 256], BF16)
            qx = sb2(u + "qx", [128, TC], BF16)
            mrow = sb2(u + "mr", [65, T_FULL], BF16)
            mneg = sb2(u + "mn", [128, 1], F32)
            mcol = [sb2(f"{u}mc{h}", [128, 97], F32) for h in range(2)]
            rec = sb2(u + "rec", [1, TC], BF16)
            expt = [sb2(f"{u}expt{i}", [128, TC], BF16) for i in range(3)]
            recb = sb2(u + "recb", [64, TC], F32)
            aoc = sb2(u + "aoc", [128, T_FULL], BF16)
            xr = [sb2(f"{u}xr{i}", [128, TC], F32) for i in range(2)]
            yv = [sb2(f"{u}yv{i}", [128, TC], F32) for i in range(2)]
            wpk2 = [[sb2(f"{u}wpk{p}_{k}", [128, 128], BF16) for k in range(NDC)]
                    for p in range(2)]
            wpq2 = [[sb2(f"{u}wpq{p}_{k}", [128, 128], BF16) for k in range(NDC)]
                    for p in range(2)]
            wpv2 = [[sb2(f"{u}wpv{p}_{k}", [128, 128], BF16) for k in range(NDC)]
                    for p in range(2)]
            if rope:
                cB = sb2(u + "cB", [128, T_FULL], BF16)
                sB = sb2(u + "sB", [128, T_FULL], BF16)
                psw = sb2(u + "psw", [128, 128], BF16)
                praw = sb2(u + "praw", [128, TC], BF16)
                rt1 = sb2(u + "rt1", [128, TC], F32)
                rt2 = sb2(u + "rt2", [128, TC], F32)
                dma_in(ctx, cB[:, :], W["a1_cos"], u + "cB")
                dma_in(ctx, sB[:, :], W["a1_sin"], u + "sB")
                dma_in(ctx, psw[:, :], W["a1_pswap"], u + "psw")
            vt3 = vtm[:, :].rearrange("p (k c) -> p k c", c=256)
            vop(ctx, lambda v: v.memset(vtm[:, :], 0.0), [], [u + "vtm"])
            vop(ctx, lambda v: v.memset(vt3[:, :, 64:65], 1.0), [], [u + "vtm"])
            vop(ctx, lambda v: v.memset(vt3[:, :, 192:193], 1.0), [], [u + "vtm"])
            for h in range(2):
                vop(ctx, lambda v, h=h: v.memset(qh[h][64:128, :], 0.0),
                    [], [f"{u}qh{h}"])
                vop(ctx, lambda v, h=h: v.memset(kx[h][64:128, :], 0.0),
                    [], [f"{u}kx{h}"])
            vop(ctx, lambda v: v.memset(qx[64:128, :], 0.0), [], [u + "qx"])

            def load_pair(pr):
                p = pr % 2
                if pre_proj is not None:
                    dma_in(ctx, wpk2[p][0][:, :], W[wk][pr], f"{u}wpk{p}_0",
                           eng="gpsimd")
                    dma_in(ctx, wpq2[p][0][:, :], W[wq][pr], f"{u}wpq{p}_0",
                           eng="gpsimd")
                    dma_in(ctx, wpv2[p][0][:, :], W[wv][pr], f"{u}wpv{p}_0",
                           eng="gpsimd")
                else:
                    for k in range(NDC):
                        dma_in(ctx, wpk2[p][k][:, :],
                               W[wk][k][:, 128 * pr:128 * (pr + 1)],
                               f"{u}wpk{p}_{k}", eng="gpsimd")
                        dma_in(ctx, wpq2[p][k][:, :],
                               W[wq][k][:, 128 * pr:128 * (pr + 1)],
                               f"{u}wpq{p}_{k}", eng="gpsimd")
                        dma_in(ctx, wpv2[p][k][:, :],
                               W[wv][k][:, 128 * pr:128 * (pr + 1)],
                               f"{u}wpv{p}_{k}", eng="gpsimd")

            load_pair(0)
            for pr in range(8):
                pp = pr % 2
                wpk, wpq, wpv = wpk2[pp], wpq2[pp], wpv2[pp]
                if pr + 1 < 8:
                    load_pair(pr + 1)
                for h in range(2):
                    vop(ctx, lambda v, h=h: v.memset(kx[h][64:65, :], 1.0),
                        [], [f"{u}kx{h}"])
                for tg in range(NTC):
                    t0 = tg * TC
                    if pre_proj is not None:
                        def pmm(bank, wg, wres, pr=pr, t0=t0):
                            mm(ctx, [(ctx.pb[bank][:, :TC], wg[0][:, :],
                                      xnb[pr][:, t0:t0 + TC], True, True)],
                               [wres + "_0", xnres[pr]], [f"psum{bank}"])
                    else:
                        def pmm(bank, wg, wres, t0=t0):
                            kacc(ctx, ctx.pb[bank][:, :TC],
                                 [wg[k][:, :] for k in range(NDC)],
                                 [xnb[k][:, t0:t0 + TC] for k in range(NDC)],
                                 [f"{wres}_{k}" for k in range(NDC)] + xnres,
                                 [f"psum{bank}"])
                    def evac_heads(bank, dsts, dres, drows, t0=t0):
                        if not rope:
                            for h in range(2):
                                vop(ctx, lambda v, h=h: v.tensor_copy(
                                    dsts[h][0:drows, t0:t0 + TC],
                                    ctx.pb[bank][64 * h:64 * h + 64, :TC]),
                                    [f"psum{bank}"], [dres[h]])
                            return
                        vop(ctx, lambda v: v.tensor_copy(
                            praw[:, :], ctx.pb[bank][:, :TC]),
                            [f"psum{bank}"], [u + "praw"])
                        mm(ctx, [(ctx.pb[4][:, :TC], psw[:, :], praw[:, :],
                                  True, True)],
                           [u + "psw", u + "praw"], ["psum4"])
                        chain(ctx, [
                            lambda v: v.tensor_mul(rt1[:, :], praw[:, :],
                                                   cB[:, t0:t0 + TC]),
                            lambda v: v.tensor_mul(rt2[:, :], ctx.pb[4][:, :TC],
                                                   sB[:, t0:t0 + TC]),
                        ], [u + "praw", u + "cB", u + "sB", "psum4"],
                            [u + "rt"])
                        for h in range(2):
                            vop(ctx, lambda v, h=h: v.tensor_add(
                                dsts[h][0:drows, t0:t0 + TC],
                                rt1[64 * h:64 * h + 64, :],
                                rt2[64 * h:64 * h + 64, :]),
                                [u + "rt"], [dres[h]])

                    pmm(0, wpk, f"{u}wpk{pp}")
                    evac_heads(0, kx, [f"{u}kx{h}" for h in range(2)], 64)
                    pmm(1, wpq, f"{u}wpq{pp}")
                    evac_heads(1, qh, [f"{u}qh{h}" for h in range(2)], 64)
                    pmm(2, wpv, f"{u}wpv{pp}")
                    vop(ctx, lambda v: v.tensor_copy(vT[:, :TC], ctx.pb[2][:, :TC]),
                        ["psum2"], [u + "vT"])
                    for j in range(4):
                        kk = 4 * tg + j
                        transpose(ctx, ctx.pb[4][:, 0:128].bitcast(BF16)[:, 0:128],
                                  vT[:, 128 * j:128 * (j + 1)], identb[:, :],
                                  [u + "vT", u + "idb"], ["psum4"])
                        chain(ctx, [
                            lambda v, kk=kk: v.tensor_copy(
                                vt3[:, kk, 0:64], ctx.pb[4][:, 0:128].bitcast(BF16)[:, 0:64]),
                            lambda v, kk=kk: v.tensor_copy(
                                vt3[:, kk, 128:192], ctx.pb[4][:, 0:128].bitcast(BF16)[:, 64:128]),
                        ], ["psum4"], [u + "vtm"])

                def p1(h, qg):
                    hq = qh[h][0:128, :]
                    hk = kx[h][0:128, :]
                    for qt in range(4 * qg, 4 * qg + 4):
                        nkc = (qt // 4 + 1) if causal else NQG
                        for c in range(nkc):
                            klen = min(TC, 128 * (qt + 1) - TC * c) if causal else TC
                            bank = c % 2
                            mm(ctx, [(ctx.pb[bank][:, :klen],
                                      hq[:, 128 * qt:128 * (qt + 1)],
                                      hk[:, TC * c:TC * c + klen], True, True)],
                               [f"{u}qh{h}", f"{u}kx{h}"], [f"psum{bank}"])
                            if causal and c == nkc - 1:
                                j = qt - 4 * c
                                vop(ctx, lambda v, bank=bank, j=j, klen=klen:
                                    v.tensor_tensor(ctx.pb[bank][:, :klen],
                                                    ctx.pb[bank][:, :klen],
                                                    maskQ[:, TC * j:TC * j + klen],
                                                    ALU.add),
                                    [u + "mq", f"psum{bank}"], [f"psum{bank}"])
                            if c == 0:
                                vop(ctx, lambda v, bank=bank, klen=klen: v.tensor_reduce(
                                    mneg[:, :], ctx.pb[bank][:, :klen], axis=AX,
                                    op=ALU.max, negate=True),
                                    [f"psum{bank}"], [u + "mn"])
                            else:
                                chain(ctx, [
                                    lambda v, bank=bank, klen=klen: v.tensor_reduce(
                                        ln.tmp[:, 0:1], ctx.pb[bank][:, :klen], axis=AX,
                                        op=ALU.max, negate=True),
                                    lambda v: v.tensor_tensor(
                                        mneg[:, :], mneg[:, :], ln.tmp[:, 0:1], ALU.min),
                                ], [f"psum{bank}", u + "mn"], [u + "mn", u + "ln_tmpres"])
                        vop(ctx, lambda v, qt=qt, h=h: v.tensor_copy(
                            mcol[h][:, 32 * (qt % 4):32 * (qt % 4) + 1], mneg[:, :]),
                            [u + "mn"], [f"{u}mc{h}"])
                    transpose(ctx, ctx.pb[4][0:97, 0:128], mcol[h][:, :],
                              identf[:, :], [f"{u}mc{h}", u + "idf"], ["psum4"])
                    for j in range(4):
                        vop(ctx, lambda v, h=h, qg=qg, j=j: v.tensor_copy(
                            mrow[64 * h:64 * h + 1,
                                 TC * qg + 128 * j:TC * qg + 128 * (j + 1)],
                            ctx.pb[4][32 * j:32 * j + 1, 0:128]),
                            ["psum4"], [f"{u}mr{h}_{qg}"])

                def p2(h, qg):
                    obank = 5
                    chain(ctx, [
                        lambda v, h=h, qg=qg: v.tensor_copy(
                            qx[0:64, :], qh[h][0:64, TC * qg:TC * (qg + 1)]),
                        lambda v, h=h, qg=qg: v.tensor_copy(
                            qx[64:65, :],
                            mrow[64 * h:64 * h + 1, TC * qg:TC * (qg + 1)]),
                    ], [f"{u}qh{h}", f"{u}mr{h}_{qg}"], [u + "qx"])
                    nkc = 4 * (qg + 1) if causal else NQT
                    def emit_v(kc):
                        mm(ctx, [(ctx.pb[obank][0:128, :TC],
                                  vt3[:, kc, 128 * h:128 * (h + 1)],
                                  expt[kc % 3][:, :TC],
                                  kc == 0, kc == nkc - 1)],
                           [u + "vtm", f"{u}expt{kc % 3}"], [f"psum{obank}"])
                    for kc in range(nkc):
                        bank = 2 + kc % 2
                        mm(ctx, [(ctx.pb[bank][:, :TC],
                                  kx[h][:, 128 * kc:128 * (kc + 1)],
                                  qx[:, :], True, True)],
                           [f"{u}kx{h}", u + "qx"], [f"psum{bank}"])
                        j = kc - 4 * qg
                        if causal and j >= 0:
                            vop(ctx, lambda v, bank=bank, j=j: v.tensor_tensor(
                                ctx.pb[bank][:, :TC], ctx.pb[bank][:, :TC],
                                maskT[:, TC * j:TC * (j + 1)], ALU.add),
                                [u + "mt", f"psum{bank}"], [f"psum{bank}"])
                        vop(ctx, lambda s, bank=bank, kc=kc: s.activation(
                            expt[kc % 3][:, :TC], ctx.pb[bank][:, :TC],
                            AF.Exp, scale=0.125),
                            [f"psum{bank}"], [f"{u}expt{kc % 3}"], eng="scalar")
                        if kc >= 2:
                            emit_v(kc - 2)
                    for kc in range(max(nkc - 2, 0), nkc):
                        emit_v(kc)
                    chain(ctx, [
                        lambda v, obank=obank: v.reciprocal(
                            recb[0:1, :TC], ctx.pb[obank][64:65, :TC]),
                        lambda v: v.tensor_copy(rec[0:1, :TC], recb[0:1, :TC]),
                    ], [f"psum{obank}"], [u + "rec"])
                    mm(ctx, [(ctx.pb[6][0:64, :TC], ln.ones1_bf[0:1, 0:64],
                              rec[0:1, :TC], True, True)],
                       [u + "rec", u + "ln_ones1"], ["psum6"])
                    vop(ctx, lambda v: v.tensor_copy(recb[:, :TC],
                                                     ctx.pb[6][0:64, :TC]),
                        ["psum6"], [u + "recb"])
                    vop(ctx, lambda v, h=h, qg=qg, obank=obank: v.tensor_mul(
                        aoc[64 * h:64 * h + 64, TC * qg:TC * (qg + 1)],
                        ctx.pb[obank][0:64, :TC], recb[:, :TC]),
                        [f"psum{obank}", u + "recb"], [u + "aoc"])

                p1(0, 0)
                p1(1, 0)
                for qg in range(NQG):
                    if qg + 1 < NQG:
                        p1(0, qg + 1)
                        p1(1, qg + 1)
                    p2(0, qg)
                    p2(1, qg)

                if wo is None:
                    for tcx in range(NTC):
                        t0 = tcx * TC
                        r = tcx % 2
                        dma_in(ctx, xr[r][:, :], xin[pr][:, t0:t0 + TC],
                               f"{u}xr{r}")
                        vop(ctx, lambda v, t0=t0, r=r: v.tensor_add(
                            yv[r][:, :], xr[r][:, :], aoc[:, t0:t0 + TC]),
                            [f"{u}xr{r}", u + "aoc"], [f"{u}yv{r}"])
                        dma_out(ctx, xout[pr][:, t0:t0 + TC], yv[r][:, :],
                                [f"{u}yv{r}"], u + "xo")
                else:
                    for tcx in range(NTC):
                        t0 = tcx * TC
                        dma_out(ctx, ao_scr[pr][:, t0:t0 + TC], aoc[:, t0:t0 + TC],
                                [u + "aoc"], u + "aos")

        if wo is not None:
            with scope(ctx) as sb3:
                wot = [sb3(f"{u}wo{k}", [128, 1024], BF16) for k in range(NDC)]
                aot = [sb3(f"{u}aot{k}", [128, TC], BF16) for k in range(NDC)]
                xr2 = [sb3(f"{u}xr2_{i}", [128, TC], F32) for i in range(2)]
                yv2 = [sb3(f"{u}yv2_{i}", [128, TC], F32) for i in range(2)]
                for k in range(NDC):
                    dma_in(ctx, wot[k][:, :], W[wo][k], f"{u}wo{k}", eng="gpsimd")
                if wo_bias is not None:
                    wob = sb3(u + "wob", [128, 8], F32)
                    dma_in(ctx, wob[:, :], W[wo_bias], u + "wob")
                for tcx in range(NTC):
                    t0 = tcx * TC
                    for k in range(NDC):
                        dma_in(ctx, aot[k][:, :], ao_scr[k][:, t0:t0 + TC],
                               f"{u}aot{k}")
                    for oc in range(NDC):
                        bank = oc % 4
                        kacc(ctx, ctx.pb[bank][:, :TC],
                             [wot[k][:, 128 * oc:128 * (oc + 1)] for k in range(NDC)],
                             [aot[k][:, :] for k in range(NDC)],
                             [f"{u}wo{k}" for k in range(NDC)] +
                             [f"{u}aot{k}" for k in range(NDC)],
                             [f"psum{bank}"])
                        r = oc % 2
                        dma_in(ctx, xr2[r][:, :], xin[oc][:, t0:t0 + TC],
                               f"{u}xr2_{r}")
                        if wo_bias is not None:
                            vop(ctx, lambda v, bank=bank, oc=oc, r=r:
                                v.scalar_tensor_tensor(
                                    yv2[r][:, :], ctx.pb[bank][:, :TC],
                                    wob[:, oc:oc + 1], xr2[r][:, :], ALU.add,
                                    ALU.add),
                                [f"psum{bank}", f"{u}xr2_{r}", u + "wob"],
                                [f"{u}yv2_{r}"])
                        else:
                            vop(ctx, lambda v, bank=bank, r=r: v.tensor_add(
                                yv2[r][:, :], ctx.pb[bank][:, :TC], xr2[r][:, :]),
                                [f"psum{bank}", f"{u}xr2_{r}"], [f"{u}yv2_{r}"])
                        dma_out(ctx, xout[oc][:, t0:t0 + TC], yv2[r][:, :],
                                [f"{u}yv2_{r}"], u + "xo2")





def emit_l0_attn(ctx, xin, xout, W):
    nc = ctx.nc
    u = f"0a{ctx.uid()}_"
    DP = 1025
    with scope(ctx) as sb:
        ln = LNTmp(ctx, sb, u)
        identb = sb(u + "idb", [128, 128], BF16)
        maskA = sb(u + "mA", [128, 4 * TC], BF16)
        wq = [sb(f"{u}wq{k}", [128, 3072], BF16) for k in range(NDC)]
        wg = [sb(f"{u}wg{k}", [128, 1024], BF16) for k in range(NDC)]
        wp = [sb(f"{u}wp{k}", [128, 1024], BF16) for k in range(NDC)]
        bqkv = sb(u + "bqkv", [128, 24], F32)
        bg = sb(u + "bg", [128, 8], F32)
        bp = sb(u + "bp", [128, 8], F32)
        lg = sb(u + "lg", [128, 8], F32)
        lb = sb(u + "lb", [128, 8], F32)
        Sb = [sb(f"{u}Sb{k}", [128, DP], BF16) for k in range(NDC)]
        xs = [sb(f"{u}x{i}", [128, TC], F32) for i in range(NDC)]
        xn0 = [sb(f"{u}xn0{i}", [128, TC], BF16) for i in range(NDC)]
        xnin = [sb(f"{u}xi{i}", [128, TC], BF16) for i in range(NDC)]
        qT = [sb(f"{u}qT{i}", [128, TC], BF16) for i in range(NDC)]
        kT = [sb(f"{u}kT{i}", [128, TC], BF16) for i in range(NDC)]
        vT = [sb(f"{u}vT{i}", [128, TC], BF16) for i in range(NDC)]
        gatet = sb(u + "gatet", [128, TC], BF16)
        tmpa = sb(u + "tmpa", [128, TC], F32)
        tmpb = sb(u + "tmpb", [128, TC], BF16)
        tmpc = sb(u + "tmpc", [128, TC], BF16)
        vtm = sb(u + "vtm", [128, 4 * 1026], BF16)
        ktm = sb(u + "ktm", [128, 4 * 1024], BF16)
        aT = [sb(f"{u}aT{j}", [128, TC], BF16) for j in range(4)]
        oT = vT
        recb = sb(u + "recb", [128, TC], BF16)
        recr = sb(u + "recr", [1, TC], BF16)
        yv = [sb(f"{u}yv{i}", [128, TC], F32) for i in range(2)]

        dma_in(ctx, identb[:, :], W["ident_bf"], u + "idb")
        dma_in(ctx, maskA[:, :], W["maskA"], u + "mA")
        for k in range(NDC):
            dma_in(ctx, wq[k][:, :], W["a0_wqkv"][k], f"{u}wq{k}", eng="gpsimd")
            dma_in(ctx, wg[k][:, :], W["a0_wg"][k], f"{u}wg{k}", eng="gpsimd")
            dma_in(ctx, wp[k][:, :], W["a0_wp"][k], f"{u}wp{k}", eng="gpsimd")
        for nm, tile, src in (("bqkv", bqkv, "a0_bqkv"), ("bg", bg, "a0_bg"),
                              ("bp", bp, "a0_bp"), ("lg", lg, "a0_lg"),
                              ("lb", lb, "a0_lb")):
            dma_in(ctx, tile[:, :], W[src], u + nm)
        for k in range(NDC):
            vop(ctx, lambda v, k=k: v.memset(Sb[k][:, :], 0.0), [], [f"{u}Sb{k}"])
        vt4 = vtm[:, :].rearrange("p (j c) -> p j c", c=1026)
        vop(ctx, lambda v: v.memset(vt4[:, :, 1024:1025], 1.0), [], [u + "vtm"])

        xres = [f"{u}x{i}" for i in range(NDC)]
        n0res = [f"{u}xn0{i}" for i in range(NDC)]
        xires = [f"{u}xi{i}" for i in range(NDC)]

        for g in range(NTC):
            t0 = g * TC
            for i in range(NDC):
                dma_in(ctx, xs[i][:, :], xin[i][:, t0:t0 + TC], xres[i])
            emit_ln(ctx, ln, [xs[i][:, :] for i in range(NDC)], xres,
                    None, [r + "_d" for r in n0res],
                    out_f32=[xn0[i][:, :] for i in range(NDC)], out_f32_resl=n0res)
            emit_ln(ctx, ln, [xn0[i][:, :] for i in range(NDC)], n0res,
                    [xnin[i][:, :] for i in range(NDC)], xires,
                    affine=([lg[:, i:i + 1] for i in range(NDC)],
                            [lb[:, i:i + 1] for i in range(NDC)]),
                    affine_res=[u + "lg", u + "lb"])
            for i in range(NDC):
                kacc(ctx, ctx.pb[0][:, :TC],
                     [wg[k][:, 128 * i:128 * (i + 1)] for k in range(NDC)],
                     [xnin[k][:, :] for k in range(NDC)],
                     [f"{u}wg{k}" for k in range(NDC)] + xires, ["psum0"])
                vop(ctx, lambda sc, i=i: sc.activation(
                    gatet[:, :], ctx.pb[0][:, :TC], AF.Sigmoid,
                    bias=bg[:, i:i + 1]),
                    ["psum0", u + "bg"], [u + "gatet"], eng="scalar")
                kacc(ctx, ctx.pb[1][:, :TC],
                     [wq[k][:, 128 * i:128 * (i + 1)] for k in range(NDC)],
                     [xnin[k][:, :] for k in range(NDC)],
                     [f"{u}wq{k}" for k in range(NDC)] + xires, ["psum1"])
                zb = ln.eps[0:1, 0:1]
                chain(ctx, [
                    lambda v, i=i: v.tensor_scalar(tmpa[:, :], ctx.pb[1][:, :TC],
                                                   bqkv[:, i:i + 1], 0.0, ALU.add,
                                                   ALU.min),
                    lambda v, i=i: v.tensor_scalar_min(tmpa[:, :], tmpa[:, :], 0.0),
                ], ["psum1", u + "bqkv"], [u + "tmpa"])
                vop(ctx, lambda sc: sc.activation(tmpb[:, :], tmpa[:, :], AF.Exp),
                    [u + "tmpa"], [u + "tmpb"], eng="scalar")
                chain(ctx, [
                    lambda v, i=i: v.tensor_scalar(tmpc[:, :], ctx.pb[1][:, :TC],
                                                   bqkv[:, i:i + 1], 0.0, ALU.add,
                                                   ALU.max),
                    lambda v, i=i: v.tensor_scalar_max(tmpc[:, :], tmpc[:, :], 0.0),
                    lambda v, i=i: v.tensor_add(qT[i][:, :], tmpb[:, :], tmpc[:, :]),
                ], ["psum1", u + "bqkv", u + "tmpb"], [f"{u}qT{i}", u + "tmpc"])
                kacc(ctx, ctx.pb[2][:, :TC],
                     [wq[k][:, 1024 + 128 * i:1024 + 128 * (i + 1)] for k in range(NDC)],
                     [xnin[k][:, :] for k in range(NDC)],
                     [f"{u}wq{k}" for k in range(NDC)] + xires, ["psum2"])
                chain(ctx, [
                    lambda v, i=i: v.scalar_tensor_tensor(
                        tmpa[:, :], ctx.pb[2][:, :TC], bqkv[:, 8 + i:9 + i],
                        gatet[:, :], ALU.add, ALU.mult),
                    lambda v: v.tensor_scalar_min(tmpb[:, :], tmpa[:, :], 0.0),
                ], ["psum2", u + "bqkv", u + "gatet"], [u + "tmpa", u + "tmpb"])
                vop(ctx, lambda sc: sc.activation(tmpb[:, :], tmpb[:, :], AF.Exp),
                    [u + "tmpb"], [u + "tmpb"], eng="scalar")
                chain(ctx, [
                    lambda v: v.tensor_scalar_max(tmpc[:, :], tmpa[:, :], 0.0),
                    lambda v, i=i: v.tensor_add(kT[i][:, :], tmpb[:, :], tmpc[:, :]),
                ], [u + "tmpa", u + "tmpb"], [f"{u}kT{i}", u + "tmpc"])
                kacc(ctx, ctx.pb[3][:, :TC],
                     [wq[k][:, 2048 + 128 * i:2048 + 128 * (i + 1)] for k in range(NDC)],
                     [xnin[k][:, :] for k in range(NDC)],
                     [f"{u}wq{k}" for k in range(NDC)] + xires, ["psum3"])
                vop(ctx, lambda v, i=i: v.tensor_scalar(
                    vT[i][:, :], ctx.pb[3][:, :TC], bqkv[:, 16 + i:17 + i], None,
                    ALU.add),
                    ["psum3", u + "bqkv"], [f"{u}vT{i}"])
            for j in range(4):
                for dc in range(NDC):
                    transpose(ctx, ctx.pb[4][:, 0:128].bitcast(BF16)[:, 0:128],
                              vT[dc][:, 128 * j:128 * (j + 1)], identb[:, :],
                              [f"{u}vT{dc}", u + "idb"], ["psum4"])
                    vop(ctx, lambda v, j=j, dc=dc: v.tensor_copy(
                        vt4[:, j, 128 * dc:128 * (dc + 1)],
                        ctx.pb[4][:, 0:128].bitcast(BF16)[:, 0:128]),
                        ["psum4"], [u + "vtm"])
                    transpose(ctx, ctx.pb[4][:, 0:128].bitcast(BF16)[:, 0:128],
                              kT[dc][:, 128 * j:128 * (j + 1)], identb[:, :],
                              [f"{u}kT{dc}", u + "idb"], ["psum4"])
                    vop(ctx, lambda v, j=j, dc=dc: v.tensor_copy(
                        ktm[:, 1024 * j + 128 * dc:1024 * j + 128 * (dc + 1)],
                        ctx.pb[4][:, 0:128].bitcast(BF16)[:, 0:128]),
                        ["psum4"], [u + "ktm"])
            for j in range(4):
                kacc(ctx, ctx.pb[3][:, :TC],
                     [kT[dc][:, 128 * j:128 * (j + 1)] for dc in range(NDC)],
                     [qT[dc][:, :] for dc in range(NDC)],
                     [f"{u}kT{dc}" for dc in range(NDC)] +
                     [f"{u}qT{dc}" for dc in range(NDC)], ["psum3"])
                vop(ctx, lambda v, j=j: v.tensor_mul(
                    aT[j][:, :], ctx.pb[3][:, :TC], maskA[:, TC * j:TC * (j + 1)]),
                    ["psum3", u + "mA"], [f"{u}aT{j}"])
            for m in range(NDC):
                bank = m % 2
                plist = []
                for j in range(4):
                    plist.append((ctx.pb[bank][:, :TC],
                                  vt4[:, j, 128 * m:128 * (m + 1)], aT[j][:, :],
                                  j == 0, False))
                for dc in range(NDC):
                    plist.append((ctx.pb[bank][:, :TC],
                                  Sb[dc][:, 128 * m:128 * (m + 1)], qT[dc][:, :],
                                  False, dc == NDC - 1))
                mm(ctx, plist,
                   [u + "vtm"] + [f"{u}aT{j}" for j in range(4)] +
                   [f"{u}Sb{dc}" for dc in range(NDC)] +
                   [f"{u}qT{dc}" for dc in range(NDC)], [f"psum{bank}"])
                if m == 0:
                    plist = []
                    for j in range(4):
                        plist.append((ctx.pb[2][0:1, :TC],
                                      vt4[:, j, 1024:1025], aT[j][:, :],
                                      j == 0, False))
                    for dc in range(NDC):
                        plist.append((ctx.pb[2][0:1, :TC],
                                      Sb[dc][:, 1024:1025], qT[dc][:, :],
                                      False, dc == NDC - 1))
                    mm(ctx, plist,
                       [u + "vtm"] + [f"{u}aT{j}" for j in range(4)] +
                       [f"{u}Sb{dc}" for dc in range(NDC)] +
                       [f"{u}qT{dc}" for dc in range(NDC)], ["psum2"])
                    chain(ctx, [
                        lambda v: v.tensor_scalar_add(recr[0:1, :].bitcast(F32)[0:1, :TC // 2] if False else tmpa[0:1, :], ctx.pb[2][0:1, :TC], 1e-6),
                        lambda v: v.reciprocal(tmpa[0:1, :], tmpa[0:1, :]),
                        lambda v: v.tensor_copy(recr[0:1, :], tmpa[0:1, :]),
                    ], ["psum2"], [u + "recr", u + "tmpa"])
                    mm(ctx, [(ctx.pb[2][:, :TC], ln.ones1_bf[0:1, :], recr[0:1, :],
                              True, True)],
                       [u + "recr", u + "ln_ones1"], ["psum2"])
                    vop(ctx, lambda v: v.tensor_copy(recb[:, :], ctx.pb[2][:, :TC]),
                        ["psum2"], [u + "recb"])
                vop(ctx, lambda v, m=m, bank=bank: v.tensor_mul(
                    oT[m][:, :], ctx.pb[bank][:, :TC], recb[:, :]),
                    [f"psum{bank}", u + "recb"], [f"{u}vT{m}"])
            for oc in range(NDC):
                bank = oc % 2
                kacc(ctx, ctx.pb[bank][:, :TC],
                     [wp[k][:, 128 * oc:128 * (oc + 1)] for k in range(NDC)],
                     [oT[k][:, :] for k in range(NDC)],
                     [f"{u}wp{k}" for k in range(NDC)] +
                     [f"{u}vT{k}" for k in range(NDC)], [f"psum{bank}"])
                r = oc % 2
                vop(ctx, lambda v, oc=oc, bank=bank, r=r: v.scalar_tensor_tensor(
                    yv[r][:, :], ctx.pb[bank][:, :TC], bp[:, oc:oc + 1],
                    xs[oc][:, :], ALU.add, ALU.add),
                    [f"psum{bank}", u + "bp", xres[oc]], [f"{u}yv{r}"])
                dma_out(ctx, xout[oc][:, t0:t0 + TC], yv[r][:, :],
                        [f"{u}yv{r}"], u + "xo")
            for dc in range(NDC):
                for (c0, c1, bank) in ((0, 512, 4), (512, 1024, 5), (1024, 1025, 6)):
                    plist = []
                    for j in range(4):
                        plist.append((ctx.pb[bank][:, 0:c1 - c0],
                                      ktm[:, 1024 * j + 128 * dc:1024 * j + 128 * (dc + 1)],
                                      vt4[:, j, c0:c1], j == 0, j == 3))
                    mm(ctx, plist, [u + "ktm", u + "vtm"], [f"psum{bank}"])
                chain(ctx, [
                    lambda v, dc=dc: v.tensor_add(Sb[dc][:, 0:512], Sb[dc][:, 0:512],
                                                  ctx.pb[4][:, :TC]),
                    lambda v, dc=dc: v.tensor_add(Sb[dc][:, 512:1024],
                                                  Sb[dc][:, 512:1024],
                                                  ctx.pb[5][:, :TC]),
                    lambda v, dc=dc: v.tensor_add(Sb[dc][:, 1024:1025],
                                                  Sb[dc][:, 1024:1025],
                                                  ctx.pb[6][:, 0:1]),
                ], ["psum4", "psum5", "psum6"], [f"{u}Sb{dc}"])



def emit_l0_ffn(ctx, xin, xout, W):
    nc = ctx.nc
    u = f"0f{ctx.uid()}_"
    HT = T_FULL // 2
    NHC = HT // TC
    with scope(ctx) as sb:
        xnbh = [sb(f"{u}xh{i}", [128, HT], BF16) for i in range(NDC)]
        xnres = [f"{u}xh{i}" for i in range(NDC)]
        yac = [sb(f"{u}ya{i}", [128, HT], BF16) for i in range(NDC)]
        cwf = sb(u + "cwf", [16, HT], BF16)
        ln = LNTmp(ctx, sb, u)
        identf = sb(u + "idf", [128, 128], F32)
        rwt = [sb(f"{u}rwt{k}", [128, 16], F32) for k in range(NDC)]
        e16 = sb(u + "e16", [16, 16 * 128], BF16)
        ebias = sb(u + "eb", [16, 1], F32)
        scf = sb(u + "scf", [16, TC], F32)
        sbf_ = sb(u + "sbf", [16, TC], F32)
        sc_tm = sb(u + "sctm", [128, 16], F32)
        sr_tm = sb(u + "srtm", [128, 16], F32)
        gs = sb(u + "gs", [128, 4], F32)
        gt4 = sb(u + "gt4", [128, 4], F32)
        gcnt4 = sb(u + "gcnt4", [128, 4], F32)
        mt = sb(u + "mt16", [128, 16], F32)
        t16 = sb(u + "t16", [128, 16], F32)
        c1 = sb(u + "c1", [128, 16], F32)
        c2 = sb(u + "c2", [128, 16], F32)
        gcol = sb(u + "gcol", [128, 1], F32)
        gwb = [[sb(f"{u}gwb{p}_{k}", [128, 512], BF16) for k in range(NDC)]
               for p in range(2)]
        uwb = [[sb(f"{u}uwb{p}_{k}", [128, 512], BF16) for k in range(NDC)]
               for p in range(2)]
        dwb = [[sb(f"{u}dwb{p}_{k}", [128, 1024], BF16) for k in range(4)]
               for p in range(2)]
        hsl = [sb(f"{u}hsl{k}", [128, TC], BF16) for k in range(8)]
        cwb = sb(u + "cwb", [128, HT], BF16)
        sil = sb(u + "sil", [128, TC], F32)
        dma_in(ctx, identf[:, :], W["ident_f32"], u + "idf")
        dma_in(ctx, e16[:, :], W["E16"], u + "e16")
        dma_in(ctx, ebias[:, :], W["f0_eb"], u + "eb")
        for k in range(NDC):
            dma_in(ctx, rwt[k][:, :], W["f0_rw"][k], f"{u}rwt{k}")

        for half in range(2):
            hb = half * HT
            with scope(ctx) as sbr:
                xs = [sbr(f"{u}x{i}", [128, TC], F32) for i in range(NDC)]
                xnf = [sbr(f"{u}xnf{i}", [128, TC], F32) for i in range(NDC)]
                for tcx in range(NHC):
                    t0 = hb + tcx * TC
                    th = tcx * TC
                    rxl = [f"{u}x{i}" for i in range(NDC)]
                    rnl = [f"{u}xnf{i}" for i in range(NDC)]
                    for i in range(NDC):
                        dma_in(ctx, xs[i][:, :], xin[i][:, t0:t0 + TC], rxl[i])
                    emit_ln(ctx, ln, [xs[i][:, :] for i in range(NDC)], rxl,
                            [xnbh[i][:, th:th + TC] for i in range(NDC)], xnres,
                            out_f32=[xnf[i][:, :] for i in range(NDC)],
                            out_f32_resl=rnl)
                    kacc(ctx, ctx.pb[0][0:16, :TC],
                         [rwt[k][:, :] for k in range(NDC)],
                         [xnf[k][:, :] for k in range(NDC)],
                         [f"{u}rwt{k}" for k in range(NDC)] + rnl, ["psum0"])
                    chain(ctx, [
                        lambda sc: sc.activation(scf[0:16, :], ctx.pb[0][0:16, :TC],
                                                 AF.Sigmoid),
                    ], ["psum0"], [u + "scf"], eng="scalar")
                    vop(ctx, lambda v: v.tensor_scalar(
                        sbf_[0:16, :], scf[0:16, :], ebias[0:16, 0:1], None, ALU.add),
                        [u + "scf", u + "eb"], [u + "sbf"])
                    for tt in range(TC // 128):
                        transpose(ctx, ctx.pb[4][0:128, 0:16],
                                  scf[0:16, 128 * tt:128 * (tt + 1)],
                                  identf[0:16, 0:16], [u + "scf", u + "idf"], ["psum4"])
                        vop(ctx, lambda v: v.tensor_copy(sr_tm[:, :],
                                                         ctx.pb[4][0:128, 0:16]),
                            ["psum4"], [u + "srtm"])
                        transpose(ctx, ctx.pb[4][0:128, 0:16],
                                  sbf_[0:16, 128 * tt:128 * (tt + 1)],
                                  identf[0:16, 0:16],
                                  [u + "sbf", u + "idf", u + "srtm"], ["psum4"])
                        vop(ctx, lambda v: v.tensor_copy(sc_tm[:, :],
                                                         ctx.pb[4][0:128, 0:16]),
                            ["psum4"], [u + "sctm"])
                        sc3 = sc_tm[:, :].rearrange("p (g j) -> p g j", j=4)
                        fns = []
                        first = [True]
                        for a in range(4):
                            for b in range(a + 1, 4):
                                if first[0]:
                                    fns.append(lambda v, a=a, b=b: v.tensor_add(
                                        gs[:, :], sc3[:, :, a], sc3[:, :, b]))
                                    first[0] = False
                                else:
                                    fns.append(lambda v, a=a, b=b: v.tensor_add(
                                        gt4[:, :], sc3[:, :, a], sc3[:, :, b]))
                                    fns.append(lambda v: v.tensor_max(
                                        gs[:, :], gs[:, :], gt4[:, :]))
                        for g in range(4):
                            fns.append(lambda v, g=g: v.tensor_scalar(
                                gt4[:, :], gs[:, :], gs[:, g:g + 1], None,
                                ALU.is_gt, ALU.add, accum_out=gcnt4[:, g:g + 1]))
                        fns.append(lambda v: v.tensor_scalar(
                            gcnt4[:, :], gcnt4[:, :], 2.0, None, ALU.is_lt))
                        fns.append(lambda v: v.tensor_copy(
                            mt[:, :].rearrange("p (g j) -> p g j", j=4),
                            gcnt4[:, :].unsqueeze(2).broadcast_to([128, 4, 4])))
                        fns.append(lambda v: v.tensor_mul(mt[:, :], mt[:, :],
                                                          sc_tm[:, :]))
                        for e in range(16):
                            fns.append(lambda v, e=e: v.tensor_scalar(
                                t16[:, :], mt[:, :], mt[:, e:e + 1], None,
                                ALU.is_gt, ALU.add, accum_out=c1[:, e:e + 1]))
                        fns.append(lambda v: v.memset(c2[:, 0:1], 0.0))
                        for e in range(1, 16):
                            fns.append(lambda v, e=e: v.tensor_scalar(
                                t16[:, 0:e], mt[:, 0:e], mt[:, e:e + 1], None,
                                ALU.is_equal, ALU.add, accum_out=c2[:, e:e + 1]))
                        fns.append(lambda v: v.tensor_add(c1[:, :], c1[:, :],
                                                          c2[:, :]))
                        fns.append(lambda v: v.tensor_scalar(
                            c1[:, :], c1[:, :], 4.0, None, ALU.is_lt))
                        fns.append(lambda v: v.tensor_mul(t16[:, :], sr_tm[:, :],
                                                          c1[:, :]))
                        fns.append(lambda v: v.reduce_sum(gcol[:, 0:1], t16[:, :],
                                                          axis=AX))
                        fns.append(lambda v: v.tensor_scalar_add(gcol[:, 0:1],
                                                                 gcol[:, 0:1], 1e-20))
                        fns.append(lambda v: v.reciprocal(gcol[:, :], gcol[:, :]))
                        fns.append(lambda v: v.tensor_scalar_mul(gcol[:, 0:1],
                                                                 gcol[:, 0:1], 2.5))
                        fns.append(lambda v: v.tensor_scalar(
                            t16[:, :], t16[:, :], gcol[:, 0:1], None, ALU.mult))
                        chain(ctx, fns, [u + "sctm", u + "srtm"],
                              [u + "t16", u + "gcol", u + "c1", u + "c2",
                               u + "gs", u + "gt4", u + "gcnt4", u + "mt16"])
                        transpose(ctx, ctx.pb[4][0:16, 0:128], t16[:, :],
                                  identf[:, :], [u + "t16", u + "idf"], ["psum4"])
                        vop(ctx, lambda v, tcx=tcx, tt=tt: v.tensor_copy(
                            cwf[0:16, tcx * TC + 128 * tt:tcx * TC + 128 * (tt + 1)],
                            ctx.pb[4][0:16, 0:128]),
                            ["psum4"], [u + "cwf"])

            for i in range(NDC):
                vop(ctx, lambda v, i=i: v.memset(yac[i][:, :], 0.0), [],
                    [f"{u}ya{i}"])

            def load_routed(e):
                p = e % 2
                for k in range(NDC):
                    dma_in(ctx, gwb[p][k][:, :], W["f0_gw"][e][k], f"{u}gwb{p}_{k}",
                           eng="gpsimd")
                    dma_in(ctx, uwb[p][k][:, :], W["f0_uw"][e][k], f"{u}uwb{p}_{k}",
                           eng="gpsimd")
                for k in range(4):
                    dma_in(ctx, dwb[p][k][:, :], W["f0_dw"][e][k], f"{u}dwb{p}_{k}",
                           eng="gpsimd")

            def run_expert(p, nhc, gtiles, utiles, dtiles, gres, ures, dres,
                           use_cw, e):
                if use_cw:
                    for tcx in range(NHC):
                        mm(ctx, [(ctx.pb[4][:, :TC], e16[:, 128 * e:128 * (e + 1)],
                                  cwf[0:16, tcx * TC:(tcx + 1) * TC], True, True)],
                           [u + "e16", u + "cwf"], ["psum4"])
                        vop(ctx, lambda v, tcx=tcx: v.tensor_copy(
                            cwb[:, tcx * TC:(tcx + 1) * TC], ctx.pb[4][:, :TC]),
                            ["psum4"], [u + "cwb"])
                for tcx in range(NHC):
                    th = tcx * TC
                    for o4 in range(nhc):
                        kacc(ctx, ctx.pb[0][:, :TC],
                             [gtiles[o4][k] for k in range(NDC)],
                             [xnbh[k][:, th:th + TC] for k in range(NDC)],
                             gres + xnres, ["psum0"])
                        kacc(ctx, ctx.pb[1][:, :TC],
                             [utiles[o4][k] for k in range(NDC)],
                             [xnbh[k][:, th:th + TC] for k in range(NDC)],
                             ures + xnres, ["psum1"])
                        vop(ctx, lambda sc: sc.activation(
                            sil[:, :], ctx.pb[0][:, :TC], AF.Silu),
                            ["psum0"], [u + "sil"], eng="scalar")
                        if use_cw:
                            chain(ctx, [
                                lambda v, o4=o4: v.tensor_mul(
                                    hsl[o4][:, :], sil[:, :], ctx.pb[1][:, :TC]),
                                lambda v, o4=o4, th=th: v.tensor_mul(
                                    hsl[o4][:, :], hsl[o4][:, :],
                                    cwb[:, th:th + TC]),
                            ], [u + "sil", "psum1", u + "cwb"], [f"{u}hsl{o4}"])
                        else:
                            vop(ctx, lambda v, o4=o4: v.tensor_mul(
                                hsl[o4][:, :], sil[:, :], ctx.pb[1][:, :TC]),
                                [u + "sil", "psum1"], [f"{u}hsl{o4}"])
                    for oc in range(NDC):
                        bank = 2 + oc % 2
                        kacc(ctx, ctx.pb[bank][:, :TC],
                             [dtiles[k][:, 128 * oc:128 * (oc + 1)]
                              for k in range(nhc)],
                             [hsl[k][:, :] for k in range(nhc)],
                             dres + [f"{u}hsl{k}" for k in range(nhc)],
                             [f"psum{bank}"])
                        vop(ctx, lambda v, oc=oc, th=th, bank=bank: v.tensor_add(
                            yac[oc][:, th:th + TC], yac[oc][:, th:th + TC],
                            ctx.pb[bank][:, :TC]),
                            [f"psum{bank}", f"{u}ya{oc}"], [f"{u}ya{oc}"])

            load_routed(0)
            for e in range(16):
                p = e % 2
                if e + 1 < 16:
                    load_routed(e + 1)
                gtiles = [[gwb[p][k][:, 128 * o4:128 * (o4 + 1)] for k in range(NDC)]
                          for o4 in range(4)]
                utiles = [[uwb[p][k][:, 128 * o4:128 * (o4 + 1)] for k in range(NDC)]
                          for o4 in range(4)]
                run_expert(p, 4, gtiles, utiles, dwb[p],
                           [f"{u}gwb{p}_{k}" for k in range(NDC)],
                           [f"{u}uwb{p}_{k}" for k in range(NDC)],
                           [f"{u}dwb{p}_{k}" for k in range(4)], True, e)
            for k in range(NDC):
                dma_in(ctx, gwb[0][k][:, :], W["f0_sg"][k][:, 0:512],
                       f"{u}gwb0_{k}", eng="gpsimd")
                dma_in(ctx, gwb[1][k][:, :], W["f0_sg"][k][:, 512:1024],
                       f"{u}gwb1_{k}", eng="gpsimd")
                dma_in(ctx, uwb[0][k][:, :], W["f0_su"][k][:, 0:512],
                       f"{u}uwb0_{k}", eng="gpsimd")
                dma_in(ctx, uwb[1][k][:, :], W["f0_su"][k][:, 512:1024],
                       f"{u}uwb1_{k}", eng="gpsimd")
            for k in range(4):
                dma_in(ctx, dwb[0][k][:, :], W["f0_sd"][k], f"{u}dwb0_{k}",
                       eng="gpsimd")
                dma_in(ctx, dwb[1][k][:, :], W["f0_sd"][4 + k], f"{u}dwb1_{k}",
                       eng="gpsimd")
            gtiles = [[gwb[o8 // 4][k][:, 128 * (o8 % 4):128 * (o8 % 4 + 1)]
                       for k in range(NDC)] for o8 in range(8)]
            utiles = [[uwb[o8 // 4][k][:, 128 * (o8 % 4):128 * (o8 % 4 + 1)]
                       for k in range(NDC)] for o8 in range(8)]
            dtiles = dwb[0] + dwb[1]
            run_expert(0, 8, gtiles, utiles, dtiles,
                       [f"{u}gwb{pp}_{k}" for pp in range(2) for k in range(NDC)],
                       [f"{u}uwb{pp}_{k}" for pp in range(2) for k in range(NDC)],
                       [f"{u}dwb{pp}_{k}" for pp in range(2) for k in range(4)],
                       False, 0)

            with scope(ctx) as sbf:
                xr = [sbf(f"{u}fxr{i}", [128, TC], F32) for i in range(2)]
                yv = [sbf(f"{u}fyv{i}", [128, TC], F32) for i in range(2)]
                for tcx in range(NHC):
                    t0 = hb + tcx * TC
                    th = tcx * TC
                    for i in range(NDC):
                        r = i % 2
                        dma_in(ctx, xr[r][:, :], xin[i][:, t0:t0 + TC],
                               f"{u}fxr{r}")
                        vop(ctx, lambda v, i=i, th=th, r=r, xr=xr, yv=yv:
                            v.tensor_add(yv[r][:, :], xr[r][:, :],
                                         yac[i][:, th:th + TC]),
                            [f"{u}fxr{r}", f"{u}ya{i}"], [f"{u}fyv{r}"])
                        dma_out(ctx, xout[i][:, t0:t0 + TC], yv[r][:, :],
                                [f"{u}fyv{r}"], u + "fxo")



def emit_l1_ffn(ctx, xin, xout, W):
    nc = ctx.nc
    u = f"1f{ctx.uid()}_"
    QT = 1024
    NQ = T_FULL // QT
    NQC = QT // TC
    RMS_EPS = 1.1920929e-07
    with scope(ctx) as sb:
        xrm = [sb(f"{u}xr{i}", [128, QT], BF16) for i in range(NDC)]
        xrmres = [f"{u}xr{i}" for i in range(NDC)]
        yac = [sb(f"{u}ya{i}", [128, QT], BF16) for i in range(NDC)]
        cwf = sb(u + "cwf", [14, QT], BF16)
        ln = LNTmp(ctx, sb, u)
        identf = sb(u + "idf", [128, 128], F32)
        rwt = [sb(f"{u}rwt{k}", [128, 14], F32) for k in range(NDC)]
        e14 = sb(u + "e14", [14, 14 * 128], BF16)
        gt = sb(u + "gt", [128, 16], F32)
        gtmp = sb(u + "gtmp", [128, 16], F32)
        gcnt = sb(u + "gcnt", [128, 16], F32)
        gcol = sb(u + "gcol", [128, 1], F32)
        gcol2 = sb(u + "gcol2", [128, 1], F32)
        lgt = sb(u + "lgt", [14, TC], F32)
        xs = [sb(f"{u}x{i}", [128, TC], F32) for i in range(NDC)]
        xnf = [sb(f"{u}xnf{i}", [128, TC], F32) for i in range(NDC)]
        sq = sb(u + "sq", [128, TC], BF16)
        r2r = sb(u + "r2r", [1, TC], F32)
        r2b = sb(u + "r2b", [1, TC], BF16)
        r2bc = sb(u + "r2bc", [128, TC], BF16)
        epst = sb(u + "epst", [1, 1], F32)
        w1t = [[sb(f"{u}w1t{p}_{k}", [128, TC], BF16) for k in range(NDC)]
               for p in range(2)]
        w2t = [[sb(f"{u}w2t{p}_{k}", [128, 1024], BF16) for k in range(8)]
               for p in range(2)]
        rwe = [sb(f"{u}rwe{p}", [128, 8], F32) for p in range(2)]
        hsl = [sb(f"{u}hsl{k}", [128, QT], BF16) for k in range(16)]
        cwb = sb(u + "cwb", [128, QT], BF16)
        sil = sb(u + "sil", [128, TC], F32)
        fxr = [sb(f"{u}fxr{i}", [128, TC], F32) for i in range(2)]
        fyv = [sb(f"{u}fyv{i}", [128, TC], F32) for i in range(2)]

        dma_in(ctx, identf[:, :], W["ident_f32"], u + "idf")
        dma_in(ctx, e14[:, :], W["E14"], u + "e14")
        vop(ctx, lambda v: v.memset(epst[:, :], RMS_EPS), [], [u + "epst"])
        for k in range(NDC):
            dma_in(ctx, rwt[k][:, :], W["f1_rw"][k], f"{u}rwt{k}")

        def wsrc(e):
            if e < 14:
                return W["f1_w1"][e], W["f1_w2"][e], W["f1_rms"][e]
            return W["f1_w1_s"][e - 14], W["f1_w2_s"][e - 14], W["f1_rms_s"][e - 14]

        sli = [0]

        def load_w1_slice(e, hc):
            p = sli[0] % 2
            sli[0] += 1
            w1d, _, _ = wsrc(e)
            for k in range(NDC):
                dma_in(ctx, w1t[p][k][:, :], w1d[k][:, TC * hc:TC * (hc + 1)],
                       f"{u}w1t{p}_{k}", eng="gpsimd")
                vop(ctx, lambda v, k=k, p=p, e=e: v.tensor_scalar(
                    w1t[p][k][:, :], w1t[p][k][:, :], rwe[e % 2][:, k:k + 1], None,
                    ALU.mult),
                    [f"{u}w1t{p}_{k}", f"{u}rwe{e % 2}"], [f"{u}w1t{p}_{k}"])
            return p

        w2i = [0]

        def load_w2_half(e, half):
            p = w2i[0] % 2
            w2i[0] += 1
            _, w2d, _ = wsrc(e)
            for k in range(8):
                dma_in(ctx, w2t[p][k][:, :], w2d[8 * half + k], f"{u}w2t{p}_{k}",
                       eng="gpsimd")
            return p

        def load_rms(e):
            _, _, rmsd = wsrc(e)
            dma_in(ctx, rwe[e % 2][:, :], rmsd, f"{u}rwe{e % 2}")

        for q in range(NQ):
            qb = q * QT
            for tcx in range(NQC):
                t0 = qb + tcx * TC
                th = tcx * TC
                rxl = [f"{u}x{i}" for i in range(NDC)]
                rnl = [f"{u}xnf{i}" for i in range(NDC)]
                for i in range(NDC):
                    dma_in(ctx, xs[i][:, :], xin[i][:, t0:t0 + TC], rxl[i])
                emit_ln(ctx, ln, [xs[i][:, :] for i in range(NDC)], rxl,
                        None, rxl, out_f32=[xnf[i][:, :] for i in range(NDC)],
                        out_f32_resl=rnl)
                for i in range(NDC):
                    vop(ctx, lambda sc, i=i: sc.activation(
                        sq[:, :], xnf[i][:, :], AF.Square),
                        [rnl[i]], [u + "sq"], eng="scalar")
                    mm(ctx, [(ctx.pb[6][0:1, :TC], ln.ones_bf[:, :], sq[:, :],
                              i == 0, i == NDC - 1)],
                       [u + "sq", u + "ln_ones"], ["psum6"])
                chain(ctx, [
                    lambda v: v.tensor_scalar_mul(r2r[0:1, :], ctx.pb[6][0:1, :TC],
                                                  1.0 / D),
                ], ["psum6"], [u + "r2r"])
                vop(ctx, lambda sc: sc.activation(r2r[0:1, :], r2r[0:1, :],
                                                  AF.Sqrt, bias=epst[0:1, 0:1]),
                    [u + "r2r", u + "epst"], [u + "r2r"], eng="scalar")
                chain(ctx, [
                    lambda v: v.reciprocal(r2r[0:1, :], r2r[0:1, :]),
                    lambda v: v.tensor_copy(r2b[0:1, :], r2r[0:1, :]),
                ], [u + "r2r"], [u + "r2b"])
                mm(ctx, [(ctx.pb[5][:, :TC], ln.ones1_bf[0:1, :], r2b[0:1, :],
                          True, True)],
                   [u + "r2b", u + "ln_ones1"], ["psum5"])
                vop(ctx, lambda v: v.tensor_copy(r2bc[:, :], ctx.pb[5][:, :TC]),
                    ["psum5"], [u + "r2bc"])
                for i in range(NDC):
                    vop(ctx, lambda v, i=i, th=th: v.tensor_mul(
                        xrm[i][:, th:th + TC], xnf[i][:, :], r2bc[:, :]),
                        [rnl[i], u + "r2bc"], [xrmres[i]])
                kacc(ctx, ctx.pb[0][0:14, :TC],
                     [rwt[k][:, :] for k in range(NDC)],
                     [xnf[k][:, :] for k in range(NDC)],
                     [f"{u}rwt{k}" for k in range(NDC)] + rnl, ["psum0"])
                vop(ctx, lambda v: v.tensor_copy(lgt[0:14, :], ctx.pb[0][0:14, :TC]),
                    ["psum0"], [u + "lgt"])
                for tt in range(TC // 128):
                    transpose(ctx, ctx.pb[4][0:128, 0:14],
                              lgt[0:14, 128 * tt:128 * (tt + 1)],
                              identf[0:14, 0:14], [u + "lgt", u + "idf"], ["psum4"])
                    chain(ctx, [
                        lambda v: v.reduce_max(gcol[:, :], ctx.pb[4][0:128, 0:14],
                                               axis=AX, negate=True),
                    ], ["psum4"], [u + "gcol"])
                    vop(ctx, lambda sc: sc.activation(
                        gt[:, 0:14], ctx.pb[4][0:128, 0:14], AF.Exp,
                        bias=gcol[:, 0:1], accum_out=gcol2[:, 0:1]),
                        ["psum4", u + "gcol"], [u + "gt", u + "gcol2"], eng="scalar")
                    fns = [
                        lambda v: v.reciprocal(gcol2[:, :], gcol2[:, :]),
                        lambda v: v.tensor_scalar(gt[:, 0:14], gt[:, 0:14],
                                                  gcol2[:, 0:1], None, ALU.mult),
                    ]
                    for e in range(14):
                        fns.append(lambda v, e=e: v.tensor_scalar(
                            gtmp[:, 0:14], gt[:, 0:14], gt[:, e:e + 1], None,
                            ALU.is_gt, ALU.add, accum_out=gcnt[:, e:e + 1]))
                    fns += [
                        lambda v: v.tensor_scalar(gcnt[:, 0:14], gcnt[:, 0:14], 2.0,
                                                  None, ALU.is_lt),
                        lambda v: v.tensor_mul(gtmp[:, 0:14], gt[:, 0:14],
                                               gcnt[:, 0:14]),
                        lambda v: v.reduce_sum(gcol[:, 0:1], gtmp[:, 0:14], axis=AX),
                        lambda v: v.tensor_scalar_add(gcol[:, 0:1], gcol[:, 0:1], 1e-9),
                        lambda v: v.reciprocal(gcol[:, :], gcol[:, :]),
                        lambda v: v.tensor_scalar(gtmp[:, 0:14], gtmp[:, 0:14],
                                                  gcol[:, 0:1], None, ALU.mult),
                    ]
                    chain(ctx, fns, [u + "gt", u + "gcol2"],
                          [u + "gtmp", u + "gcnt", u + "gcol", u + "gcol2", u + "gt"])
                    transpose(ctx, ctx.pb[4][0:16, 0:128], gtmp[:, 0:16],
                              identf[:, :], [u + "gtmp", u + "idf"], ["psum4"])
                    vop(ctx, lambda v, tcx=tcx, tt=tt: v.tensor_copy(
                        cwf[0:14, tcx * TC + 128 * tt:tcx * TC + 128 * (tt + 1)],
                        ctx.pb[4][0:14, 0:128]),
                        ["psum4"], [u + "cwf"])

            for i in range(NDC):
                vop(ctx, lambda v, i=i: v.memset(yac[i][:, :], 0.0), [],
                    [f"{u}ya{i}"])
            load_rms(0)
            p1 = load_w1_slice(0, 0)
            p2a = load_w2_half(0, 0)
            for e in range(16):
                routed = e < 14
                if e + 1 < 16:
                    load_rms(e + 1)
                if routed:
                    for tcx in range(NQC):
                        mm(ctx, [(ctx.pb[4][:, :TC], e14[:, 128 * e:128 * (e + 1)],
                                  cwf[0:14, tcx * TC:(tcx + 1) * TC], True, True)],
                           [u + "e14", u + "cwf"], ["psum4"])
                        vop(ctx, lambda v, tcx=tcx: v.tensor_copy(
                            cwb[:, tcx * TC:(tcx + 1) * TC], ctx.pb[4][:, :TC]),
                            ["psum4"], [u + "cwb"])
                for hc in range(4):
                    pcur = p1
                    if hc < 3:
                        p1 = load_w1_slice(e, hc + 1)
                    elif e + 1 < 16:
                        p1 = load_w1_slice(e + 1, 0)
                    for tcx in range(NQC):
                        th = tcx * TC
                        for o4 in range(4):
                            bank = o4 % 4
                            kacc(ctx, ctx.pb[bank][:, :TC],
                                 [w1t[pcur][k][:, 128 * o4:128 * (o4 + 1)]
                                  for k in range(NDC)],
                                 [xrm[k][:, th:th + TC] for k in range(NDC)],
                                 [f"{u}w1t{pcur}_{k}" for k in range(NDC)] + xrmres,
                                 [f"psum{bank}"])
                            hi = 4 * hc + o4
                            if routed:
                                vop(ctx, lambda sc, bank=bank: sc.activation(
                                    sil[:, :], ctx.pb[bank][:, :TC], AF.Silu),
                                    [f"psum{bank}"], [u + "sil"], eng="scalar")
                                vop(ctx, lambda v, hi=hi, th=th: v.tensor_mul(
                                    hsl[hi][:, th:th + TC], sil[:, :],
                                    cwb[:, th:th + TC]),
                                    [u + "sil", u + "cwb"], [f"{u}hsl{hi}"])
                            else:
                                vop(ctx, lambda sc, hi=hi, th=th, bank=bank:
                                    sc.activation(hsl[hi][:, th:th + TC],
                                                  ctx.pb[bank][:, :TC], AF.Silu),
                                    [f"psum{bank}"], [f"{u}hsl{hi}"], eng="scalar")
                p2b = load_w2_half(e, 1)
                for tcx in range(NQC):
                    th = tcx * TC
                    for oc in range(NDC):
                        kacc2(ctx, ctx.pb[oc][:, :TC],
                              [w2t[p2a][k][:, 128 * oc:128 * (oc + 1)]
                               for k in range(8)],
                              [hsl[k][:, th:th + TC] for k in range(8)],
                              [f"{u}w2t{p2a}_{k}" for k in range(8)] +
                              [f"{u}hsl{k}" for k in range(8)],
                              [f"psum{oc}"], start=True, stop=False)
                    if tcx == NQC - 1 and e + 1 < 16:
                        p2a_next = load_w2_half(e + 1, 0)
                    for oc in range(NDC):
                        kacc2(ctx, ctx.pb[oc][:, :TC],
                              [w2t[p2b][k][:, 128 * oc:128 * (oc + 1)]
                               for k in range(8)],
                              [hsl[8 + k][:, th:th + TC] for k in range(8)],
                              [f"{u}w2t{p2b}_{k}" for k in range(8)] +
                              [f"{u}hsl{8 + k}" for k in range(8)],
                              [f"psum{oc}"], start=False, stop=True)
                        vop(ctx, lambda v, oc=oc, th=th: v.tensor_add(
                            yac[oc][:, th:th + TC], yac[oc][:, th:th + TC],
                            ctx.pb[oc][:, :TC]),
                            [f"psum{oc}", f"{u}ya{oc}"], [f"{u}ya{oc}"])
                if e + 1 < 16:
                    p2a = p2a_next

            for tcx in range(NQC):
                t0 = qb + tcx * TC
                th = tcx * TC
                for i in range(NDC):
                    r = i % 2
                    dma_in(ctx, fxr[r][:, :], xin[i][:, t0:t0 + TC], f"{u}fxr{r}")
                    vop(ctx, lambda v, i=i, th=th, r=r: v.tensor_add(
                        fyv[r][:, :], fxr[r][:, :], yac[i][:, th:th + TC]),
                        [f"{u}fxr{r}", f"{u}ya{i}"], [f"{u}fyv{r}"])
                    dma_out(ctx, xout[i][:, t0:t0 + TC], fyv[r][:, :],
                            [f"{u}fyv{r}"], u + "fxo")



def emit_l2_ffn(ctx, xin, xout, W):
    nc = ctx.nc
    u = f"2f{ctx.uid()}_"
    h_scr = nc.dram_tensor(u + "h", [32, 128, T_FULL], BF16, kind="Internal").ap()
    g_scr = nc.dram_tensor(u + "g", [32, 128, T_FULL], BF16, kind="Internal").ap()

    with scope(ctx) as sb:
        xnb = [sb(f"{u}xn{i}", [128, T_FULL], BF16) for i in range(NDC)]
        xnres = [f"{u}xn{i}" for i in range(NDC)]
        xs = [sb(f"{u}x{i}", [128, TC], F32) for i in range(NDC)]
        ln = LNTmp(ctx, sb, u)
        for tcx in range(NTC):
            t0 = tcx * TC
            rxl = [f"{u}x{i}" for i in range(NDC)]
            for i in range(NDC):
                dma_in(ctx, xs[i][:, :], xin[i][:, t0:t0 + TC], rxl[i])
            emit_ln(ctx, ln, [xs[i][:, :] for i in range(NDC)], rxl,
                    [xnb[i][:, t0:t0 + TC] for i in range(NDC)], xnres)

        w1g = [[[sb(f"{u}aw1{p}_{k}_{j}", [128, 128], BF16) for k in range(NDC)]
                for j in range(4)] for p in range(2)]
        w2g = [[[sb(f"{u}aw2{p}_{k}_{j}", [128, 128], BF16) for k in range(NDC)]
                for j in range(4)] for p in range(2)]
        s1 = sb(u + "s1", [128, 32], F32)
        s2 = sb(u + "s2", [128, 32], F32)
        sil = sb(u + "sil", [128, TC], F32)
        hbf = [sb(f"{u}hbf{i}", [128, TC], BF16) for i in range(2)]
        dma_in(ctx, s1[:, :], W["f2_s1"], u + "s1")
        dma_in(ctx, s2[:, :], W["f2_s2"], u + "s2")

        def load_a(og):
            p = og % 2
            for j in range(4):
                oc = og * 4 + j
                for k in range(NDC):
                    dma_in(ctx, w1g[p][j][k][:, :],
                           W["f2_w1"][k][:, 128 * oc:128 * (oc + 1)],
                           f"{u}aw1{p}_{k}_{j}", eng="gpsimd")
                    dma_in(ctx, w2g[p][j][k][:, :],
                           W["f2_w2"][k][:, 128 * oc:128 * (oc + 1)],
                           f"{u}aw2{p}_{k}_{j}", eng="gpsimd")

        load_a(0)
        for og in range(8):
            p = og % 2
            if og + 1 < 8:
                load_a(og + 1)
            for tcx in range(NTC):
                t0 = tcx * TC
                for j in range(4):
                    oc = og * 4 + j
                    ba, bb = (2 * j) % 4, (2 * j + 1) % 4
                    kacc(ctx, ctx.pb[ba][:, :TC],
                         [w1g[p][j][k][:, :] for k in range(NDC)],
                         [xnb[k][:, t0:t0 + TC] for k in range(NDC)],
                         [f"{u}aw1{p}_{k}_{j}" for k in range(NDC)] + xnres,
                         [f"psum{ba}"])
                    kacc(ctx, ctx.pb[bb][:, :TC],
                         [w2g[p][j][k][:, :] for k in range(NDC)],
                         [xnb[k][:, t0:t0 + TC] for k in range(NDC)],
                         [f"{u}aw2{p}_{k}_{j}" for k in range(NDC)] + xnres,
                         [f"psum{bb}"])
                    vop(ctx, lambda s_, oc=oc, ba=ba: s_.activation(
                        sil[:, :], ctx.pb[ba][:, :TC], AF.Silu,
                        scale=s1[:, oc:oc + 1]),
                        [f"psum{ba}", u + "s1"], [u + "sil"], eng="scalar")
                    hb_ = hbf[oc % 2]
                    vop(ctx, lambda v, oc=oc, bb=bb, hb_=hb_: v.scalar_tensor_tensor(
                        hb_[:, :], ctx.pb[bb][:, :TC], s2[:, oc:oc + 1],
                        sil[:, :], ALU.mult, ALU.mult),
                        [f"psum{bb}", u + "s2", u + "sil"], [f"{u}hbf{oc % 2}"])
                    dma_out(ctx, h_scr[oc][:, t0:t0 + TC], hb_[:, :],
                            [f"{u}hbf{oc % 2}"], u + "hscr")

    def gelu_stage(src_scr, dst_scr, w_key, ws_key, stag):
        with scope(ctx) as sb2:
            wgt = [[[sb2(f"{u}{stag}w{p}_{k}_{j}", [128, 128], BF16)
                     for k in range(32)] for j in range(8)] for p in range(2)]
            hs = sb2(u + stag + "hs", [128, 32], F32)
            htile = [sb2(f"{u}{stag}h{k}", [128, TC], BF16) for k in range(32)]
            obf = [sb2(f"{u}{stag}obf{i}", [128, TC], BF16) for i in range(2)]
            dma_in(ctx, hs[:, :], W[ws_key], u + stag + "hs")

            def load_g(og):
                p = og % 2
                for j in range(8):
                    oc = og * 8 + j
                    for k in range(32):
                        dma_in(ctx, wgt[p][j][k][:, :],
                               W[w_key][k][:, 128 * oc:128 * (oc + 1)],
                               f"{u}{stag}w{p}_{k}_{j}", eng="gpsimd")

            load_g(0)
            for og in range(4):
                p = og % 2
                if og + 1 < 4:
                    load_g(og + 1)
                for tcx in range(NTC):
                    t0 = tcx * TC
                    hres = [f"{u}{stag}h{k}" for k in range(32)]
                    for k in range(32):
                        dma_in(ctx, htile[k][:, :], src_scr[k][:, t0:t0 + TC],
                               hres[k])
                    for j in range(8):
                        oc = og * 8 + j
                        bank = j % 4
                        kacc(ctx, ctx.pb[bank][:, :TC],
                             [wgt[p][j][k][:, :] for k in range(32)],
                             [htile[k][:, :] for k in range(32)],
                             [f"{u}{stag}w{p}_{k}_{j}" for k in range(32)] + hres,
                             [f"psum{bank}"])
                        ob_ = obf[oc % 2]
                        vop(ctx, lambda s_, oc=oc, bank=bank, ob_=ob_: s_.activation(
                            ob_[:, :], ctx.pb[bank][:, :TC], AF.Gelu,
                            scale=hs[:, oc:oc + 1]),
                            [f"psum{bank}", u + stag + "hs"],
                            [f"{u}{stag}obf{oc % 2}"], eng="scalar")
                        dma_out(ctx, dst_scr[oc][:, t0:t0 + TC], ob_[:, :],
                                [f"{u}{stag}obf{oc % 2}"], u + stag + "dst")

    gelu_stage(h_scr, g_scr, "f2_h1", "f2_h1s", "b")
    gelu_stage(g_scr, h_scr, "f2_h2", "f2_h2s", "c")

    with scope(ctx) as sb2:
        w3t = [[sb2(f"{u}dw{k}_{j}", [128, 128], BF16) for k in range(32)]
               for j in range(8)]
        s3 = sb2(u + "s3", [128, 8], F32)
        htile = [sb2(f"{u}dh{k}", [128, TC], BF16) for k in range(32)]
        xr = [sb2(f"{u}dxr{i}", [128, TC], F32) for i in range(2)]
        yv = [sb2(f"{u}yv{i}", [128, TC], F32) for i in range(2)]
        dma_in(ctx, s3[:, :], W["f2_w3s"], u + "s3")
        for j in range(8):
            for k in range(32):
                dma_in(ctx, w3t[j][k][:, :], W["f2_w3"][k][:, 128 * j:128 * (j + 1)],
                       f"{u}dw{k}_{j}", eng="gpsimd")
        for tcx in range(NTC):
            t0 = tcx * TC
            hres = [f"{u}dh{k}" for k in range(32)]
            for k in range(32):
                dma_in(ctx, htile[k][:, :], h_scr[k][:, t0:t0 + TC], hres[k])
            for oc in range(8):
                bank = oc % 4
                kacc(ctx, ctx.pb[bank][:, :TC],
                     [w3t[oc][k][:, :] for k in range(32)],
                     [htile[k][:, :] for k in range(32)],
                     [f"{u}dw{k}_{oc}" for k in range(32)] + hres,
                     [f"psum{bank}"])
                r = oc % 2
                dma_in(ctx, xr[r][:, :], xin[oc][:, t0:t0 + TC], f"{u}dxr{r}")
                vop(ctx, lambda v, oc=oc, bank=bank, r=r: v.scalar_tensor_tensor(
                    yv[r][:, :], ctx.pb[bank][:, :TC], s3[:, oc:oc + 1],
                    xr[r][:, :], ALU.mult, ALU.add),
                    [f"psum{bank}", u + "s3", f"{u}dxr{r}"], [f"{u}yv{r}"])
                dma_out(ctx, xout[oc][:, t0:t0 + TC], yv[r][:, :],
                        [f"{u}yv{r}"], u + "xo")



def emit_l3_ffn(ctx, xin, xout, W):
    nc = ctx.nc
    with contextlib.ExitStack() as bs:
        def sb(n, s, d):
            return bs.enter_context(nc.sbuf_tensor(n, s, d))
        tag = "f3_"

        wq = [sb(f"f3_w{i}", [128, 1024], F32) for i in range(8)]
        wdT = [sb(f"f3_wT{i}", [128, 1024], F32) for i in range(8)]
        ident = sb("f3_id", [128, 128], F32)
        bias = sb("f3_b", [128, 8], F32)
        xs = [sb(f"f3_x{i}", [128, TC], F32) for i in range(NDC)]
        xn = [sb(f"f3_xn{i}", [128, TC], F32) for i in range(NDC)]
        xnb = [sb(f"f3_xnb{i}", [128, TC], BF16) for i in range(NDC)]
        yout = [sb(f"f3_y{i}", [128, TC], F32) for i in range(NDC)]
        qt = QTmp(sb, tag, 1024)
        ln = LNTmp(ctx, sb, tag)

        dma_in(ctx, ident[:, :], W["ident_f32"], "f3_id")
        dma_in(ctx, bias[:, :], W["w3f_b"], "f3_b")
        for i in range(8):
            dma_in(ctx, wq[i][:, :], W["w3f_w"][i], f"f3_w{i}", eng="gpsimd")
            emit_quant(ctx, qt, wq[i][:, :], wq[i][:, :], 1024, "free",
                       [f"f3_w{i}"], [f"f3_w{i}"])
        for i in range(8):
            for j in range(8):
                transpose(ctx, ctx.pb[4][:, 0:128], wq[i][:, 128 * j:128 * (j + 1)],
                          ident[:, :], [f"f3_w{i}", "f3_id"], ["psum4"])
                vop(ctx, lambda v, i=i, j=j: v.tensor_copy(
                    wdT[j][:, 128 * i:128 * (i + 1)], ctx.pb[4][:, 0:128]),
                    ["psum4"], [f"f3_wT{j}"])

        for tcx in range(NTC):
            t0 = tcx * TC
            rxl = [f"f3_x{i}" for i in range(NDC)]
            rnl = [f"f3_xn{i}" for i in range(NDC)]
            for i in range(NDC):
                dma_in(ctx, xs[i][:, :], xin[i][:, t0:t0 + TC], rxl[i])
            emit_ln(ctx, ln, [xs[i][:, :] for i in range(NDC)], rxl,
                    [xnb[i][:, :] for i in range(NDC)], [f"f3_xnb{i}" for i in range(NDC)],
                    out_f32=[xn[i][:, :] for i in range(NDC)], out_f32_resl=rnl)
            for k in range(NDC):
                emit_quant(ctx, qt, xn[k][:, :], xn[k][:, :], TC, "part",
                           [rnl[k]], [rnl[k]])
                for oc in range(8):
                    mm(ctx, [(ctx.pb[oc][:, :TC],
                              wdT[k][:, 128 * oc:128 * (oc + 1)], xn[k][:, :],
                              k == 0, k == NDC - 1)],
                       [rnl[k], f"f3_wT{k}"], [f"psum{oc}"])
            for oc in range(8):
                vop(ctx, lambda v, oc=oc: v.scalar_tensor_tensor(
                    yout[oc][:, :], ctx.pb[oc][:, :TC], bias[:, oc:oc + 1],
                    xs[oc][:, :], ALU.add, ALU.add),
                    [f"psum{oc}", "f3_b", rxl[oc]], [f"f3_y{oc}"])
                dma_out(ctx, xout[oc][:, t0:t0 + TC], yout[oc][:, :],
                        [f"f3_y{oc}"], f"f3_xo{tcx}")



def host_weight_layouts(inputs):
    W = {}
    f = np.ascontiguousarray
    W["a1_wq"] = f(inputs["L1_attn_W_q"].T.reshape(8, 128, 1024))
    W["a1_wk"] = f(inputs["L1_attn_W_k"].T.reshape(8, 128, 1024))
    W["a1_wv"] = f(inputs["L1_attn_W_v"].T.reshape(8, 128, 1024))
    W["a1_wo"] = f(inputs["L1_attn_W_o"].T.reshape(8, 128, 1024))
    _i = np.arange(32, dtype=np.float64)
    _t = np.minimum(np.arange(T_FULL), 2047).astype(np.float64)
    _ang = _t[None, :] * (10000.0 ** (-2.0 * _i / 64))[:, None]
    _c = np.cos(_ang)
    _s = np.sin(_ang)
    _p = np.arange(128)
    _iofp = (_p % 64) // 2
    cB = _c[_iofp]
    sB = _s[_iofp] * np.where(_p % 2 == 0, -1.0, 1.0)[:, None]
    W["a1_cos"] = f(cB.astype(np.float32).astype(bfdt))
    W["a1_sin"] = f(sB.astype(np.float32).astype(bfdt))
    Pswap = np.zeros((128, 128), dtype=np.float32)
    Pswap[np.arange(128), np.arange(128) ^ 1] = 1.0
    W["a1_pswap"] = f(Pswap.astype(bfdt))
    def _bd(m):
        out = np.zeros((8, 128, 128), dtype=np.float32)
        for p in range(8):
            out[p, 0:64, 0:64] = m[2 * p].T
            out[p, 64:128, 64:128] = m[2 * p + 1].T
        return out
    W["a3_wd"] = f(inputs["L3_attn_W_down"].reshape(8, 128, 1024))
    W["a3_tk"] = f(_bd(inputs["L3_attn_tokeys"]))
    W["a3_tq"] = f(_bd(inputs["L3_attn_toqueries"]))
    W["a3_tv"] = f(_bd(inputs["L3_attn_tovalues"]))
    W["a3_wu"] = f(inputs["L3_attn_W_unify"].reshape(8, 128, 1024))
    W["a3_bu"] = f(inputs["L3_attn_b_unify"].reshape(8, 128).T)
    W["a0_wqkv"] = f(inputs["L0_attn_w_qkv"].reshape(8, 128, 3072))
    W["a0_wg"] = f(inputs["L0_attn_w_gate"].reshape(8, 128, 1024))
    W["a0_wp"] = f(inputs["L0_attn_w_proj"].reshape(8, 128, 1024))
    W["a0_bqkv"] = f(inputs["L0_attn_b_qkv"].reshape(24, 128).T)
    W["a0_bg"] = f(inputs["L0_attn_b_gate"].reshape(8, 128).T)
    W["a0_bp"] = f(inputs["L0_attn_b_proj"].reshape(8, 128).T)
    W["a0_lg"] = f(inputs["L0_attn_ln_g"].reshape(8, 128).T)
    W["a0_lb"] = f(inputs["L0_attn_ln_b"].reshape(8, 128).T)
    s_ = np.arange(128)[:, None]
    t_ = np.arange(512)[None, :]
    W["maskA"] = f(np.concatenate([(128 * j + s_ <= t_) for j in range(4)],
                                  axis=1).astype(np.float32).astype(bfdt))
    W["f0_rw"] = f(inputs["L0_ffn_router_weight"].T.reshape(8, 128, 16))
    W["f0_eb"] = f(inputs["L0_ffn_e_bias"].reshape(16, 1))
    W["f0_gw"] = f(inputs["L0_ffn_gate_w"].transpose(0, 2, 1).reshape(16, 8, 128, 512))
    W["f0_uw"] = f(inputs["L0_ffn_up_w"].transpose(0, 2, 1).reshape(16, 8, 128, 512))
    W["f0_dw"] = f(inputs["L0_ffn_down_w"].transpose(0, 2, 1).reshape(16, 4, 128, 1024))
    W["f0_sg"] = f(inputs["L0_ffn_sh_gate_w"].T.reshape(8, 128, 1024))
    W["f0_su"] = f(inputs["L0_ffn_sh_up_w"].T.reshape(8, 128, 1024))
    W["f0_sd"] = f(inputs["L0_ffn_sh_down_w"].T.reshape(8, 128, 1024))
    E16 = np.zeros((16, 16 * 128), dtype=np.float32)
    for e in range(16):
        E16[e, 128 * e:128 * (e + 1)] = 1.0
    W["E16"] = f(E16.astype(bfdt))
    W["f1_rw"] = f(inputs["L1_ffn_router_w"].reshape(8, 128, 14))
    W["f1_rms"] = f(inputs["L1_ffn_rms_w"].reshape(14, 8, 128).transpose(0, 2, 1))
    W["f1_rms_s"] = f(inputs["L1_ffn_rms_w_s"].reshape(2, 8, 128).transpose(0, 2, 1))
    W["f1_w1"] = f(inputs["L1_ffn_w1"].reshape(14, 8, 128, 2048))
    W["f1_w2"] = f(inputs["L1_ffn_w2"].reshape(14, 16, 128, 1024))
    W["f1_w1_s"] = f(inputs["L1_ffn_w1_s"].reshape(2, 8, 128, 2048))
    W["f1_w2_s"] = f(inputs["L1_ffn_w2_s"].reshape(2, 16, 128, 1024))
    E14 = np.zeros((14, 14 * 128), dtype=np.float32)
    for e in range(14):
        E14[e, 128 * e:128 * (e + 1)] = 1.0
    W["E14"] = f(E14.astype(bfdt))
    W["a2_wk"] = f(inputs["L2_attn_W_K"].transpose(2, 0, 1).reshape(1024, 1024).reshape(8, 128, 1024))
    W["a2_wq"] = f(inputs["L2_attn_W_Q"].transpose(2, 0, 1).reshape(1024, 1024).reshape(8, 128, 1024))
    W["a2_wv"] = f(inputs["L2_attn_W_V"].transpose(2, 0, 1).reshape(1024, 1024).reshape(8, 128, 1024))
    q_ = np.arange(128)[:, None]
    k_ = np.arange(512)[None, :]
    W["maskQ"] = f(np.concatenate([(k_ > 128 * j + q_) * np.float32(-1e30)
                                   for j in range(4)], axis=1).astype(bfdt))
    kk_ = np.arange(128)[:, None]
    qq_ = np.arange(512)[None, :]
    W["maskT"] = f(np.concatenate([(qq_ < 128 * j + kk_) * np.float32(-1e30)
                                   for j in range(4)], axis=1).astype(bfdt))
    W["f2_w1"] = f(inputs["L2_ffn_w1_w"].T.reshape(8, 128, 4096))
    W["f2_w2"] = f(inputs["L2_ffn_w2_w"].T.reshape(8, 128, 4096))
    W["f2_h1"] = f(inputs["L2_ffn_h1_w"].T.reshape(32, 128, 4096))
    W["f2_h2"] = f(inputs["L2_ffn_h2_w"].T.reshape(32, 128, 4096))
    W["f2_w3"] = f(inputs["L2_ffn_w3_w"].T.reshape(32, 128, 1024))
    W["f2_s1"] = f(inputs["L2_ffn_w1_s"].reshape(32, 128).T)
    W["f2_s2"] = f(inputs["L2_ffn_w2_s"].reshape(32, 128).T)
    W["f2_h1s"] = f(inputs["L2_ffn_h1_s"].reshape(32, 128).T)
    W["f2_h2s"] = f(inputs["L2_ffn_h2_s"].reshape(32, 128).T)
    W["f2_w3s"] = f(inputs["L2_ffn_w3_s"].reshape(8, 128).T)
    W["w3f_w"] = f(inputs["L3_ffn_weight"].reshape(8, 128, 1024))
    W["w3f_b"] = f(inputs["L3_ffn_bias"].reshape(8, 128).T)
    W["ident_f32"] = np.eye(128, dtype=np.float32)
    W["ident_bf"] = np.eye(128, dtype=np.float32).astype(bfdt)
    return W


def declare_weights(nc, W):
    aps = {}
    for k, v in W.items():
        aps[k] = nc.dram_tensor(k, list(v.shape), mybir.dt.from_np(v.dtype),
                                kind="ExternalInput").ap()
    return aps



BLOCK_SEQ = ["l0_attn", "l0_ffn", "l1_attn", "l1_ffn",
             "l2_attn", "l2_ffn", "l3_attn", "l3_ffn"]
BLOCK_WKEYS = {
    "l0_attn": ["a0_wqkv", "a0_wg", "a0_wp", "a0_bqkv", "a0_bg", "a0_bp",
                "a0_lg", "a0_lb", "maskA", "ident_bf"],
    "l0_ffn": ["f0_rw", "f0_eb", "f0_gw", "f0_uw", "f0_dw", "f0_sg", "f0_su",
               "f0_sd", "E16", "ident_f32"],
    "l1_attn": ["a1_wq", "a1_wk", "a1_wv", "a1_wo", "a1_cos", "a1_sin",
                "a1_pswap", "maskQ", "maskT", "ident_bf", "ident_f32"],
    "l1_ffn": ["f1_rw", "f1_rms", "f1_rms_s", "f1_w1", "f1_w2", "f1_w1_s",
               "f1_w2_s", "E14", "ident_f32"],
    "l2_attn": ["a2_wk", "a2_wq", "a2_wv", "maskQ", "maskT", "ident_bf",
                "ident_f32"],
    "l2_ffn": ["f2_w1", "f2_w2", "f2_h1", "f2_h2", "f2_w3", "f2_s1", "f2_s2",
               "f2_h1s", "f2_h2s", "f2_w3s"],
    "l3_attn": ["a3_wd", "a3_tk", "a3_tq", "a3_tv", "a3_wu", "a3_bu", "maskQ",
                "maskT", "ident_bf", "ident_f32"],
    "l3_ffn": ["w3f_w", "w3f_b", "ident_f32"],
}
BLOCK_EMITTERS = {"l0_attn": emit_l0_attn, "l0_ffn": emit_l0_ffn,
                  "l1_attn": emit_l1_attn, "l2_attn": emit_l2_attn,
                  "l3_attn": emit_l3_attn, "l1_ffn": emit_l1_ffn,
                  "l2_ffn": emit_l2_ffn, "l3_ffn": emit_l3_ffn}


def build_single_block(block_name, W):
    nc = bass.Bass("TRN2", target_bir_lowering=False)
    xin_t = nc.dram_tensor("xin", [NDC, 128, T_FULL], F32, kind="ExternalInput")
    xout_t = nc.dram_tensor("xout", [NDC, 128, T_FULL], F32, kind="ExternalOutput")
    Wsub = {k: W[k] for k in BLOCK_WKEYS[block_name]}
    waps = declare_weights(nc, Wsub)
    with contextlib.ExitStack() as stack:
        ctx = Ctx(nc, stack)
        xin = [xin_t.ap()[i] for i in range(NDC)]
        xout = [xout_t.ap()[i] for i in range(NDC)]
        BLOCK_EMITTERS[block_name](ctx, xin, xout, waps)
        ctx.p.wait_all()
        ctx.p.replay()
    return nc


def build_fused(W):
    nc = bass.Bass("TRN2", target_bir_lowering=False)
    xin_t = nc.dram_tensor("xin", [NDC, 128, T_FULL], F32, kind="ExternalInput")
    xout_t = nc.dram_tensor("xout", [NDC, 128, T_FULL], F32, kind="ExternalOutput")
    xa_t = nc.dram_tensor("xscrA", [NDC, 128, T_FULL], F32, kind="Internal")
    xb_t = nc.dram_tensor("xscrB", [NDC, 128, T_FULL], F32, kind="Internal")
    waps = declare_weights(nc, W)
    hop = [xa_t.ap(), xb_t.ap()]
    with contextlib.ExitStack() as stack:
        ctx = Ctx(nc, stack)
        nblk = len(BLOCK_SEQ)
        for bi, name in enumerate(BLOCK_SEQ):
            src = xin_t.ap() if bi == 0 else hop[bi % 2]
            dst = xout_t.ap() if bi == nblk - 1 else hop[(bi + 1) % 2]
            xin = [src[i] for i in range(NDC)]
            xout = [dst[i] for i in range(NDC)]
            BLOCK_EMITTERS[name](ctx, xin, xout, waps)
        ctx.p.wait_all()
        ctx.p.replay()
    return nc


def kernel(**inputs):
    x = inputs["L0_attn_x"]
    W = host_weight_layouts(inputs)
    xs = [np.ascontiguousarray(x[b].T).reshape(NDC, 128, T_FULL) for b in range(8)]
    nc = build_fused(W)
    in_maps = [dict(W, xin=xs[b]) for b in range(8)]
    res = run_bass_kernel_spmd(nc, in_maps, core_ids=list(range(8)))
    out = np.stack([r["xout"].reshape(D, T_FULL).T for r in res.results])
    return out


def kernel_unfused(**inputs):
    x = inputs["L0_attn_x"]
    W = host_weight_layouts(inputs)
    xs = [np.ascontiguousarray(x[b].T).reshape(NDC, 128, T_FULL) for b in range(8)]
    cur = xs
    for name in BLOCK_SEQ:
        nc = build_single_block(name, W)
        Wsub = {k: W[k] for k in BLOCK_WKEYS[name]}
        in_maps = [dict(Wsub, xin=cur[b]) for b in range(8)]
        res = run_bass_kernel_spmd(nc, in_maps, core_ids=list(range(8)))
        cur = [r["xout"].reshape(NDC, 128, T_FULL) for r in res.results]
    out = np.stack([c.reshape(D, T_FULL).T for c in cur])
    return out
